# version 08
# baseline (speedup 1.0000x reference)
import numpy as np
import concourse.bass as bass
import concourse.mybir as mybir
from concourse.bass_utils import run_bass_kernel_spmd
from concourse.alu_op_type import AluOpType as OP
import ml_dtypes

F32 = mybir.dt.float32
BF16 = mybir.dt.bfloat16
AF = mybir.ActivationFunctionType
AX = mybir.AxisListType.X

B = 8
D = 1024
T = 2048
KC = 8
HA = 16
HD = 64
DEPTH = 2
HF = 8
WIN = 512
NW = T // WIN
HID = 2730
HIDP = 2816
NM = HIDP // 128
NEGBIG = -3.0e38
LN2_EPS = 1e-5
RMS_EPS = 1e-6

ENGINES = ("sync", "gpsimd", "tensor", "vector", "scalar")


class Prog:
    def __init__(self, nc):
        self.nc = nc
        self.items = {e: [] for e in ENGINES}
        self.sems = {}
        self.counts = {}

    def sem(self, name):
        h = self.nc.semaphore(name)
        self.sems[name] = h.__enter__()
        self.counts[name] = 0

    def now(self, s):
        return self.counts[s]

    def emit(self, eng, fn, waits=(), inc=None, inc_amt=1):
        for s, v in waits:
            assert v <= self.counts[s], f"wait on future: {s} {v} > {self.counts[s]}"
        if inc is not None:
            self.counts[inc] += inc_amt
        waits = tuple((s, v) for s, v in waits if v > 0)
        sems = self.sems

        def item(engobj, fn=fn, waits=waits, inc=inc, inc_amt=inc_amt):
            for s, v in waits:
                engobj.wait_ge(sems[s], v)
            inst = fn(engobj)
            if inc is not None:
                inst.then_inc(sems[inc], inc_amt)
        self.items[eng].append(item)

    def dma(self, eng, out, in_, waits=(), inc=None):
        self.emit(eng, lambda e: e.dma_start(out=out, in_=in_), waits, inc, 16)

    def finish(self):
        nc = self.nc
        with nc.Block() as block:
            for e in ENGINES:
                lst = self.items[e]
                if not lst:
                    continue

                def body(engobj, lst=lst):
                    for item in lst:
                        item(engobj)
                getattr(block, e)(body)



def _stage_weights(iw):
    f32 = np.float32
    out = {}
    wq = np.asarray(iw["L0_attn_Wq"], f32)
    wqp = np.zeros((D, 1152), f32)
    wqp[:, :1040] = wq
    out["a_wq"] = np.ascontiguousarray(wqp)
    out["a_wk"] = np.ascontiguousarray(np.asarray(iw["L0_attn_Wk"], f32))
    out["a_wv"] = np.ascontiguousarray(np.asarray(iw["L0_attn_Wv"], f32))
    out["a_wo"] = np.ascontiguousarray(np.asarray(iw["L0_attn_Wo"], f32))
    wqkv = np.asarray(iw["L0_ffn_wqkv"], f32)
    s = wqkv.reshape(DEPTH, KC, 128, 3, 512).transpose(0, 3, 2, 1, 4)
    out["f_wqkv"] = np.ascontiguousarray(s.reshape(DEPTH, 3, 128, KC * 512))
    wo = np.asarray(iw["L0_ffn_wo"], f32)
    s = wo.reshape(DEPTH, 4, 128, 1024).transpose(0, 2, 1, 3)
    out["f_wo"] = np.ascontiguousarray(s.reshape(DEPTH, 128, 4 * 1024))
    w1 = np.asarray(iw["L0_ffn_w1"], f32)
    a = np.zeros((DEPTH, D, HIDP), f32); a[:, :, :HID] = w1[:, :, :HID]
    g = np.zeros((DEPTH, D, HIDP), f32); g[:, :, :HID] = w1[:, :, HID:]
    inter = np.empty((DEPTH, D, NM, 256), f32)
    inter[:, :, :, :128] = a.reshape(DEPTH, D, NM, 128)
    inter[:, :, :, 128:] = g.reshape(DEPTH, D, NM, 128)
    s = inter.transpose(0, 2, 1, 3).reshape(DEPTH, NM, KC, 128, 256).transpose(0, 1, 3, 2, 4)
    out["f_w1"] = np.ascontiguousarray(s.reshape(DEPTH, NM, 128, KC * 256))
    w2 = np.asarray(iw["L0_ffn_w2"], f32)
    w2p = np.zeros((DEPTH, HIDP, D), f32); w2p[:, :HID, :] = w2
    s = w2p.reshape(DEPTH, NM, 128, 2, 512).transpose(0, 3, 1, 2, 4)
    out["f_w2"] = np.ascontiguousarray(s)
    cols = []

    def addcol(v):
        v = np.asarray(v, f32)
        if v.ndim == 1:
            v = v[:, None]
        cols.append(v)
        return sum(c.shape[1] for c in cols) - v.shape[1]

    ix = {}
    bq = np.asarray(iw["L0_attn_bq"], f32)
    ix["a_bq"] = addcol(bq[:D].reshape(KC, 128).T)
    ix["a_bgate"] = addcol(np.pad(bq[D:], (0, 112)))
    ix["a_bk"] = addcol(np.asarray(iw["L0_attn_bk"], f32).reshape(KC, 128).T)
    ix["a_bv"] = addcol(np.asarray(iw["L0_attn_bv"], f32).reshape(KC, 128).T)
    ix["a_bo"] = addcol(np.asarray(iw["L0_attn_bo"], f32).reshape(KC, 128).T)
    ix["a_qn"] = addcol(np.tile(np.asarray(iw["L0_attn_qn_w"], f32), 2))
    ix["a_kn"] = addcol(np.tile(np.asarray(iw["L0_attn_kn_w"], f32), 2))
    qs = np.asarray(iw["L0_ffn_q_scale"], f32)
    ks = np.asarray(iw["L0_ffn_k_scale"], f32)
    ix["f_qs"] = addcol(np.tile(qs, (1, 2)).T.reshape(128, DEPTH, order="F") if False else np.stack([np.tile(qs[l], 2) for l in range(DEPTH)], 1))
    ix["f_ks"] = addcol(np.stack([np.tile(ks[l], 2) for l in range(DEPTH)], 1))
    lng = np.asarray(iw["L0_ffn_ln_g"], f32)
    lnb = np.asarray(iw["L0_ffn_ln_b"], f32)
    ix["f_lng"] = addcol(np.concatenate([lng[l].reshape(KC, 128).T for l in range(DEPTH)], 1))
    ix["f_lnb"] = addcol(np.concatenate([lnb[l].reshape(KC, 128).T for l in range(DEPTH)], 1))
    b1 = np.asarray(iw["L0_ffn_b1"], f32)
    b1a = np.zeros((DEPTH, HIDP), f32); b1a[:, :HID] = b1[:, :HID]
    b1g = np.zeros((DEPTH, HIDP), f32); b1g[:, :HID] = b1[:, HID:]
    ix["f_b1a"] = addcol(np.concatenate([b1a[l].reshape(NM, 128).T for l in range(DEPTH)], 1))
    ix["f_b1g"] = addcol(np.concatenate([b1g[l].reshape(NM, 128).T for l in range(DEPTH)], 1))
    b2 = np.asarray(iw["L0_ffn_b2"], f32)
    ix["f_b2"] = addcol(np.concatenate([b2[l].reshape(KC, 128).T for l in range(DEPTH)], 1))
    ix["a_qn_bc"] = addcol(np.broadcast_to(np.asarray(iw["L0_attn_qn_w"], f32), (128, HD)))
    ix["a_kn_bc"] = addcol(np.broadcast_to(np.asarray(iw["L0_attn_kn_w"], f32), (128, HD)))
    ix["f_qs_bc"] = addcol(np.concatenate([np.broadcast_to(qs[l], (128, HD)) for l in range(DEPTH)], 1))
    ix["f_ks_bc"] = addcol(np.concatenate([np.broadcast_to(ks[l], (128, HD)) for l in range(DEPTH)], 1))
    vecs = np.concatenate(cols, 1)
    out["vecs"] = np.ascontiguousarray(vecs)
    out["bvb"] = np.ascontiguousarray(np.broadcast_to(np.asarray(iw["L0_attn_bv"], f32), (128, D)))
    return out, ix, vecs.shape[1]



def build_program(nv_cols, vix, stages=("ffn0", "attn", "ffn1"), dbg_tap=None, attn_groups=(0, 1)):
    dbg_tap = dbg_tap
    nc = bass.Bass("TRN2", target_bir_lowering=False)
    PI = 3.141592653589793
    xin = nc.dram_tensor("xin", [T, D], F32, kind="ExternalInput").ap()
    d_wq = nc.dram_tensor("a_wq", [D, 1152], F32, kind="ExternalInput").ap()
    d_wk = nc.dram_tensor("a_wk", [D, D], F32, kind="ExternalInput").ap()
    d_wv = nc.dram_tensor("a_wv", [D, D], F32, kind="ExternalInput").ap()
    d_wo = nc.dram_tensor("a_wo", [D, D], F32, kind="ExternalInput").ap()
    d_fqkv = nc.dram_tensor("f_wqkv", [DEPTH, 3, 128, KC * 512], F32, kind="ExternalInput").ap()
    d_fwo = nc.dram_tensor("f_wo", [DEPTH, 128, 4 * 1024], F32, kind="ExternalInput").ap()
    d_fw1 = nc.dram_tensor("f_w1", [DEPTH, NM, 128, KC * 256], F32, kind="ExternalInput").ap()
    d_fw2 = nc.dram_tensor("f_w2", [DEPTH, 2, NM, 128, 512], F32, kind="ExternalInput").ap()
    d_vecs = nc.dram_tensor("vecs", [128, nv_cols], F32, kind="ExternalInput").ap()
    d_bvb = nc.dram_tensor("bvb", [128, D], F32, kind="ExternalInput").ap()
    d_out = nc.dram_tensor("out", [T, D], F32, kind="ExternalOutput").ap()
    d_tabs = nc.dram_tensor("tabs", [4, 128, T], BF16, kind="Internal").ap()

    ctx = []

    def sb(name, shape, dt):
        t = nc.sbuf_tensor(name, shape, dt); ctx.append(t); return t.__enter__()

    xfm = sb("xfm", [128, KC * T], F32)
    zfm = sb("zfm", [128, KC * (T + 1)], F32)
    hbf_t = sb("hbf", [128, 8192], BF16)
    wst_t = sb("wst", [128, 8192], BF16)
    pool = sb("pool", [128, 10088], F32)
    rows = sb("rows", [128, 1024], F32)
    vecs = sb("vecs_sb", [128, nv_cols], F32)
    pcol = sb("pcol", [128, 8], F32)
    cb = sb("cb", [128, 520], BF16)
    e2tf = sb("e2tf", [2, 128], F32)
    onesf = sb("onesf", [128, 128], F32)
    identg = sb("identg", [16, 16], F32)
    epsc = sb("epsc", [128, 4], F32)
    pst = nc.psum_tensor("pst", [128, 4096], F32); ctx.append(pst); ps = pst.__enter__()

    poolb = pool.bitcast(BF16)
    zfmb = zfm.bitcast(BF16)

    def bank(i, n=1):
        return ps[:, i * 512:(i + n) * 512]

    mgt = cb[:, 0:128]; mlt = cb[:, 128:256]; identb = cb[:, 256:384]; e2b = cb[:, 384:386]

    PC_BA, PC_BF0, PC_BF1, PC_QS80, PC_QS81, PC_QN8 = 0, 1, 2, 3, 4, 5

    def zc(c, t0, n):
        return zfm[:, c * (T + 1) + 1 + t0: c * (T + 1) + 1 + t0 + n]

    def xc(c, t0, n):
        return xfm[:, c * T + t0: c * T + t0 + n]

    V = lambda name: vecs[:, vix[name]:vix[name] + 1]

    def Vs(name, off, n=1):
        return vecs[:, vix[name] + off: vix[name] + off + n]

    p0 = Prog(nc)
    for s in ("LS", "LW", "MM", "VE", "SE", "GP"):
        p0.sem(s)
    P = p0

    def t_(fn, waits=(), inc=None):
        P.emit("tensor", fn, waits, inc)

    def v_(fn, waits=(), inc=None):
        P.emit("vector", fn, waits, inc)

    def s_(fn, waits=(), inc=None):
        P.emit("scalar", fn, waits, inc)

    def g_(fn, waits=(), inc=None):
        P.emit("gpsimd", fn, waits, inc)

    p0.dma("sync", vecs[:], d_vecs, inc="LS")
    io128 = pool[:, 2048:2176]
    ident = pool[:, 2176:2304]
    iop = pool[:, 2304:2305]
    g_(lambda e: e.iota(io128, [[1, 128]], channel_multiplier=-1, allow_small_or_imprecise_dtypes=True), inc="GP")
    g_(lambda e: e.iota(iop, [[1, 1]], channel_multiplier=1, allow_small_or_imprecise_dtypes=True), inc="GP")
    gp0 = P.now("GP")
    v_(lambda e: e.tensor_scalar(ident, io128, 0.0, 1.0, op0=OP.is_equal, op1=OP.mult), waits=[("GP", gp0)])
    v_(lambda e: e.tensor_scalar(identg[:, :], io128[0:16, 0:16], 0.0, 1.0, op0=OP.is_equal, op1=OP.mult),
       waits=[("GP", gp0)])
    v_(lambda e: e.tensor_scalar(mgt, io128, 0.0, NEGBIG, op0=OP.is_gt, op1=OP.mult), waits=[("GP", gp0)])
    v_(lambda e: e.tensor_scalar(mlt, io128, 0.0, NEGBIG, op0=OP.is_lt, op1=OP.mult), waits=[("GP", gp0)])
    v_(lambda e: e.tensor_copy(identb, ident), waits=[("GP", gp0)])
    g_(lambda e: e.tensor_scalar(e2b[:, 0:1], iop, 64.0, 1.0, op0=OP.is_lt, op1=OP.mult), waits=[("GP", gp0)])
    g_(lambda e: e.tensor_scalar(e2b[:, 1:2], iop, 64.0, 1.0, op0=OP.is_ge, op1=OP.mult), waits=[("GP", gp0)])
    v_(lambda e: e.memset(onesf[:], 1.0))
    v_(lambda e: e.memset(epsc[:, 0:1], 1e-5))
    v_(lambda e: e.memset(epsc[:, 1:2], 1e-30))
    v_(lambda e: e.memset(epsc[:, 2:3], 1e-6))
    v_(lambda e: e.memset(epsc[:, 3:4], -80.0))
    io2r = pool[:, 2305:2433]
    g_(lambda e: e.iota(pool[0:2, 2305:2433], [[1, 128]], channel_multiplier=64, allow_small_or_imprecise_dtypes=True), inc="GP")
    gp1 = P.now("GP")
    v_(lambda e: e.tensor_scalar(e2tf[:, :], pool[0:2, 2305:2433], 64.0, 1.0, op0=OP.is_lt, op1=OP.mult), waits=[("GP", gp1)])
    v_(lambda e: e.tensor_scalar(pool[0:2, 2433:2561], pool[0:2, 2305:2433], 128.0, 1.0, op0=OP.is_ge, op1=OP.mult), waits=[("GP", gp1)])
    v_(lambda e: e.tensor_tensor(e2tf[:, :], e2tf[:, :], pool[0:2, 2433:2561], op=OP.add), inc="VE")
    tin0 = pool[:, 0:1024]
    tin1 = pool[:, 1024:2048]
    v_(lambda e: e.engine_nop() if False else e.memset(rows[:, 0:1], 0.0), inc="VE")
    ve_ident = P.now("VE")
    mm_done_tin = [0, 0]
    for tc in range(16):
        tin = (tin0, tin1)[tc % 2]
        P.dma("sync", tin, xin[tc * 128:(tc + 1) * 128, :],
              waits=[("MM", mm_done_tin[tc % 2])], inc="LS")
        ls_need = P.now("LS")
        for k in range(KC):
            bk_ = bank(k % 2)
            t_(lambda e, o=bk_[:, 0:128], i=tin[:, k * 128:(k + 1) * 128]:
               e.transpose(o, i, ident),
               waits=([("LS", ls_need), ("VE", max(ve_ident, P.now("VE")))] if k == 0 else
                      [("VE", P.now("VE"))]),
               inc="MM")
            v_(lambda e, o=xfm[:, k * T + tc * 128: k * T + (tc) * 128 + 128], i=bk_[:, 0:128]:
               e.tensor_copy(o, i),
               waits=[("MM", P.now("MM"))], inc="VE")
        mm_done_tin[tc % 2] = P.now("MM")
    rotb = cb[:, 387:515]
    iof = pool[:, 2561:2689]
    fm64 = pool[:, 2689:2817]
    sA = pool[:, 2817:2945]
    sB = pool[:, 2945:3073]
    g_(lambda e: e.iota(iof, [[1, 128]], channel_multiplier=0, allow_small_or_imprecise_dtypes=True), inc="GP")
    gp2 = P.now("GP")
    v_(lambda e: e.tensor_scalar(fm64, iof, 64.0, 64.0, op0=OP.is_ge, op1=OP.mult), waits=[("GP", gp2)])
    v_(lambda e: e.tensor_tensor(fm64, iof, fm64, op=OP.subtract))
    v_(lambda e: e.tensor_single_scalar(sA, io128, -32.0, op=OP.is_equal), waits=[("GP", gp2)])
    v_(lambda e: e.tensor_scalar(sB, fm64, 32.0, -1.0, op0=OP.is_lt, op1=OP.mult))
    v_(lambda e: e.tensor_tensor(sA, sA, sB, op=OP.mult))
    v_(lambda e: e.tensor_single_scalar(sB, io128, 32.0, op=OP.is_equal))
    v_(lambda e: e.tensor_copy(rotb, sA))
    v_(lambda e: e.tensor_scalar(sA, fm64, 32.0, 1.0, op0=OP.is_ge, op1=OP.mult))
    v_(lambda e: e.tensor_tensor(sA, sA, sB, op=OP.mult))
    v_(lambda e: e.tensor_tensor(pool[:, 2817:2945], sA, sA, op=OP.bypass) if False else
       e.tensor_tensor(sB, sA, sA, op=OP.mult))
    v_(lambda e: e.scalar_tensor_tensor(rotb, sA, 1.0, rotb, op0=OP.mult, op1=OP.add), inc="VE")
    iop_m = pool[:, 3072:3073]
    invf = pool[:, 3073:3074]
    lnb_ = pool[:, 3074:3075]
    trow = pool[:, 3584:5632]
    fr = pool[:, 5632:7680]
    tA = pool[:, 7680:9728]
    sin_t = rows[:, 0:1024]
    LN1E4_32 = 9.210340371976184 / 32.0
    _t32 = pool[:, 3075:3076]
    g_(lambda e: e.tensor_scalar(iop_m, iop, 32.0, 32.0, op0=OP.is_ge, op1=OP.mult), waits=[("GP", gp0)])
    g_(lambda e: e.tensor_scalar(_t32, iop, 64.0, 32.0, op0=OP.is_ge, op1=OP.mult))
    g_(lambda e: e.tensor_tensor(iop_m, iop_m, _t32, op=OP.add))
    g_(lambda e: e.tensor_scalar(_t32, iop, 96.0, 32.0, op0=OP.is_ge, op1=OP.mult))
    g_(lambda e: e.tensor_tensor(iop_m, iop_m, _t32, op=OP.add))
    g_(lambda e: e.tensor_tensor(iop_m, iop, iop_m, op=OP.subtract))
    g_(lambda e: e.tensor_scalar(lnb_, iop_m, 2.0 / 89.6, 25.6 / 89.6, op0=OP.mult, op1=OP.add), inc="GP")
    s_(lambda e: e.activation(invf, iop_m, AF.Exp, bias=0.0, scale=-LN1E4_32), waits=[("GP", P.now("GP"))])
    s_(lambda e: e.activation(lnb_, lnb_, AF.Ln, bias=0.0, scale=1.0), inc="SE")
    g_(lambda e: e.iota(trow, [[1, 2048]], channel_multiplier=0, allow_small_or_imprecise_dtypes=True), inc="GP")
    gp3 = P.now("GP")
    PIh = PI / 2.0
    TWO_PI = 2.0 * PI
    pooli = pool.bitcast(mybir.dt.int32)
    R_K = pool[:, 0:2048]
    R_Ki = pooli[:, 0:2048]
    tAb = poolb[:, 0:4096]
    se_tab = [P.now("SE")]
    C1 = 6.28125
    C2 = 6.283185307179586 - 6.28125
    INV2PI = 1.0 / 6.283185307179586

    def build_tab(idx, is_sin, inv_scale):
        ve_tin = P.now("VE")
        g_(lambda e: e.iota(trow, [[1, 2048]], channel_multiplier=0,
                            allow_small_or_imprecise_dtypes=True),
           waits=[("VE", P.now("VE"))], inc="GP")
        gp_t = P.now("GP")
        v_(lambda e: e.tensor_scalar_mul(fr, trow, invf),
           waits=[("GP", gp_t), ("SE", se_tab[0]), ("VE", ve_tin)])
        if not is_sin:
            v_(lambda e: e.tensor_single_scalar(fr, fr, PIh, op=OP.add))
        v_(lambda e: e.tensor_scalar(trow, trow, -1024.0, 1.0 / 256.0, op0=OP.add, op1=OP.mult))
        v_(lambda e: e.tensor_scalar_mul(tA, fr, INV2PI))
        v_(lambda e: e.tensor_copy(R_Ki, tA), waits=[("LS", P.now("LS"))])
        v_(lambda e: e.tensor_copy(R_K, R_Ki))
        v_(lambda e: e.scalar_tensor_tensor(fr, R_K, -C1, fr, op0=OP.mult, op1=OP.add))
        v_(lambda e: e.scalar_tensor_tensor(fr, R_K, -C2, fr, op0=OP.mult, op1=OP.add))
        v_(lambda e: e.tensor_scalar(tA, fr, PI, TWO_PI, op0=OP.is_gt, op1=OP.mult))
        v_(lambda e: e.tensor_tensor(fr, fr, tA, op=OP.subtract))
        v_(lambda e: e.tensor_scalar(tA, fr, -PI, TWO_PI, op0=OP.is_lt, op1=OP.mult))
        v_(lambda e: e.tensor_tensor(fr, fr, tA, op=OP.add), inc="VE")
        s_(lambda e: e.activation(fr, fr, AF.Sin, bias=0.0, scale=1.0), waits=[("VE", P.now("VE"))], inc="SE")
        v_(lambda e: e.tensor_scalar_mul(tA, trow, lnb_), waits=[("SE", P.now("SE"))], inc="VE")
        s_(lambda e: e.activation(tA, tA, AF.Exp, bias=0.0, scale=(-1.0 if inv_scale else 1.0)),
           waits=[("VE", P.now("VE"))], inc="SE")
        v_(lambda e: e.tensor_tensor(tA, fr, tA, op=OP.mult), waits=[("SE", P.now("SE"))])
        v_(lambda e: e.tensor_copy(tAb[:, 0:2048], tA), inc="VE")
        P.dma("sync", d_tabs[idx], tAb[:, 0:2048], waits=[("VE", P.now("VE"))], inc="LS")
        se_tab[0] = P.now("SE")
    build_tab(0, False, False)
    build_tab(1, True, False)
    build_tab(2, False, True)
    build_tab(3, True, True)
    ls_vec = 16
    v_(lambda e: e.reduce_max(pcol[:, 6:7], Vs("a_qn_bc", 0, HD), axis=AX, apply_absolute_value=True),
       waits=[("LS", ls_vec)])
    v_(lambda e: e.reduce_max(pcol[:, 7:8], Vs("a_kn_bc", 0, HD), axis=AX, apply_absolute_value=True))
    g_(lambda e: e.tensor_tensor(pcol[:, PC_BA:PC_BA + 1], pcol[:, 6:7], pcol[:, 7:8], op=OP.mult),
       waits=[("VE", P.now("VE"))], inc="GP")
    s_(lambda e: e.activation(pcol[:, PC_BA:PC_BA + 1], pcol[:, PC_BA:PC_BA + 1], AF.Relu,
                              bias=epsc[:, 3:4], scale=8.0), waits=[("GP", P.now("GP"))])
    s_(lambda e: e.mul(pcol[:, PC_BA:PC_BA + 1], pcol[:, PC_BA:PC_BA + 1], -1.0), inc="SE")
    for l in range(DEPTH):
        v_(lambda e, l=l: e.reduce_max(pcol[:, 6:7], Vs("f_qs_bc", l * HD, HD), axis=AX, apply_absolute_value=True))
        v_(lambda e, l=l: e.reduce_max(pcol[:, 7:8], Vs("f_ks_bc", l * HD, HD), axis=AX, apply_absolute_value=True))
        g_(lambda e, l=l: e.tensor_tensor(pcol[:, PC_BF0 + l:PC_BF0 + l + 1], pcol[:, 6:7], pcol[:, 7:8], op=OP.mult),
           waits=[("VE", P.now("VE"))], inc="GP")
        s_(lambda e, l=l: e.activation(pcol[:, PC_BF0 + l:PC_BF0 + l + 1], pcol[:, PC_BF0 + l:PC_BF0 + l + 1],
                                       AF.Relu, bias=epsc[:, 3:4], scale=8.0), waits=[("GP", P.now("GP"))])
        s_(lambda e, l=l: e.mul(pcol[:, PC_BF0 + l:PC_BF0 + l + 1], pcol[:, PC_BF0 + l:PC_BF0 + l + 1], -1.0), inc="SE")
        g_(lambda e, l=l: e.tensor_single_scalar(pcol[:, PC_QS80 + l:PC_QS80 + l + 1],
                                                 Vs("f_qs", l), 8.0, op=OP.mult))
    g_(lambda e: e.tensor_single_scalar(pcol[:, PC_QN8:PC_QN8 + 1], V("a_qn"), 0.125, op=OP.mult), inc="GP")
    onesb = cb[:, 386:387]
    v_(lambda e: e.memset(onesb, 1.0))
    for c in range(KC):
        v_(lambda e, c=c: e.memset(zfm[:, c * (T + 1):c * (T + 1) + 1], 0.0))
    p0.finish()

    p = Prog(nc)
    for s in ("LS2", "LW2", "MM2", "VE2", "SE2", "GP2"):
        p.sem(s)
    for a, b in (("LS", "LS2"), ("LW", "LW2"), ("MM", "MM2"), ("VE", "VE2"), ("SE", "SE2"), ("GP", "GP2")):
        p.sems[a] = p.sems[b]
        p.counts[a] = 0

    def T_(fn, waits=(), inc="MM"):
        p.emit("tensor", fn, waits, inc)

    def V_(fn, waits=(), inc="VE"):
        p.emit("vector", fn, waits, inc)

    def S_(fn, waits=(), inc="SE"):
        p.emit("scalar", fn, waits, inc)

    def MMW(out, lhsT, rhs, first, last, waits=(), inc="MM"):
        T_(lambda e, o=out, l=lhsT, r=rhs, f=first, s=last: e.matmul(o, l, r, start=f, stop=s),
           waits=waits, inc=inc)

    now = p.now

    TAB = [0, 2048, 4096, 6144]
    KRING = 8192
    VRING = 12288
    QWIN = 16448
    T1F = 9248
    SQB = 19520
    LNC0, LNC1 = 16448, 17472
    LNTMP = 9576
    ZB = 0
    OW = 4104
    PTR = 6152

    def h2ap(j, c0, n):
        if j < 19:
            return poolb[:, j * 1024 + c0: j * 1024 + c0 + n]
        return wst_t[:, 5120 + (j - 19) * 1024 + c0: 5120 + (j - 19) * 1024 + c0 + n]

    ms = {"pool_mm": 0, "pool_ve": 0, "pool_se": 0,
          "wstA_mm": 0, "wstB_mm": 0,
          "b45_ve": 0, "b45_se": 0, "b67_ve": 0,
          "zb_mm": 0}

    RW0 = rows[0:1, 0:512]
    RW1 = rows[0:1, 512:1024]

    def emit_outer_ln(dst):
        cast_ve = [ms["pool_mm"], ms["pool_mm"]]
        e0 = (now("VE"), now("SE"))
        stat_guard = {0: e0, 1: e0}
        rw = [(rows[0:1, 0:512], rows[0:1, 512:1024]),
              (rows[32:33, 0:512], rows[32:33, 512:1024])]
        pend = [None]
        for tc in range(4):
            t0 = tc * 512
            par = tc % 2
            bS1, bS2 = (5, 4) if par == 0 else (3, 2)
            for k in range(KC):
                cbA = poolb[:, LNC0 + (k % 2) * 512: LNC0 + (k % 2) * 512 + 512]
                cbS = poolb[:, LNC1 + (k % 2) * 512: LNC1 + (k % 2) * 512 + 512]
                V_(lambda e, o=cbA, i=xc(k, t0, 512): e.tensor_copy(o, i),
                   waits=[("MM", cast_ve[k % 2])])
                V_(lambda e, o=cbS, i=cbA: e.tensor_tensor(o, i, i, op=OP.mult))
                ve_n = now("VE")
                MMW(ps[0:1, bS1 * 512:bS1 * 512 + 512], onesb, cbA, k == 0, k == KC - 1,
                    waits=[("VE", ve_n)] + ([("VE", stat_guard[par][0]), ("SE", stat_guard[par][1])]
                                            if k == 0 else []))
                MMW(ps[0:1, bS2 * 512:bS2 * 512 + 512], onesb, cbS, k == 0, k == KC - 1)
                cast_ve[k % 2] = now("MM")
            mm_stats = now("MM")
            if pend[0] is not None:
                pend[0]()

            def mk_chain(tc=tc, t0=t0, par=par, bS1=bS1, bS2=bS2, mm_stats=mm_stats):
                RA, RB = rw[par]
                rbase = 32 * par
                V_(lambda e: e.tensor_single_scalar(RA, ps[0:1, bS1 * 512:bS1 * 512 + 512], 1.0 / D, op=OP.mult),
                   waits=[("MM", mm_stats)])
                V_(lambda e: e.tensor_tensor(RB, RA, RA, op=OP.mult))
                V_(lambda e: e.scalar_tensor_tensor(RB, ps[0:1, bS2 * 512:bS2 * 512 + 512], 1.0 / D, RB,
                                                    op0=OP.mult, op1=OP.subtract))
                ve_rows = now("VE")
                S_(lambda e: e.activation(RB, RB, AF.Ln, bias=epsc[rbase:rbase + 1, 0:1], scale=1.0),
                   waits=[("VE", ve_rows)])
                S_(lambda e: e.activation(RB, RB, AF.Exp, bias=0.0, scale=-0.5))
                se_rows = now("SE")
                MMW(ps[:, 6 * 512:7 * 512], onesf[rbase:rbase + 1, :], RA, True, True,
                    waits=[("SE", se_rows), ("VE", ms["b67_ve"])])
                MMW(ps[:, 7 * 512:8 * 512], onesf[rbase:rbase + 1, :], RB, True, True)
                mm_bc = now("MM")
                stat_guard[par] = (now("VE"), se_rows)
                for k in range(KC):
                    if dst == "z":
                        V_(lambda e, o=zc(k, t0, 512), i=xc(k, t0, 512):
                           e.tensor_tensor(o, i, ps[:, 6 * 512:7 * 512], op=OP.subtract),
                           waits=[("MM", mm_bc)] if k == 0 else ())
                        V_(lambda e, o=zc(k, t0, 512): e.tensor_tensor(o, o, ps[:, 7 * 512:8 * 512], op=OP.mult))
                    else:
                        yap = zfmb[:, k * T + t0: k * T + t0 + 512]
                        V_(lambda e, i=xc(k, t0, 512):
                           e.tensor_tensor(pool[:, LNTMP:LNTMP + 512], i, ps[:, 6 * 512:7 * 512], op=OP.subtract),
                           waits=[("MM", mm_bc)] if k == 0 else ())
                        V_(lambda e, o=yap: e.tensor_tensor(o, pool[:, LNTMP:LNTMP + 512], ps[:, 7 * 512:8 * 512], op=OP.mult))
                ms["b67_ve"] = now("VE")
            pend[0] = mk_chain
        pend[0]()
        pend[0] = None
        ms["b45_ve"] = now("VE")
        ms["b45_se"] = now("SE")

    RL2 = rows[0:2, 0:512]
    RREC = rows[0:1, 0:512]
    T2F = rows[:, 512:1024]
    wslotA = wst_t[:, 0:4096]
    wslotB = wst_t[:, 4096:8192]
    slot_guard = {"A": 0, "B": 0}
    ms["b2_se"] = 0; ms["b3_ve"] = 0; ms["b6_ve"] = 0; ms["b7_ve"] = 0; ms["z_ve"] = 0
    ms["w2p_ve"] = 0; ms["x_ve"] = 0

    def stream_phase(slot, src_ap):
        tgt = wslotA if slot == "A" else wslotB
        p.dma("gpsimd", tgt[:, 0:src_ap.shape[-1]], src_ap,
              waits=[("MM", slot_guard[slot])], inc="LW")
        return p.now("LW")

    def cast_zb(w):
        for k in range(KC):
            V_(lambda e, o=hbf_t[:, k * 513: k * 513 + 513],
               i=zfm[:, k * (T + 1) + w * 512: k * (T + 1) + w * 512 + 513]:
               e.tensor_copy(o, i),
               waits=[("MM", ms["zb_mm"])] if k == 0 else ())
        return p.now("VE")

    def zb_in(k, off, n):
        sh = 1 if k < 4 else 0
        return hbf_t[:, k * 513 + sh + off: k * 513 + sh + off + n]

    def l2norm_chain(l, bankP, out_ap, scol, cs_ap, ss_ap):
        mm0 = now("MM")
        sqb = poolb[:, SQB:SQB + 512]
        S_(lambda e: e.activation(sqb, bankP, AF.Square, bias=0.0, scale=1.0), waits=[("MM", mm0)])
        MMW(ps[0:2, 2 * 512:2 * 512 + 512], e2b, sqb, True, True,
            waits=[("SE", now("SE")), ("SE", ms["b2_se"])])
        S_(lambda e: e.activation(RL2, ps[0:2, 2 * 512:2 * 512 + 512], AF.Ln, bias=epsc[0:2, 1:2], scale=1.0),
           waits=[("MM", now("MM"))])
        S_(lambda e: e.activation(RL2, RL2, AF.Exp, bias=0.0, scale=-0.5))
        ms["b2_se"] = now("SE")
        MMW(ps[:, 3 * 512:4 * 512], e2tf[:, :], RL2, True, True,
            waits=[("SE", now("SE")), ("VE", ms["b3_ve"])])
        mmb = now("MM")
        V_(lambda e, s=scol: e.tensor_scalar(pool[:, T1F:T1F + 512], bankP, s, None, op0=OP.mult),
           waits=[("MM", mmb)])
        V_(lambda e, o=out_ap: e.tensor_tensor(o, pool[:, T1F:T1F + 512], ps[:, 3 * 512:4 * 512],
                                               op=OP.mult))
        ms["b3_ve"] = now("VE")
        MMW(ps[:, 7 * 512:8 * 512], rotb, out_ap, True, True,
            waits=[("VE", now("VE")), ("VE", ms["b7_ve"])])
        mmr = now("MM")
        V_(lambda e, i=out_ap, c_=cs_ap: e.tensor_tensor(pool[:, T1F:T1F + 512], i, c_, op=OP.mult))
        V_(lambda e, s=ss_ap: e.tensor_tensor(T2F, ps[:, 7 * 512:8 * 512], s, op=OP.mult),
           waits=[("MM", mmr)])
        ms["b7_ve"] = now("VE")
        V_(lambda e, o=out_ap: e.tensor_tensor(o, pool[:, T1F:T1F + 512], T2F, op=OP.add))

    def emit_ffn_layer(l, parts=("la", "mlp")):
        for i in range(4):
            p.dma("sync", poolb[:, TAB[i]:TAB[i] + 2048], d_tabs[i],
                  waits=[("MM", ms["pool_mm"]), ("VE", ms["pool_ve"])], inc="LS")
        tab_ls = now("LS")
        zb_ve = [0]
        pbank_ve = [ms.get("pbank0_ve", 0), ms.get("pbank1_ve", 0)]

        def proj_q(w):
            ls_need = stream_phase("A", d_fqkv[l, 0])
            pend = [None]
            for c in range(4):
                bk = ps[:, (c % 2) * 512:(c % 2) * 512 + 512]
                for k in range(KC):
                    w_ = [("VE", pbank_ve[c % 2])] if k == 0 else []
                    if c == 0 and k == 0:
                        w_ += [("LW", ls_need), ("VE", zb_ve[0]), ("LS", tab_ls)]
                    MMW(bk, wslotA[:, k * 512 + c * 128: k * 512 + c * 128 + 128],
                        zb_in(k, 0, 512), k == 0, k == KC - 1, waits=w_)
                if pend[0] is not None:
                    pend[0]()

                def mk(c=c, bk=bk):
                    cs = poolb[:, TAB[0] + w * 512: TAB[0] + w * 512 + 512]
                    ss = poolb[:, TAB[1] + w * 512: TAB[1] + w * 512 + 512]
                    l2norm_chain(l, bk, poolb[:, QWIN + c * 512: QWIN + c * 512 + 512],
                                 pcol[:, PC_QS80 + l:PC_QS80 + l + 1], cs, ss)
                    pbank_ve[c % 2] = now("VE")
                pend[0] = mk
            pend[0]()
            slot_guard["A"] = now("MM")

        def proj_k(wk):
            ls_need = stream_phase("A", d_fqkv[l, 1])
            kbase = KRING + (wk % 2) * 2048
            pend = [None]
            for c in range(4):
                bk = ps[:, (c % 2) * 512:(c % 2) * 512 + 512]
                for k in range(KC):
                    w_ = [("VE", pbank_ve[c % 2])] if k == 0 else []
                    if c == 0 and k == 0:
                        w_ += [("LW", ls_need), ("VE", zb_ve[0])]
                    MMW(bk, wslotA[:, k * 512 + c * 128: k * 512 + c * 128 + 128],
                        zb_in(k, 0, 512), k == 0, k == KC - 1, waits=w_)
                if pend[0] is not None:
                    pend[0]()

                def mk(c=c, bk=bk):
                    cs = poolb[:, TAB[2] + wk * 512: TAB[2] + wk * 512 + 512]
                    ss = poolb[:, TAB[3] + wk * 512: TAB[3] + wk * 512 + 512]
                    l2norm_chain(l, bk, poolb[:, kbase + c * 512: kbase + c * 512 + 512],
                                 Vs("f_ks", l), cs, ss)
                    pbank_ve[c % 2] = now("VE")
                pend[0] = mk
            pend[0]()
            slot_guard["A"] = now("MM")

        def proj_v(wk):
            ls_need = stream_phase("B", d_fqkv[l, 2])
            vbase = VRING + (wk % 2) * 2080
            pend = [None]
            for tch in range(4):
                bk = ps[:, (tch % 2) * 512:(tch % 2) * 512 + 512]
                for k in range(KC):
                    w_ = [("VE", pbank_ve[tch % 2])] if k == 0 else []
                    if tch == 0 and k == 0:
                        w_ += [("LW", ls_need), ("VE", zb_ve[0])]
                    MMW(bk, zb_in(k, tch * 128, 128), wslotB[:, k * 512: k * 512 + 512],
                        k == 0, k == KC - 1, waits=w_)
                mmv = now("MM")
                if pend[0] is not None:
                    pend[0]()

                def mk(tch=tch, bk=bk, mmv=mmv):
                    dst = poolb[:, vbase + tch * 520: vbase + tch * 520 + 520]
                    V_(lambda e, o=dst.rearrange("p (h c) -> p h c", c=65),
                       i=bk.rearrange("p (h c) -> p h c", c=64):
                       e.tensor_copy(o[:, :, 0:64], i), waits=[("MM", mmv)])
                    V_(lambda e, o=dst.rearrange("p (h c) -> p h c", c=65): e.memset(o[:, :, 64:65], 1.0))
                    pbank_ve[tch % 2] = now("VE")
                pend[0] = mk
            pend[0]()
            slot_guard["B"] = now("MM")

        def attention(w):
            ve_ready = now("VE")
            st_se = [ms["b2_se"], ms["b2_se"]]
            ptr_mm = [0, 0]
            st_i = [0]
            pend_evac = [None]

            def emit_evac(h, pvb, mm_pv):
                V_(lambda e, i=ps[64:65, pvb * 512: pvb * 512 + 512]: e.reciprocal(RREC, i),
                   waits=[("MM", mm_pv)])
                MMW(ps[0:64, 3 * 512:3 * 512 + 512], onesf[0:1, 0:64], RREC, True, True,
                    waits=[("VE", now("VE")), ("VE", ms["b3_ve"])])
                mmb = now("MM")
                r0 = 64 * (h % 2)
                oap = hbf_t[r0:r0 + 64, OW + (h // 2) * 512: OW + (h // 2) * 512 + 512]
                V_(lambda e, o=oap, i=ps[0:64, pvb * 512: pvb * 512 + 512]:
                   e.tensor_copy(o, i), waits=[("MM", mmb)])
                V_(lambda e, o=oap:
                   e.tensor_tensor(o, o, ps[0:64, 3 * 512:3 * 512 + 512], op=OP.mult))
                ms["b3_ve"] = now("VE")
                if pvb == 6:
                    ms["b6_ve"] = now("VE")
                else:
                    ms["b7_ve"] = now("VE")

            for h in range(HF):
                pvb = 6 + (h % 2)
                pv_guard = ms["b6_ve"] if pvb == 6 else ms["b7_ve"]
                pv_started = [False]
                cx_list = list(range(4, 8) if w == 0 else range(8))
                pend_pv = [None]
                for cx in cx_list:
                    par = st_i[0] % 2
                    st_i[0] += 1
                    lo = max(0, 128 * (cx - 4)); hi = min(512, 128 * (cx + 1)); width = hi - lo
                    win_of_cx = (w - 1) if cx < 4 else w
                    kbase = KRING + (win_of_cx % 2) * 2048
                    vbase = VRING + (win_of_cx % 2) * 2080
                    r0 = 64 * (h % 2)
                    stb = ps[:, (4 + par) * 512:(4 + par) * 512 + 512]
                    ktile = poolb[r0:r0 + 64, kbase + (h // 2) * 512 + (cx % 4) * 128:
                                  kbase + (h // 2) * 512 + (cx % 4) * 128 + 128]
                    qtile = poolb[r0:r0 + 64, QWIN + (h // 2) * 512 + lo: QWIN + (h // 2) * 512 + hi]
                    MMW(stb[:, 0:width], ktile, qtile, True, False,
                        waits=[("SE", st_se[par]), ("VE", ve_ready)])
                    if cx >= 4:
                        MMW(stb[:, 0:128], mgt, identb, False, True)
                    else:
                        MMW(stb[:, width - 128:width], mlt, identb, False, True)
                    mm_st = now("MM")
                    if pend_pv[0] is not None:
                        pend_pv[0]()
                        pend_pv[0] = None
                    ptr = hbf_t[:, PTR + par * 512: PTR + par * 512 + width]
                    S_(lambda e, o=ptr, i=stb[:, 0:width], b=pcol[:, PC_BF0 + l:PC_BF0 + l + 1]:
                       e.activation(o, i, AF.Exp, bias=b, scale=1.0),
                       waits=[("MM", mm_st), ("MM", ptr_mm[par])])
                    st_se[par] = now("SE")
                    se_exp = now("SE")

                    def mk_pv(cx=cx, lo=lo, se_exp=se_exp, vbase=vbase, ptr=ptr, par=par, pvb=pvb):
                        for d in range(max(0, cx - 4), min(3, cx) + 1):
                            first = not pv_started[0]; pv_started[0] = True
                            last = (cx == 7 and d == 3)
                            vtile = poolb[:, vbase + (cx % 4) * 520 + 65 * h:
                                          vbase + (cx % 4) * 520 + 65 * h + 65]
                            w_ = [("SE", se_exp)]
                            if first:
                                w_.append(("VE", pv_guard))
                            MMW(ps[0:65, pvb * 512 + 128 * d: pvb * 512 + 128 * d + 128],
                                vtile, ptr[:, 128 * d - lo:128 * d - lo + 128], first, last, waits=w_)
                        ptr_mm[par] = now("MM")
                    pend_pv[0] = mk_pv
                pend_pv[0]()
                pend_pv[0] = None
                mm_pv = now("MM")
                if pend_evac[0] is not None:
                    pend_evac[0]()
                pend_evac[0] = (lambda h=h, pvb=pvb, mm_pv=mm_pv: emit_evac(h, pvb, mm_pv))
            if pend_evac[0] is not None:
                pend_evac[0]()
                pend_evac[0] = None

        def wo_stage(w):
            ls_need = stream_phase("B", d_fwo[l])
            ve_o = now("VE")
            for m in range(KC):
                bk = ps[:, (m % 2) * 512:(m % 2) * 512 + 512]
                for kc in range(4):
                    w_ = [("VE", pbank_ve[m % 2])] if kc == 0 else []
                    if m == 0 and kc == 0:
                        w_ += [("LW", ls_need), ("VE", ve_o)]
                    MMW(bk, wslotB[:, kc * 1024 + m * 128: kc * 1024 + m * 128 + 128],
                        hbf_t[:, OW + kc * 512: OW + kc * 512 + 512], kc == 0, kc == 3, waits=w_)
                mmw = now("MM")
                V_(lambda e, o=zc(m, w * 512, 512), b=bk: e.tensor_tensor(o, o, b, op=OP.add),
                   waits=[("MM", mmw)])
                pbank_ve[m % 2] = now("VE")
            slot_guard["B"] = now("MM")
            ms["z_ve"] = now("VE")

        for wi, w in enumerate([3, 2, 1, 0] if "la" in parts else []):
            if wi == 0:
                zb_ve[0] = cast_zb(w)
                proj_q(w)
                proj_k(w)
                proj_v(w)
                ms["zb_mm"] = now("MM")
                zb_ve[0] = cast_zb(w - 1)
                proj_k(w - 1)
                proj_v(w - 1)
                ms["zb_mm"] = now("MM")
            else:
                proj_q(w)
                ms["zb_mm"] = now("MM")
                if w > 0:
                    zb_ve[0] = cast_zb(w - 1)
                    proj_k(w - 1)
                    proj_v(w - 1)
                    ms["zb_mm"] = now("MM")
            attention(w)
            wo_stage(w)
        ms["pbank0_ve"] = pbank_ve[0]
        ms["pbank1_ve"] = pbank_ve[1]
        ms["pool_mm"] = now("MM")
        ms["pool_ve"] = now("VE")

        w1g = [slot_guard["A"], slot_guard["B"]]
        w2g = [0, 0]
        n_w1 = [0]
        n_w2 = [0]

        def load_w1(j):
            sl = n_w1[0] % 2
            p.dma("gpsimd", wst_t[:, sl * 2048:(sl + 1) * 2048], d_fw1[l, j],
                  waits=[("MM", w1g[sl])], inc="LW")
            n_w1[0] += 1
            return p.now("LW")

        def load_w2(mg, j):
            sl = n_w2[0] % 2
            p.dma("gpsimd", wst_t[:, 4096 + sl * 512:4096 + (sl + 1) * 512], d_fw2[l, mg, j],
                  waits=[("MM", w2g[sl])], inc="LW")
            n_w2[0] += 1
            return p.now("LW")

        ag_ve = [now("VE"), now("VE")]
        for th in ((1, 0) if "mlp" in parts else ()):
            t0h = th * 1024
            cast_g = [ms["pool_mm"], ms["pool_mm"]]
            e0 = (now("VE"), now("SE"))
            stat_guard = {0: e0, 1: e0}
            rw = [(rows[0:1, 0:512], rows[0:1, 512:1024]),
                  (rows[32:33, 0:512], rows[32:33, 512:1024])]
            pendln = [None]
            for tc in range(2):
                par = tc % 2
                bS1, bS2 = (5, 4) if par == 0 else (3, 2)
                for k in range(KC):
                    sh = 1 if k < 4 else 0
                    zin = zfm[:, k * (T + 1) + sh + t0h + tc * 512: k * (T + 1) + sh + t0h + tc * 512 + 512]
                    cbA = poolb[:, (k % 2) * 512:(k % 2) * 512 + 512]
                    cbS = poolb[:, 1024 + (k % 2) * 512:1024 + (k % 2) * 512 + 512]
                    V_(lambda e, o=cbA, i=zin: e.tensor_copy(o, i), waits=[("MM", cast_g[k % 2])])
                    V_(lambda e, o=cbS, i=cbA: e.tensor_tensor(o, i, i, op=OP.mult))
                    ve_n = now("VE")
                    MMW(ps[0:1, bS1 * 512:bS1 * 512 + 512], onesb, cbA, k == 0, k == KC - 1,
                        waits=[("VE", ve_n)] + ([("VE", stat_guard[par][0]), ("SE", stat_guard[par][1])]
                                                if k == 0 else []))
                    MMW(ps[0:1, bS2 * 512:bS2 * 512 + 512], onesb, cbS, k == 0, k == KC - 1)
                    cast_g[k % 2] = now("MM")
                mm_stats = now("MM")
                if pendln[0] is not None:
                    pendln[0]()

                def mk_lnchain(tc=tc, par=par, bS1=bS1, bS2=bS2, mm_stats=mm_stats):
                    RA, RB = rw[par]
                    rbase = 32 * par
                    V_(lambda e: e.tensor_single_scalar(RA, ps[0:1, bS1 * 512:bS1 * 512 + 512], 1.0 / D, op=OP.mult),
                       waits=[("MM", mm_stats)])
                    V_(lambda e: e.tensor_tensor(RB, RA, RA, op=OP.mult))
                    V_(lambda e: e.scalar_tensor_tensor(RB, ps[0:1, bS2 * 512:bS2 * 512 + 512], 1.0 / D, RB,
                                                        op0=OP.mult, op1=OP.subtract))
                    ve_rows = now("VE")
                    S_(lambda e: e.activation(RB, RB, AF.Ln, bias=epsc[rbase:rbase + 1, 0:1], scale=1.0),
                       waits=[("VE", ve_rows)])
                    S_(lambda e: e.activation(RB, RB, AF.Exp, bias=0.0, scale=-0.5))
                    se_rows = now("SE")
                    MMW(ps[:, 6 * 512:7 * 512], onesf[rbase:rbase + 1, :], RA, True, True,
                        waits=[("SE", se_rows), ("VE", ms["b67_ve"])])
                    MMW(ps[:, 7 * 512:8 * 512], onesf[rbase:rbase + 1, :], RB, True, True)
                    mm_bc = now("MM")
                    stat_guard[par] = (now("VE"), se_rows)
                    for k in range(KC):
                        sh = 1 if k < 4 else 0
                        zin = zfm[:, k * (T + 1) + sh + t0h + tc * 512: k * (T + 1) + sh + t0h + tc * 512 + 512]
                        V_(lambda e, i=zin: e.tensor_tensor(pool[:, LNTMP:LNTMP + 512], i,
                                                            ps[:, 6 * 512:7 * 512], op=OP.subtract),
                           waits=[("MM", mm_bc)] if k == 0 else ())
                        V_(lambda e: e.tensor_tensor(pool[:, LNTMP:LNTMP + 512], pool[:, LNTMP:LNTMP + 512],
                                                     ps[:, 7 * 512:8 * 512], op=OP.mult))
                        V_(lambda e, o=hbf_t[:, k * 1024 + tc * 512: k * 1024 + tc * 512 + 512],
                           g_=Vs("f_lng", l * KC + k), b_=Vs("f_lnb", l * KC + k):
                           e.tensor_scalar(o, pool[:, LNTMP:LNTMP + 512], g_, b_, op0=OP.mult, op1=OP.add))
                    ms["b67_ve"] = now("VE")
                pendln[0] = mk_lnchain
            pendln[0]()
            pendln[0] = None
            ln_ve = now("VE")
            load_w1(0)
            slot_of = {0: (n_w1[0] - 1) % 2}
            lw_of = {0: p.now("LW")}
            load_w1(1)
            slot_of[1] = (n_w1[0] - 1) % 2
            lw_of[1] = p.now("LW")
            for j in range(NM):
                sl = slot_of[j]
                w1v = wst_t[:, sl * 2048:(sl + 1) * 2048]
                for tc in range(2):
                    ssb = (j * 2 + tc) % 2
                    agA = ps[:, (ssb * 2) * 512:(ssb * 2) * 512 + 512]
                    agG = ps[:, (ssb * 2 + 1) * 512:(ssb * 2 + 1) * 512 + 512]
                    for k in range(KC):
                        rhs = hbf_t[:, k * 1024 + tc * 512: k * 1024 + tc * 512 + 512]
                        first = (k == 0); lastk = (k == KC - 1)
                        wts = []
                        if k == 0:
                            wts = [("LW", lw_of[j]), ("VE", max(ag_ve[ssb], ln_ve))]
                        MMW(agA, w1v[:, k * 256: k * 256 + 128], rhs, first, lastk, waits=wts)
                        MMW(agG, w1v[:, k * 256 + 128: k * 256 + 256], rhs, first, lastk)
                    w1g[sl] = now("MM")
                    mm_ag = now("MM")
                    glA = rows[:, 0:512] if ssb == 0 else rows[:, 512:1024]
                    S_(lambda e, o=glA, i=agG, b=Vs("f_b1g", l * NM + j):
                       e.activation(o, i, AF.Gelu, bias=b, scale=1.0), waits=[("MM", mm_ag)])
                    V_(lambda e, o=h2ap(j, tc * 512, 512), a=agA, s=Vs("f_b1a", l * NM + j), g_=glA:
                       e.scalar_tensor_tensor(o, a, s, g_, op0=OP.add, op1=OP.mult),
                       waits=[("SE", now("SE"))])
                    ag_ve[ssb] = now("VE")
                if j + 2 < NM:
                    slot_of[j + 2] = n_w1[0] % 2
                    lw_of[j + 2] = load_w1(j + 2)
                elif th == 1 and j + 2 >= NM and (j + 2 - NM) < 2:
                    pass
            h2_ve = now("VE")
            w2g[0] = now("MM"); w2g[1] = now("MM")
            load_w2(0, 0)
            w2slot_of = {(0, 0): (n_w2[0] - 1) % 2}
            w2lw_of = {(0, 0): p.now("LW")}
            load_w2(0, 1)
            w2slot_of[(0, 1)] = (n_w2[0] - 1) % 2
            w2lw_of[(0, 1)] = p.now("LW")
            for mg in range(2):
                for j in range(NM):
                    sl = w2slot_of[(mg, j)]
                    w2v = wst_t[:, 4096 + sl * 512:4096 + (sl + 1) * 512]
                    first = (j == 0); lastj = (j == NM - 1)
                    for m in range(4):
                        for q in range(2):
                            wts = []
                            if m == 0 and q == 0:
                                wts = [("LW", w2lw_of[(mg, j)])]
                                if j == 0:
                                    wts += [("VE", h2_ve), ("VE", ms["w2p_ve"])]
                            MMW(ps[:, m * 1024 + q * 512: m * 1024 + q * 512 + 512],
                                w2v[:, m * 128: m * 128 + 128],
                                h2ap(j, q * 512, 512), first, lastj, waits=wts)
                    w2g[sl] = now("MM")
                    nxt = j + 2
                    if nxt < NM:
                        w2slot_of[(mg, nxt)] = n_w2[0] % 2
                        w2lw_of[(mg, nxt)] = load_w2(mg, nxt)
                    elif mg == 0 and (nxt - NM) < 2:
                        w2slot_of[(1, nxt - NM)] = n_w2[0] % 2
                        w2lw_of[(1, nxt - NM)] = load_w2(1, nxt - NM)
                mm_w2 = now("MM")
                for m in range(4):
                    mo = mg * 4 + m
                    for q in range(2):
                        V_(lambda e, o=zc(mo, t0h + q * 512, 512),
                           i=ps[:, m * 1024 + q * 512: m * 1024 + q * 512 + 512],
                           s=Vs("f_b2", l * KC + mo):
                           e.scalar_tensor_tensor(o, i, s, o, op0=OP.add, op1=OP.add),
                           waits=[("MM", mm_w2)] if (m == 0 and q == 0) else ())
                ms["w2p_ve"] = now("VE")
            ms["z_ve"] = now("VE")
            ms["pool_mm"] = now("MM")
            ag_ve = [now("VE"), now("VE")]
        ms["pbank0_ve"] = now("VE")
        ms["pbank1_ve"] = now("VE")
        ms["pool_ve"] = now("VE")

    def emit_ffn_module():
        mmg0 = now("MM")
        for c in range(KC):
            V_(lambda e, c=c: e.memset(zfm[:, c * (T + 1):c * (T + 1) + 1], 0.0),
               waits=[("MM", mmg0)] if c == 0 else ())
        emit_outer_ln("z")
        for l in range(DEPTH):
            emit_ffn_layer(l)
        mm_g = now("MM")
        for k in range(KC):
            V_(lambda e, o=xc(k, 0, T), i=zc(k, 0, T):
               e.scalar_tensor_tensor(o, i, 0.5, o, op0=OP.mult, op1=OP.add))
        ms["x_ve"] = now("VE")

    def emit_attn_module():
        emit_outer_ln("ybf")
        p.dma("sync", pool[:, 0:1024], d_bvb, waits=[("MM", ms["pool_mm"]), ("VE", ms["pool_ve"])], inc="LS")
        bvb_ls = now("LS")
        sig = pool[0:16, 1024:3072]
        QRB = [6144, 8192]
        PTB = [10240, 11264]

        def ybf_t(k, t0, n):
            return zfmb[:, k * T + t0: k * T + t0 + n]

        def vg_ap(tch, lg):
            if tch < 15:
                return zfmb[:, 24576 + tch * 520 + 65 * lg: 24576 + tch * 520 + 65 * lg + 65]
            return poolb[:, 14336 + 65 * lg: 14336 + 65 * lg + 65]

        def vg_full(tch):
            if tch < 15:
                return zfmb[:, 24576 + tch * 520: 24576 + tch * 520 + 520]
            return poolb[:, 14336: 14336 + 520]

        def o_ap(c, r0, t0, n):
            if c < 3:
                return hbf_t[r0:r0 + 64, c * 2048 + t0: c * 2048 + t0 + n]
            return poolb[r0:r0 + 64, 12288 + t0: 12288 + t0 + n]

        R0s = [pool[0:1, 7428:8452], pool[32:33, 7428:8452]]
        R2s = [pool[0:1, 8452:9476], pool[32:33, 8452:9476]]
        upar = [0]
        def o_full(c, qc):
            if c < 3:
                return hbf_t[:, c * 2048 + qc * 512: c * 2048 + qc * 512 + 512]
            return poolb[:, 12288 + qc * 512: 12288 + qc * 512 + 512]

        qr_mm = [0, 0]
        pt_mm = [0, 0]
        st_se_a = [0, 0]
        pend_gchain = [None]
        abank = [ms.get("pbank0_ve", 0), ms.get("pbank1_ve", 0)]
        ms["b67_att_ve"] = 0

        for hg in attn_groups:
            for k in range(KC):
                p.dma("gpsimd", wslotA[:, k * 512: k * 512 + 512],
                      d_wv[k * 128:(k + 1) * 128, hg * 512:(hg + 1) * 512],
                      waits=[("MM", slot_guard["A"])] if k == 0 else (), inc="LW")
            lw_v = now("LW")
            pendv = [None]
            for tch in range(16):
                bk = ps[:, (tch % 2) * 512:(tch % 2) * 512 + 512]
                for k in range(KC):
                    w_ = [("VE", abank[tch % 2])] if k == 0 else []
                    if tch == 0 and k == 0:
                        w_ += [("LW", lw_v), ("VE", now("VE")), ("LS", bvb_ls)]
                    MMW(bk, ybf_t(k, tch * 128, 128), wslotA[:, k * 512: k * 512 + 512],
                        k == 0, k == KC - 1, waits=w_)
                mmv = now("MM")
                if pendv[0] is not None:
                    pendv[0]()

                def mkv(tch=tch, bk=bk, mmv=mmv):
                    dst = vg_full(tch)
                    V_(lambda e, o=dst.rearrange("p (h c) -> p h c", c=65),
                       i=bk.rearrange("p (h c) -> p h c", c=64),
                       bv=pool[:, hg * 512:hg * 512 + 512].rearrange("p (h c) -> p h c", c=64):
                       e.tensor_tensor(o[:, :, 0:64], i, bv, op=OP.add), waits=[("MM", mmv)])
                    V_(lambda e, o=dst.rearrange("p (h c) -> p h c", c=65): e.memset(o[:, :, 64:65], 1.0))
                    abank[tch % 2] = now("VE")
                pendv[0] = mkv
            pendv[0]()
            slot_guard["A"] = now("MM")
            for k in range(KC):
                p.dma("gpsimd", wslotB[:, k * 512: k * 512 + 512],
                      d_wk[k * 128:(k + 1) * 128, hg * 512:(hg + 1) * 512],
                      waits=[("MM", slot_guard["B"])] if k == 0 else (), inc="LW")
            lw_k = now("LW")

            def norm_chain(bk, bias_col, wcol, out_ap):
                ktmp = pool[:, 7428:7940]
                sqb = poolb[:, 15880:16392]
                mm0 = now("MM")
                V_(lambda e, i=bk, b=bias_col: e.tensor_scalar(ktmp, i, b, None, op0=OP.add),
                   waits=[("MM", mm0)])
                V_(lambda e: e.tensor_tensor(sqb, ktmp, ktmp, op=OP.mult))
                MMW(ps[0:2, 2 * 512:2 * 512 + 512], e2b, sqb, True, True,
                    waits=[("VE", now("VE")), ("SE", now("SE"))])
                S_(lambda e: e.activation(RL2, ps[0:2, 2 * 512:2 * 512 + 512], AF.Ln,
                                          bias=epsc[0:2, 2:3], scale=1.0 / HD), waits=[("MM", now("MM"))])
                S_(lambda e: e.activation(RL2, RL2, AF.Exp, bias=0.0, scale=-0.5))
                ms["b2_se"] = now("SE")
                MMW(ps[:, 3 * 512:4 * 512], e2tf[:, :], RL2, True, True,
                    waits=[("SE", now("SE")), ("VE", ms["b3_ve"]), ("SE", now("SE"))])
                mmb = now("MM")
                V_(lambda e, o=out_ap, w=wcol: e.scalar_tensor_tensor(o, ktmp, w, ps[:, 3 * 512:4 * 512],
                                                                      op0=OP.mult, op1=OP.mult),
                   waits=[("MM", mmb)])
                ms["b3_ve"] = now("VE")

            pendn = [None]
            for c in range(4):
                for qc in range(4):
                    bk = ps[:, (qc % 2) * 512:(qc % 2) * 512 + 512]
                    for k in range(KC):
                        w_ = [("VE", abank[qc % 2])] if k == 0 else []
                        if c == 0 and qc == 0 and k == 0:
                            w_ += [("LW", lw_k)]
                        MMW(bk, wslotB[:, k * 512 + c * 128: k * 512 + c * 128 + 128],
                            ybf_t(k, qc * 512, 512), k == 0, k == KC - 1, waits=w_)
                    if pendn[0] is not None:
                        pendn[0]()

                    def mkn(c=c, qc=qc, bk=bk):
                        norm_chain(bk, Vs("a_bk", hg * 4 + c), V("a_kn"),
                                   zfmb[:, 16384 + c * 2048 + qc * 512: 16384 + c * 2048 + qc * 512 + 512])
                        abank[qc % 2] = now("VE")
                    pendn[0] = mkn
            pendn[0]()
            slot_guard["B"] = now("MM")
            for k in range(KC):
                p.dma("gpsimd", wslotA[:, k * 512: k * 512 + 512],
                      d_wq[k * 128:(k + 1) * 128, hg * 512:(hg + 1) * 512],
                      waits=[("MM", slot_guard["A"])] if k == 0 else (), inc="LW")
            lw_q = now("LW")
            if hg == 0:
                for k in range(KC):
                    p.dma("gpsimd", wslotB[:, k * 128: k * 128 + 128],
                          d_wq[k * 128:(k + 1) * 128, 1024:1152],
                          waits=[("MM", slot_guard["B"])] if k == 0 else (), inc="LW")
                lw_g = now("LW")
                for qc in range(4):
                    bk = ps[:, (qc % 2) * 512:(qc % 2) * 512 + 512]
                    for k in range(KC):
                        w_ = [("VE", abank[qc % 2]), ("SE", now("SE"))] if k == 0 else []
                        if qc == 0 and k == 0:
                            w_ += [("LW", lw_g)]
                        MMW(bk, wslotB[:, k * 128: k * 128 + 128], ybf_t(k, qc * 512, 512),
                            k == 0, k == KC - 1, waits=w_)
                    mmg = now("MM")
                    S_(lambda e, o=sig[:, qc * 512: qc * 512 + 512], i=bk[0:16, :]:
                       e.activation(o, i, AF.Sigmoid, bias=V("a_bgate")[0:16, :], scale=1.0),
                       waits=[("MM", mmg)])
                    abank[qc % 2] = now("MM") * 0 + abank[qc % 2]
                gate_se = now("SE")
                slot_guard["B"] = now("MM")
            sig_se = now("SE")
            for c in range(4):
                qsl = c % 2
                qbase = QRB[qsl]
                pendq = [None]
                for qc in range(4):
                    bk = ps[:, (qc % 2) * 512:(qc % 2) * 512 + 512]
                    for k in range(KC):
                        w_ = [("VE", abank[qc % 2]), ("SE", now("SE"))] if k == 0 else []
                        if qc == 0 and k == 0:
                            w_ += [("LW", lw_q), ("MM", qr_mm[qsl])]
                        MMW(bk, wslotA[:, k * 512 + c * 128: k * 512 + c * 128 + 128],
                            ybf_t(k, qc * 512, 512), k == 0, k == KC - 1, waits=w_)
                    if pendq[0] is not None:
                        pendq[0]()

                    def mkq(qc=qc, bk=bk, c=c):
                        norm_chain(bk, Vs("a_bq", hg * 4 + c), pcol[:, PC_QN8:PC_QN8 + 1],
                                   poolb[:, qbase + qc * 512: qbase + qc * 512 + 512])
                        abank[qc % 2] = now("VE")
                    pendq[0] = mkq
                pendq[0]()
                ve_pair = now("VE")
                for lh in range(2):
                    h = 2 * c + lh
                    hh = hg * 8 + h
                    r0 = 64 * lh
                    for qh in range(2):
                        pend_pv = [None]
                        pv_started = [False]
                        for kc in range(16):
                            stpar = kc % 2
                            stbase = (2 + 2 * stpar) * 512
                            for qq in range(2):
                                MMW(ps[:, stbase + qq * 512: stbase + qq * 512 + 512],
                                    zfmb[r0:r0 + 64, 16384 + c * 2048 + kc * 128: 16384 + c * 2048 + kc * 128 + 128],
                                    poolb[r0:r0 + 64, qbase + qh * 1024 + qq * 512: qbase + qh * 1024 + qq * 512 + 512],
                                    True, True,
                                    waits=([("VE", ve_pair), ("SE", st_se_a[stpar]), ("SE", ms["b2_se"])]
                                           if qq == 0 else ()))
                            mm_st = now("MM")
                            if pend_pv[0] is not None:
                                pend_pv[0]()
                                pend_pv[0] = None
                            ptr = poolb[:, PTB[stpar]:PTB[stpar] + 1024]
                            se_exps = []
                            for qq in range(2):
                                S_(lambda e, o=ptr[:, qq * 512: qq * 512 + 512],
                                   i=ps[:, stbase + qq * 512: stbase + qq * 512 + 512]:
                                   e.activation(o, i, AF.Exp, bias=pcol[:, PC_BA:PC_BA + 1], scale=1.0),
                                   waits=[("MM", mm_st), ("MM", pt_mm[stpar])] if qq == 0 else ())
                                se_exps.append(now("SE"))
                            st_se_a[stpar] = now("SE")

                            def mk_pv(kc=kc, h=h, ptr=ptr, stpar=stpar, se_exps=tuple(se_exps)):
                                for qq in range(2):
                                    first = (kc == 0)
                                    w_ = [("SE", se_exps[qq])]
                                    if first and qq == 0:
                                        w_.append(("VE", ms["b67_att_ve"]))
                                    MMW(ps[0:65, 6 * 512 + qq * 512: 6 * 512 + qq * 512 + 512],
                                        vg_ap(kc, h), ptr[:, qq * 512: qq * 512 + 512],
                                        first, kc == 15 and qq == 1, waits=w_)
                                pt_mm[stpar] = now("MM")
                            pend_pv[0] = mk_pv
                        pend_pv[0]()
                        pend_pv[0] = None
                        mm_pv = now("MM")
                        up = upar[0]; upar[0] ^= 1
                        R0 = R0s[up]; R2 = R2s[up]
                        V_(lambda e, o=o_ap(c, r0, qh * 1024, 1024), i=ps[0:64, 6 * 512: 6 * 512 + 1024]:
                           e.tensor_copy(o, i), waits=[("MM", mm_pv)])
                        V_(lambda e, R2=R2: e.tensor_copy(R2, ps[64:65, 6 * 512: 6 * 512 + 1024]))
                        ms["b67_att_ve"] = now("VE")
                        ve_sum = now("VE")
                        if pend_gchain[0] is not None:
                            pend_gchain[0]()

                        def mk_gchain(c=c, r0=r0, qh=qh, hh=hh, ve_sum=ve_sum, R0=R0, R2=R2, up=up):
                            rbase = 32 * up
                            for qq in range(2):
                                MMW(ps[0:1, qq * 512: qq * 512 + 512],
                                    identg[:, hh:hh + 1], sig[:, qh * 1024 + qq * 512: qh * 1024 + qq * 512 + 512],
                                    True, True,
                                    waits=[("VE", ve_sum), ("SE", sig_se), ("VE", abank[qq])] if qq == 0
                                          else [("VE", abank[qq])])
                            mm_gx = now("MM")
                            V_(lambda e: e.tensor_copy(R0, ps[0:1, 0:1024]),
                               waits=[("MM", mm_gx)])
                            V_(lambda e: e.reciprocal(R2, R2))
                            V_(lambda e: e.tensor_tensor(R0, R0, R2, op=OP.mult))
                            ve_row = now("VE")
                            for qq in range(2):
                                MMW(ps[0:64, qq * 512: qq * 512 + 512],
                                    onesf[rbase:rbase + 1, 0:64],
                                    pool[rbase:rbase + 1, 7428 + qq * 512: 7428 + qq * 512 + 512],
                                    True, True, waits=[("VE", ve_row)] if qq == 0 else ())
                            mm_g2 = now("MM")
                            V_(lambda e, o=o_ap(c, r0, qh * 1024, 1024):
                               e.tensor_tensor(o, o, ps[0:64, 0:1024], op=OP.mult),
                               waits=[("MM", mm_g2)])
                            abank[0] = now("VE")
                            abank[1] = now("VE")
                        pend_gchain[0] = mk_gchain
                if pend_gchain[0] is not None:
                    pend_gchain[0]()
                    pend_gchain[0] = None
                qr_mm[qsl] = now("MM")
            slot_guard["A"] = now("MM")
            for kc in range(4):
                p.dma("gpsimd", wslotB[:, kc * 1024: kc * 1024 + 1024],
                      d_wo[hg * 512 + kc * 128: hg * 512 + kc * 128 + 128, :],
                      waits=[("MM", slot_guard["B"])] if kc == 0 else (), inc="LW")
            lw_o = now("LW")
            ve_o = now("VE")
            for m in range(KC):
                for qc in range(4):
                    bk = ps[:, (qc % 2) * 512:(qc % 2) * 512 + 512]
                    for kc in range(4):
                        w_ = [("VE", abank[qc % 2])] if kc == 0 else []
                        if m == 0 and qc == 0 and kc == 0:
                            w_ += [("LW", lw_o), ("VE", ve_o)]
                        MMW(bk, wslotB[:, kc * 1024 + m * 128: kc * 1024 + m * 128 + 128],
                            o_full(kc, qc),
                            kc == 0, kc == 3, waits=w_)
                    mmw = now("MM")
                    if hg == 0:
                        V_(lambda e, o=xc(m, qc * 512, 512), b=bk, s=Vs("a_bo", m):
                           e.scalar_tensor_tensor(o, b, s, o, op0=OP.add, op1=OP.add),
                           waits=[("MM", mmw)])
                    else:
                        V_(lambda e, o=xc(m, qc * 512, 512), b=bk: e.tensor_tensor(o, o, b, op=OP.add),
                           waits=[("MM", mmw)])
                    abank[qc % 2] = now("VE")
            slot_guard["B"] = now("MM")
        ms["pbank0_ve"] = now("VE")
        ms["pbank1_ve"] = now("VE")
        ms["pool_mm"] = now("MM")
        ms["pool_ve"] = now("VE")
        ms["x_ve"] = now("VE")

    dbg = [s for s in stages if s.startswith("dbg_")]
    if dbg and dbg[0] == "dbg_attn":
        emit_attn_module()
        d_dbg = nc.dram_tensor("dbg", [128, 32768], BF16, kind="ExternalOutput").ap()
        d_dbg2 = nc.dram_tensor("dbg2", [128, 3072], F32, kind="ExternalOutput").ap()
        wts = [("VE", now("VE")), ("MM", now("MM")), ("SE", now("SE"))]
        p.dma("sync", d_dbg[:, 0:16384], zfmb[:, 0:16384], waits=wts, inc="LS")
        p.dma("sync", d_dbg[:, 16384:24576], zfmb[:, 16384:24576], inc="LS")
        p.dma("sync", d_dbg[:, 24576:26624], poolb[:, 6144:8192], inc="LS")
        p.dma("sync", d_dbg[:, 26624:28672], poolb[:, 8192:10240], inc="LS")
        p.dma("sync", d_dbg[:, 28672:30720], hbf_t[:, 0:2048], inc="LS")
        p.dma("sync", d_dbg2[:, 0:2048], pool[:, 1024:3072], inc="LS")
        p.dma("sync", d_dbg2[:, 2048:3072], pool[:, 0:1024], inc="LS")
        dbg = []
        dbg_tap = None
    elif dbg:
        sub = dbg[0][4:]
        emit_outer_ln("z")
        if sub in ("l0a", "l0"):
            emit_ffn_layer(0, parts=("la",) if sub == "l0a" else ("la", "mlp"))
        dbg_tap = "z"
        d_dbg = nc.dram_tensor("dbg", [128, 32768], BF16, kind="ExternalOutput").ap()
        d_dbg2 = nc.dram_tensor("dbg2", [128, 64], F32, kind="ExternalOutput").ap()
        p.dma("sync", d_dbg[:, 0:8192], poolb[:, 0:8192], waits=[("VE", now("VE")), ("MM", now("MM")), ("SE", now("SE"))], inc="LS")
        p.dma("sync", d_dbg[:, 8192:10240], poolb[:, QWIN:QWIN + 2048], inc="LS")
        p.dma("sync", d_dbg[:, 10240:14336], poolb[:, KRING:KRING + 4096], inc="LS")
        p.dma("sync", d_dbg[:, 14336:18496], poolb[:, VRING:VRING + 4160], inc="LS")
        p.dma("sync", d_dbg[:, 18496:22600], hbf_t[:, 0:4104], inc="LS")
        p.dma("sync", d_dbg[:, 22600:24648], hbf_t[:, OW:OW + 2048], inc="LS")
        p.dma("sync", d_dbg[:, 24648:25672], hbf_t[:, PTR:PTR + 1024], inc="LS")
        p.dma("sync", d_dbg2[:, 0:8], pcol[:], inc="LS")
    else:
        if "ffn0" in stages:
            emit_ffn_module()
        if "attn" in stages:
            emit_attn_module()
        if "ffn1" in stages:
            emit_ffn_module()
    io_e = pool[:, 2048:2176]
    ident_e = pool[:, 2176:2304]
    p.emit("gpsimd", lambda e: e.iota(io_e, [[1, 128]], channel_multiplier=-1,
                                      allow_small_or_imprecise_dtypes=True),
           waits=[("MM", ms["pool_mm"]), ("VE", ms["pool_ve"]), ("LS", now("LS"))], inc="GP")
    V_(lambda e: e.tensor_scalar(ident_e, io_e, 0.0, 1.0, op0=OP.is_equal, op1=OP.mult),
       waits=[("GP", p.now("GP"))])
    ve_id = now("VE")
    tout = [pool[:, 0:1024], pool[:, 1024:2048]]
    st_ls = [0, 0]

    def src_ap(k, tc):
        if dbg_tap == "z":
            return zc(k, tc * 128, 128)
        return xfm[:, k * T + tc * 128: k * T + tc * 128 + 128]

    for tc in range(16):
        to = tout[tc % 2]
        for k in range(KC):
            bkp = ps[:, (k % 2) * 512:(k % 2) * 512 + 512]
            T_(lambda e, o=bkp[:, 0:128], i=src_ap(k, tc), idn=ident_e: e.transpose(o, i, idn),
               waits=([("VE", ve_id), ("LS", st_ls[tc % 2])] if k == 0 else [("VE", now("VE"))]))
            V_(lambda e, o=to[:, k * 128: k * 128 + 128], i=bkp[:, 0:128]: e.tensor_copy(o, i),
               waits=[("MM", now("MM"))])
        p.dma("sync", d_out[tc * 128:(tc + 1) * 128, :], to, waits=[("VE", now("VE"))], inc="LS")
        st_ls[tc % 2] = now("LS")
    p.emit("sync", lambda e: e.nop(), waits=[("LS", now("LS"))])
    p.finish()
    for t in reversed(ctx):
        t.__exit__(None, None, None)
    return nc



def kernel(**inputs):
    staged, vix, nv = _stage_weights(inputs)
    nc = build_program(nv, vix)
    x = np.asarray(inputs["L0_attn_x"], np.float32)
    in_maps = []
    for i in range(B):
        m = dict(staged)
        m["xin"] = np.ascontiguousarray(x[i])
        in_maps.append(m)
    res = run_bass_kernel_spmd(nc, in_maps, core_ids=list(range(B)))
    return np.stack([np.asarray(r["out"], np.float32) for r in res.results], 0)


# version 11
# speedup vs baseline: 1.0532x; 1.0532x over previous
import numpy as np
import concourse.bass as bass
import concourse.mybir as mybir
from concourse.bass_utils import run_bass_kernel_spmd
from concourse.alu_op_type import AluOpType as OP
import ml_dtypes

F32 = mybir.dt.float32
BF16 = mybir.dt.bfloat16
AF = mybir.ActivationFunctionType
AX = mybir.AxisListType.X

B = 8
D = 1024
T = 2048
KC = 8
HA = 16
HD = 64
DEPTH = 2
HF = 8
WIN = 512
NW = T // WIN
HID = 2730
HIDP = 2816
NM = HIDP // 128
NEGBIG = -3.0e38
LN2_EPS = 1e-5
RMS_EPS = 1e-6

ENGINES = ("sync", "gpsimd", "tensor", "vector", "scalar")


class Prog:
    def __init__(self, nc):
        self.nc = nc
        self.items = {e: [] for e in ENGINES}
        self.sems = {}
        self.counts = {}

    def sem(self, name):
        h = self.nc.semaphore(name)
        self.sems[name] = h.__enter__()
        self.counts[name] = 0

    def now(self, s):
        return self.counts[s]

    def emit(self, eng, fn, waits=(), inc=None, inc_amt=1):
        for s, v in waits:
            assert v <= self.counts[s], f"wait on future: {s} {v} > {self.counts[s]}"
        if inc is not None:
            self.counts[inc] += inc_amt
        waits = tuple((s, v) for s, v in waits if v > 0)
        sems = self.sems

        def item(engobj, fn=fn, waits=waits, inc=inc, inc_amt=inc_amt):
            for s, v in waits:
                engobj.wait_ge(sems[s], v)
            inst = fn(engobj)
            if inc is not None:
                inst.then_inc(sems[inc], inc_amt)
        self.items[eng].append(item)

    def dma(self, eng, out, in_, waits=(), inc=None):
        self.emit(eng, lambda e: e.dma_start(out=out, in_=in_), waits, inc, 16)

    def finish(self):
        nc = self.nc
        with nc.Block() as block:
            for e in ENGINES:
                lst = self.items[e]
                if not lst:
                    continue

                def body(engobj, lst=lst):
                    for item in lst:
                        item(engobj)
                getattr(block, e)(body)



def _stage_weights(iw):
    f32 = np.float32
    out = {}
    wq = np.asarray(iw["L0_attn_Wq"], f32)
    wqp = np.zeros((D, 1152), f32)
    wqp[:, :1040] = wq
    out["a_wq"] = np.ascontiguousarray(wqp)
    out["a_wk"] = np.ascontiguousarray(np.asarray(iw["L0_attn_Wk"], f32))
    out["a_wv"] = np.ascontiguousarray(np.asarray(iw["L0_attn_Wv"], f32))
    out["a_wo"] = np.ascontiguousarray(np.asarray(iw["L0_attn_Wo"], f32))
    wqkv = np.asarray(iw["L0_ffn_wqkv"], f32)
    s = wqkv.reshape(DEPTH, KC, 128, 3, 512).transpose(0, 3, 2, 1, 4)
    out["f_wqkv"] = np.ascontiguousarray(s.reshape(DEPTH, 3, 128, KC * 512))
    wo = np.asarray(iw["L0_ffn_wo"], f32)
    s = wo.reshape(DEPTH, 4, 128, 1024).transpose(0, 2, 1, 3)
    out["f_wo"] = np.ascontiguousarray(s.reshape(DEPTH, 128, 4 * 1024))
    w1 = np.asarray(iw["L0_ffn_w1"], f32)
    a = np.zeros((DEPTH, D, HIDP), f32); a[:, :, :HID] = w1[:, :, :HID]
    g = np.zeros((DEPTH, D, HIDP), f32); g[:, :, :HID] = w1[:, :, HID:]
    inter = np.empty((DEPTH, D, NM, 256), f32)
    inter[:, :, :, :128] = a.reshape(DEPTH, D, NM, 128)
    inter[:, :, :, 128:] = g.reshape(DEPTH, D, NM, 128)
    s = inter.transpose(0, 2, 1, 3).reshape(DEPTH, NM, KC, 128, 256).transpose(0, 1, 3, 2, 4)
    out["f_w1"] = np.ascontiguousarray(s.reshape(DEPTH, NM, 128, KC * 256))
    w2 = np.asarray(iw["L0_ffn_w2"], f32)
    w2p = np.zeros((DEPTH, HIDP, D), f32); w2p[:, :HID, :] = w2
    s = w2p.reshape(DEPTH, NM, 128, 2, 512).transpose(0, 3, 1, 2, 4)
    out["f_w2"] = np.ascontiguousarray(s)
    cols = []

    def addcol(v):
        v = np.asarray(v, f32)
        if v.ndim == 1:
            v = v[:, None]
        cols.append(v)
        return sum(c.shape[1] for c in cols) - v.shape[1]

    ix = {}
    bq = np.asarray(iw["L0_attn_bq"], f32)
    ix["a_bq"] = addcol(bq[:D].reshape(KC, 128).T)
    ix["a_bgate"] = addcol(np.pad(bq[D:], (0, 112)))
    ix["a_bk"] = addcol(np.asarray(iw["L0_attn_bk"], f32).reshape(KC, 128).T)
    ix["a_bv"] = addcol(np.asarray(iw["L0_attn_bv"], f32).reshape(KC, 128).T)
    ix["a_bo"] = addcol(np.asarray(iw["L0_attn_bo"], f32).reshape(KC, 128).T)
    ix["a_qn"] = addcol(np.tile(np.asarray(iw["L0_attn_qn_w"], f32), 2))
    ix["a_kn"] = addcol(np.tile(np.asarray(iw["L0_attn_kn_w"], f32), 2))
    qs = np.asarray(iw["L0_ffn_q_scale"], f32)
    ks = np.asarray(iw["L0_ffn_k_scale"], f32)
    ix["f_qs"] = addcol(np.tile(qs, (1, 2)).T.reshape(128, DEPTH, order="F") if False else np.stack([np.tile(qs[l], 2) for l in range(DEPTH)], 1))
    ix["f_ks"] = addcol(np.stack([np.tile(ks[l], 2) for l in range(DEPTH)], 1))
    lng = np.asarray(iw["L0_ffn_ln_g"], f32)
    lnb = np.asarray(iw["L0_ffn_ln_b"], f32)
    ix["f_lng"] = addcol(np.concatenate([lng[l].reshape(KC, 128).T for l in range(DEPTH)], 1))
    ix["f_lnb"] = addcol(np.concatenate([lnb[l].reshape(KC, 128).T for l in range(DEPTH)], 1))
    b1 = np.asarray(iw["L0_ffn_b1"], f32)
    b1a = np.zeros((DEPTH, HIDP), f32); b1a[:, :HID] = b1[:, :HID]
    b1g = np.zeros((DEPTH, HIDP), f32); b1g[:, :HID] = b1[:, HID:]
    ix["f_b1a"] = addcol(np.concatenate([b1a[l].reshape(NM, 128).T for l in range(DEPTH)], 1))
    ix["f_b1g"] = addcol(np.concatenate([b1g[l].reshape(NM, 128).T for l in range(DEPTH)], 1))
    b2 = np.asarray(iw["L0_ffn_b2"], f32)
    ix["f_b2"] = addcol(np.concatenate([b2[l].reshape(KC, 128).T for l in range(DEPTH)], 1))
    ix["a_qn_bc"] = addcol(np.broadcast_to(np.asarray(iw["L0_attn_qn_w"], f32), (128, HD)))
    ix["a_kn_bc"] = addcol(np.broadcast_to(np.asarray(iw["L0_attn_kn_w"], f32), (128, HD)))
    ix["f_qs_bc"] = addcol(np.concatenate([np.broadcast_to(qs[l], (128, HD)) for l in range(DEPTH)], 1))
    ix["f_ks_bc"] = addcol(np.concatenate([np.broadcast_to(ks[l], (128, HD)) for l in range(DEPTH)], 1))
    vecs = np.concatenate(cols, 1)
    out["vecs"] = np.ascontiguousarray(vecs)
    out["bvb"] = np.ascontiguousarray(np.broadcast_to(np.asarray(iw["L0_attn_bv"], f32), (128, D)))
    return out, ix, vecs.shape[1]



def build_program(nv_cols, vix, stages=("ffn0", "attn", "ffn1"), dbg_tap=None, attn_groups=(0, 1)):
    dbg_tap = dbg_tap
    nc = bass.Bass("TRN2", target_bir_lowering=False, detect_race_conditions=False)
    PI = 3.141592653589793
    xin = nc.dram_tensor("xin", [T, D], F32, kind="ExternalInput").ap()
    d_wq = nc.dram_tensor("a_wq", [D, 1152], F32, kind="ExternalInput").ap()
    d_wk = nc.dram_tensor("a_wk", [D, D], F32, kind="ExternalInput").ap()
    d_wv = nc.dram_tensor("a_wv", [D, D], F32, kind="ExternalInput").ap()
    d_wo = nc.dram_tensor("a_wo", [D, D], F32, kind="ExternalInput").ap()
    d_fqkv = nc.dram_tensor("f_wqkv", [DEPTH, 3, 128, KC * 512], F32, kind="ExternalInput").ap()
    d_fwo = nc.dram_tensor("f_wo", [DEPTH, 128, 4 * 1024], F32, kind="ExternalInput").ap()
    d_fw1 = nc.dram_tensor("f_w1", [DEPTH, NM, 128, KC * 256], F32, kind="ExternalInput").ap()
    d_fw2 = nc.dram_tensor("f_w2", [DEPTH, 2, NM, 128, 512], F32, kind="ExternalInput").ap()
    d_vecs = nc.dram_tensor("vecs", [128, nv_cols], F32, kind="ExternalInput").ap()
    d_bvb = nc.dram_tensor("bvb", [128, D], F32, kind="ExternalInput").ap()
    d_out = nc.dram_tensor("out", [T, D], F32, kind="ExternalOutput").ap()
    d_tabs = nc.dram_tensor("tabs", [4, 128, T], BF16, kind="Internal").ap()

    ctx = []

    def sb(name, shape, dt):
        t = nc.sbuf_tensor(name, shape, dt); ctx.append(t); return t.__enter__()

    xfm = sb("xfm", [128, KC * T], F32)
    zfm = sb("zfm", [128, KC * (T + 1)], F32)
    hbf_t = sb("hbf", [128, 8192], BF16)
    wst_t = sb("wst", [128, 8192], BF16)
    pool = sb("pool", [128, 10016], F32)
    rows = sb("rows", [128, 1024], F32)
    vecs = sb("vecs_sb", [128, nv_cols], F32)
    pcol = sb("pcol", [128, 8], F32)
    cb = sb("cb", [128, 584], BF16)
    e2tf = sb("e2tf", [2, 128], F32)
    e2tb = sb("e2tb", [2, 128], BF16)
    onesf = sb("onesf", [128, 128], F32)
    identg = sb("identg", [16, 16], BF16)
    epsc = sb("epsc", [128, 4], F32)
    pst = nc.psum_tensor("pst", [128, 4096], F32); ctx.append(pst); ps = pst.__enter__()

    poolb = pool.bitcast(BF16)
    hbff = hbf_t.bitcast(F32)
    zfmb = zfm.bitcast(BF16)

    def bank(i, n=1):
        return ps[:, i * 512:(i + n) * 512]

    mgt = cb[:, 0:128]; mlt = cb[:, 128:256]; identb = cb[:, 256:384]; e2b = cb[:, 384:386]

    PC_BA, PC_BF0, PC_BF1, PC_QS80, PC_QS81, PC_QN8 = 0, 1, 2, 3, 4, 5

    def zc(c, t0, n):
        return zfm[:, c * (T + 1) + 1 + t0: c * (T + 1) + 1 + t0 + n]

    def xc(c, t0, n):
        return xfm[:, c * T + t0: c * T + t0 + n]

    V = lambda name: vecs[:, vix[name]:vix[name] + 1]

    def Vs(name, off, n=1):
        return vecs[:, vix[name] + off: vix[name] + off + n]

    p0 = Prog(nc)
    for s in ("LS", "LW", "MM", "VE", "SE", "GP"):
        p0.sem(s)
    P = p0

    def t_(fn, waits=(), inc=None):
        P.emit("tensor", fn, waits, inc)

    def v_(fn, waits=(), inc=None):
        P.emit("vector", fn, waits, inc)

    def s_(fn, waits=(), inc=None):
        P.emit("scalar", fn, waits, inc)

    def g_(fn, waits=(), inc=None):
        P.emit("gpsimd", fn, waits, inc)

    p0.dma("sync", vecs[:], d_vecs, inc="LS")
    io128 = pool[:, 2048:2176]
    ident = pool[:, 2176:2304]
    iop = pool[:, 2304:2305]
    g_(lambda e: e.iota(io128, [[1, 128]], channel_multiplier=-1, allow_small_or_imprecise_dtypes=True), inc="GP")
    g_(lambda e: e.iota(iop, [[1, 1]], channel_multiplier=1, allow_small_or_imprecise_dtypes=True), inc="GP")
    gp0 = P.now("GP")
    v_(lambda e: e.tensor_scalar(ident, io128, 0.0, 1.0, op0=OP.is_equal, op1=OP.mult), waits=[("GP", gp0)])
    g_(lambda e: e.tensor_scalar(identg[:, :], io128[0:16, 0:16], 0.0, 1.0, op0=OP.is_equal, op1=OP.mult),
       waits=[("GP", gp0)], inc="GP")
    v_(lambda e: e.tensor_scalar(mgt, io128, 0.0, NEGBIG, op0=OP.is_gt, op1=OP.mult), waits=[("GP", gp0)])
    v_(lambda e: e.tensor_scalar(mlt, io128, 0.0, NEGBIG, op0=OP.is_lt, op1=OP.mult), waits=[("GP", gp0)])
    v_(lambda e: e.tensor_copy(identb, ident), waits=[("GP", gp0)])
    g_(lambda e: e.tensor_scalar(e2b[:, 0:1], iop, 64.0, 1.0, op0=OP.is_lt, op1=OP.mult), waits=[("GP", gp0)])
    g_(lambda e: e.tensor_scalar(e2b[:, 1:2], iop, 64.0, 1.0, op0=OP.is_ge, op1=OP.mult), waits=[("GP", gp0)])
    v_(lambda e: e.memset(onesf[:], 1.0))
    v_(lambda e: e.memset(epsc[:, 0:1], 1e-5))
    v_(lambda e: e.memset(epsc[:, 1:2], 1e-30))
    v_(lambda e: e.memset(epsc[:, 2:3], 1e-6))
    v_(lambda e: e.memset(epsc[:, 3:4], -80.0))
    io2r = pool[:, 2305:2433]
    g_(lambda e: e.iota(pool[0:2, 2305:2433], [[1, 128]], channel_multiplier=64, allow_small_or_imprecise_dtypes=True), inc="GP")
    gp1 = P.now("GP")
    v_(lambda e: e.tensor_scalar(e2tf[:, :], pool[0:2, 2305:2433], 64.0, 1.0, op0=OP.is_lt, op1=OP.mult), waits=[("GP", gp1)])
    v_(lambda e: e.tensor_scalar(pool[0:2, 2433:2561], pool[0:2, 2305:2433], 128.0, 1.0, op0=OP.is_ge, op1=OP.mult), waits=[("GP", gp1)])
    v_(lambda e: e.tensor_tensor(e2tf[:, :], e2tf[:, :], pool[0:2, 2433:2561], op=OP.add), inc="VE")
    v_(lambda e: e.tensor_copy(e2tb[:, :], e2tf[:, :]))
    v_(lambda e: e.memset(cb[:, 520:584], 1.0))
    tin0 = pool[:, 0:1024]
    tin1 = pool[:, 1024:2048]
    v_(lambda e: e.engine_nop() if False else e.memset(rows[:, 0:1], 0.0), inc="VE")
    ve_ident = P.now("VE")
    mm_done_tin = [0, 0]
    for tc in range(16):
        tin = (tin0, tin1)[tc % 2]
        P.dma("sync", tin, xin[tc * 128:(tc + 1) * 128, :],
              waits=[("MM", mm_done_tin[tc % 2])], inc="LS")
        ls_need = P.now("LS")
        for k in range(KC):
            bk_ = bank(k % 2)
            t_(lambda e, o=bk_[:, 0:128], i=tin[:, k * 128:(k + 1) * 128]:
               e.transpose(o, i, ident),
               waits=([("LS", ls_need), ("VE", max(ve_ident, P.now("VE")))] if k == 0 else
                      [("VE", P.now("VE"))]),
               inc="MM")
            v_(lambda e, o=xfm[:, k * T + tc * 128: k * T + (tc) * 128 + 128], i=bk_[:, 0:128]:
               e.tensor_copy(o, i),
               waits=[("MM", P.now("MM"))], inc="VE")
        mm_done_tin[tc % 2] = P.now("MM")
    rotb = cb[:, 387:515]
    iof = pool[:, 2561:2689]
    fm64 = pool[:, 2689:2817]
    sA = pool[:, 2817:2945]
    sB = pool[:, 2945:3073]
    g_(lambda e: e.iota(iof, [[1, 128]], channel_multiplier=0, allow_small_or_imprecise_dtypes=True), inc="GP")
    gp2 = P.now("GP")
    v_(lambda e: e.tensor_scalar(fm64, iof, 64.0, 64.0, op0=OP.is_ge, op1=OP.mult), waits=[("GP", gp2)])
    v_(lambda e: e.tensor_tensor(fm64, iof, fm64, op=OP.subtract))
    v_(lambda e: e.tensor_single_scalar(sA, io128, -32.0, op=OP.is_equal), waits=[("GP", gp2)])
    v_(lambda e: e.tensor_scalar(sB, fm64, 32.0, -1.0, op0=OP.is_lt, op1=OP.mult))
    v_(lambda e: e.tensor_tensor(sA, sA, sB, op=OP.mult))
    v_(lambda e: e.tensor_single_scalar(sB, io128, 32.0, op=OP.is_equal))
    v_(lambda e: e.tensor_copy(rotb, sA))
    v_(lambda e: e.tensor_scalar(sA, fm64, 32.0, 1.0, op0=OP.is_ge, op1=OP.mult))
    v_(lambda e: e.tensor_tensor(sA, sA, sB, op=OP.mult))
    v_(lambda e: e.tensor_tensor(pool[:, 2817:2945], sA, sA, op=OP.bypass) if False else
       e.tensor_tensor(sB, sA, sA, op=OP.mult))
    v_(lambda e: e.scalar_tensor_tensor(rotb, sA, 1.0, rotb, op0=OP.mult, op1=OP.add), inc="VE")
    iop_m = pool[:, 3072:3073]
    invf = pool[:, 3073:3074]
    lnb_ = pool[:, 3074:3075]
    trow = pool[:, 3584:5632]
    fr = pool[:, 5632:7680]
    tA = pool[:, 7680:9728]
    sin_t = rows[:, 0:1024]
    LN1E4_32 = 9.210340371976184 / 32.0
    _t32 = pool[:, 3075:3076]
    g_(lambda e: e.tensor_scalar(iop_m, iop, 32.0, 32.0, op0=OP.is_ge, op1=OP.mult), waits=[("GP", gp0)], inc="GP")
    g_(lambda e: e.tensor_scalar(_t32, iop, 64.0, 32.0, op0=OP.is_ge, op1=OP.mult), waits=[("GP", P.now("GP"))], inc="GP")
    g_(lambda e: e.tensor_tensor(iop_m, iop_m, _t32, op=OP.add), waits=[("GP", P.now("GP"))], inc="GP")
    g_(lambda e: e.tensor_scalar(_t32, iop, 96.0, 32.0, op0=OP.is_ge, op1=OP.mult), waits=[("GP", P.now("GP"))], inc="GP")
    g_(lambda e: e.tensor_tensor(iop_m, iop_m, _t32, op=OP.add), waits=[("GP", P.now("GP"))], inc="GP")
    g_(lambda e: e.tensor_tensor(iop_m, iop, iop_m, op=OP.subtract), waits=[("GP", P.now("GP"))], inc="GP")
    g_(lambda e: e.tensor_scalar(lnb_, iop_m, 2.0 / 89.6, 25.6 / 89.6, op0=OP.mult, op1=OP.add), waits=[("GP", P.now("GP"))], inc="GP")
    s_(lambda e: e.activation(invf, iop_m, AF.Exp, bias=0.0, scale=-LN1E4_32), waits=[("GP", P.now("GP"))])
    s_(lambda e: e.activation(lnb_, lnb_, AF.Ln, bias=0.0, scale=1.0), inc="SE")
    g_(lambda e: e.iota(trow, [[1, 2048]], channel_multiplier=0, allow_small_or_imprecise_dtypes=True), inc="GP")
    gp3 = P.now("GP")
    PIh = PI / 2.0
    TWO_PI = 2.0 * PI
    pooli = pool.bitcast(mybir.dt.int32)
    R_K = pool[:, 0:2048]
    R_Ki = pooli[:, 0:2048]
    tAb = poolb[:, 0:4096]
    se_tab = [P.now("SE")]
    C1 = 6.28125
    C2 = 6.283185307179586 - 6.28125
    INV2PI = 1.0 / 6.283185307179586

    def build_tab(idx, is_sin, inv_scale):
        ve_tin = P.now("VE")
        g_(lambda e: e.iota(trow, [[1, 2048]], channel_multiplier=0,
                            allow_small_or_imprecise_dtypes=True),
           waits=[("VE", P.now("VE"))], inc="GP")
        gp_t = P.now("GP")
        v_(lambda e: e.tensor_scalar_mul(fr, trow, invf),
           waits=[("GP", gp_t), ("SE", se_tab[0]), ("VE", ve_tin)])
        if not is_sin:
            v_(lambda e: e.tensor_single_scalar(fr, fr, PIh, op=OP.add))
        v_(lambda e: e.tensor_scalar(trow, trow, -1024.0, 1.0 / 256.0, op0=OP.add, op1=OP.mult))
        v_(lambda e: e.tensor_scalar_mul(tA, fr, INV2PI))
        v_(lambda e: e.tensor_copy(R_Ki, tA), waits=[("LS", P.now("LS"))])
        v_(lambda e: e.tensor_copy(R_K, R_Ki))
        v_(lambda e: e.scalar_tensor_tensor(fr, R_K, -C1, fr, op0=OP.mult, op1=OP.add))
        v_(lambda e: e.scalar_tensor_tensor(fr, R_K, -C2, fr, op0=OP.mult, op1=OP.add))
        v_(lambda e: e.tensor_scalar(tA, fr, PI, TWO_PI, op0=OP.is_gt, op1=OP.mult))
        v_(lambda e: e.tensor_tensor(fr, fr, tA, op=OP.subtract))
        v_(lambda e: e.tensor_scalar(tA, fr, -PI, TWO_PI, op0=OP.is_lt, op1=OP.mult))
        v_(lambda e: e.tensor_tensor(fr, fr, tA, op=OP.add), inc="VE")
        s_(lambda e: e.activation(fr, fr, AF.Sin, bias=0.0, scale=1.0), waits=[("VE", P.now("VE"))], inc="SE")
        v_(lambda e: e.tensor_scalar_mul(tA, trow, lnb_), waits=[("SE", P.now("SE"))], inc="VE")
        s_(lambda e: e.activation(tA, tA, AF.Exp, bias=0.0, scale=(-1.0 if inv_scale else 1.0)),
           waits=[("VE", P.now("VE"))], inc="SE")
        v_(lambda e: e.tensor_tensor(tA, fr, tA, op=OP.mult), waits=[("SE", P.now("SE"))])
        v_(lambda e: e.tensor_copy(tAb[:, 0:2048], tA), inc="VE")
        P.dma("sync", d_tabs[idx], tAb[:, 0:2048], waits=[("VE", P.now("VE"))], inc="LS")
        se_tab[0] = P.now("SE")
    build_tab(0, False, False)
    build_tab(1, True, False)
    build_tab(2, False, True)
    build_tab(3, True, True)
    ls_vec = 16
    v_(lambda e: e.reduce_max(pcol[:, 6:7], Vs("a_qn_bc", 0, HD), axis=AX, apply_absolute_value=True),
       waits=[("LS", ls_vec)], inc="VE")
    v_(lambda e: e.reduce_max(pcol[:, 7:8], Vs("a_kn_bc", 0, HD), axis=AX, apply_absolute_value=True), inc="VE")
    g_(lambda e: e.tensor_tensor(pcol[:, PC_BA:PC_BA + 1], pcol[:, 6:7], pcol[:, 7:8], op=OP.mult),
       waits=[("VE", P.now("VE"))], inc="GP")
    s_(lambda e: e.activation(pcol[:, PC_BA:PC_BA + 1], pcol[:, PC_BA:PC_BA + 1], AF.Relu,
                              bias=epsc[:, 3:4], scale=8.0), waits=[("GP", P.now("GP"))])
    s_(lambda e: e.mul(pcol[:, PC_BA:PC_BA + 1], pcol[:, PC_BA:PC_BA + 1], -1.0), inc="SE")
    for l in range(DEPTH):
        v_(lambda e, l=l: e.reduce_max(pcol[:, 6:7], Vs("f_qs_bc", l * HD, HD), axis=AX, apply_absolute_value=True), inc="VE")
        v_(lambda e, l=l: e.reduce_max(pcol[:, 7:8], Vs("f_ks_bc", l * HD, HD), axis=AX, apply_absolute_value=True), inc="VE")
        g_(lambda e, l=l: e.tensor_tensor(pcol[:, PC_BF0 + l:PC_BF0 + l + 1], pcol[:, 6:7], pcol[:, 7:8], op=OP.mult),
           waits=[("VE", P.now("VE"))], inc="GP")
        s_(lambda e, l=l: e.activation(pcol[:, PC_BF0 + l:PC_BF0 + l + 1], pcol[:, PC_BF0 + l:PC_BF0 + l + 1],
                                       AF.Relu, bias=epsc[:, 3:4], scale=8.0), waits=[("GP", P.now("GP"))])
        s_(lambda e, l=l: e.mul(pcol[:, PC_BF0 + l:PC_BF0 + l + 1], pcol[:, PC_BF0 + l:PC_BF0 + l + 1], -1.0), inc="SE")
        g_(lambda e, l=l: e.tensor_single_scalar(pcol[:, PC_QS80 + l:PC_QS80 + l + 1],
                                                 Vs("f_qs", l), 8.0, op=OP.mult))
    g_(lambda e: e.tensor_single_scalar(pcol[:, PC_QN8:PC_QN8 + 1], V("a_qn"), 0.125, op=OP.mult), inc="GP")
    onesb = cb[:, 386:387]
    v_(lambda e: e.memset(onesb, 1.0))
    for c in range(KC):
        v_(lambda e, c=c: e.memset(zfm[:, c * (T + 1):c * (T + 1) + 1], 0.0))
    p0.finish()

    p = Prog(nc)
    for s in ("LS2", "LW2", "MM2", "VE2", "SE2", "GP2"):
        p.sem(s)
    for a, b in (("LS", "LS2"), ("LW", "LW2"), ("MM", "MM2"), ("VE", "VE2"), ("SE", "SE2"), ("GP", "GP2")):
        p.sems[a] = p.sems[b]
        p.counts[a] = 0

    def T_(fn, waits=(), inc="MM"):
        p.emit("tensor", fn, waits, inc)

    def V_(fn, waits=(), inc="VE"):
        p.emit("vector", fn, waits, inc)

    def S_(fn, waits=(), inc="SE"):
        p.emit("scalar", fn, waits, inc)

    def MMW(out, lhsT, rhs, first, last, waits=(), inc="MM"):
        T_(lambda e, o=out, l=lhsT, r=rhs, f=first, s=last: e.matmul(o, l, r, start=f, stop=s),
           waits=waits, inc=inc)

    now = p.now

    TAB = [0, 2048, 4096, 6144]
    KRING = 8192
    VRING = 12288
    QWIN = 16448
    T1F = 9248
    SQB = 19520
    LNC0, LNC1 = 16448, 17472
    LNTMP = 9504
    ZB = 0
    OW = 4104
    PTR = 6152

    def h2ap(j, c0, n):
        if j < 19:
            return poolb[:, j * 1024 + c0: j * 1024 + c0 + n]
        return wst_t[:, 5120 + (j - 19) * 1024 + c0: 5120 + (j - 19) * 1024 + c0 + n]

    ms = {"pool_mm": 0, "pool_ve": 0, "pool_se": 0,
          "wstA_mm": 0, "wstB_mm": 0,
          "b45_ve": 0, "b45_se": 0, "b67_ve": 0,
          "zb_mm": 0}

    RW0 = rows[0:1, 0:512]
    RW1 = rows[0:1, 512:1024]

    def emit_outer_ln(dst, tc_order=(0, 1, 2, 3)):
        cast_ve = [ms["pool_mm"], ms["pool_mm"]]
        e0v, e0s = now("VE"), now("SE")
        pairs = {tc_order[i]: bp for i, bp in enumerate([(5, 4), (3, 2), (7, 6), (1, 0)])}
        rowsets = [(rows[0:1, 0:512], rows[0:1, 512:1024]),
                   (hbff[0:1, 3072:3584], hbff[0:1, 3584:4096])]
        pend = [None]
        for i, tc in enumerate(tc_order):
            t0 = tc * 512
            bS1, bS2 = pairs[tc]
            for k in range(KC):
                cbA = poolb[:, LNC0 + (k % 2) * 512: LNC0 + (k % 2) * 512 + 512]
                cbS = poolb[:, LNC1 + (k % 2) * 512: LNC1 + (k % 2) * 512 + 512]
                V_(lambda e, o=cbA, i=xc(k, t0, 512): e.tensor_copy(o, i),
                   waits=[("MM", cast_ve[k % 2])])
                V_(lambda e, o=cbS, i=cbA: e.tensor_tensor(o, i, i, op=OP.mult))
                ve_n = now("VE")
                MMW(ps[0:1, bS1 * 512:bS1 * 512 + 512], onesb, cbA, k == 0, k == KC - 1,
                    waits=[("VE", ve_n)] + ([("VE", e0v), ("SE", e0s)] if k == 0 else []))
                MMW(ps[0:1, bS2 * 512:bS2 * 512 + 512], onesb, cbS, k == 0, k == KC - 1)
                cast_ve[k % 2] = now("MM")
            mm_stats = now("MM")
            if pend[0] is not None:
                pend[0]()

            def mk_chain(tc=tc, t0=t0, bS1=bS1, bS2=bS2, mm_stats=mm_stats, i=i):
                RA, RB = rowsets[i % 2]
                V_(lambda e, RA=RA: e.tensor_single_scalar(RA, ps[0:1, bS1 * 512:bS1 * 512 + 512], 1.0 / D, op=OP.mult),
                   waits=[("MM", mm_stats)])
                V_(lambda e, RA=RA, RB=RB: e.tensor_tensor(RB, RA, RA, op=OP.mult))
                V_(lambda e, RB=RB: e.scalar_tensor_tensor(RB, ps[0:1, bS2 * 512:bS2 * 512 + 512], 1.0 / D, RB,
                                                           op0=OP.mult, op1=OP.subtract))
                ve_rows = now("VE")
                S_(lambda e, RB=RB: e.activation(RB, RB, AF.Ln, bias=epsc[0:1, 0:1], scale=1.0),
                   waits=[("VE", ve_rows)])
                S_(lambda e, RB=RB: e.activation(RB, RB, AF.Exp, bias=0.0, scale=-0.5))
                se_rows = now("SE")
                MMW(ps[:, bS1 * 512:bS1 * 512 + 512], onesf[0:1, :], RA, True, True,
                    waits=[("SE", se_rows)])
                MMW(ps[:, bS2 * 512:bS2 * 512 + 512], onesf[0:1, :], RB, True, True)
                mm_bc = now("MM")
                for k in range(KC):
                    if dst == "z":
                        V_(lambda e, o=zc(k, t0, 512), i=xc(k, t0, 512):
                           e.tensor_tensor(o, i, ps[:, bS1 * 512:bS1 * 512 + 512], op=OP.subtract),
                           waits=[("MM", mm_bc)] if k == 0 else ())
                        V_(lambda e, o=zc(k, t0, 512): e.tensor_tensor(o, o, ps[:, bS2 * 512:bS2 * 512 + 512], op=OP.mult))
                    else:
                        yap = zfmb[:, k * T + t0: k * T + t0 + 512]
                        V_(lambda e, i=xc(k, t0, 512):
                           e.tensor_tensor(pool[:, LNTMP:LNTMP + 512], i, ps[:, bS1 * 512:bS1 * 512 + 512], op=OP.subtract),
                           waits=[("MM", mm_bc)] if k == 0 else ())
                        V_(lambda e, o=yap: e.tensor_tensor(o, pool[:, LNTMP:LNTMP + 512], ps[:, bS2 * 512:bS2 * 512 + 512], op=OP.mult))
            pend[0] = mk_chain
        pend[0]()
        pend[0] = None
        ms["b67_ve"] = now("VE")
        ms["b45_ve"] = now("VE")
        ms["b45_se"] = now("SE")
        ms["pbank0_ve"] = now("VE")
        ms["pbank1_ve"] = now("VE")
        ms["b6_ve"] = now("VE")
        ms["b7_ve"] = now("VE")
        ms["b3_ve"] = now("VE")
        ms["b2_se"] = now("SE")

    RL2 = rows[0:2, 0:512]
    RL2B = hbf_t[0:2, 7176:7688]
    RREC = rows[0:1, 0:512]
    RRECB = hbf_t[0:1, 7176:7688]
    T2F = rows[:, 512:1024]
    wslotA = wst_t[:, 0:4096]
    wslotB = wst_t[:, 4096:8192]
    slot_guard = {"A": 0, "B": 0}
    ms["b2_se"] = 0; ms["b3_ve"] = 0; ms["b6_ve"] = 0; ms["b7_ve"] = 0; ms["z_ve"] = 0
    ms["w2p_ve"] = 0; ms["x_ve"] = 0

    def stream_phase(slot, src_ap):
        tgt = wslotA if slot == "A" else wslotB
        p.dma("gpsimd", tgt[:, 0:src_ap.shape[-1]], src_ap,
              waits=[("MM", slot_guard[slot])], inc="LW")
        return p.now("LW")

    def cast_zb(w):
        for k in range(KC):
            V_(lambda e, o=hbf_t[:, k * 513: k * 513 + 513],
               i=zfm[:, k * (T + 1) + w * 512: k * (T + 1) + w * 512 + 513]:
               e.tensor_copy(o, i),
               waits=[("MM", ms["zb_mm"])] if k == 0 else ())
        return p.now("VE")

    def zb_in(k, off, n):
        sh = 1 if k < 4 else 0
        return hbf_t[:, k * 513 + sh + off: k * 513 + sh + off + n]

    def l2norm_chain(l, bankP, out_ap, scol, cs_ap, ss_ap):
        mm0 = now("MM")
        sqb = poolb[:, SQB:SQB + 512]
        S_(lambda e: e.activation(sqb, bankP, AF.Square, bias=0.0, scale=1.0), waits=[("MM", mm0)])
        MMW(ps[0:2, 2 * 512:2 * 512 + 512], e2b, sqb, True, True,
            waits=[("SE", now("SE")), ("SE", ms["b2_se"]), ("VE", ms["b3_ve"])])
        S_(lambda e: e.activation(RL2, ps[0:2, 2 * 512:2 * 512 + 512], AF.Ln, bias=epsc[0:2, 1:2], scale=1.0),
           waits=[("MM", now("MM"))])
        S_(lambda e: e.activation(RL2B, RL2, AF.Exp, bias=0.0, scale=-0.5))
        ms["b2_se"] = now("SE")
        MMW(ps[:, 3 * 512:4 * 512], e2tb[:, :], RL2B, True, True,
            waits=[("SE", now("SE")), ("VE", ms["b3_ve"])])
        mmb = now("MM")
        V_(lambda e, s=scol: e.tensor_scalar(pool[:, T1F:T1F + 512], bankP, s, None, op0=OP.mult),
           waits=[("MM", mmb)])
        V_(lambda e, o=out_ap: e.tensor_tensor(o, pool[:, T1F:T1F + 512], ps[:, 3 * 512:4 * 512],
                                               op=OP.mult))
        ms["b3_ve"] = now("VE")
        MMW(ps[:, 7 * 512:8 * 512], rotb, out_ap, True, True,
            waits=[("VE", now("VE")), ("VE", ms["b7_ve"])])
        mmr = now("MM")
        V_(lambda e, i=out_ap, c_=cs_ap: e.tensor_tensor(pool[:, T1F:T1F + 512], i, c_, op=OP.mult))
        V_(lambda e, s=ss_ap: e.tensor_tensor(T2F, ps[:, 7 * 512:8 * 512], s, op=OP.mult),
           waits=[("MM", mmr)])
        ms["b7_ve"] = now("VE")
        V_(lambda e, o=out_ap: e.tensor_tensor(o, pool[:, T1F:T1F + 512], T2F, op=OP.add))

    def emit_mlp_ln(l, th, cast_base_bf, cast_guard_mm):
        t0h = th * 1024
        e0v, e0s = now("VE"), now("SE")
        cast_ve = [cast_guard_mm, cast_guard_mm]
        pend = [None]
        for tc in range(2):
            bS1, bS2 = (5, 4) if tc == 0 else (7, 6)
            for k in range(KC):
                sh = 1 if k < 4 else 0
                zin = zfm[:, k * (T + 1) + sh + t0h + tc * 512: k * (T + 1) + sh + t0h + tc * 512 + 512]
                cbA = cast_base_bf((k % 2) * 512)
                cbS = cast_base_bf(1024 + (k % 2) * 512)
                V_(lambda e, o=cbA, i=zin: e.tensor_copy(o, i), waits=[("MM", cast_ve[k % 2])])
                V_(lambda e, o=cbS, i=cbA: e.tensor_tensor(o, i, i, op=OP.mult))
                ve_n = now("VE")
                MMW(ps[0:1, bS1 * 512:bS1 * 512 + 512], onesb, cbA, k == 0, k == KC - 1,
                    waits=[("VE", ve_n)] + ([("VE", e0v), ("SE", e0s)] if k == 0 else []))
                MMW(ps[0:1, bS2 * 512:bS2 * 512 + 512], onesb, cbS, k == 0, k == KC - 1)
                cast_ve[k % 2] = now("MM")
            mm_stats = now("MM")
            if pend[0] is not None:
                pend[0]()

            def mk_chain(tc=tc, bS1=bS1, bS2=bS2, mm_stats=mm_stats):
                rb_ = 0
                RA = rows[0:1, 0:512]
                RB = rows[0:1, 512:1024]
                V_(lambda e, RA=RA: e.tensor_single_scalar(RA, ps[0:1, bS1 * 512:bS1 * 512 + 512], 1.0 / D, op=OP.mult),
                   waits=[("MM", now("MM"))])
                V_(lambda e, RA=RA, RB=RB: e.tensor_tensor(RB, RA, RA, op=OP.mult))
                V_(lambda e, RB=RB: e.scalar_tensor_tensor(RB, ps[0:1, bS2 * 512:bS2 * 512 + 512], 1.0 / D, RB,
                                                           op0=OP.mult, op1=OP.subtract))
                ve_rows = now("VE")
                S_(lambda e, RB=RB: e.activation(RB, RB, AF.Ln, bias=epsc[0:1, 0:1], scale=1.0),
                   waits=[("VE", ve_rows)])
                S_(lambda e, RB=RB: e.activation(RB, RB, AF.Exp, bias=0.0, scale=-0.5))
                se_rows = now("SE")
                MMW(ps[:, bS1 * 512:bS1 * 512 + 512], onesf[rb_:rb_ + 1, :], RA, True, True,
                    waits=[("SE", se_rows)])
                MMW(ps[:, bS2 * 512:bS2 * 512 + 512], onesf[rb_:rb_ + 1, :], RB, True, True)
                mm_bc = now("MM")
                for k in range(KC):
                    sh = 1 if k < 4 else 0
                    zin = zfm[:, k * (T + 1) + sh + t0h + tc * 512: k * (T + 1) + sh + t0h + tc * 512 + 512]
                    V_(lambda e, i=zin: e.tensor_tensor(pool[:, LNTMP:LNTMP + 512], i,
                                                        ps[:, bS1 * 512:bS1 * 512 + 512], op=OP.subtract),
                       waits=[("MM", mm_bc)] if k == 0 else ())
                    V_(lambda e: e.tensor_tensor(pool[:, LNTMP:LNTMP + 512], pool[:, LNTMP:LNTMP + 512],
                                                 ps[:, bS2 * 512:bS2 * 512 + 512], op=OP.mult))
                    V_(lambda e, o=hbf_t[:, k * 1024 + tc * 512: k * 1024 + tc * 512 + 512],
                       g_=Vs("f_lng", l * KC + k), b_=Vs("f_lnb", l * KC + k):
                       e.tensor_scalar(o, pool[:, LNTMP:LNTMP + 512], g_, b_, op0=OP.mult, op1=OP.add))
                ms["mlpln_tc%d_ve" % tc] = now("VE")
            pend[0] = mk_chain
        pend[0]()
        ms["mlpln_ve"] = now("VE")
        ms["mlpln_mm"] = now("MM")

    def emit_ffn_layer(l, parts=("la", "mlp")):
        for i in range(4):
            p.dma("sync", poolb[:, TAB[i]:TAB[i] + 2048], d_tabs[i],
                  waits=[("MM", ms["pool_mm"]), ("VE", ms["pool_ve"])], inc="LS")
        tab_ls = now("LS")
        zb_ve = [0]
        pbank_ve = [ms.get("pbank0_ve", 0), ms.get("pbank1_ve", 0)]

        def proj_q(w):
            ls_need = stream_phase("A", d_fqkv[l, 0])
            pend = [None]
            for c in range(4):
                bk = ps[:, (c % 2) * 512:(c % 2) * 512 + 512]
                for k in range(KC):
                    w_ = [("VE", pbank_ve[c % 2])] if k == 0 else []
                    if c == 0 and k == 0:
                        w_ += [("LW", ls_need), ("VE", zb_ve[0]), ("LS", tab_ls)]
                    MMW(bk, wslotA[:, k * 512 + c * 128: k * 512 + c * 128 + 128],
                        zb_in(k, 0, 512), k == 0, k == KC - 1, waits=w_)
                if pend[0] is not None:
                    pend[0]()

                def mk(c=c, bk=bk):
                    cs = poolb[:, TAB[0] + w * 512: TAB[0] + w * 512 + 512]
                    ss = poolb[:, TAB[1] + w * 512: TAB[1] + w * 512 + 512]
                    l2norm_chain(l, bk, poolb[:, QWIN + c * 512: QWIN + c * 512 + 512],
                                 pcol[:, PC_QS80 + l:PC_QS80 + l + 1], cs, ss)
                    pbank_ve[c % 2] = now("VE")
                pend[0] = mk
            pend[0]()
            slot_guard["A"] = now("MM")

        def proj_k(wk):
            ls_need = stream_phase("B", d_fqkv[l, 1])
            kbase = KRING + (wk % 2) * 2048
            pend = [None]
            for c in range(4):
                bk = ps[:, (c % 2) * 512:(c % 2) * 512 + 512]
                for k in range(KC):
                    w_ = [("VE", pbank_ve[c % 2])] if k == 0 else []
                    if c == 0 and k == 0:
                        w_ += [("LW", ls_need), ("VE", zb_ve[0])]
                    MMW(bk, wslotB[:, k * 512 + c * 128: k * 512 + c * 128 + 128],
                        zb_in(k, 0, 512), k == 0, k == KC - 1, waits=w_)
                if pend[0] is not None:
                    pend[0]()

                def mk(c=c, bk=bk):
                    cs = poolb[:, TAB[2] + wk * 512: TAB[2] + wk * 512 + 512]
                    ss = poolb[:, TAB[3] + wk * 512: TAB[3] + wk * 512 + 512]
                    l2norm_chain(l, bk, poolb[:, kbase + c * 512: kbase + c * 512 + 512],
                                 Vs("f_ks", l), cs, ss)
                    pbank_ve[c % 2] = now("VE")
                pend[0] = mk
            pend[0]()
            slot_guard["B"] = now("MM")

        def proj_v(wk):
            ls_need = stream_phase("A", d_fqkv[l, 2])
            vbase = VRING + (wk % 2) * 2080
            pend = [None]
            for tch in range(4):
                bk = ps[:, (tch % 2) * 512:(tch % 2) * 512 + 512]
                for k in range(KC):
                    w_ = [("VE", pbank_ve[tch % 2])] if k == 0 else []
                    if tch == 0 and k == 0:
                        w_ += [("LW", ls_need), ("VE", zb_ve[0])]
                    MMW(bk, zb_in(k, tch * 128, 128), wslotA[:, k * 512: k * 512 + 512],
                        k == 0, k == KC - 1, waits=w_)
                mmv = now("MM")
                if pend[0] is not None:
                    pend[0]()

                def mk(tch=tch, bk=bk, mmv=mmv):
                    dst = poolb[:, vbase + tch * 520: vbase + tch * 520 + 520]
                    V_(lambda e, o=dst.rearrange("p (h c) -> p h c", c=65),
                       i=bk.rearrange("p (h c) -> p h c", c=64):
                       e.tensor_copy(o[:, :, 0:64], i), waits=[("MM", mmv)])
                    V_(lambda e, o=dst.rearrange("p (h c) -> p h c", c=65): e.memset(o[:, :, 64:65], 1.0))
                    pbank_ve[tch % 2] = now("VE")
                pend[0] = mk
            pend[0]()
            slot_guard["A"] = now("MM")

        def attention(w):
            ve_ready = now("VE")
            st_se = [ms["b2_se"], ms["b2_se"]]
            ptr_mm = [0, 0]
            st_i = [0]
            pend_evac = [None]

            def emit_evac(h, pvb, mm_pv):
                S_(lambda e, i=ps[64:65, pvb * 512: pvb * 512 + 512]:
                   e.activation(RREC, i, AF.Ln, bias=0.0, scale=1.0), waits=[("MM", mm_pv)])
                S_(lambda e: e.activation(RRECB, RREC, AF.Exp, bias=0.0, scale=-1.0))
                MMW(ps[0:64, 3 * 512:3 * 512 + 512], cb[0:1, 520:584], RRECB, True, True,
                    waits=[("SE", now("SE")), ("VE", ms["b3_ve"])])
                mmb = now("MM")
                r0 = 64 * (h % 2)
                oap = hbf_t[r0:r0 + 64, OW + (h // 2) * 512: OW + (h // 2) * 512 + 512]
                V_(lambda e, o=oap, i=ps[0:64, pvb * 512: pvb * 512 + 512]:
                   e.tensor_copy(o, i), waits=[("MM", mmb)])
                V_(lambda e, o=oap:
                   e.tensor_tensor(o, o, ps[0:64, 3 * 512:3 * 512 + 512], op=OP.mult))
                ms["b3_ve"] = now("VE")
                if pvb == 6:
                    ms["b6_ve"] = now("VE")
                else:
                    ms["b7_ve"] = now("VE")

            for h in range(HF):
                pvb = 6 + (h % 2)
                pv_guard = ms["b6_ve"] if pvb == 6 else ms["b7_ve"]
                pv_started = [False]
                cx_list = list(range(4, 8) if w == 0 else range(8))
                pend_pv = [None]
                for cx in cx_list:
                    par = st_i[0] % 2
                    st_i[0] += 1
                    lo = max(0, 128 * (cx - 4)); hi = min(512, 128 * (cx + 1)); width = hi - lo
                    win_of_cx = (w - 1) if cx < 4 else w
                    kbase = KRING + (win_of_cx % 2) * 2048
                    vbase = VRING + (win_of_cx % 2) * 2080
                    r0 = 64 * (h % 2)
                    stb = ps[:, (4 + par) * 512:(4 + par) * 512 + 512]
                    ktile = poolb[r0:r0 + 64, kbase + (h // 2) * 512 + (cx % 4) * 128:
                                  kbase + (h // 2) * 512 + (cx % 4) * 128 + 128]
                    qtile = poolb[r0:r0 + 64, QWIN + (h // 2) * 512 + lo: QWIN + (h // 2) * 512 + hi]
                    MMW(stb[:, 0:width], ktile, qtile, True, False,
                        waits=[("SE", st_se[par]), ("VE", ve_ready)])
                    if cx >= 4:
                        MMW(stb[:, 0:128], mgt, identb, False, True)
                    else:
                        MMW(stb[:, width - 128:width], mlt, identb, False, True)
                    mm_st = now("MM")
                    if pend_pv[0] is not None:
                        pend_pv[0]()
                        pend_pv[0] = None
                    ptr = hbf_t[:, PTR + par * 512: PTR + par * 512 + width]
                    S_(lambda e, o=ptr, i=stb[:, 0:width], b=pcol[:, PC_BF0 + l:PC_BF0 + l + 1]:
                       e.activation(o, i, AF.Exp, bias=b, scale=1.0),
                       waits=[("MM", mm_st), ("MM", ptr_mm[par])])
                    st_se[par] = now("SE")
                    se_exp = now("SE")

                    def mk_pv(cx=cx, lo=lo, se_exp=se_exp, vbase=vbase, ptr=ptr, par=par, pvb=pvb):
                        for d in range(max(0, cx - 4), min(3, cx) + 1):
                            first = not pv_started[0]; pv_started[0] = True
                            last = (cx == 7 and d == 3)
                            vtile = poolb[:, vbase + (cx % 4) * 520 + 65 * h:
                                          vbase + (cx % 4) * 520 + 65 * h + 65]
                            w_ = [("SE", se_exp)]
                            if first:
                                w_.append(("VE", pv_guard))
                            MMW(ps[0:65, pvb * 512 + 128 * d: pvb * 512 + 128 * d + 128],
                                vtile, ptr[:, 128 * d - lo:128 * d - lo + 128], first, last, waits=w_)
                        ptr_mm[par] = now("MM")
                    pend_pv[0] = mk_pv
                pend_pv[0]()
                pend_pv[0] = None
                mm_pv = now("MM")
                if pend_evac[0] is not None:
                    pend_evac[0]()
                pend_evac[0] = (lambda h=h, pvb=pvb, mm_pv=mm_pv: emit_evac(h, pvb, mm_pv))
            if pend_evac[0] is not None:
                pend_evac[0]()
                pend_evac[0] = None

        def wo_stage(w):
            ls_need = stream_phase("B", d_fwo[l])
            ve_o = now("VE")
            for m in range(KC):
                bk = ps[:, (m % 2) * 512:(m % 2) * 512 + 512]
                for kc in range(4):
                    w_ = [("VE", pbank_ve[m % 2])] if kc == 0 else []
                    if m == 0 and kc == 0:
                        w_ += [("LW", ls_need), ("VE", ve_o)]
                    MMW(bk, wslotB[:, kc * 1024 + m * 128: kc * 1024 + m * 128 + 128],
                        hbf_t[:, OW + kc * 512: OW + kc * 512 + 512], kc == 0, kc == 3, waits=w_)
                mmw = now("MM")
                V_(lambda e, o=zc(m, w * 512, 512), b=bk: e.tensor_tensor(o, o, b, op=OP.add),
                   waits=[("MM", mmw)])
                pbank_ve[m % 2] = now("VE")
            slot_guard["B"] = now("MM")
            ms["z_ve"] = now("VE")

        for wi, w in enumerate([3, 2, 1, 0] if "la" in parts else []):
            if wi == 0:
                zb_ve[0] = cast_zb(w)
                proj_q(w)
                proj_k(w)
                proj_v(w)
                ms["zb_mm"] = now("MM")
                zb_ve[0] = cast_zb(w - 1)
                proj_k(w - 1)
                proj_v(w - 1)
                ms["zb_mm"] = now("MM")
            else:
                proj_q(w)
                ms["zb_mm"] = now("MM")
                if w > 0:
                    zb_ve[0] = cast_zb(w - 1)
                    proj_k(w - 1)
                    proj_v(w - 1)
                    ms["zb_mm"] = now("MM")
            attention(w)
            wo_stage(w)
        ms["pbank0_ve"] = pbank_ve[0]
        ms["pbank1_ve"] = pbank_ve[1]
        ms["pool_mm"] = now("MM")
        ms["pool_ve"] = now("VE")

        w1g = [slot_guard["A"], slot_guard["B"]]
        w2g = [0, 0]
        n_w1 = [0]
        n_w2 = [0]

        def load_w1(j):
            sl = n_w1[0] % 2
            p.dma("gpsimd", wst_t[:, sl * 2048:(sl + 1) * 2048], d_fw1[l, j],
                  waits=[("MM", w1g[sl])], inc="LW")
            n_w1[0] += 1
            return p.now("LW")

        def load_w2(mg, j):
            sl = n_w2[0] % 2
            p.dma("gpsimd", wst_t[:, 4096 + sl * 512:4096 + (sl + 1) * 512], d_fw2[l, mg, j],
                  waits=[("MM", w2g[sl])], inc="LW")
            n_w2[0] += 1
            return p.now("LW")

        ag_ve = [now("VE"), now("VE")]
        for th in ((1, 0) if "mlp" in parts else ()):
            t0h = th * 1024
            emit_mlp_ln(l, th, (lambda c0: poolb[:, c0:c0 + 512]), ms["pool_mm"])
            ln_ve = now("VE")
            ln_tc0_ve = ms["mlpln_tc0_ve"]
            load_w1(0)
            slot_of = {0: (n_w1[0] - 1) % 2}
            lw_of = {0: p.now("LW")}
            load_w1(1)
            slot_of[1] = (n_w1[0] - 1) % 2
            lw_of[1] = p.now("LW")
            for j in range(NM):
                sl = slot_of[j]
                w1v = wst_t[:, sl * 2048:(sl + 1) * 2048]
                for tc in range(2):
                    ssb = (j * 2 + tc) % 2
                    agA = ps[:, (ssb * 2) * 512:(ssb * 2) * 512 + 512]
                    agG = ps[:, (ssb * 2 + 1) * 512:(ssb * 2 + 1) * 512 + 512]
                    for k in range(KC):
                        rhs = hbf_t[:, k * 1024 + tc * 512: k * 1024 + tc * 512 + 512]
                        first = (k == 0); lastk = (k == KC - 1)
                        wts = []
                        if k == 0:
                            wts = [("LW", lw_of[j]), ("VE", max(ag_ve[ssb], ln_ve))]
                        MMW(agA, w1v[:, k * 256: k * 256 + 128], rhs, first, lastk, waits=wts)
                        MMW(agG, w1v[:, k * 256 + 128: k * 256 + 256], rhs, first, lastk)
                    w1g[sl] = now("MM")
                    mm_ag = now("MM")
                    glA = rows[:, 0:512] if ssb == 0 else rows[:, 512:1024]
                    S_(lambda e, o=glA, i=agG, b=Vs("f_b1g", l * NM + j):
                       e.activation(o, i, AF.Gelu, bias=b, scale=1.0), waits=[("MM", mm_ag)])
                    V_(lambda e, o=h2ap(j, tc * 512, 512), a=agA, s=Vs("f_b1a", l * NM + j), g_=glA:
                       e.scalar_tensor_tensor(o, a, s, g_, op0=OP.add, op1=OP.mult),
                       waits=[("SE", now("SE"))])
                    ag_ve[ssb] = now("VE")
                if j + 2 < NM:
                    slot_of[j + 2] = n_w1[0] % 2
                    lw_of[j + 2] = load_w1(j + 2)
                elif th == 1 and j + 2 >= NM and (j + 2 - NM) < 2:
                    pass
            h2_ve = now("VE")
            w2g[0] = now("MM"); w2g[1] = now("MM")
            load_w2(0, 0)
            w2slot_of = {(0, 0): (n_w2[0] - 1) % 2}
            w2lw_of = {(0, 0): p.now("LW")}
            load_w2(0, 1)
            w2slot_of[(0, 1)] = (n_w2[0] - 1) % 2
            w2lw_of[(0, 1)] = p.now("LW")
            for mg in range(2):
                for j in range(NM):
                    sl = w2slot_of[(mg, j)]
                    w2v = wst_t[:, 4096 + sl * 512:4096 + (sl + 1) * 512]
                    first = (j == 0); lastj = (j == NM - 1)
                    for m in range(4):
                        for q in range(2):
                            wts = []
                            if m == 0 and q == 0:
                                wts = [("LW", w2lw_of[(mg, j)])]
                                if j == 0:
                                    wts += [("VE", h2_ve), ("VE", ms["w2p_ve"])]
                            MMW(ps[:, m * 1024 + q * 512: m * 1024 + q * 512 + 512],
                                w2v[:, m * 128: m * 128 + 128],
                                h2ap(j, q * 512, 512), first, lastj, waits=wts)
                    w2g[sl] = now("MM")
                    nxt = j + 2
                    if nxt < NM:
                        w2slot_of[(mg, nxt)] = n_w2[0] % 2
                        w2lw_of[(mg, nxt)] = load_w2(mg, nxt)
                    elif mg == 0 and (nxt - NM) < 2:
                        w2slot_of[(1, nxt - NM)] = n_w2[0] % 2
                        w2lw_of[(1, nxt - NM)] = load_w2(1, nxt - NM)
                mm_w2 = now("MM")
                for m in range(4):
                    mo = mg * 4 + m
                    for q in range(2):
                        V_(lambda e, o=zc(mo, t0h + q * 512, 512),
                           i=ps[:, m * 1024 + q * 512: m * 1024 + q * 512 + 512],
                           s=Vs("f_b2", l * KC + mo):
                           e.scalar_tensor_tensor(o, i, s, o, op0=OP.add, op1=OP.add),
                           waits=[("MM", mm_w2)] if (m == 0 and q == 0) else ())
                ms["w2p_ve"] = now("VE")
            ms["z_ve"] = now("VE")
            ms["pool_mm"] = now("MM")
            ag_ve = [now("VE"), now("VE")]
        ms["pbank0_ve"] = now("VE")
        ms["pbank1_ve"] = now("VE")
        slot_guard["A"] = now("MM")
        slot_guard["B"] = now("MM")
        ms["pool_ve"] = now("VE")

    def emit_ffn_module():
        mmg0 = now("MM")
        for c in range(KC):
            V_(lambda e, c=c: e.memset(zfm[:, c * (T + 1):c * (T + 1) + 1], 0.0),
               waits=[("MM", mmg0)] if c == 0 else ())
        emit_outer_ln("z", (3, 2, 1, 0))
        for l in range(DEPTH):
            emit_ffn_layer(l)
        mm_g = now("MM")
        for k in range(KC):
            V_(lambda e, o=xc(k, 0, T), i=zc(k, 0, T):
               e.scalar_tensor_tensor(o, i, 0.5, o, op0=OP.mult, op1=OP.add))
        ms["x_ve"] = now("VE")

    def emit_attn_module():
        emit_outer_ln("ybf")
        p.dma("sync", pool[:, 0:1024], d_bvb, waits=[("MM", ms["pool_mm"]), ("VE", ms["pool_ve"])], inc="LS")
        bvb_ls = now("LS")
        sig = poolb[0:16, 2048:4096]
        QRB = [6144, 8192]
        PTB = [10240, 11264]

        def ybf_t(k, t0, n):
            return zfmb[:, k * T + t0: k * T + t0 + n]

        def vg_ap(tch, lg):
            if tch < 15:
                return zfmb[:, 24576 + tch * 520 + 65 * lg: 24576 + tch * 520 + 65 * lg + 65]
            return poolb[:, 14336 + 65 * lg: 14336 + 65 * lg + 65]

        def vg_full(tch):
            if tch < 15:
                return zfmb[:, 24576 + tch * 520: 24576 + tch * 520 + 520]
            return poolb[:, 14336: 14336 + 520]

        def o_ap(c, r0, t0, n):
            if c < 3:
                return hbf_t[r0:r0 + 64, c * 2048 + t0: c * 2048 + t0 + n]
            return poolb[r0:r0 + 64, 12288 + t0: 12288 + t0 + n]

        R0s = [pool[0:1, 7428:8452], pool[32:33, 7428:8452]]
        R2s = [pool[0:1, 8452:9476], pool[32:33, 8452:9476]]
        upar = [0]
        def o_full(c, qc):
            if c < 3:
                return hbf_t[:, c * 2048 + qc * 512: c * 2048 + qc * 512 + 512]
            return poolb[:, 12288 + qc * 512: 12288 + qc * 512 + 512]

        qr_mm = [0, 0]
        pt_mm = [0, 0]
        st_se_a = [0, 0]
        pend_gchain = [None]
        abank = [ms.get("pbank0_ve", 0), ms.get("pbank1_ve", 0)]
        ms["b67_att_ve"] = now("VE")

        for hg in attn_groups:
            for k in range(KC):
                p.dma("gpsimd", wslotA[:, k * 512: k * 512 + 512],
                      d_wv[k * 128:(k + 1) * 128, hg * 512:(hg + 1) * 512],
                      waits=[("MM", slot_guard["A"])] if k == 0 else (), inc="LW")
            lw_v = now("LW")
            pendv = [None]
            for tch in range(16):
                bk = ps[:, (tch % 2) * 512:(tch % 2) * 512 + 512]
                for k in range(KC):
                    w_ = [("VE", abank[tch % 2])] if k == 0 else []
                    if tch == 0 and k == 0:
                        w_ += [("LW", lw_v), ("VE", now("VE")), ("LS", bvb_ls)]
                    MMW(bk, ybf_t(k, tch * 128, 128), wslotA[:, k * 512: k * 512 + 512],
                        k == 0, k == KC - 1, waits=w_)
                mmv = now("MM")
                if pendv[0] is not None:
                    pendv[0]()

                def mkv(tch=tch, bk=bk, mmv=mmv):
                    dst = vg_full(tch)
                    V_(lambda e, o=dst.rearrange("p (h c) -> p h c", c=65),
                       i=bk.rearrange("p (h c) -> p h c", c=64),
                       bv=pool[:, hg * 512:hg * 512 + 512].rearrange("p (h c) -> p h c", c=64):
                       e.tensor_tensor(o[:, :, 0:64], i, bv, op=OP.add), waits=[("MM", mmv)])
                    V_(lambda e, o=dst.rearrange("p (h c) -> p h c", c=65): e.memset(o[:, :, 64:65], 1.0))
                    abank[tch % 2] = now("VE")
                pendv[0] = mkv
            pendv[0]()
            slot_guard["A"] = now("MM")
            for k in range(KC):
                p.dma("gpsimd", wslotB[:, k * 512: k * 512 + 512],
                      d_wk[k * 128:(k + 1) * 128, hg * 512:(hg + 1) * 512],
                      waits=[("MM", slot_guard["B"])] if k == 0 else (), inc="LW")
            lw_k = now("LW")

            def norm_chain(bk, bias_col, wcol, out_ap):
                ktmp = pool[:, 7428:7940]
                sqb = poolb[:, 15880:16392]
                mm0 = now("MM")
                V_(lambda e, i=bk, b=bias_col: e.tensor_scalar(ktmp, i, b, None, op0=OP.add),
                   waits=[("MM", mm0)])
                V_(lambda e: e.tensor_tensor(sqb, ktmp, ktmp, op=OP.mult))
                MMW(ps[0:2, 2 * 512:2 * 512 + 512], e2b, sqb, True, True,
                    waits=[("VE", now("VE")), ("SE", now("SE"))])
                S_(lambda e: e.activation(RL2, ps[0:2, 2 * 512:2 * 512 + 512], AF.Ln,
                                          bias=epsc[0:2, 2:3], scale=1.0 / HD), waits=[("MM", now("MM"))])
                S_(lambda e: e.activation(RL2B, RL2, AF.Exp, bias=0.0, scale=-0.5))
                ms["b2_se"] = now("SE")
                MMW(ps[:, 3 * 512:4 * 512], e2tb[:, :], RL2B, True, True,
                    waits=[("SE", now("SE")), ("VE", ms["b3_ve"]), ("SE", now("SE"))])
                mmb = now("MM")
                V_(lambda e, o=out_ap, w=wcol: e.scalar_tensor_tensor(o, ktmp, w, ps[:, 3 * 512:4 * 512],
                                                                      op0=OP.mult, op1=OP.mult),
                   waits=[("MM", mmb)])
                ms["b3_ve"] = now("VE")

            pendn = [None]
            for c in range(4):
                for qc in range(4):
                    bk = ps[:, (qc % 2) * 512:(qc % 2) * 512 + 512]
                    for k in range(KC):
                        w_ = [("VE", abank[qc % 2])] if k == 0 else []
                        if c == 0 and qc == 0 and k == 0:
                            w_ += [("LW", lw_k)]
                        MMW(bk, wslotB[:, k * 512 + c * 128: k * 512 + c * 128 + 128],
                            ybf_t(k, qc * 512, 512), k == 0, k == KC - 1, waits=w_)
                    if pendn[0] is not None:
                        pendn[0]()

                    def mkn(c=c, qc=qc, bk=bk):
                        norm_chain(bk, Vs("a_bk", hg * 4 + c), V("a_kn"),
                                   zfmb[:, 16384 + c * 2048 + qc * 512: 16384 + c * 2048 + qc * 512 + 512])
                        abank[qc % 2] = now("VE")
                    pendn[0] = mkn
            pendn[0]()
            slot_guard["B"] = now("MM")
            for k in range(KC):
                p.dma("gpsimd", wslotA[:, k * 512: k * 512 + 512],
                      d_wq[k * 128:(k + 1) * 128, hg * 512:(hg + 1) * 512],
                      waits=[("MM", slot_guard["A"])] if k == 0 else (), inc="LW")
            lw_q = now("LW")
            if hg == 0:
                for k in range(KC):
                    p.dma("gpsimd", wslotB[:, k * 128: k * 128 + 128],
                          d_wq[k * 128:(k + 1) * 128, 1024:1152],
                          waits=[("MM", slot_guard["B"])] if k == 0 else (), inc="LW")
                lw_g = now("LW")
                for qc in range(4):
                    bk = ps[:, (qc % 2) * 512:(qc % 2) * 512 + 512]
                    for k in range(KC):
                        w_ = [("VE", abank[qc % 2]), ("SE", now("SE"))] if k == 0 else []
                        if qc == 0 and k == 0:
                            w_ += [("LW", lw_g)]
                        MMW(bk, wslotB[:, k * 128: k * 128 + 128], ybf_t(k, qc * 512, 512),
                            k == 0, k == KC - 1, waits=w_)
                    mmg = now("MM")
                    S_(lambda e, o=sig[:, qc * 512: qc * 512 + 512], i=bk[0:16, :]:
                       e.activation(o, i, AF.Sigmoid, bias=V("a_bgate")[0:16, :], scale=1.0),
                       waits=[("MM", mmg)])
                    abank[qc % 2] = now("MM") * 0 + abank[qc % 2]
                gate_se = now("SE")
                slot_guard["B"] = now("MM")
            sig_se = now("SE")
            for c in range(4):
                qsl = c % 2
                qbase = QRB[qsl]
                pendq = [None]
                for qc in range(4):
                    bk = ps[:, (qc % 2) * 512:(qc % 2) * 512 + 512]
                    for k in range(KC):
                        w_ = [("VE", abank[qc % 2]), ("SE", now("SE"))] if k == 0 else []
                        if qc == 0 and k == 0:
                            w_ += [("LW", lw_q), ("MM", qr_mm[qsl])]
                        MMW(bk, wslotA[:, k * 512 + c * 128: k * 512 + c * 128 + 128],
                            ybf_t(k, qc * 512, 512), k == 0, k == KC - 1, waits=w_)
                    if pendq[0] is not None:
                        pendq[0]()

                    def mkq(qc=qc, bk=bk, c=c):
                        norm_chain(bk, Vs("a_bq", hg * 4 + c), pcol[:, PC_QN8:PC_QN8 + 1],
                                   poolb[:, qbase + qc * 512: qbase + qc * 512 + 512])
                        abank[qc % 2] = now("VE")
                    pendq[0] = mkq
                pendq[0]()
                ve_pair = now("VE")
                for lh in range(2):
                    h = 2 * c + lh
                    hh = hg * 8 + h
                    r0 = 64 * lh
                    for qh in range(2):
                        pend_pv = [None]
                        pv_started = [False]
                        for kc in range(16):
                            stpar = kc % 2
                            stbase = (2 + 2 * stpar) * 512
                            for qq in range(2):
                                MMW(ps[:, stbase + qq * 512: stbase + qq * 512 + 512],
                                    zfmb[r0:r0 + 64, 16384 + c * 2048 + kc * 128: 16384 + c * 2048 + kc * 128 + 128],
                                    poolb[r0:r0 + 64, qbase + qh * 1024 + qq * 512: qbase + qh * 1024 + qq * 512 + 512],
                                    True, True,
                                    waits=([("VE", ve_pair), ("SE", st_se_a[stpar]), ("SE", ms["b2_se"])]
                                           if qq == 0 else ()))
                            mm_st = now("MM")
                            if pend_pv[0] is not None:
                                pend_pv[0]()
                                pend_pv[0] = None
                            ptr = poolb[:, PTB[stpar]:PTB[stpar] + 1024]
                            se_exps = []
                            for qq in range(2):
                                S_(lambda e, o=ptr[:, qq * 512: qq * 512 + 512],
                                   i=ps[:, stbase + qq * 512: stbase + qq * 512 + 512]:
                                   e.activation(o, i, AF.Exp, bias=pcol[:, PC_BA:PC_BA + 1], scale=1.0),
                                   waits=[("MM", mm_st), ("MM", pt_mm[stpar])] if qq == 0 else ())
                                se_exps.append(now("SE"))
                            st_se_a[stpar] = now("SE")

                            def mk_pv(kc=kc, h=h, ptr=ptr, stpar=stpar, se_exps=tuple(se_exps)):
                                for qq in range(2):
                                    first = (kc == 0)
                                    w_ = [("SE", se_exps[qq])]
                                    if first and qq == 0:
                                        w_.append(("VE", ms["b67_att_ve"]))
                                    MMW(ps[0:65, 6 * 512 + qq * 512: 6 * 512 + qq * 512 + 512],
                                        vg_ap(kc, h), ptr[:, qq * 512: qq * 512 + 512],
                                        first, kc == 15, waits=w_)
                                pt_mm[stpar] = now("MM")
                            pend_pv[0] = mk_pv
                        pend_pv[0]()
                        pend_pv[0] = None
                        mm_pv = now("MM")
                        up = upar[0]; upar[0] ^= 1
                        R0 = R0s[up]; R2 = R2s[up]
                        V_(lambda e, o=o_ap(c, r0, qh * 1024, 1024), i=ps[0:64, 6 * 512: 6 * 512 + 1024]:
                           e.tensor_copy(o, i), waits=[("MM", mm_pv)])
                        V_(lambda e, R2=R2: e.tensor_copy(R2, ps[64:65, 6 * 512: 6 * 512 + 1024]))
                        ms["b67_att_ve"] = now("VE")
                        ve_sum = now("VE")
                        if pend_gchain[0] is not None:
                            pend_gchain[0]()

                        def mk_gchain(c=c, r0=r0, qh=qh, hh=hh, ve_sum=ve_sum, R0=R0, R2=R2, up=up):
                            rbase = 32 * up
                            for qq in range(2):
                                MMW(ps[0:1, qq * 512: qq * 512 + 512],
                                    identg[:, hh:hh + 1], sig[:, qh * 1024 + qq * 512: qh * 1024 + qq * 512 + 512],
                                    True, True,
                                    waits=[("VE", ve_sum), ("SE", sig_se), ("VE", abank[qq])] if qq == 0
                                          else [("VE", abank[qq])])
                            mm_gx = now("MM")
                            V_(lambda e: e.tensor_copy(R0, ps[0:1, 0:1024]),
                               waits=[("MM", mm_gx)])
                            S_(lambda e: e.activation(R2, R2, AF.Ln, bias=0.0, scale=1.0),
                               waits=[("VE", now("VE"))])
                            S_(lambda e: e.activation(R2, R2, AF.Exp, bias=0.0, scale=-1.0))
                            R0B = poolb[rbase:rbase + 1, 2 * 9504: 2 * 9504 + 1024]
                            V_(lambda e, R0B=R0B: e.tensor_tensor(R0B, R0, R2, op=OP.mult),
                               waits=[("SE", now("SE"))])
                            ve_row = now("VE")
                            for qq in range(2):
                                MMW(ps[0:64, qq * 512: qq * 512 + 512],
                                    cb[rbase:rbase + 1, 520:584],
                                    R0B[:, qq * 512: qq * 512 + 512],
                                    True, True, waits=[("VE", ve_row)] if qq == 0 else ())
                            mm_g2 = now("MM")
                            V_(lambda e, o=o_ap(c, r0, qh * 1024, 1024):
                               e.tensor_tensor(o, o, ps[0:64, 0:1024], op=OP.mult),
                               waits=[("MM", mm_g2)])
                            abank[0] = now("VE")
                            abank[1] = now("VE")
                        pend_gchain[0] = mk_gchain
                if pend_gchain[0] is not None:
                    pend_gchain[0]()
                    pend_gchain[0] = None
                qr_mm[qsl] = now("MM")
            slot_guard["A"] = now("MM")
            for kc in range(4):
                p.dma("gpsimd", wslotB[:, kc * 1024: kc * 1024 + 1024],
                      d_wo[hg * 512 + kc * 128: hg * 512 + kc * 128 + 128, :],
                      waits=[("MM", slot_guard["B"])] if kc == 0 else (), inc="LW")
            lw_o = now("LW")
            ve_o = now("VE")
            for m in range(KC):
                for qc in range(4):
                    bk = ps[:, (qc % 2) * 512:(qc % 2) * 512 + 512]
                    for kc in range(4):
                        w_ = [("VE", abank[qc % 2])] if kc == 0 else []
                        if m == 0 and qc == 0 and kc == 0:
                            w_ += [("LW", lw_o), ("VE", ve_o)]
                        MMW(bk, wslotB[:, kc * 1024 + m * 128: kc * 1024 + m * 128 + 128],
                            o_full(kc, qc),
                            kc == 0, kc == 3, waits=w_)
                    mmw = now("MM")
                    if hg == 0:
                        V_(lambda e, o=xc(m, qc * 512, 512), b=bk, s=Vs("a_bo", m):
                           e.scalar_tensor_tensor(o, b, s, o, op0=OP.add, op1=OP.add),
                           waits=[("MM", mmw)])
                    else:
                        V_(lambda e, o=xc(m, qc * 512, 512), b=bk: e.tensor_tensor(o, o, b, op=OP.add),
                           waits=[("MM", mmw)])
                    abank[qc % 2] = now("VE")
            slot_guard["B"] = now("MM")
        ms["pbank0_ve"] = now("VE")
        ms["pbank1_ve"] = now("VE")
        ms["pool_mm"] = now("MM")
        ms["pool_ve"] = now("VE")
        ms["x_ve"] = now("VE")

    dbg = [s for s in stages if s.startswith("dbg_")]
    if dbg and dbg[0] == "dbg_attn":
        emit_attn_module()
        d_dbg = nc.dram_tensor("dbg", [128, 32768], BF16, kind="ExternalOutput").ap()
        d_dbg2 = nc.dram_tensor("dbg2", [128, 3072], F32, kind="ExternalOutput").ap()
        wts = [("VE", now("VE")), ("MM", now("MM")), ("SE", now("SE"))]
        p.dma("sync", d_dbg[:, 0:16384], zfmb[:, 0:16384], waits=wts, inc="LS")
        p.dma("sync", d_dbg[:, 16384:24576], zfmb[:, 16384:24576], inc="LS")
        p.dma("sync", d_dbg[:, 24576:26624], poolb[:, 6144:8192], inc="LS")
        p.dma("sync", d_dbg[:, 26624:28672], poolb[:, 8192:10240], inc="LS")
        p.dma("sync", d_dbg[:, 28672:30720], hbf_t[:, 0:2048], inc="LS")
        p.dma("sync", d_dbg2[:, 0:2048], pool[:, 1024:3072], inc="LS")
        p.dma("sync", d_dbg2[:, 2048:3072], pool[:, 0:1024], inc="LS")
        dbg = []
        dbg_tap = None
    elif dbg:
        sub = dbg[0][4:]
        emit_outer_ln("z")
        if sub in ("l0a", "l0"):
            emit_ffn_layer(0, parts=("la",) if sub == "l0a" else ("la", "mlp"))
        dbg_tap = "z"
        d_dbg = nc.dram_tensor("dbg", [128, 32768], BF16, kind="ExternalOutput").ap()
        d_dbg2 = nc.dram_tensor("dbg2", [128, 64], F32, kind="ExternalOutput").ap()
        p.dma("sync", d_dbg[:, 0:8192], poolb[:, 0:8192], waits=[("VE", now("VE")), ("MM", now("MM")), ("SE", now("SE"))], inc="LS")
        p.dma("sync", d_dbg[:, 8192:10240], poolb[:, QWIN:QWIN + 2048], inc="LS")
        p.dma("sync", d_dbg[:, 10240:14336], poolb[:, KRING:KRING + 4096], inc="LS")
        p.dma("sync", d_dbg[:, 14336:18496], poolb[:, VRING:VRING + 4160], inc="LS")
        p.dma("sync", d_dbg[:, 18496:22600], hbf_t[:, 0:4104], inc="LS")
        p.dma("sync", d_dbg[:, 22600:24648], hbf_t[:, OW:OW + 2048], inc="LS")
        p.dma("sync", d_dbg[:, 24648:25672], hbf_t[:, PTR:PTR + 1024], inc="LS")
        p.dma("sync", d_dbg2[:, 0:8], pcol[:], inc="LS")
    else:
        if "ffn0" in stages:
            emit_ffn_module()
        if "attn" in stages:
            emit_attn_module()
        if "ffn1" in stages:
            emit_ffn_module()
    io_e = pool[:, 2048:2176]
    ident_e = pool[:, 2176:2304]
    p.emit("gpsimd", lambda e: e.iota(io_e, [[1, 128]], channel_multiplier=-1,
                                      allow_small_or_imprecise_dtypes=True),
           waits=[("MM", ms["pool_mm"]), ("VE", ms["pool_ve"]), ("LS", now("LS"))], inc="GP")
    V_(lambda e: e.tensor_scalar(ident_e, io_e, 0.0, 1.0, op0=OP.is_equal, op1=OP.mult),
       waits=[("GP", p.now("GP"))])
    ve_id = now("VE")
    tout = [pool[:, 0:1024], pool[:, 1024:2048]]
    st_ls = [0, 0]

    def src_ap(k, tc):
        if dbg_tap == "z":
            return zc(k, tc * 128, 128)
        return xfm[:, k * T + tc * 128: k * T + tc * 128 + 128]

    for tc in range(16):
        to = tout[tc % 2]
        for k in range(KC):
            bkp = ps[:, (k % 2) * 512:(k % 2) * 512 + 512]
            T_(lambda e, o=bkp[:, 0:128], i=src_ap(k, tc), idn=ident_e: e.transpose(o, i, idn),
               waits=([("VE", ve_id), ("LS", st_ls[tc % 2])] if k == 0 else [("VE", now("VE"))]))
            V_(lambda e, o=to[:, k * 128: k * 128 + 128], i=bkp[:, 0:128]: e.tensor_copy(o, i),
               waits=[("MM", now("MM"))])
        p.dma("sync", d_out[tc * 128:(tc + 1) * 128, :], to, waits=[("VE", now("VE"))], inc="LS")
        st_ls[tc % 2] = now("LS")
    p.emit("sync", lambda e: e.nop(), waits=[("LS", now("LS"))])
    p.finish()
    for t in reversed(ctx):
        t.__exit__(None, None, None)
    return nc



def kernel(**inputs):
    staged, vix, nv = _stage_weights(inputs)
    nc = build_program(nv, vix)
    x = np.asarray(inputs["L0_attn_x"], np.float32)
    in_maps = []
    for i in range(B):
        m = dict(staged)
        m["xin"] = np.ascontiguousarray(x[i])
        in_maps.append(m)
    res = run_bass_kernel_spmd(nc, in_maps, core_ids=list(range(B)))
    return np.stack([np.asarray(r["out"], np.float32) for r in res.results], 0)


# version 12
# speedup vs baseline: 1.0566x; 1.0032x over previous
import numpy as np
import concourse.bass as bass
import concourse.mybir as mybir
from concourse.bass_utils import run_bass_kernel_spmd
from concourse.alu_op_type import AluOpType as OP
import ml_dtypes

F32 = mybir.dt.float32
BF16 = mybir.dt.bfloat16
AF = mybir.ActivationFunctionType
AX = mybir.AxisListType.X

B = 8
D = 1024
T = 2048
KC = 8
HA = 16
HD = 64
DEPTH = 2
HF = 8
WIN = 512
NW = T // WIN
HID = 2730
HIDP = 2816
NM = HIDP // 128
NEGBIG = -3.0e38
LN2_EPS = 1e-5
RMS_EPS = 1e-6

ENGINES = ("sync", "gpsimd", "tensor", "vector", "scalar")


class Prog:
    def __init__(self, nc):
        self.nc = nc
        self.items = {e: [] for e in ENGINES}
        self.sems = {}
        self.counts = {}

    def sem(self, name):
        h = self.nc.semaphore(name)
        self.sems[name] = h.__enter__()
        self.counts[name] = 0

    def now(self, s):
        return self.counts[s]

    def emit(self, eng, fn, waits=(), inc=None, inc_amt=1):
        for s, v in waits:
            assert v <= self.counts[s], f"wait on future: {s} {v} > {self.counts[s]}"
        if inc is not None:
            self.counts[inc] += inc_amt
        waits = tuple((s, v) for s, v in waits if v > 0)
        sems = self.sems

        def item(engobj, fn=fn, waits=waits, inc=inc, inc_amt=inc_amt):
            for s, v in waits:
                engobj.wait_ge(sems[s], v)
            inst = fn(engobj)
            if inc is not None:
                inst.then_inc(sems[inc], inc_amt)
        self.items[eng].append(item)

    def dma(self, eng, out, in_, waits=(), inc=None):
        self.emit(eng, lambda e: e.dma_start(out=out, in_=in_), waits, inc, 16)

    def finish(self):
        nc = self.nc
        with nc.Block() as block:
            for e in ENGINES:
                lst = self.items[e]
                if not lst:
                    continue

                def body(engobj, lst=lst):
                    for item in lst:
                        item(engobj)
                getattr(block, e)(body)



def _stage_weights(iw):
    f32 = np.float32
    out = {}
    wq = np.asarray(iw["L0_attn_Wq"], f32)
    wqp = np.zeros((D, 1152), f32)
    wqp[:, :1040] = wq
    out["a_wq"] = np.ascontiguousarray(wqp)
    out["a_wk"] = np.ascontiguousarray(np.asarray(iw["L0_attn_Wk"], f32))
    out["a_wv"] = np.ascontiguousarray(np.asarray(iw["L0_attn_Wv"], f32))
    out["a_wo"] = np.ascontiguousarray(np.asarray(iw["L0_attn_Wo"], f32))
    wqkv = np.asarray(iw["L0_ffn_wqkv"], f32)
    s = wqkv.reshape(DEPTH, KC, 128, 3, 512).transpose(0, 3, 2, 1, 4)
    out["f_wqkv"] = np.ascontiguousarray(s.reshape(DEPTH, 3, 128, KC * 512))
    wo = np.asarray(iw["L0_ffn_wo"], f32)
    s = wo.reshape(DEPTH, 4, 128, 1024).transpose(0, 2, 1, 3)
    out["f_wo"] = np.ascontiguousarray(s.reshape(DEPTH, 128, 4 * 1024))
    w1 = np.asarray(iw["L0_ffn_w1"], f32)
    a = np.zeros((DEPTH, D, HIDP), f32); a[:, :, :HID] = w1[:, :, :HID]
    g = np.zeros((DEPTH, D, HIDP), f32); g[:, :, :HID] = w1[:, :, HID:]
    inter = np.empty((DEPTH, D, NM, 256), f32)
    inter[:, :, :, :128] = a.reshape(DEPTH, D, NM, 128)
    inter[:, :, :, 128:] = g.reshape(DEPTH, D, NM, 128)
    s = inter.transpose(0, 2, 1, 3).reshape(DEPTH, NM, KC, 128, 256).transpose(0, 1, 3, 2, 4)
    out["f_w1"] = np.ascontiguousarray(s.reshape(DEPTH, NM, 128, KC * 256))
    w2 = np.asarray(iw["L0_ffn_w2"], f32)
    w2p = np.zeros((DEPTH, HIDP, D), f32); w2p[:, :HID, :] = w2
    s = w2p.reshape(DEPTH, NM, 128, 2, 512).transpose(0, 3, 1, 2, 4)
    out["f_w2"] = np.ascontiguousarray(s)
    cols = []

    def addcol(v):
        v = np.asarray(v, f32)
        if v.ndim == 1:
            v = v[:, None]
        cols.append(v)
        return sum(c.shape[1] for c in cols) - v.shape[1]

    ix = {}
    bq = np.asarray(iw["L0_attn_bq"], f32)
    ix["a_bq"] = addcol(bq[:D].reshape(KC, 128).T)
    ix["a_bgate"] = addcol(np.pad(bq[D:], (0, 112)))
    ix["a_bk"] = addcol(np.asarray(iw["L0_attn_bk"], f32).reshape(KC, 128).T)
    ix["a_bv"] = addcol(np.asarray(iw["L0_attn_bv"], f32).reshape(KC, 128).T)
    ix["a_bo"] = addcol(np.asarray(iw["L0_attn_bo"], f32).reshape(KC, 128).T)
    ix["a_qn"] = addcol(np.tile(np.asarray(iw["L0_attn_qn_w"], f32), 2))
    ix["a_kn"] = addcol(np.tile(np.asarray(iw["L0_attn_kn_w"], f32), 2))
    qs = np.asarray(iw["L0_ffn_q_scale"], f32)
    ks = np.asarray(iw["L0_ffn_k_scale"], f32)
    ix["f_qs"] = addcol(np.tile(qs, (1, 2)).T.reshape(128, DEPTH, order="F") if False else np.stack([np.tile(qs[l], 2) for l in range(DEPTH)], 1))
    ix["f_ks"] = addcol(np.stack([np.tile(ks[l], 2) for l in range(DEPTH)], 1))
    lng = np.asarray(iw["L0_ffn_ln_g"], f32)
    lnb = np.asarray(iw["L0_ffn_ln_b"], f32)
    ix["f_lng"] = addcol(np.concatenate([lng[l].reshape(KC, 128).T for l in range(DEPTH)], 1))
    ix["f_lnb"] = addcol(np.concatenate([lnb[l].reshape(KC, 128).T for l in range(DEPTH)], 1))
    b1 = np.asarray(iw["L0_ffn_b1"], f32)
    b1a = np.zeros((DEPTH, HIDP), f32); b1a[:, :HID] = b1[:, :HID]
    b1g = np.zeros((DEPTH, HIDP), f32); b1g[:, :HID] = b1[:, HID:]
    ix["f_b1a"] = addcol(np.concatenate([b1a[l].reshape(NM, 128).T for l in range(DEPTH)], 1))
    ix["f_b1g"] = addcol(np.concatenate([b1g[l].reshape(NM, 128).T for l in range(DEPTH)], 1))
    b2 = np.asarray(iw["L0_ffn_b2"], f32)
    ix["f_b2"] = addcol(np.concatenate([b2[l].reshape(KC, 128).T for l in range(DEPTH)], 1))
    ix["a_qn_bc"] = addcol(np.broadcast_to(np.asarray(iw["L0_attn_qn_w"], f32), (128, HD)))
    ix["a_kn_bc"] = addcol(np.broadcast_to(np.asarray(iw["L0_attn_kn_w"], f32), (128, HD)))
    ix["f_qs_bc"] = addcol(np.concatenate([np.broadcast_to(qs[l], (128, HD)) for l in range(DEPTH)], 1))
    ix["f_ks_bc"] = addcol(np.concatenate([np.broadcast_to(ks[l], (128, HD)) for l in range(DEPTH)], 1))
    vecs = np.concatenate(cols, 1)
    out["vecs"] = np.ascontiguousarray(vecs)
    out["bvb"] = np.ascontiguousarray(np.broadcast_to(np.asarray(iw["L0_attn_bv"], f32), (128, D)))
    return out, ix, vecs.shape[1]



def build_program(nv_cols, vix, stages=("ffn0", "attn", "ffn1"), dbg_tap=None, attn_groups=(0, 1)):
    dbg_tap = dbg_tap
    nc = bass.Bass("TRN2", target_bir_lowering=False, detect_race_conditions=False)
    PI = 3.141592653589793
    xin = nc.dram_tensor("xin", [T, D], F32, kind="ExternalInput").ap()
    d_wq = nc.dram_tensor("a_wq", [D, 1152], F32, kind="ExternalInput").ap()
    d_wk = nc.dram_tensor("a_wk", [D, D], F32, kind="ExternalInput").ap()
    d_wv = nc.dram_tensor("a_wv", [D, D], F32, kind="ExternalInput").ap()
    d_wo = nc.dram_tensor("a_wo", [D, D], F32, kind="ExternalInput").ap()
    d_fqkv = nc.dram_tensor("f_wqkv", [DEPTH, 3, 128, KC * 512], F32, kind="ExternalInput").ap()
    d_fwo = nc.dram_tensor("f_wo", [DEPTH, 128, 4 * 1024], F32, kind="ExternalInput").ap()
    d_fw1 = nc.dram_tensor("f_w1", [DEPTH, NM, 128, KC * 256], F32, kind="ExternalInput").ap()
    d_fw2 = nc.dram_tensor("f_w2", [DEPTH, 2, NM, 128, 512], F32, kind="ExternalInput").ap()
    d_vecs = nc.dram_tensor("vecs", [128, nv_cols], F32, kind="ExternalInput").ap()
    d_bvb = nc.dram_tensor("bvb", [128, D], F32, kind="ExternalInput").ap()
    d_out = nc.dram_tensor("out", [T, D], F32, kind="ExternalOutput").ap()
    d_tabs = nc.dram_tensor("tabs", [4, 128, T], BF16, kind="Internal").ap()

    ctx = []

    def sb(name, shape, dt):
        t = nc.sbuf_tensor(name, shape, dt); ctx.append(t); return t.__enter__()

    xfm = sb("xfm", [128, KC * T], F32)
    zfm = sb("zfm", [128, KC * (T + 1)], F32)
    hbf_t = sb("hbf", [128, 8192], BF16)
    wst_t = sb("wst", [128, 8192], BF16)
    pool = sb("pool", [128, 10016], F32)
    rows = sb("rows", [128, 1024], F32)
    vecs = sb("vecs_sb", [128, nv_cols], F32)
    pcol = sb("pcol", [128, 8], F32)
    cb = sb("cb", [128, 584], BF16)
    e2tf = sb("e2tf", [2, 128], F32)
    e2tb = sb("e2tb", [2, 128], BF16)
    onesf = sb("onesf", [128, 128], F32)
    identg = sb("identg", [16, 16], BF16)
    epsc = sb("epsc", [128, 4], F32)
    pst = nc.psum_tensor("pst", [128, 4096], F32); ctx.append(pst); ps = pst.__enter__()

    poolb = pool.bitcast(BF16)
    hbff = hbf_t.bitcast(F32)
    zfmb = zfm.bitcast(BF16)

    def bank(i, n=1):
        return ps[:, i * 512:(i + n) * 512]

    mgt = cb[:, 0:128]; mlt = cb[:, 128:256]; identb = cb[:, 256:384]; e2b = cb[:, 384:386]

    PC_BA, PC_BF0, PC_BF1, PC_QS80, PC_QS81, PC_QN8 = 0, 1, 2, 3, 4, 5

    def zc(c, t0, n):
        return zfm[:, c * (T + 1) + 1 + t0: c * (T + 1) + 1 + t0 + n]

    def xc(c, t0, n):
        return xfm[:, c * T + t0: c * T + t0 + n]

    V = lambda name: vecs[:, vix[name]:vix[name] + 1]

    def Vs(name, off, n=1):
        return vecs[:, vix[name] + off: vix[name] + off + n]

    p0 = Prog(nc)
    for s in ("LS", "LW", "MM", "VE", "SE", "GP"):
        p0.sem(s)
    P = p0

    def t_(fn, waits=(), inc=None):
        P.emit("tensor", fn, waits, inc)

    def v_(fn, waits=(), inc=None):
        P.emit("vector", fn, waits, inc)

    def s_(fn, waits=(), inc=None):
        P.emit("scalar", fn, waits, inc)

    def g_(fn, waits=(), inc=None):
        P.emit("gpsimd", fn, waits, inc)

    p0.dma("sync", vecs[:], d_vecs, inc="LS")
    io128 = pool[:, 2048:2176]
    ident = pool[:, 2176:2304]
    iop = pool[:, 2304:2305]
    g_(lambda e: e.iota(io128, [[1, 128]], channel_multiplier=-1, allow_small_or_imprecise_dtypes=True), inc="GP")
    g_(lambda e: e.iota(iop, [[1, 1]], channel_multiplier=1, allow_small_or_imprecise_dtypes=True), inc="GP")
    gp0 = P.now("GP")
    v_(lambda e: e.tensor_scalar(ident, io128, 0.0, 1.0, op0=OP.is_equal, op1=OP.mult), waits=[("GP", gp0)])
    g_(lambda e: e.tensor_scalar(identg[:, :], io128[0:16, 0:16], 0.0, 1.0, op0=OP.is_equal, op1=OP.mult),
       waits=[("GP", gp0)], inc="GP")
    v_(lambda e: e.tensor_scalar(mgt, io128, 0.0, NEGBIG, op0=OP.is_gt, op1=OP.mult), waits=[("GP", gp0)])
    v_(lambda e: e.tensor_scalar(mlt, io128, 0.0, NEGBIG, op0=OP.is_lt, op1=OP.mult), waits=[("GP", gp0)])
    v_(lambda e: e.tensor_copy(identb, ident), waits=[("GP", gp0)])
    g_(lambda e: e.tensor_scalar(e2b[:, 0:1], iop, 64.0, 1.0, op0=OP.is_lt, op1=OP.mult), waits=[("GP", gp0)])
    g_(lambda e: e.tensor_scalar(e2b[:, 1:2], iop, 64.0, 1.0, op0=OP.is_ge, op1=OP.mult), waits=[("GP", gp0)])
    v_(lambda e: e.memset(onesf[:], 1.0))
    v_(lambda e: e.memset(epsc[:, 0:1], 1e-5))
    v_(lambda e: e.memset(epsc[:, 1:2], 1e-30))
    v_(lambda e: e.memset(epsc[:, 2:3], 1e-6))
    v_(lambda e: e.memset(epsc[:, 3:4], -80.0))
    io2r = pool[:, 2305:2433]
    g_(lambda e: e.iota(pool[0:2, 2305:2433], [[1, 128]], channel_multiplier=64, allow_small_or_imprecise_dtypes=True), inc="GP")
    gp1 = P.now("GP")
    v_(lambda e: e.tensor_scalar(e2tf[:, :], pool[0:2, 2305:2433], 64.0, 1.0, op0=OP.is_lt, op1=OP.mult), waits=[("GP", gp1)])
    v_(lambda e: e.tensor_scalar(pool[0:2, 2433:2561], pool[0:2, 2305:2433], 128.0, 1.0, op0=OP.is_ge, op1=OP.mult), waits=[("GP", gp1)])
    v_(lambda e: e.tensor_tensor(e2tf[:, :], e2tf[:, :], pool[0:2, 2433:2561], op=OP.add), inc="VE")
    v_(lambda e: e.tensor_copy(e2tb[:, :], e2tf[:, :]))
    v_(lambda e: e.memset(cb[:, 520:584], 1.0))
    rotb = cb[:, 387:515]
    iof = pool[:, 2561:2689]
    fm64 = pool[:, 2689:2817]
    sA = pool[:, 2817:2945]
    sB = pool[:, 2945:3073]
    g_(lambda e: e.iota(iof, [[1, 128]], channel_multiplier=0, allow_small_or_imprecise_dtypes=True), inc="GP")
    gp2 = P.now("GP")
    v_(lambda e: e.tensor_scalar(fm64, iof, 64.0, 64.0, op0=OP.is_ge, op1=OP.mult), waits=[("GP", gp2)])
    v_(lambda e: e.tensor_tensor(fm64, iof, fm64, op=OP.subtract))
    v_(lambda e: e.tensor_single_scalar(sA, io128, -32.0, op=OP.is_equal), waits=[("GP", gp2)])
    v_(lambda e: e.tensor_scalar(sB, fm64, 32.0, -1.0, op0=OP.is_lt, op1=OP.mult))
    v_(lambda e: e.tensor_tensor(sA, sA, sB, op=OP.mult))
    v_(lambda e: e.tensor_single_scalar(sB, io128, 32.0, op=OP.is_equal))
    v_(lambda e: e.tensor_copy(rotb, sA))
    v_(lambda e: e.tensor_scalar(sA, fm64, 32.0, 1.0, op0=OP.is_ge, op1=OP.mult))
    v_(lambda e: e.tensor_tensor(sA, sA, sB, op=OP.mult))
    v_(lambda e: e.tensor_tensor(pool[:, 2817:2945], sA, sA, op=OP.bypass) if False else
       e.tensor_tensor(sB, sA, sA, op=OP.mult))
    v_(lambda e: e.scalar_tensor_tensor(rotb, sA, 1.0, rotb, op0=OP.mult, op1=OP.add), inc="VE")
    iop_m = pool[:, 3076:3077]
    invf = pool[:, 3073:3074]
    lnb_ = pool[:, 3074:3075]
    trow = pool[:, 3584:5632]
    fr = pool[:, 5632:7680]
    tA = pool[:, 7680:9728]
    sin_t = rows[:, 0:1024]
    LN1E4_32 = 9.210340371976184 / 32.0
    _t32 = pool[:, 3075:3076]
    g_(lambda e: e.tensor_scalar(iop_m, iop, 32.0, 32.0, op0=OP.is_ge, op1=OP.mult),
       waits=[("GP", gp0), ("VE", P.now("VE"))], inc="GP")
    g_(lambda e: e.tensor_scalar(_t32, iop, 64.0, 32.0, op0=OP.is_ge, op1=OP.mult), waits=[("GP", P.now("GP"))], inc="GP")
    g_(lambda e: e.tensor_tensor(iop_m, iop_m, _t32, op=OP.add), waits=[("GP", P.now("GP"))], inc="GP")
    g_(lambda e: e.tensor_scalar(_t32, iop, 96.0, 32.0, op0=OP.is_ge, op1=OP.mult), waits=[("GP", P.now("GP"))], inc="GP")
    g_(lambda e: e.tensor_tensor(iop_m, iop_m, _t32, op=OP.add), waits=[("GP", P.now("GP"))], inc="GP")
    g_(lambda e: e.tensor_tensor(iop_m, iop, iop_m, op=OP.subtract), waits=[("GP", P.now("GP"))], inc="GP")
    g_(lambda e: e.tensor_scalar(lnb_, iop_m, 2.0 / 89.6, 25.6 / 89.6, op0=OP.mult, op1=OP.add), waits=[("GP", P.now("GP"))], inc="GP")
    s_(lambda e: e.activation(invf, iop_m, AF.Exp, bias=0.0, scale=-LN1E4_32), waits=[("GP", P.now("GP"))])
    s_(lambda e: e.activation(lnb_, lnb_, AF.Ln, bias=0.0, scale=1.0), inc="SE")
    g_(lambda e: e.iota(trow, [[1, 2048]], channel_multiplier=0, allow_small_or_imprecise_dtypes=True), inc="GP")
    gp3 = P.now("GP")
    PIh = PI / 2.0
    TWO_PI = 2.0 * PI
    pooli = pool.bitcast(mybir.dt.int32)
    R_K = pool[:, 0:2048]
    R_Ki = pooli[:, 0:2048]
    tAb = poolb[:, 0:4096]
    se_tab = [P.now("SE")]
    C1 = 6.28125
    C2 = 6.283185307179586 - 6.28125
    INV2PI = 1.0 / 6.283185307179586

    def build_tab(idx, is_sin, inv_scale):
        g_(lambda e: e.iota(trow, [[1, 2048]], channel_multiplier=0,
                            allow_small_or_imprecise_dtypes=True),
           waits=[("VE", P.now("VE"))], inc="GP")
        gp_t = P.now("GP")
        v_(lambda e: e.tensor_scalar_mul(fr, trow, invf),
           waits=[("GP", gp_t), ("SE", se_tab[0])])
        if not is_sin:
            v_(lambda e: e.tensor_single_scalar(fr, fr, PIh, op=OP.add))
        v_(lambda e: e.tensor_scalar(trow, trow, -1024.0, 1.0 / 256.0, op0=OP.add, op1=OP.mult))
        v_(lambda e: e.tensor_scalar_mul(tA, fr, INV2PI))
        v_(lambda e: e.tensor_copy(R_Ki, tA), waits=[("LS", P.now("LS"))])
        v_(lambda e: e.tensor_copy(R_K, R_Ki))
        v_(lambda e: e.scalar_tensor_tensor(fr, R_K, -C1, fr, op0=OP.mult, op1=OP.add))
        v_(lambda e: e.scalar_tensor_tensor(fr, R_K, -C2, fr, op0=OP.mult, op1=OP.add))
        v_(lambda e: e.tensor_scalar(tA, fr, PI, TWO_PI, op0=OP.is_gt, op1=OP.mult))
        v_(lambda e: e.tensor_tensor(fr, fr, tA, op=OP.subtract))
        v_(lambda e: e.tensor_scalar(tA, fr, -PI, TWO_PI, op0=OP.is_lt, op1=OP.mult))
        v_(lambda e: e.tensor_tensor(fr, fr, tA, op=OP.add), inc="VE")
        s_(lambda e: e.activation(fr, fr, AF.Sin, bias=0.0, scale=1.0), waits=[("VE", P.now("VE"))], inc="SE")
        v_(lambda e: e.tensor_scalar_mul(tA, trow, lnb_), waits=[("SE", P.now("SE"))], inc="VE")
        s_(lambda e: e.activation(tA, tA, AF.Exp, bias=0.0, scale=(-1.0 if inv_scale else 1.0)),
           waits=[("VE", P.now("VE"))], inc="SE")
        v_(lambda e: e.tensor_tensor(tA, fr, tA, op=OP.mult), waits=[("SE", P.now("SE"))])
        v_(lambda e: e.tensor_copy(tAb[:, 0:2048], tA), inc="VE")
        P.dma("sync", d_tabs[idx], tAb[:, 0:2048], waits=[("VE", P.now("VE"))], inc="LS")
        se_tab[0] = P.now("SE")
    build_tab(0, False, False)
    build_tab(1, True, False)
    build_tab(2, False, True)
    build_tab(3, True, True)
    ls_vec = 16
    v_(lambda e: e.reduce_max(pcol[:, 6:7], Vs("a_qn_bc", 0, HD), axis=AX, apply_absolute_value=True),
       waits=[("LS", ls_vec)], inc="VE")
    v_(lambda e: e.reduce_max(pcol[:, 7:8], Vs("a_kn_bc", 0, HD), axis=AX, apply_absolute_value=True), inc="VE")
    g_(lambda e: e.tensor_tensor(pcol[:, PC_BA:PC_BA + 1], pcol[:, 6:7], pcol[:, 7:8], op=OP.mult),
       waits=[("VE", P.now("VE"))], inc="GP")
    s_(lambda e: e.activation(pcol[:, PC_BA:PC_BA + 1], pcol[:, PC_BA:PC_BA + 1], AF.Relu,
                              bias=epsc[:, 3:4], scale=8.0), waits=[("GP", P.now("GP"))])
    s_(lambda e: e.mul(pcol[:, PC_BA:PC_BA + 1], pcol[:, PC_BA:PC_BA + 1], -1.0), inc="SE")
    for l in range(DEPTH):
        v_(lambda e, l=l: e.reduce_max(pcol[:, 6:7], Vs("f_qs_bc", l * HD, HD), axis=AX, apply_absolute_value=True), inc="VE")
        v_(lambda e, l=l: e.reduce_max(pcol[:, 7:8], Vs("f_ks_bc", l * HD, HD), axis=AX, apply_absolute_value=True), inc="VE")
        g_(lambda e, l=l: e.tensor_tensor(pcol[:, PC_BF0 + l:PC_BF0 + l + 1], pcol[:, 6:7], pcol[:, 7:8], op=OP.mult),
           waits=[("VE", P.now("VE"))], inc="GP")
        s_(lambda e, l=l: e.activation(pcol[:, PC_BF0 + l:PC_BF0 + l + 1], pcol[:, PC_BF0 + l:PC_BF0 + l + 1],
                                       AF.Relu, bias=epsc[:, 3:4], scale=8.0), waits=[("GP", P.now("GP"))])
        s_(lambda e, l=l: e.mul(pcol[:, PC_BF0 + l:PC_BF0 + l + 1], pcol[:, PC_BF0 + l:PC_BF0 + l + 1], -1.0), inc="SE")
        g_(lambda e, l=l: e.tensor_single_scalar(pcol[:, PC_QS80 + l:PC_QS80 + l + 1],
                                                 Vs("f_qs", l), 8.0, op=OP.mult))
    g_(lambda e: e.tensor_single_scalar(pcol[:, PC_QN8:PC_QN8 + 1], V("a_qn"), 0.125, op=OP.mult), inc="GP")
    tin0 = pool[:, 0:1024]
    tin1 = pool[:, 1024:2048]
    v_(lambda e: e.engine_nop() if False else e.memset(rows[:, 0:1], 0.0), inc="VE")
    ve_ident = P.now("VE")
    mm_done_tin = [0, 0]
    for tc in range(16):
        tin = (tin0, tin1)[tc % 2]
        P.dma("sync", tin, xin[tc * 128:(tc + 1) * 128, :],
              waits=[("MM", mm_done_tin[tc % 2])], inc="LS")
        ls_need = P.now("LS")
        for k in range(KC):
            bk_ = bank(k % 2)
            t_(lambda e, o=bk_[:, 0:128], i=tin[:, k * 128:(k + 1) * 128]:
               e.transpose(o, i, ident),
               waits=([("LS", ls_need), ("VE", max(ve_ident, P.now("VE")))] if k == 0 else
                      [("VE", P.now("VE"))]),
               inc="MM")
            v_(lambda e, o=xfm[:, k * T + tc * 128: k * T + (tc) * 128 + 128], i=bk_[:, 0:128]:
               e.tensor_copy(o, i),
               waits=[("MM", P.now("MM"))], inc="VE")
        mm_done_tin[tc % 2] = P.now("MM")
    onesb = cb[:, 386:387]
    v_(lambda e: e.memset(onesb, 1.0))
    for c in range(KC):
        v_(lambda e, c=c: e.memset(zfm[:, c * (T + 1):c * (T + 1) + 1], 0.0))
    p0.finish()

    p = Prog(nc)
    for s in ("LS2", "LW2", "MM2", "VE2", "SE2", "GP2"):
        p.sem(s)
    for a, b in (("LS", "LS2"), ("LW", "LW2"), ("MM", "MM2"), ("VE", "VE2"), ("SE", "SE2"), ("GP", "GP2")):
        p.sems[a] = p.sems[b]
        p.counts[a] = 0

    def T_(fn, waits=(), inc="MM"):
        p.emit("tensor", fn, waits, inc)

    def V_(fn, waits=(), inc="VE"):
        p.emit("vector", fn, waits, inc)

    def S_(fn, waits=(), inc="SE"):
        p.emit("scalar", fn, waits, inc)

    def MMW(out, lhsT, rhs, first, last, waits=(), inc="MM"):
        T_(lambda e, o=out, l=lhsT, r=rhs, f=first, s=last: e.matmul(o, l, r, start=f, stop=s),
           waits=waits, inc=inc)

    now = p.now

    TAB = [0, 2048, 4096, 6144]
    KRING = 8192
    VRING = 12288
    QWIN = 16448
    T1F = 9248
    SQB = 19520
    LNC0, LNC1 = 16448, 17472
    LNTMP = 9504
    ZB = 0
    OW = 4104
    PTR = 6152

    def h2ap(j, c0, n):
        if j < 19:
            return poolb[:, j * 1024 + c0: j * 1024 + c0 + n]
        return wst_t[:, 5120 + (j - 19) * 1024 + c0: 5120 + (j - 19) * 1024 + c0 + n]

    ms = {"pool_mm": 0, "pool_ve": 0, "pool_se": 0,
          "wstA_mm": 0, "wstB_mm": 0,
          "b45_ve": 0, "b45_se": 0, "b67_ve": 0,
          "zb_mm": 0}

    RW0 = rows[0:1, 0:512]
    RW1 = rows[0:1, 512:1024]

    def emit_outer_ln(dst, tc_order=(0, 1, 2, 3)):
        cast_ve = [ms["pool_mm"], ms["pool_mm"]]
        e0v, e0s = now("VE"), now("SE")
        pairs = {tc_order[i]: bp for i, bp in enumerate([(5, 4), (3, 2), (7, 6), (1, 0)])}
        rowsets = [(rows[0:1, 0:512], rows[0:1, 512:1024]),
                   (hbff[0:1, 3072:3584], hbff[0:1, 3584:4096])]
        pend = [None]
        for i, tc in enumerate(tc_order):
            t0 = tc * 512
            bS1, bS2 = pairs[tc]
            for k in range(KC):
                cbA = poolb[:, LNC0 + (k % 2) * 512: LNC0 + (k % 2) * 512 + 512]
                cbS = poolb[:, LNC1 + (k % 2) * 512: LNC1 + (k % 2) * 512 + 512]
                V_(lambda e, o=cbA, i=xc(k, t0, 512): e.tensor_copy(o, i),
                   waits=[("MM", cast_ve[k % 2])])
                V_(lambda e, o=cbS, i=cbA: e.tensor_tensor(o, i, i, op=OP.mult))
                ve_n = now("VE")
                MMW(ps[0:1, bS1 * 512:bS1 * 512 + 512], onesb, cbA, k == 0, k == KC - 1,
                    waits=[("VE", ve_n)] + ([("VE", e0v), ("SE", e0s)] if k == 0 else []))
                MMW(ps[0:1, bS2 * 512:bS2 * 512 + 512], onesb, cbS, k == 0, k == KC - 1)
                cast_ve[k % 2] = now("MM")
            mm_stats = now("MM")
            if pend[0] is not None:
                pend[0]()

            def mk_chain(tc=tc, t0=t0, bS1=bS1, bS2=bS2, mm_stats=mm_stats, i=i):
                RA, RB = rowsets[i % 2]
                V_(lambda e, RA=RA: e.tensor_single_scalar(RA, ps[0:1, bS1 * 512:bS1 * 512 + 512], 1.0 / D, op=OP.mult),
                   waits=[("MM", mm_stats)])
                V_(lambda e, RA=RA, RB=RB: e.tensor_tensor(RB, RA, RA, op=OP.mult))
                V_(lambda e, RB=RB: e.scalar_tensor_tensor(RB, ps[0:1, bS2 * 512:bS2 * 512 + 512], 1.0 / D, RB,
                                                           op0=OP.mult, op1=OP.subtract))
                ve_rows = now("VE")
                S_(lambda e, RB=RB: e.activation(RB, RB, AF.Ln, bias=epsc[0:1, 0:1], scale=1.0),
                   waits=[("VE", ve_rows)])
                S_(lambda e, RB=RB: e.activation(RB, RB, AF.Exp, bias=0.0, scale=-0.5))
                se_rows = now("SE")
                MMW(ps[:, bS1 * 512:bS1 * 512 + 512], onesf[0:1, :], RA, True, True,
                    waits=[("SE", se_rows)])
                MMW(ps[:, bS2 * 512:bS2 * 512 + 512], onesf[0:1, :], RB, True, True)
                mm_bc = now("MM")
                for k in range(KC):
                    if dst == "z":
                        V_(lambda e, o=zc(k, t0, 512), i=xc(k, t0, 512):
                           e.tensor_tensor(o, i, ps[:, bS1 * 512:bS1 * 512 + 512], op=OP.subtract),
                           waits=[("MM", mm_bc)] if k == 0 else ())
                        V_(lambda e, o=zc(k, t0, 512): e.tensor_tensor(o, o, ps[:, bS2 * 512:bS2 * 512 + 512], op=OP.mult))
                    else:
                        yap = zfmb[:, k * T + t0: k * T + t0 + 512]
                        V_(lambda e, i=xc(k, t0, 512):
                           e.tensor_tensor(pool[:, LNTMP:LNTMP + 512], i, ps[:, bS1 * 512:bS1 * 512 + 512], op=OP.subtract),
                           waits=[("MM", mm_bc)] if k == 0 else ())
                        V_(lambda e, o=yap: e.tensor_tensor(o, pool[:, LNTMP:LNTMP + 512], ps[:, bS2 * 512:bS2 * 512 + 512], op=OP.mult))
            pend[0] = mk_chain
        pend[0]()
        pend[0] = None
        ms["b67_ve"] = now("VE")
        ms["b45_ve"] = now("VE")
        ms["b45_se"] = now("SE")
        ms["pbank0_ve"] = now("VE")
        ms["pbank1_ve"] = now("VE")
        ms["b6_ve"] = now("VE")
        ms["b7_ve"] = now("VE")
        ms["b3_ve"] = now("VE")
        ms["b2_se"] = now("SE")

    RL2 = rows[0:2, 0:512]
    RL2B = hbf_t[0:2, 7176:7688]
    RREC = rows[0:1, 0:512]
    RRECB = hbf_t[0:1, 7176:7688]
    T2F = rows[:, 512:1024]
    wslotA = wst_t[:, 0:4096]
    wslotB = wst_t[:, 4096:8192]
    slot_guard = {"A": 0, "B": 0}
    ms["b2_se"] = 0; ms["b3_ve"] = 0; ms["b6_ve"] = 0; ms["b7_ve"] = 0; ms["z_ve"] = 0
    ms["w2p_ve"] = 0; ms["x_ve"] = 0

    def stream_phase(slot, src_ap):
        tgt = wslotA if slot == "A" else wslotB
        p.dma("gpsimd", tgt[:, 0:src_ap.shape[-1]], src_ap,
              waits=[("MM", slot_guard[slot])], inc="LW")
        return p.now("LW")

    def cast_zb(w):
        for k in range(KC):
            V_(lambda e, o=hbf_t[:, k * 513: k * 513 + 513],
               i=zfm[:, k * (T + 1) + w * 512: k * (T + 1) + w * 512 + 513]:
               e.tensor_copy(o, i),
               waits=[("MM", ms["zb_mm"])] if k == 0 else ())
        return p.now("VE")

    def zb_in(k, off, n):
        sh = 1 if k < 4 else 0
        return hbf_t[:, k * 513 + sh + off: k * 513 + sh + off + n]

    def l2norm_chain(l, bankP, out_ap, scol, cs_ap, ss_ap):
        mm0 = now("MM")
        sqb = poolb[:, SQB:SQB + 512]
        S_(lambda e: e.activation(sqb, bankP, AF.Square, bias=0.0, scale=1.0), waits=[("MM", mm0)])
        MMW(ps[0:2, 2 * 512:2 * 512 + 512], e2b, sqb, True, True,
            waits=[("SE", now("SE")), ("SE", ms["b2_se"]), ("VE", ms["b3_ve"])])
        S_(lambda e: e.activation(RL2, ps[0:2, 2 * 512:2 * 512 + 512], AF.Ln, bias=epsc[0:2, 1:2], scale=1.0),
           waits=[("MM", now("MM"))])
        S_(lambda e: e.activation(RL2B, RL2, AF.Exp, bias=0.0, scale=-0.5))
        ms["b2_se"] = now("SE")
        MMW(ps[:, 3 * 512:4 * 512], e2tb[:, :], RL2B, True, True,
            waits=[("SE", now("SE")), ("VE", ms["b3_ve"])])
        mmb = now("MM")
        V_(lambda e, s=scol: e.tensor_scalar(pool[:, T1F:T1F + 512], bankP, s, None, op0=OP.mult),
           waits=[("MM", mmb)])
        V_(lambda e, o=out_ap: e.tensor_tensor(o, pool[:, T1F:T1F + 512], ps[:, 3 * 512:4 * 512],
                                               op=OP.mult))
        ms["b3_ve"] = now("VE")
        MMW(ps[:, 7 * 512:8 * 512], rotb, out_ap, True, True,
            waits=[("VE", now("VE")), ("VE", ms["b7_ve"])])
        mmr = now("MM")
        V_(lambda e, i=out_ap, c_=cs_ap: e.tensor_tensor(pool[:, T1F:T1F + 512], i, c_, op=OP.mult))
        V_(lambda e, s=ss_ap: e.tensor_tensor(T2F, ps[:, 7 * 512:8 * 512], s, op=OP.mult),
           waits=[("MM", mmr)])
        ms["b7_ve"] = now("VE")
        V_(lambda e, o=out_ap: e.tensor_tensor(o, pool[:, T1F:T1F + 512], T2F, op=OP.add))

    def emit_mlp_ln(l, th, cast_base_bf, cast_guard_mm):
        t0h = th * 1024
        e0v, e0s = now("VE"), now("SE")
        cast_ve = [cast_guard_mm, cast_guard_mm]
        pend = [None]
        for tc in range(2):
            bS1, bS2 = (5, 4) if tc == 0 else (7, 6)
            for k in range(KC):
                sh = 1 if k < 4 else 0
                zin = zfm[:, k * (T + 1) + sh + t0h + tc * 512: k * (T + 1) + sh + t0h + tc * 512 + 512]
                cbA = cast_base_bf((k % 2) * 512)
                cbS = cast_base_bf(1024 + (k % 2) * 512)
                V_(lambda e, o=cbA, i=zin: e.tensor_copy(o, i), waits=[("MM", cast_ve[k % 2])])
                V_(lambda e, o=cbS, i=cbA: e.tensor_tensor(o, i, i, op=OP.mult))
                ve_n = now("VE")
                MMW(ps[0:1, bS1 * 512:bS1 * 512 + 512], onesb, cbA, k == 0, k == KC - 1,
                    waits=[("VE", ve_n)] + ([("VE", e0v), ("SE", e0s)] if k == 0 else []))
                MMW(ps[0:1, bS2 * 512:bS2 * 512 + 512], onesb, cbS, k == 0, k == KC - 1)
                cast_ve[k % 2] = now("MM")
            mm_stats = now("MM")
            if pend[0] is not None:
                pend[0]()

            def mk_chain(tc=tc, bS1=bS1, bS2=bS2, mm_stats=mm_stats):
                rb_ = 0
                RA = rows[0:1, 0:512]
                RB = rows[0:1, 512:1024]
                V_(lambda e, RA=RA: e.tensor_single_scalar(RA, ps[0:1, bS1 * 512:bS1 * 512 + 512], 1.0 / D, op=OP.mult),
                   waits=[("MM", now("MM"))])
                V_(lambda e, RA=RA, RB=RB: e.tensor_tensor(RB, RA, RA, op=OP.mult))
                V_(lambda e, RB=RB: e.scalar_tensor_tensor(RB, ps[0:1, bS2 * 512:bS2 * 512 + 512], 1.0 / D, RB,
                                                           op0=OP.mult, op1=OP.subtract))
                ve_rows = now("VE")
                S_(lambda e, RB=RB: e.activation(RB, RB, AF.Ln, bias=epsc[0:1, 0:1], scale=1.0),
                   waits=[("VE", ve_rows)])
                S_(lambda e, RB=RB: e.activation(RB, RB, AF.Exp, bias=0.0, scale=-0.5))
                se_rows = now("SE")
                MMW(ps[:, bS1 * 512:bS1 * 512 + 512], onesf[rb_:rb_ + 1, :], RA, True, True,
                    waits=[("SE", se_rows)])
                MMW(ps[:, bS2 * 512:bS2 * 512 + 512], onesf[rb_:rb_ + 1, :], RB, True, True)
                mm_bc = now("MM")
                for k in range(KC):
                    sh = 1 if k < 4 else 0
                    zin = zfm[:, k * (T + 1) + sh + t0h + tc * 512: k * (T + 1) + sh + t0h + tc * 512 + 512]
                    V_(lambda e, i=zin: e.tensor_tensor(pool[:, LNTMP:LNTMP + 512], i,
                                                        ps[:, bS1 * 512:bS1 * 512 + 512], op=OP.subtract),
                       waits=[("MM", mm_bc)] if k == 0 else ())
                    V_(lambda e: e.tensor_tensor(pool[:, LNTMP:LNTMP + 512], pool[:, LNTMP:LNTMP + 512],
                                                 ps[:, bS2 * 512:bS2 * 512 + 512], op=OP.mult))
                    V_(lambda e, o=hbf_t[:, k * 1024 + tc * 512: k * 1024 + tc * 512 + 512],
                       g_=Vs("f_lng", l * KC + k), b_=Vs("f_lnb", l * KC + k):
                       e.tensor_scalar(o, pool[:, LNTMP:LNTMP + 512], g_, b_, op0=OP.mult, op1=OP.add))
                ms["mlpln_tc%d_ve" % tc] = now("VE")
            pend[0] = mk_chain
        pend[0]()
        ms["mlpln_ve"] = now("VE")
        ms["mlpln_mm"] = now("MM")

    def emit_ffn_layer(l, parts=("la", "mlp")):
        for i in range(4):
            p.dma("sync", poolb[:, TAB[i]:TAB[i] + 2048], d_tabs[i],
                  waits=[("MM", ms["pool_mm"]), ("VE", ms["pool_ve"])], inc="LS")
        tab_ls = now("LS")
        zb_ve = [0]
        pbank_ve = [ms.get("pbank0_ve", 0), ms.get("pbank1_ve", 0)]

        def proj_q(w):
            ls_need = stream_phase("A", d_fqkv[l, 0])
            pend = [None]
            for c in range(4):
                bk = ps[:, (c % 2) * 512:(c % 2) * 512 + 512]
                for k in range(KC):
                    w_ = [("VE", pbank_ve[c % 2])] if k == 0 else []
                    if c == 0 and k == 0:
                        w_ += [("LW", ls_need), ("VE", zb_ve[0]), ("LS", tab_ls)]
                    MMW(bk, wslotA[:, k * 512 + c * 128: k * 512 + c * 128 + 128],
                        zb_in(k, 0, 512), k == 0, k == KC - 1, waits=w_)
                if pend[0] is not None:
                    pend[0]()

                def mk(c=c, bk=bk):
                    cs = poolb[:, TAB[0] + w * 512: TAB[0] + w * 512 + 512]
                    ss = poolb[:, TAB[1] + w * 512: TAB[1] + w * 512 + 512]
                    l2norm_chain(l, bk, poolb[:, QWIN + c * 512: QWIN + c * 512 + 512],
                                 pcol[:, PC_QS80 + l:PC_QS80 + l + 1], cs, ss)
                    pbank_ve[c % 2] = now("VE")
                pend[0] = mk
            pend[0]()
            slot_guard["A"] = now("MM")

        def proj_k(wk):
            ls_need = stream_phase("B", d_fqkv[l, 1])
            kbase = KRING + (wk % 2) * 2048
            pend = [None]
            for c in range(4):
                bk = ps[:, (c % 2) * 512:(c % 2) * 512 + 512]
                for k in range(KC):
                    w_ = [("VE", pbank_ve[c % 2])] if k == 0 else []
                    if c == 0 and k == 0:
                        w_ += [("LW", ls_need), ("VE", zb_ve[0])]
                    MMW(bk, wslotB[:, k * 512 + c * 128: k * 512 + c * 128 + 128],
                        zb_in(k, 0, 512), k == 0, k == KC - 1, waits=w_)
                if pend[0] is not None:
                    pend[0]()

                def mk(c=c, bk=bk):
                    cs = poolb[:, TAB[2] + wk * 512: TAB[2] + wk * 512 + 512]
                    ss = poolb[:, TAB[3] + wk * 512: TAB[3] + wk * 512 + 512]
                    l2norm_chain(l, bk, poolb[:, kbase + c * 512: kbase + c * 512 + 512],
                                 Vs("f_ks", l), cs, ss)
                    pbank_ve[c % 2] = now("VE")
                pend[0] = mk
            pend[0]()
            slot_guard["B"] = now("MM")

        def proj_v(wk):
            ls_need = stream_phase("A", d_fqkv[l, 2])
            vbase = VRING + (wk % 2) * 2080
            pend = [None]
            for tch in range(4):
                bk = ps[:, (tch % 2) * 512:(tch % 2) * 512 + 512]
                for k in range(KC):
                    w_ = [("VE", pbank_ve[tch % 2])] if k == 0 else []
                    if tch == 0 and k == 0:
                        w_ += [("LW", ls_need), ("VE", zb_ve[0])]
                    MMW(bk, zb_in(k, tch * 128, 128), wslotA[:, k * 512: k * 512 + 512],
                        k == 0, k == KC - 1, waits=w_)
                mmv = now("MM")
                if pend[0] is not None:
                    pend[0]()

                def mk(tch=tch, bk=bk, mmv=mmv):
                    dst = poolb[:, vbase + tch * 520: vbase + tch * 520 + 520]
                    V_(lambda e, o=dst.rearrange("p (h c) -> p h c", c=65),
                       i=bk.rearrange("p (h c) -> p h c", c=64):
                       e.tensor_copy(o[:, :, 0:64], i), waits=[("MM", mmv)])
                    V_(lambda e, o=dst.rearrange("p (h c) -> p h c", c=65): e.memset(o[:, :, 64:65], 1.0))
                    pbank_ve[tch % 2] = now("VE")
                pend[0] = mk
            pend[0]()
            slot_guard["A"] = now("MM")

        def attention(w):
            ve_ready = now("VE")
            st_se = [ms["b2_se"], ms["b2_se"]]
            ptr_mm = [0, 0]
            st_i = [0]
            pend_evac = [None]

            def emit_evac(h, pvb, mm_pv):
                S_(lambda e, i=ps[64:65, pvb * 512: pvb * 512 + 512]:
                   e.activation(RREC, i, AF.Ln, bias=0.0, scale=1.0), waits=[("MM", mm_pv)])
                S_(lambda e: e.activation(RRECB, RREC, AF.Exp, bias=0.0, scale=-1.0))
                MMW(ps[0:64, 3 * 512:3 * 512 + 512], cb[0:1, 520:584], RRECB, True, True,
                    waits=[("SE", now("SE")), ("VE", ms["b3_ve"])])
                mmb = now("MM")
                r0 = 64 * (h % 2)
                oap = hbf_t[r0:r0 + 64, OW + (h // 2) * 512: OW + (h // 2) * 512 + 512]
                V_(lambda e, o=oap, i=ps[0:64, pvb * 512: pvb * 512 + 512]:
                   e.tensor_copy(o, i), waits=[("MM", mmb)])
                V_(lambda e, o=oap:
                   e.tensor_tensor(o, o, ps[0:64, 3 * 512:3 * 512 + 512], op=OP.mult))
                ms["b3_ve"] = now("VE")
                if pvb == 6:
                    ms["b6_ve"] = now("VE")
                else:
                    ms["b7_ve"] = now("VE")

            for h in range(HF):
                pvb = 6 + (h % 2)
                pv_guard = ms["b6_ve"] if pvb == 6 else ms["b7_ve"]
                pv_started = [False]
                cx_list = list(range(4, 8) if w == 0 else range(8))
                pend_pv = [None]
                for cx in cx_list:
                    par = st_i[0] % 2
                    st_i[0] += 1
                    lo = max(0, 128 * (cx - 4)); hi = min(512, 128 * (cx + 1)); width = hi - lo
                    win_of_cx = (w - 1) if cx < 4 else w
                    kbase = KRING + (win_of_cx % 2) * 2048
                    vbase = VRING + (win_of_cx % 2) * 2080
                    r0 = 64 * (h % 2)
                    stb = ps[:, (4 + par) * 512:(4 + par) * 512 + 512]
                    ktile = poolb[r0:r0 + 64, kbase + (h // 2) * 512 + (cx % 4) * 128:
                                  kbase + (h // 2) * 512 + (cx % 4) * 128 + 128]
                    qtile = poolb[r0:r0 + 64, QWIN + (h // 2) * 512 + lo: QWIN + (h // 2) * 512 + hi]
                    MMW(stb[:, 0:width], ktile, qtile, True, False,
                        waits=[("SE", st_se[par]), ("VE", ve_ready)])
                    if cx >= 4:
                        MMW(stb[:, 0:128], mgt, identb, False, True)
                    else:
                        MMW(stb[:, width - 128:width], mlt, identb, False, True)
                    mm_st = now("MM")
                    if pend_pv[0] is not None:
                        pend_pv[0]()
                        pend_pv[0] = None
                    ptr = hbf_t[:, PTR + par * 512: PTR + par * 512 + width]
                    S_(lambda e, o=ptr, i=stb[:, 0:width], b=pcol[:, PC_BF0 + l:PC_BF0 + l + 1]:
                       e.activation(o, i, AF.Exp, bias=b, scale=1.0),
                       waits=[("MM", mm_st), ("MM", ptr_mm[par])])
                    st_se[par] = now("SE")
                    se_exp = now("SE")

                    def mk_pv(cx=cx, lo=lo, se_exp=se_exp, vbase=vbase, ptr=ptr, par=par, pvb=pvb):
                        for d in range(max(0, cx - 4), min(3, cx) + 1):
                            first = not pv_started[0]; pv_started[0] = True
                            last = (cx == 7 and d == 3)
                            vtile = poolb[:, vbase + (cx % 4) * 520 + 65 * h:
                                          vbase + (cx % 4) * 520 + 65 * h + 65]
                            w_ = [("SE", se_exp)]
                            if first:
                                w_.append(("VE", pv_guard))
                            MMW(ps[0:65, pvb * 512 + 128 * d: pvb * 512 + 128 * d + 128],
                                vtile, ptr[:, 128 * d - lo:128 * d - lo + 128], first, last, waits=w_)
                        ptr_mm[par] = now("MM")
                    pend_pv[0] = mk_pv
                pend_pv[0]()
                pend_pv[0] = None
                mm_pv = now("MM")
                if pend_evac[0] is not None:
                    pend_evac[0]()
                pend_evac[0] = (lambda h=h, pvb=pvb, mm_pv=mm_pv: emit_evac(h, pvb, mm_pv))
            if pend_evac[0] is not None:
                pend_evac[0]()
                pend_evac[0] = None

        def wo_stage(w):
            ls_need = stream_phase("B", d_fwo[l])
            ve_o = now("VE")
            for m in range(KC):
                bk = ps[:, (m % 2) * 512:(m % 2) * 512 + 512]
                for kc in range(4):
                    w_ = [("VE", pbank_ve[m % 2])] if kc == 0 else []
                    if m == 0 and kc == 0:
                        w_ += [("LW", ls_need), ("VE", ve_o)]
                    MMW(bk, wslotB[:, kc * 1024 + m * 128: kc * 1024 + m * 128 + 128],
                        hbf_t[:, OW + kc * 512: OW + kc * 512 + 512], kc == 0, kc == 3, waits=w_)
                mmw = now("MM")
                V_(lambda e, o=zc(m, w * 512, 512), b=bk: e.tensor_tensor(o, o, b, op=OP.add),
                   waits=[("MM", mmw)])
                pbank_ve[m % 2] = now("VE")
            slot_guard["B"] = now("MM")
            ms["z_ve"] = now("VE")

        for wi, w in enumerate([3, 2, 1, 0] if "la" in parts else []):
            if wi == 0:
                zb_ve[0] = cast_zb(w)
                proj_q(w)
                proj_k(w)
                proj_v(w)
                ms["zb_mm"] = now("MM")
                zb_ve[0] = cast_zb(w - 1)
                proj_k(w - 1)
                proj_v(w - 1)
                ms["zb_mm"] = now("MM")
            else:
                proj_q(w)
                ms["zb_mm"] = now("MM")
                if w > 0:
                    zb_ve[0] = cast_zb(w - 1)
                    proj_k(w - 1)
                    proj_v(w - 1)
                    ms["zb_mm"] = now("MM")
            attention(w)
            wo_stage(w)
        ms["pbank0_ve"] = pbank_ve[0]
        ms["pbank1_ve"] = pbank_ve[1]
        ms["pool_mm"] = now("MM")
        ms["pool_ve"] = now("VE")

        w1g = [slot_guard["A"], slot_guard["B"]]
        w2g = [0, 0]
        n_w1 = [0]
        n_w2 = [0]

        def load_w1(j):
            sl = n_w1[0] % 2
            p.dma("gpsimd", wst_t[:, sl * 2048:(sl + 1) * 2048], d_fw1[l, j],
                  waits=[("MM", w1g[sl])], inc="LW")
            n_w1[0] += 1
            return p.now("LW")

        def load_w2(mg, j):
            sl = n_w2[0] % 2
            p.dma("gpsimd", wst_t[:, 4096 + sl * 512:4096 + (sl + 1) * 512], d_fw2[l, mg, j],
                  waits=[("MM", w2g[sl])], inc="LW")
            n_w2[0] += 1
            return p.now("LW")

        ag_ve = [now("VE"), now("VE")]
        for th in ((1, 0) if "mlp" in parts else ()):
            t0h = th * 1024
            emit_mlp_ln(l, th, (lambda c0: poolb[:, c0:c0 + 512]), ms["pool_mm"])
            ln_ve = now("VE")
            ln_tc0_ve = ms["mlpln_tc0_ve"]
            load_w1(0)
            slot_of = {0: (n_w1[0] - 1) % 2}
            lw_of = {0: p.now("LW")}
            load_w1(1)
            slot_of[1] = (n_w1[0] - 1) % 2
            lw_of[1] = p.now("LW")
            for j in range(NM):
                sl = slot_of[j]
                w1v = wst_t[:, sl * 2048:(sl + 1) * 2048]
                for tc in range(2):
                    ssb = (j * 2 + tc) % 2
                    agA = ps[:, (ssb * 2) * 512:(ssb * 2) * 512 + 512]
                    agG = ps[:, (ssb * 2 + 1) * 512:(ssb * 2 + 1) * 512 + 512]
                    for k in range(KC):
                        rhs = hbf_t[:, k * 1024 + tc * 512: k * 1024 + tc * 512 + 512]
                        first = (k == 0); lastk = (k == KC - 1)
                        wts = []
                        if k == 0:
                            wts = [("LW", lw_of[j]), ("VE", max(ag_ve[ssb], ln_ve))]
                        MMW(agA, w1v[:, k * 256: k * 256 + 128], rhs, first, lastk, waits=wts)
                        MMW(agG, w1v[:, k * 256 + 128: k * 256 + 256], rhs, first, lastk)
                    w1g[sl] = now("MM")
                    mm_ag = now("MM")
                    glA = rows[:, 0:512] if ssb == 0 else rows[:, 512:1024]
                    S_(lambda e, o=glA, i=agG, b=Vs("f_b1g", l * NM + j):
                       e.activation(o, i, AF.Gelu, bias=b, scale=1.0), waits=[("MM", mm_ag)])
                    V_(lambda e, o=h2ap(j, tc * 512, 512), a=agA, s=Vs("f_b1a", l * NM + j), g_=glA:
                       e.scalar_tensor_tensor(o, a, s, g_, op0=OP.add, op1=OP.mult),
                       waits=[("SE", now("SE"))])
                    ag_ve[ssb] = now("VE")
                if j + 2 < NM:
                    slot_of[j + 2] = n_w1[0] % 2
                    lw_of[j + 2] = load_w1(j + 2)
                elif th == 1 and j + 2 >= NM and (j + 2 - NM) < 2:
                    pass
            h2_ve = now("VE")
            w2g[0] = now("MM"); w2g[1] = now("MM")
            load_w2(0, 0)
            w2slot_of = {(0, 0): (n_w2[0] - 1) % 2}
            w2lw_of = {(0, 0): p.now("LW")}
            load_w2(0, 1)
            w2slot_of[(0, 1)] = (n_w2[0] - 1) % 2
            w2lw_of[(0, 1)] = p.now("LW")
            for mg in range(2):
                for j in range(NM):
                    sl = w2slot_of[(mg, j)]
                    w2v = wst_t[:, 4096 + sl * 512:4096 + (sl + 1) * 512]
                    first = (j == 0); lastj = (j == NM - 1)
                    for m in range(4):
                        for q in range(2):
                            wts = []
                            if m == 0 and q == 0:
                                wts = [("LW", w2lw_of[(mg, j)])]
                                if j == 0:
                                    wts += [("VE", h2_ve), ("VE", ms["w2p_ve"])]
                            MMW(ps[:, m * 1024 + q * 512: m * 1024 + q * 512 + 512],
                                w2v[:, m * 128: m * 128 + 128],
                                h2ap(j, q * 512, 512), first, lastj, waits=wts)
                    w2g[sl] = now("MM")
                    nxt = j + 2
                    if nxt < NM:
                        w2slot_of[(mg, nxt)] = n_w2[0] % 2
                        w2lw_of[(mg, nxt)] = load_w2(mg, nxt)
                    elif mg == 0 and (nxt - NM) < 2:
                        w2slot_of[(1, nxt - NM)] = n_w2[0] % 2
                        w2lw_of[(1, nxt - NM)] = load_w2(1, nxt - NM)
                mm_w2 = now("MM")
                for m in range(4):
                    mo = mg * 4 + m
                    for q in range(2):
                        V_(lambda e, o=zc(mo, t0h + q * 512, 512),
                           i=ps[:, m * 1024 + q * 512: m * 1024 + q * 512 + 512],
                           s=Vs("f_b2", l * KC + mo):
                           e.scalar_tensor_tensor(o, i, s, o, op0=OP.add, op1=OP.add),
                           waits=[("MM", mm_w2)] if (m == 0 and q == 0) else ())
                ms["w2p_ve"] = now("VE")
            ms["z_ve"] = now("VE")
            ms["pool_mm"] = now("MM")
            ag_ve = [now("VE"), now("VE")]
        ms["pbank0_ve"] = now("VE")
        ms["pbank1_ve"] = now("VE")
        slot_guard["A"] = now("MM")
        slot_guard["B"] = now("MM")
        ms["pool_ve"] = now("VE")

    def emit_ffn_module():
        mmg0 = now("MM")
        for c in range(KC):
            V_(lambda e, c=c: e.memset(zfm[:, c * (T + 1):c * (T + 1) + 1], 0.0),
               waits=[("MM", mmg0)] if c == 0 else ())
        emit_outer_ln("z", (3, 2, 1, 0))
        for l in range(DEPTH):
            emit_ffn_layer(l)
        mm_g = now("MM")
        for k in range(KC):
            V_(lambda e, o=xc(k, 0, T), i=zc(k, 0, T):
               e.scalar_tensor_tensor(o, i, 0.5, o, op0=OP.mult, op1=OP.add))
        ms["x_ve"] = now("VE")

    def emit_attn_module():
        emit_outer_ln("ybf")
        p.dma("sync", pool[:, 0:1024], d_bvb, waits=[("MM", ms["pool_mm"]), ("VE", ms["pool_ve"])], inc="LS")
        bvb_ls = now("LS")
        sig = poolb[0:16, 2048:4096]
        QRB = [6144, 8192]
        PTB = [10240, 11264]

        def ybf_t(k, t0, n):
            return zfmb[:, k * T + t0: k * T + t0 + n]

        def vg_ap(tch, lg):
            if tch < 15:
                return zfmb[:, 24576 + tch * 520 + 65 * lg: 24576 + tch * 520 + 65 * lg + 65]
            return poolb[:, 14336 + 65 * lg: 14336 + 65 * lg + 65]

        def vg_full(tch):
            if tch < 15:
                return zfmb[:, 24576 + tch * 520: 24576 + tch * 520 + 520]
            return poolb[:, 14336: 14336 + 520]

        def o_ap(c, r0, t0, n):
            if c < 3:
                return hbf_t[r0:r0 + 64, c * 2048 + t0: c * 2048 + t0 + n]
            return poolb[r0:r0 + 64, 12288 + t0: 12288 + t0 + n]

        R0s = [pool[0:1, 7428:8452], pool[32:33, 7428:8452]]
        R2s = [pool[0:1, 8452:9476], pool[32:33, 8452:9476]]
        upar = [0]
        def o_full(c, qc):
            if c < 3:
                return hbf_t[:, c * 2048 + qc * 512: c * 2048 + qc * 512 + 512]
            return poolb[:, 12288 + qc * 512: 12288 + qc * 512 + 512]

        qr_mm = [0, 0]
        pt_mm = [0, 0]
        st_se_a = [0, 0]
        pend_gchain = [None]
        abank = [ms.get("pbank0_ve", 0), ms.get("pbank1_ve", 0)]
        ms["b67_att_ve"] = now("VE")

        for hg in attn_groups:
            for k in range(KC):
                p.dma("gpsimd", wslotA[:, k * 512: k * 512 + 512],
                      d_wv[k * 128:(k + 1) * 128, hg * 512:(hg + 1) * 512],
                      waits=[("MM", slot_guard["A"])] if k == 0 else (), inc="LW")
            lw_v = now("LW")
            pendv = [None]
            for tch in range(16):
                bk = ps[:, (tch % 2) * 512:(tch % 2) * 512 + 512]
                for k in range(KC):
                    w_ = [("VE", abank[tch % 2])] if k == 0 else []
                    if tch == 0 and k == 0:
                        w_ += [("LW", lw_v), ("VE", now("VE")), ("LS", bvb_ls)]
                    MMW(bk, ybf_t(k, tch * 128, 128), wslotA[:, k * 512: k * 512 + 512],
                        k == 0, k == KC - 1, waits=w_)
                mmv = now("MM")
                if pendv[0] is not None:
                    pendv[0]()

                def mkv(tch=tch, bk=bk, mmv=mmv):
                    dst = vg_full(tch)
                    V_(lambda e, o=dst.rearrange("p (h c) -> p h c", c=65),
                       i=bk.rearrange("p (h c) -> p h c", c=64),
                       bv=pool[:, hg * 512:hg * 512 + 512].rearrange("p (h c) -> p h c", c=64):
                       e.tensor_tensor(o[:, :, 0:64], i, bv, op=OP.add), waits=[("MM", mmv)])
                    V_(lambda e, o=dst.rearrange("p (h c) -> p h c", c=65): e.memset(o[:, :, 64:65], 1.0))
                    abank[tch % 2] = now("VE")
                pendv[0] = mkv
            pendv[0]()
            slot_guard["A"] = now("MM")
            for k in range(KC):
                p.dma("gpsimd", wslotB[:, k * 512: k * 512 + 512],
                      d_wk[k * 128:(k + 1) * 128, hg * 512:(hg + 1) * 512],
                      waits=[("MM", slot_guard["B"])] if k == 0 else (), inc="LW")
            lw_k = now("LW")

            def norm_chain(bk, bias_col, wcol, out_ap):
                ktmp = pool[:, 7428:7940]
                sqb = poolb[:, 15880:16392]
                mm0 = now("MM")
                V_(lambda e, i=bk, b=bias_col: e.tensor_scalar(ktmp, i, b, None, op0=OP.add),
                   waits=[("MM", mm0)])
                V_(lambda e: e.tensor_tensor(sqb, ktmp, ktmp, op=OP.mult))
                MMW(ps[0:2, 2 * 512:2 * 512 + 512], e2b, sqb, True, True,
                    waits=[("VE", now("VE")), ("SE", now("SE"))])
                S_(lambda e: e.activation(RL2, ps[0:2, 2 * 512:2 * 512 + 512], AF.Ln,
                                          bias=epsc[0:2, 2:3], scale=1.0 / HD), waits=[("MM", now("MM"))])
                S_(lambda e: e.activation(RL2B, RL2, AF.Exp, bias=0.0, scale=-0.5))
                ms["b2_se"] = now("SE")
                MMW(ps[:, 3 * 512:4 * 512], e2tb[:, :], RL2B, True, True,
                    waits=[("SE", now("SE")), ("VE", ms["b3_ve"]), ("SE", now("SE"))])
                mmb = now("MM")
                V_(lambda e, o=out_ap, w=wcol: e.scalar_tensor_tensor(o, ktmp, w, ps[:, 3 * 512:4 * 512],
                                                                      op0=OP.mult, op1=OP.mult),
                   waits=[("MM", mmb)])
                ms["b3_ve"] = now("VE")

            pendn = [None]
            for c in range(4):
                for qc in range(4):
                    bk = ps[:, (qc % 2) * 512:(qc % 2) * 512 + 512]
                    for k in range(KC):
                        w_ = [("VE", abank[qc % 2])] if k == 0 else []
                        if c == 0 and qc == 0 and k == 0:
                            w_ += [("LW", lw_k)]
                        MMW(bk, wslotB[:, k * 512 + c * 128: k * 512 + c * 128 + 128],
                            ybf_t(k, qc * 512, 512), k == 0, k == KC - 1, waits=w_)
                    if pendn[0] is not None:
                        pendn[0]()

                    def mkn(c=c, qc=qc, bk=bk):
                        norm_chain(bk, Vs("a_bk", hg * 4 + c), V("a_kn"),
                                   zfmb[:, 16384 + c * 2048 + qc * 512: 16384 + c * 2048 + qc * 512 + 512])
                        abank[qc % 2] = now("VE")
                    pendn[0] = mkn
            pendn[0]()
            slot_guard["B"] = now("MM")
            for k in range(KC):
                p.dma("gpsimd", wslotA[:, k * 512: k * 512 + 512],
                      d_wq[k * 128:(k + 1) * 128, hg * 512:(hg + 1) * 512],
                      waits=[("MM", slot_guard["A"])] if k == 0 else (), inc="LW")
            lw_q = now("LW")
            if hg == 0:
                for k in range(KC):
                    p.dma("gpsimd", wslotB[:, k * 128: k * 128 + 128],
                          d_wq[k * 128:(k + 1) * 128, 1024:1152],
                          waits=[("MM", slot_guard["B"])] if k == 0 else (), inc="LW")
                lw_g = now("LW")
                for qc in range(4):
                    bk = ps[:, (qc % 2) * 512:(qc % 2) * 512 + 512]
                    for k in range(KC):
                        w_ = [("VE", abank[qc % 2]), ("SE", now("SE"))] if k == 0 else []
                        if qc == 0 and k == 0:
                            w_ += [("LW", lw_g)]
                        MMW(bk, wslotB[:, k * 128: k * 128 + 128], ybf_t(k, qc * 512, 512),
                            k == 0, k == KC - 1, waits=w_)
                    mmg = now("MM")
                    S_(lambda e, o=sig[:, qc * 512: qc * 512 + 512], i=bk[0:16, :]:
                       e.activation(o, i, AF.Sigmoid, bias=V("a_bgate")[0:16, :], scale=1.0),
                       waits=[("MM", mmg)])
                    abank[qc % 2] = now("MM") * 0 + abank[qc % 2]
                gate_se = now("SE")
                slot_guard["B"] = now("MM")
            sig_se = now("SE")
            for c in range(4):
                qsl = c % 2
                qbase = QRB[qsl]
                pendq = [None]
                for qc in range(4):
                    bk = ps[:, (qc % 2) * 512:(qc % 2) * 512 + 512]
                    for k in range(KC):
                        w_ = [("VE", abank[qc % 2]), ("SE", now("SE"))] if k == 0 else []
                        if qc == 0 and k == 0:
                            w_ += [("LW", lw_q), ("MM", qr_mm[qsl])]
                        MMW(bk, wslotA[:, k * 512 + c * 128: k * 512 + c * 128 + 128],
                            ybf_t(k, qc * 512, 512), k == 0, k == KC - 1, waits=w_)
                    if pendq[0] is not None:
                        pendq[0]()

                    def mkq(qc=qc, bk=bk, c=c):
                        norm_chain(bk, Vs("a_bq", hg * 4 + c), pcol[:, PC_QN8:PC_QN8 + 1],
                                   poolb[:, qbase + qc * 512: qbase + qc * 512 + 512])
                        abank[qc % 2] = now("VE")
                    pendq[0] = mkq
                pendq[0]()
                ve_pair = now("VE")
                for lh in range(2):
                    h = 2 * c + lh
                    hh = hg * 8 + h
                    r0 = 64 * lh
                    for qh in range(2):
                        pend_pv = [None]
                        pv_started = [False]
                        for kc in range(16):
                            stpar = kc % 2
                            stbase = (2 + 2 * stpar) * 512
                            for qq in range(2):
                                MMW(ps[:, stbase + qq * 512: stbase + qq * 512 + 512],
                                    zfmb[r0:r0 + 64, 16384 + c * 2048 + kc * 128: 16384 + c * 2048 + kc * 128 + 128],
                                    poolb[r0:r0 + 64, qbase + qh * 1024 + qq * 512: qbase + qh * 1024 + qq * 512 + 512],
                                    True, True,
                                    waits=([("VE", ve_pair), ("SE", st_se_a[stpar]), ("SE", ms["b2_se"])]
                                           if qq == 0 else ()))
                            mm_st = now("MM")
                            if pend_pv[0] is not None:
                                pend_pv[0]()
                                pend_pv[0] = None
                            ptr = poolb[:, PTB[stpar]:PTB[stpar] + 1024]
                            se_exps = []
                            for qq in range(2):
                                S_(lambda e, o=ptr[:, qq * 512: qq * 512 + 512],
                                   i=ps[:, stbase + qq * 512: stbase + qq * 512 + 512]:
                                   e.activation(o, i, AF.Exp, bias=pcol[:, PC_BA:PC_BA + 1], scale=1.0),
                                   waits=[("MM", mm_st), ("MM", pt_mm[stpar])] if qq == 0 else ())
                                se_exps.append(now("SE"))
                            st_se_a[stpar] = now("SE")

                            def mk_pv(kc=kc, h=h, ptr=ptr, stpar=stpar, se_exps=tuple(se_exps)):
                                for qq in range(2):
                                    first = (kc == 0)
                                    w_ = [("SE", se_exps[qq])]
                                    if first and qq == 0:
                                        w_.append(("VE", ms["b67_att_ve"]))
                                    MMW(ps[0:65, 6 * 512 + qq * 512: 6 * 512 + qq * 512 + 512],
                                        vg_ap(kc, h), ptr[:, qq * 512: qq * 512 + 512],
                                        first, kc == 15, waits=w_)
                                pt_mm[stpar] = now("MM")
                            pend_pv[0] = mk_pv
                        pend_pv[0]()
                        pend_pv[0] = None
                        mm_pv = now("MM")
                        up = upar[0]; upar[0] ^= 1
                        R0 = R0s[up]; R2 = R2s[up]
                        V_(lambda e, o=o_ap(c, r0, qh * 1024, 1024), i=ps[0:64, 6 * 512: 6 * 512 + 1024]:
                           e.tensor_copy(o, i), waits=[("MM", mm_pv)])
                        V_(lambda e, R2=R2: e.tensor_copy(R2, ps[64:65, 6 * 512: 6 * 512 + 1024]))
                        ms["b67_att_ve"] = now("VE")
                        ve_sum = now("VE")
                        if pend_gchain[0] is not None:
                            pend_gchain[0]()

                        def mk_gchain(c=c, r0=r0, qh=qh, hh=hh, ve_sum=ve_sum, R0=R0, R2=R2, up=up):
                            rbase = 32 * up
                            for qq in range(2):
                                MMW(ps[0:1, qq * 512: qq * 512 + 512],
                                    identg[:, hh:hh + 1], sig[:, qh * 1024 + qq * 512: qh * 1024 + qq * 512 + 512],
                                    True, True,
                                    waits=[("VE", ve_sum), ("SE", sig_se), ("VE", abank[qq])] if qq == 0
                                          else [("VE", abank[qq])])
                            mm_gx = now("MM")
                            V_(lambda e: e.tensor_copy(R0, ps[0:1, 0:1024]),
                               waits=[("MM", mm_gx)])
                            S_(lambda e: e.activation(R2, R2, AF.Ln, bias=0.0, scale=1.0),
                               waits=[("VE", now("VE"))])
                            S_(lambda e: e.activation(R2, R2, AF.Exp, bias=0.0, scale=-1.0))
                            R0B = poolb[rbase:rbase + 1, 2 * 9504: 2 * 9504 + 1024]
                            V_(lambda e, R0B=R0B: e.tensor_tensor(R0B, R0, R2, op=OP.mult),
                               waits=[("SE", now("SE"))])
                            ve_row = now("VE")
                            for qq in range(2):
                                MMW(ps[0:64, qq * 512: qq * 512 + 512],
                                    cb[rbase:rbase + 1, 520:584],
                                    R0B[:, qq * 512: qq * 512 + 512],
                                    True, True, waits=[("VE", ve_row)] if qq == 0 else ())
                            mm_g2 = now("MM")
                            V_(lambda e, o=o_ap(c, r0, qh * 1024, 1024):
                               e.tensor_tensor(o, o, ps[0:64, 0:1024], op=OP.mult),
                               waits=[("MM", mm_g2)])
                            abank[0] = now("VE")
                            abank[1] = now("VE")
                        pend_gchain[0] = mk_gchain
                if pend_gchain[0] is not None:
                    pend_gchain[0]()
                    pend_gchain[0] = None
                qr_mm[qsl] = now("MM")
            slot_guard["A"] = now("MM")
            for kc in range(4):
                p.dma("gpsimd", wslotB[:, kc * 1024: kc * 1024 + 1024],
                      d_wo[hg * 512 + kc * 128: hg * 512 + kc * 128 + 128, :],
                      waits=[("MM", slot_guard["B"])] if kc == 0 else (), inc="LW")
            lw_o = now("LW")
            ve_o = now("VE")
            for m in range(KC):
                for qc in range(4):
                    bk = ps[:, (qc % 2) * 512:(qc % 2) * 512 + 512]
                    for kc in range(4):
                        w_ = [("VE", abank[qc % 2])] if kc == 0 else []
                        if m == 0 and qc == 0 and kc == 0:
                            w_ += [("LW", lw_o), ("VE", ve_o)]
                        MMW(bk, wslotB[:, kc * 1024 + m * 128: kc * 1024 + m * 128 + 128],
                            o_full(kc, qc),
                            kc == 0, kc == 3, waits=w_)
                    mmw = now("MM")
                    if hg == 0:
                        V_(lambda e, o=xc(m, qc * 512, 512), b=bk, s=Vs("a_bo", m):
                           e.scalar_tensor_tensor(o, b, s, o, op0=OP.add, op1=OP.add),
                           waits=[("MM", mmw)])
                    else:
                        V_(lambda e, o=xc(m, qc * 512, 512), b=bk: e.tensor_tensor(o, o, b, op=OP.add),
                           waits=[("MM", mmw)])
                    abank[qc % 2] = now("VE")
            slot_guard["B"] = now("MM")
        ms["pbank0_ve"] = now("VE")
        ms["pbank1_ve"] = now("VE")
        ms["pool_mm"] = now("MM")
        ms["pool_ve"] = now("VE")
        ms["x_ve"] = now("VE")

    dbg = [s for s in stages if s.startswith("dbg_")]
    if dbg and dbg[0] == "dbg_attn":
        emit_attn_module()
        d_dbg = nc.dram_tensor("dbg", [128, 32768], BF16, kind="ExternalOutput").ap()
        d_dbg2 = nc.dram_tensor("dbg2", [128, 3072], F32, kind="ExternalOutput").ap()
        wts = [("VE", now("VE")), ("MM", now("MM")), ("SE", now("SE"))]
        p.dma("sync", d_dbg[:, 0:16384], zfmb[:, 0:16384], waits=wts, inc="LS")
        p.dma("sync", d_dbg[:, 16384:24576], zfmb[:, 16384:24576], inc="LS")
        p.dma("sync", d_dbg[:, 24576:26624], poolb[:, 6144:8192], inc="LS")
        p.dma("sync", d_dbg[:, 26624:28672], poolb[:, 8192:10240], inc="LS")
        p.dma("sync", d_dbg[:, 28672:30720], hbf_t[:, 0:2048], inc="LS")
        p.dma("sync", d_dbg2[:, 0:2048], pool[:, 1024:3072], inc="LS")
        p.dma("sync", d_dbg2[:, 2048:3072], pool[:, 0:1024], inc="LS")
        dbg = []
        dbg_tap = None
    elif dbg:
        sub = dbg[0][4:]
        emit_outer_ln("z")
        if sub in ("l0a", "l0"):
            emit_ffn_layer(0, parts=("la",) if sub == "l0a" else ("la", "mlp"))
        dbg_tap = "z"
        d_dbg = nc.dram_tensor("dbg", [128, 32768], BF16, kind="ExternalOutput").ap()
        d_dbg2 = nc.dram_tensor("dbg2", [128, 64], F32, kind="ExternalOutput").ap()
        p.dma("sync", d_dbg[:, 0:8192], poolb[:, 0:8192], waits=[("VE", now("VE")), ("MM", now("MM")), ("SE", now("SE"))], inc="LS")
        p.dma("sync", d_dbg[:, 8192:10240], poolb[:, QWIN:QWIN + 2048], inc="LS")
        p.dma("sync", d_dbg[:, 10240:14336], poolb[:, KRING:KRING + 4096], inc="LS")
        p.dma("sync", d_dbg[:, 14336:18496], poolb[:, VRING:VRING + 4160], inc="LS")
        p.dma("sync", d_dbg[:, 18496:22600], hbf_t[:, 0:4104], inc="LS")
        p.dma("sync", d_dbg[:, 22600:24648], hbf_t[:, OW:OW + 2048], inc="LS")
        p.dma("sync", d_dbg[:, 24648:25672], hbf_t[:, PTR:PTR + 1024], inc="LS")
        p.dma("sync", d_dbg2[:, 0:8], pcol[:], inc="LS")
    else:
        if "ffn0" in stages:
            emit_ffn_module()
        if "attn" in stages:
            emit_attn_module()
        if "ffn1" in stages:
            emit_ffn_module()
    io_e = pool[:, 2048:2176]
    ident_e = pool[:, 2176:2304]
    p.emit("gpsimd", lambda e: e.iota(io_e, [[1, 128]], channel_multiplier=-1,
                                      allow_small_or_imprecise_dtypes=True),
           waits=[("MM", ms["pool_mm"]), ("VE", ms["pool_ve"]), ("LS", now("LS"))], inc="GP")
    V_(lambda e: e.tensor_scalar(ident_e, io_e, 0.0, 1.0, op0=OP.is_equal, op1=OP.mult),
       waits=[("GP", p.now("GP"))])
    ve_id = now("VE")
    tout = [pool[:, 0:1024], pool[:, 1024:2048]]
    st_ls = [0, 0]

    def src_ap(k, tc):
        if dbg_tap == "z":
            return zc(k, tc * 128, 128)
        return xfm[:, k * T + tc * 128: k * T + tc * 128 + 128]

    for tc in range(16):
        to = tout[tc % 2]
        for k in range(KC):
            bkp = ps[:, (k % 2) * 512:(k % 2) * 512 + 512]
            T_(lambda e, o=bkp[:, 0:128], i=src_ap(k, tc), idn=ident_e: e.transpose(o, i, idn),
               waits=([("VE", ve_id), ("LS", st_ls[tc % 2])] if k == 0 else [("VE", now("VE"))]))
            V_(lambda e, o=to[:, k * 128: k * 128 + 128], i=bkp[:, 0:128]: e.tensor_copy(o, i),
               waits=[("MM", now("MM"))])
        p.dma("sync", d_out[tc * 128:(tc + 1) * 128, :], to, waits=[("VE", now("VE"))], inc="LS")
        st_ls[tc % 2] = now("LS")
    p.emit("sync", lambda e: e.nop(), waits=[("LS", now("LS"))])
    p.finish()
    for t in reversed(ctx):
        t.__exit__(None, None, None)
    return nc



def kernel(**inputs):
    staged, vix, nv = _stage_weights(inputs)
    nc = build_program(nv, vix)
    x = np.asarray(inputs["L0_attn_x"], np.float32)
    in_maps = []
    for i in range(B):
        m = dict(staged)
        m["xin"] = np.ascontiguousarray(x[i])
        in_maps.append(m)
    res = run_bass_kernel_spmd(nc, in_maps, core_ids=list(range(B)))
    return np.stack([np.asarray(r["out"], np.float32) for r in res.results], 0)


# version 13
# speedup vs baseline: 1.1161x; 1.0563x over previous
import numpy as np
import concourse.bass as bass
import concourse.mybir as mybir
import concourse.bass_utils as _bass_utils
from concourse.bass_utils import run_bass_kernel_spmd
from concourse.alu_op_type import AluOpType as OP
import ml_dtypes

_orig_run_command = _bass_utils.run_command


def _run_command_ldwopt(cmd, **kw):
    if isinstance(cmd, list):
        cmd = ["--enable-ldw-opt=true" if c == "--enable-ldw-opt=false" else c
               for c in cmd]
    return _orig_run_command(cmd, **kw)


_bass_utils.run_command = _run_command_ldwopt

F32 = mybir.dt.float32
BF16 = mybir.dt.bfloat16
AF = mybir.ActivationFunctionType
AX = mybir.AxisListType.X

B = 8
D = 1024
T = 2048
KC = 8
HA = 16
HD = 64
DEPTH = 2
HF = 8
WIN = 512
NW = T // WIN
HID = 2730
HIDP = 2816
NM = HIDP // 128
NEGBIG = -3.0e38
LN2_EPS = 1e-5
RMS_EPS = 1e-6

ENGINES = ("sync", "gpsimd", "tensor", "vector", "scalar")


class Prog:
    def __init__(self, nc):
        self.nc = nc
        self.items = {e: [] for e in ENGINES}
        self.sems = {}
        self.counts = {}

    def sem(self, name):
        h = self.nc.semaphore(name)
        self.sems[name] = h.__enter__()
        self.counts[name] = 0

    def now(self, s):
        return self.counts[s]

    def emit(self, eng, fn, waits=(), inc=None, inc_amt=1):
        for s, v in waits:
            assert v <= self.counts[s], f"wait on future: {s} {v} > {self.counts[s]}"
        if inc is not None:
            self.counts[inc] += inc_amt
        waits = tuple((s, v) for s, v in waits if v > 0)
        sems = self.sems

        def item(engobj, fn=fn, waits=waits, inc=inc, inc_amt=inc_amt):
            for s, v in waits:
                engobj.wait_ge(sems[s], v)
            inst = fn(engobj)
            if inc is not None:
                inst.then_inc(sems[inc], inc_amt)
        self.items[eng].append(item)

    def dma(self, eng, out, in_, waits=(), inc=None):
        self.emit(eng, lambda e: e.dma_start(out=out, in_=in_), waits, inc, 16)

    def finish(self):
        nc = self.nc
        with nc.Block() as block:
            for e in ENGINES:
                lst = self.items[e]
                if not lst:
                    continue

                def body(engobj, lst=lst):
                    for item in lst:
                        item(engobj)
                getattr(block, e)(body)



def _stage_weights(iw):
    f32 = np.float32
    out = {}
    wq = np.asarray(iw["L0_attn_Wq"], f32)
    wqp = np.zeros((D, 1152), f32)
    wqp[:, :1040] = wq
    out["a_wq"] = np.ascontiguousarray(wqp)
    out["a_wk"] = np.ascontiguousarray(np.asarray(iw["L0_attn_Wk"], f32))
    out["a_wv"] = np.ascontiguousarray(np.asarray(iw["L0_attn_Wv"], f32))
    out["a_wo"] = np.ascontiguousarray(np.asarray(iw["L0_attn_Wo"], f32))
    wqkv = np.asarray(iw["L0_ffn_wqkv"], f32)
    s = wqkv.reshape(DEPTH, KC, 128, 3, 512).transpose(0, 3, 2, 1, 4)
    out["f_wqkv"] = np.ascontiguousarray(s.reshape(DEPTH, 3, 128, KC * 512))
    wo = np.asarray(iw["L0_ffn_wo"], f32)
    s = wo.reshape(DEPTH, 4, 128, 1024).transpose(0, 2, 1, 3)
    out["f_wo"] = np.ascontiguousarray(s.reshape(DEPTH, 128, 4 * 1024))
    w1 = np.asarray(iw["L0_ffn_w1"], f32)
    a = np.zeros((DEPTH, D, HIDP), f32); a[:, :, :HID] = w1[:, :, :HID]
    g = np.zeros((DEPTH, D, HIDP), f32); g[:, :, :HID] = w1[:, :, HID:]
    inter = np.empty((DEPTH, D, NM, 256), f32)
    inter[:, :, :, :128] = a.reshape(DEPTH, D, NM, 128)
    inter[:, :, :, 128:] = g.reshape(DEPTH, D, NM, 128)
    s = inter.transpose(0, 2, 1, 3).reshape(DEPTH, NM, KC, 128, 256).transpose(0, 1, 3, 2, 4)
    out["f_w1"] = np.ascontiguousarray(s.reshape(DEPTH, NM, 128, KC * 256))
    w2 = np.asarray(iw["L0_ffn_w2"], f32)
    w2p = np.zeros((DEPTH, HIDP, D), f32); w2p[:, :HID, :] = w2
    s = w2p.reshape(DEPTH, NM, 128, 2, 512).transpose(0, 3, 1, 2, 4)
    out["f_w2"] = np.ascontiguousarray(s)
    cols = []

    def addcol(v):
        v = np.asarray(v, f32)
        if v.ndim == 1:
            v = v[:, None]
        cols.append(v)
        return sum(c.shape[1] for c in cols) - v.shape[1]

    ix = {}
    bq = np.asarray(iw["L0_attn_bq"], f32)
    ix["a_bq"] = addcol(bq[:D].reshape(KC, 128).T)
    ix["a_bgate"] = addcol(np.pad(bq[D:], (0, 112)))
    ix["a_bk"] = addcol(np.asarray(iw["L0_attn_bk"], f32).reshape(KC, 128).T)
    ix["a_bv"] = addcol(np.asarray(iw["L0_attn_bv"], f32).reshape(KC, 128).T)
    ix["a_bo"] = addcol(np.asarray(iw["L0_attn_bo"], f32).reshape(KC, 128).T)
    ix["a_qn"] = addcol(np.tile(np.asarray(iw["L0_attn_qn_w"], f32), 2))
    ix["a_kn"] = addcol(np.tile(np.asarray(iw["L0_attn_kn_w"], f32), 2))
    qs = np.asarray(iw["L0_ffn_q_scale"], f32)
    ks = np.asarray(iw["L0_ffn_k_scale"], f32)
    ix["f_qs"] = addcol(np.tile(qs, (1, 2)).T.reshape(128, DEPTH, order="F") if False else np.stack([np.tile(qs[l], 2) for l in range(DEPTH)], 1))
    ix["f_ks"] = addcol(np.stack([np.tile(ks[l], 2) for l in range(DEPTH)], 1))
    lng = np.asarray(iw["L0_ffn_ln_g"], f32)
    lnb = np.asarray(iw["L0_ffn_ln_b"], f32)
    ix["f_lng"] = addcol(np.concatenate([lng[l].reshape(KC, 128).T for l in range(DEPTH)], 1))
    ix["f_lnb"] = addcol(np.concatenate([lnb[l].reshape(KC, 128).T for l in range(DEPTH)], 1))
    b1 = np.asarray(iw["L0_ffn_b1"], f32)
    b1a = np.zeros((DEPTH, HIDP), f32); b1a[:, :HID] = b1[:, :HID]
    b1g = np.zeros((DEPTH, HIDP), f32); b1g[:, :HID] = b1[:, HID:]
    ix["f_b1a"] = addcol(np.concatenate([b1a[l].reshape(NM, 128).T for l in range(DEPTH)], 1))
    ix["f_b1g"] = addcol(np.concatenate([b1g[l].reshape(NM, 128).T for l in range(DEPTH)], 1))
    b2 = np.asarray(iw["L0_ffn_b2"], f32)
    ix["f_b2"] = addcol(np.concatenate([b2[l].reshape(KC, 128).T for l in range(DEPTH)], 1))
    ix["a_qn_bc"] = addcol(np.broadcast_to(np.asarray(iw["L0_attn_qn_w"], f32), (128, HD)))
    ix["a_kn_bc"] = addcol(np.broadcast_to(np.asarray(iw["L0_attn_kn_w"], f32), (128, HD)))
    ix["f_qs_bc"] = addcol(np.concatenate([np.broadcast_to(qs[l], (128, HD)) for l in range(DEPTH)], 1))
    ix["f_ks_bc"] = addcol(np.concatenate([np.broadcast_to(ks[l], (128, HD)) for l in range(DEPTH)], 1))
    vecs = np.concatenate(cols, 1)
    out["vecs"] = np.ascontiguousarray(vecs)
    out["bvb"] = np.ascontiguousarray(np.broadcast_to(np.asarray(iw["L0_attn_bv"], f32), (128, D)))
    return out, ix, vecs.shape[1]



def build_program(nv_cols, vix, stages=("ffn0", "attn", "ffn1"), dbg_tap=None, attn_groups=(0, 1)):
    dbg_tap = dbg_tap
    nc = bass.Bass("TRN2", target_bir_lowering=False, detect_race_conditions=False)
    PI = 3.141592653589793
    xin = nc.dram_tensor("xin", [T, D], F32, kind="ExternalInput").ap()
    d_wq = nc.dram_tensor("a_wq", [D, 1152], F32, kind="ExternalInput").ap()
    d_wk = nc.dram_tensor("a_wk", [D, D], F32, kind="ExternalInput").ap()
    d_wv = nc.dram_tensor("a_wv", [D, D], F32, kind="ExternalInput").ap()
    d_wo = nc.dram_tensor("a_wo", [D, D], F32, kind="ExternalInput").ap()
    d_fqkv = nc.dram_tensor("f_wqkv", [DEPTH, 3, 128, KC * 512], F32, kind="ExternalInput").ap()
    d_fwo = nc.dram_tensor("f_wo", [DEPTH, 128, 4 * 1024], F32, kind="ExternalInput").ap()
    d_fw1 = nc.dram_tensor("f_w1", [DEPTH, NM, 128, KC * 256], F32, kind="ExternalInput").ap()
    d_fw2 = nc.dram_tensor("f_w2", [DEPTH, 2, NM, 128, 512], F32, kind="ExternalInput").ap()
    d_vecs = nc.dram_tensor("vecs", [128, nv_cols], F32, kind="ExternalInput").ap()
    d_bvb = nc.dram_tensor("bvb", [128, D], F32, kind="ExternalInput").ap()
    d_out = nc.dram_tensor("out", [T, D], F32, kind="ExternalOutput").ap()
    d_tabs = nc.dram_tensor("tabs", [4, 128, T], BF16, kind="Internal").ap()

    ctx = []

    def sb(name, shape, dt):
        t = nc.sbuf_tensor(name, shape, dt); ctx.append(t); return t.__enter__()

    xfm = sb("xfm", [128, KC * T], F32)
    zfm = sb("zfm", [128, KC * (T + 1)], F32)
    hbf_t = sb("hbf", [128, 8192], BF16)
    wst_t = sb("wst", [128, 8192], BF16)
    pool = sb("pool", [128, 10016], F32)
    rows = sb("rows", [128, 1024], F32)
    vecs = sb("vecs_sb", [128, nv_cols], F32)
    pcol = sb("pcol", [128, 8], F32)
    cb = sb("cb", [128, 584], BF16)
    e2tf = sb("e2tf", [2, 128], F32)
    e2tb = sb("e2tb", [2, 128], BF16)
    onesf = sb("onesf", [128, 128], F32)
    identg = sb("identg", [16, 16], BF16)
    epsc = sb("epsc", [128, 4], F32)
    pst = nc.psum_tensor("pst", [128, 4096], F32); ctx.append(pst); ps = pst.__enter__()

    poolb = pool.bitcast(BF16)
    hbff = hbf_t.bitcast(F32)
    zfmb = zfm.bitcast(BF16)

    def bank(i, n=1):
        return ps[:, i * 512:(i + n) * 512]

    mgt = cb[:, 0:128]; mlt = cb[:, 128:256]; identb = cb[:, 256:384]; e2b = cb[:, 384:386]

    PC_BA, PC_BF0, PC_BF1, PC_QS80, PC_QS81, PC_QN8 = 0, 1, 2, 3, 4, 5

    def zc(c, t0, n):
        return zfm[:, c * (T + 1) + 1 + t0: c * (T + 1) + 1 + t0 + n]

    def xc(c, t0, n):
        return xfm[:, c * T + t0: c * T + t0 + n]

    V = lambda name: vecs[:, vix[name]:vix[name] + 1]

    def Vs(name, off, n=1):
        return vecs[:, vix[name] + off: vix[name] + off + n]

    p0 = Prog(nc)
    for s in ("LS", "LW", "MM", "VE", "SE", "GP"):
        p0.sem(s)
    P = p0

    def t_(fn, waits=(), inc=None):
        P.emit("tensor", fn, waits, inc)

    def v_(fn, waits=(), inc=None):
        P.emit("vector", fn, waits, inc)

    def s_(fn, waits=(), inc=None):
        P.emit("scalar", fn, waits, inc)

    def g_(fn, waits=(), inc=None):
        P.emit("gpsimd", fn, waits, inc)

    p0.dma("sync", vecs[:], d_vecs, inc="LS")
    io128 = pool[:, 2048:2176]
    ident = pool[:, 2176:2304]
    iop = pool[:, 2304:2305]
    g_(lambda e: e.iota(io128, [[1, 128]], channel_multiplier=-1, allow_small_or_imprecise_dtypes=True), inc="GP")
    g_(lambda e: e.iota(iop, [[1, 1]], channel_multiplier=1, allow_small_or_imprecise_dtypes=True), inc="GP")
    gp0 = P.now("GP")
    v_(lambda e: e.tensor_scalar(ident, io128, 0.0, 1.0, op0=OP.is_equal, op1=OP.mult), waits=[("GP", gp0)])
    g_(lambda e: e.tensor_scalar(identg[:, :], io128[0:16, 0:16], 0.0, 1.0, op0=OP.is_equal, op1=OP.mult),
       waits=[("GP", gp0)], inc="GP")
    v_(lambda e: e.tensor_scalar(mgt, io128, 0.0, NEGBIG, op0=OP.is_gt, op1=OP.mult), waits=[("GP", gp0)])
    v_(lambda e: e.tensor_scalar(mlt, io128, 0.0, NEGBIG, op0=OP.is_lt, op1=OP.mult), waits=[("GP", gp0)])
    v_(lambda e: e.tensor_copy(identb, ident), waits=[("GP", gp0)])
    g_(lambda e: e.tensor_scalar(e2b[:, 0:1], iop, 64.0, 1.0, op0=OP.is_lt, op1=OP.mult), waits=[("GP", gp0)])
    g_(lambda e: e.tensor_scalar(e2b[:, 1:2], iop, 64.0, 1.0, op0=OP.is_ge, op1=OP.mult), waits=[("GP", gp0)])
    v_(lambda e: e.memset(onesf[:], 1.0))
    v_(lambda e: e.memset(epsc[:, 0:1], 1e-5))
    v_(lambda e: e.memset(epsc[:, 1:2], 1e-30))
    v_(lambda e: e.memset(epsc[:, 2:3], 1e-6))
    v_(lambda e: e.memset(epsc[:, 3:4], -80.0))
    io2r = pool[:, 2305:2433]
    g_(lambda e: e.iota(pool[0:2, 2305:2433], [[1, 128]], channel_multiplier=64, allow_small_or_imprecise_dtypes=True), inc="GP")
    gp1 = P.now("GP")
    v_(lambda e: e.tensor_scalar(e2tf[:, :], pool[0:2, 2305:2433], 64.0, 1.0, op0=OP.is_lt, op1=OP.mult), waits=[("GP", gp1)])
    v_(lambda e: e.tensor_scalar(pool[0:2, 2433:2561], pool[0:2, 2305:2433], 128.0, 1.0, op0=OP.is_ge, op1=OP.mult), waits=[("GP", gp1)])
    v_(lambda e: e.tensor_tensor(e2tf[:, :], e2tf[:, :], pool[0:2, 2433:2561], op=OP.add), inc="VE")
    v_(lambda e: e.tensor_copy(e2tb[:, :], e2tf[:, :]))
    v_(lambda e: e.memset(cb[:, 520:584], 1.0))
    rotb = cb[:, 387:515]
    iof = pool[:, 2561:2689]
    fm64 = pool[:, 2689:2817]
    sA = pool[:, 2817:2945]
    sB = pool[:, 2945:3073]
    g_(lambda e: e.iota(iof, [[1, 128]], channel_multiplier=0, allow_small_or_imprecise_dtypes=True), inc="GP")
    gp2 = P.now("GP")
    v_(lambda e: e.tensor_scalar(fm64, iof, 64.0, 64.0, op0=OP.is_ge, op1=OP.mult), waits=[("GP", gp2)])
    v_(lambda e: e.tensor_tensor(fm64, iof, fm64, op=OP.subtract))
    v_(lambda e: e.tensor_single_scalar(sA, io128, -32.0, op=OP.is_equal), waits=[("GP", gp2)])
    v_(lambda e: e.tensor_scalar(sB, fm64, 32.0, -1.0, op0=OP.is_lt, op1=OP.mult))
    v_(lambda e: e.tensor_tensor(sA, sA, sB, op=OP.mult))
    v_(lambda e: e.tensor_single_scalar(sB, io128, 32.0, op=OP.is_equal))
    v_(lambda e: e.tensor_copy(rotb, sA))
    v_(lambda e: e.tensor_scalar(sA, fm64, 32.0, 1.0, op0=OP.is_ge, op1=OP.mult))
    v_(lambda e: e.tensor_tensor(sA, sA, sB, op=OP.mult))
    v_(lambda e: e.tensor_tensor(pool[:, 2817:2945], sA, sA, op=OP.bypass) if False else
       e.tensor_tensor(sB, sA, sA, op=OP.mult))
    v_(lambda e: e.scalar_tensor_tensor(rotb, sA, 1.0, rotb, op0=OP.mult, op1=OP.add), inc="VE")
    iop_m = pool[:, 3076:3077]
    invf = pool[:, 3073:3074]
    lnb_ = pool[:, 3074:3075]
    trow = pool[:, 3584:5632]
    fr = pool[:, 5632:7680]
    tA = pool[:, 7680:9728]
    sin_t = rows[:, 0:1024]
    LN1E4_32 = 9.210340371976184 / 32.0
    _t32 = pool[:, 3075:3076]
    g_(lambda e: e.tensor_scalar(iop_m, iop, 32.0, 32.0, op0=OP.is_ge, op1=OP.mult),
       waits=[("GP", gp0), ("VE", P.now("VE"))], inc="GP")
    g_(lambda e: e.tensor_scalar(_t32, iop, 64.0, 32.0, op0=OP.is_ge, op1=OP.mult), waits=[("GP", P.now("GP"))], inc="GP")
    g_(lambda e: e.tensor_tensor(iop_m, iop_m, _t32, op=OP.add), waits=[("GP", P.now("GP"))], inc="GP")
    g_(lambda e: e.tensor_scalar(_t32, iop, 96.0, 32.0, op0=OP.is_ge, op1=OP.mult), waits=[("GP", P.now("GP"))], inc="GP")
    g_(lambda e: e.tensor_tensor(iop_m, iop_m, _t32, op=OP.add), waits=[("GP", P.now("GP"))], inc="GP")
    g_(lambda e: e.tensor_tensor(iop_m, iop, iop_m, op=OP.subtract), waits=[("GP", P.now("GP"))], inc="GP")
    g_(lambda e: e.tensor_scalar(lnb_, iop_m, 2.0 / 89.6, 25.6 / 89.6, op0=OP.mult, op1=OP.add), waits=[("GP", P.now("GP"))], inc="GP")
    s_(lambda e: e.activation(invf, iop_m, AF.Exp, bias=0.0, scale=-LN1E4_32), waits=[("GP", P.now("GP"))])
    s_(lambda e: e.activation(lnb_, lnb_, AF.Ln, bias=0.0, scale=1.0), inc="SE")
    g_(lambda e: e.iota(trow, [[1, 2048]], channel_multiplier=0, allow_small_or_imprecise_dtypes=True), inc="GP")
    gp3 = P.now("GP")
    PIh = PI / 2.0
    TWO_PI = 2.0 * PI
    pooli = pool.bitcast(mybir.dt.int32)
    R_K = pool[:, 0:2048]
    R_Ki = pooli[:, 0:2048]
    tAb = poolb[:, 0:4096]
    se_tab = [P.now("SE")]
    C1 = 6.28125
    C2 = 6.283185307179586 - 6.28125
    INV2PI = 1.0 / 6.283185307179586

    def build_tab(idx, is_sin, inv_scale):
        g_(lambda e: e.iota(trow, [[1, 2048]], channel_multiplier=0,
                            allow_small_or_imprecise_dtypes=True),
           waits=[("VE", P.now("VE"))], inc="GP")
        gp_t = P.now("GP")
        v_(lambda e: e.tensor_scalar_mul(fr, trow, invf),
           waits=[("GP", gp_t), ("SE", se_tab[0])])
        if not is_sin:
            v_(lambda e: e.tensor_single_scalar(fr, fr, PIh, op=OP.add))
        v_(lambda e: e.tensor_scalar(trow, trow, -1024.0, 1.0 / 256.0, op0=OP.add, op1=OP.mult))
        v_(lambda e: e.tensor_scalar_mul(tA, fr, INV2PI))
        v_(lambda e: e.tensor_copy(R_Ki, tA), waits=[("LS", P.now("LS"))])
        v_(lambda e: e.tensor_copy(R_K, R_Ki))
        v_(lambda e: e.scalar_tensor_tensor(fr, R_K, -C1, fr, op0=OP.mult, op1=OP.add))
        v_(lambda e: e.scalar_tensor_tensor(fr, R_K, -C2, fr, op0=OP.mult, op1=OP.add))
        v_(lambda e: e.tensor_scalar(tA, fr, PI, TWO_PI, op0=OP.is_gt, op1=OP.mult))
        v_(lambda e: e.tensor_tensor(fr, fr, tA, op=OP.subtract))
        v_(lambda e: e.tensor_scalar(tA, fr, -PI, TWO_PI, op0=OP.is_lt, op1=OP.mult))
        v_(lambda e: e.tensor_tensor(fr, fr, tA, op=OP.add), inc="VE")
        s_(lambda e: e.activation(fr, fr, AF.Sin, bias=0.0, scale=1.0), waits=[("VE", P.now("VE"))], inc="SE")
        v_(lambda e: e.tensor_scalar_mul(tA, trow, lnb_), waits=[("SE", P.now("SE"))], inc="VE")
        s_(lambda e: e.activation(tA, tA, AF.Exp, bias=0.0, scale=(-1.0 if inv_scale else 1.0)),
           waits=[("VE", P.now("VE"))], inc="SE")
        v_(lambda e: e.tensor_tensor(tA, fr, tA, op=OP.mult), waits=[("SE", P.now("SE"))])
        v_(lambda e: e.tensor_copy(tAb[:, 0:2048], tA), inc="VE")
        P.dma("sync", d_tabs[idx], tAb[:, 0:2048], waits=[("VE", P.now("VE"))], inc="LS")
        se_tab[0] = P.now("SE")
    build_tab(0, False, False)
    build_tab(1, True, False)
    build_tab(2, False, True)
    build_tab(3, True, True)
    ls_vec = 16
    v_(lambda e: e.reduce_max(pcol[:, 6:7], Vs("a_qn_bc", 0, HD), axis=AX, apply_absolute_value=True),
       waits=[("LS", ls_vec)], inc="VE")
    v_(lambda e: e.reduce_max(pcol[:, 7:8], Vs("a_kn_bc", 0, HD), axis=AX, apply_absolute_value=True), inc="VE")
    g_(lambda e: e.tensor_tensor(pcol[:, PC_BA:PC_BA + 1], pcol[:, 6:7], pcol[:, 7:8], op=OP.mult),
       waits=[("VE", P.now("VE"))], inc="GP")
    s_(lambda e: e.activation(pcol[:, PC_BA:PC_BA + 1], pcol[:, PC_BA:PC_BA + 1], AF.Relu,
                              bias=epsc[:, 3:4], scale=8.0), waits=[("GP", P.now("GP"))])
    s_(lambda e: e.mul(pcol[:, PC_BA:PC_BA + 1], pcol[:, PC_BA:PC_BA + 1], -1.0), inc="SE")
    for l in range(DEPTH):
        v_(lambda e, l=l: e.reduce_max(pcol[:, 6:7], Vs("f_qs_bc", l * HD, HD), axis=AX, apply_absolute_value=True), inc="VE")
        v_(lambda e, l=l: e.reduce_max(pcol[:, 7:8], Vs("f_ks_bc", l * HD, HD), axis=AX, apply_absolute_value=True), inc="VE")
        g_(lambda e, l=l: e.tensor_tensor(pcol[:, PC_BF0 + l:PC_BF0 + l + 1], pcol[:, 6:7], pcol[:, 7:8], op=OP.mult),
           waits=[("VE", P.now("VE"))], inc="GP")
        s_(lambda e, l=l: e.activation(pcol[:, PC_BF0 + l:PC_BF0 + l + 1], pcol[:, PC_BF0 + l:PC_BF0 + l + 1],
                                       AF.Relu, bias=epsc[:, 3:4], scale=8.0), waits=[("GP", P.now("GP"))])
        s_(lambda e, l=l: e.mul(pcol[:, PC_BF0 + l:PC_BF0 + l + 1], pcol[:, PC_BF0 + l:PC_BF0 + l + 1], -1.0), inc="SE")
        g_(lambda e, l=l: e.tensor_single_scalar(pcol[:, PC_QS80 + l:PC_QS80 + l + 1],
                                                 Vs("f_qs", l), 8.0, op=OP.mult))
    g_(lambda e: e.tensor_single_scalar(pcol[:, PC_QN8:PC_QN8 + 1], V("a_qn"), 0.125, op=OP.mult), inc="GP")
    tin0 = pool[:, 0:1024]
    tin1 = pool[:, 1024:2048]
    v_(lambda e: e.engine_nop() if False else e.memset(rows[:, 0:1], 0.0), inc="VE")
    ve_ident = P.now("VE")
    mm_done_tin = [0, 0]
    for tc in range(16):
        tin = (tin0, tin1)[tc % 2]
        P.dma("sync", tin, xin[tc * 128:(tc + 1) * 128, :],
              waits=[("MM", mm_done_tin[tc % 2])], inc="LS")
        ls_need = P.now("LS")
        for k in range(KC):
            bk_ = bank(k % 2)
            t_(lambda e, o=bk_[:, 0:128], i=tin[:, k * 128:(k + 1) * 128]:
               e.transpose(o, i, ident),
               waits=([("LS", ls_need), ("VE", max(ve_ident, P.now("VE")))] if k == 0 else
                      [("VE", P.now("VE"))]),
               inc="MM")
            v_(lambda e, o=xfm[:, k * T + tc * 128: k * T + (tc) * 128 + 128], i=bk_[:, 0:128]:
               e.tensor_copy(o, i),
               waits=[("MM", P.now("MM"))], inc="VE")
        mm_done_tin[tc % 2] = P.now("MM")
    onesb = cb[:, 386:387]
    v_(lambda e: e.memset(onesb, 1.0))
    for c in range(KC):
        v_(lambda e, c=c: e.memset(zfm[:, c * (T + 1):c * (T + 1) + 1], 0.0))
    p0.finish()

    p = Prog(nc)
    for s in ("LS2", "LW2", "MM2", "VE2", "SE2", "GP2"):
        p.sem(s)
    for a, b in (("LS", "LS2"), ("LW", "LW2"), ("MM", "MM2"), ("VE", "VE2"), ("SE", "SE2"), ("GP", "GP2")):
        p.sems[a] = p.sems[b]
        p.counts[a] = 0

    def T_(fn, waits=(), inc="MM"):
        p.emit("tensor", fn, waits, inc)

    def V_(fn, waits=(), inc="VE"):
        p.emit("vector", fn, waits, inc)

    def S_(fn, waits=(), inc="SE"):
        p.emit("scalar", fn, waits, inc)

    def MMW(out, lhsT, rhs, first, last, waits=(), inc="MM"):
        T_(lambda e, o=out, l=lhsT, r=rhs, f=first, s=last: e.matmul(o, l, r, start=f, stop=s),
           waits=waits, inc=inc)

    now = p.now

    TAB = [0, 2048, 4096, 6144]
    KRING = 8192
    VRING = 12288
    QWIN = 16448
    T1F = 9248
    SQB = 19520
    LNC0, LNC1 = 16448, 17472
    LNTMP = 9504
    ZB = 0
    OW = 4104
    PTR = 6152

    def h2ap(j, c0, n):
        if j < 19:
            return poolb[:, j * 1024 + c0: j * 1024 + c0 + n]
        return wst_t[:, 5120 + (j - 19) * 1024 + c0: 5120 + (j - 19) * 1024 + c0 + n]

    ms = {"pool_mm": 0, "pool_ve": 0, "pool_se": 0,
          "wstA_mm": 0, "wstB_mm": 0,
          "b45_ve": 0, "b45_se": 0, "b67_ve": 0,
          "zb_mm": 0}

    RW0 = rows[0:1, 0:512]
    RW1 = rows[0:1, 512:1024]

    def emit_outer_ln(dst, tc_order=(0, 1, 2, 3)):
        cast_ve = [ms["pool_mm"], ms["pool_mm"]]
        e0v, e0s = now("VE"), now("SE")
        pairs = {tc_order[i]: bp for i, bp in enumerate([(5, 4), (3, 2), (7, 6), (1, 0)])}
        rowsets = [(rows[0:1, 0:512], rows[0:1, 512:1024]),
                   (hbff[0:1, 3072:3584], hbff[0:1, 3584:4096])]
        pend = [None]
        for i, tc in enumerate(tc_order):
            t0 = tc * 512
            bS1, bS2 = pairs[tc]
            for k in range(KC):
                cbA = poolb[:, LNC0 + (k % 2) * 512: LNC0 + (k % 2) * 512 + 512]
                cbS = poolb[:, LNC1 + (k % 2) * 512: LNC1 + (k % 2) * 512 + 512]
                V_(lambda e, o=cbA, i=xc(k, t0, 512): e.tensor_copy(o, i),
                   waits=[("MM", cast_ve[k % 2])])
                V_(lambda e, o=cbS, i=cbA: e.tensor_tensor(o, i, i, op=OP.mult))
                ve_n = now("VE")
                MMW(ps[0:1, bS1 * 512:bS1 * 512 + 512], onesb, cbA, k == 0, k == KC - 1,
                    waits=[("VE", ve_n)] + ([("VE", e0v), ("SE", e0s)] if k == 0 else []))
                MMW(ps[0:1, bS2 * 512:bS2 * 512 + 512], onesb, cbS, k == 0, k == KC - 1)
                cast_ve[k % 2] = now("MM")
            mm_stats = now("MM")
            if pend[0] is not None:
                pend[0]()

            def mk_chain(tc=tc, t0=t0, bS1=bS1, bS2=bS2, mm_stats=mm_stats, i=i):
                RA, RB = rowsets[i % 2]
                V_(lambda e, RA=RA: e.tensor_single_scalar(RA, ps[0:1, bS1 * 512:bS1 * 512 + 512], 1.0 / D, op=OP.mult),
                   waits=[("MM", mm_stats)])
                V_(lambda e, RA=RA, RB=RB: e.tensor_tensor(RB, RA, RA, op=OP.mult))
                V_(lambda e, RB=RB: e.scalar_tensor_tensor(RB, ps[0:1, bS2 * 512:bS2 * 512 + 512], 1.0 / D, RB,
                                                           op0=OP.mult, op1=OP.subtract))
                ve_rows = now("VE")
                S_(lambda e, RB=RB: e.activation(RB, RB, AF.Ln, bias=epsc[0:1, 0:1], scale=1.0),
                   waits=[("VE", ve_rows)])
                S_(lambda e, RB=RB: e.activation(RB, RB, AF.Exp, bias=0.0, scale=-0.5))
                se_rows = now("SE")
                MMW(ps[:, bS1 * 512:bS1 * 512 + 512], onesf[0:1, :], RA, True, True,
                    waits=[("SE", se_rows)])
                MMW(ps[:, bS2 * 512:bS2 * 512 + 512], onesf[0:1, :], RB, True, True)
                mm_bc = now("MM")
                for k in range(KC):
                    if dst == "z":
                        V_(lambda e, o=zc(k, t0, 512), i=xc(k, t0, 512):
                           e.tensor_tensor(o, i, ps[:, bS1 * 512:bS1 * 512 + 512], op=OP.subtract),
                           waits=[("MM", mm_bc)] if k == 0 else ())
                        V_(lambda e, o=zc(k, t0, 512): e.tensor_tensor(o, o, ps[:, bS2 * 512:bS2 * 512 + 512], op=OP.mult))
                    else:
                        yap = zfmb[:, k * T + t0: k * T + t0 + 512]
                        V_(lambda e, i=xc(k, t0, 512):
                           e.tensor_tensor(pool[:, LNTMP:LNTMP + 512], i, ps[:, bS1 * 512:bS1 * 512 + 512], op=OP.subtract),
                           waits=[("MM", mm_bc)] if k == 0 else ())
                        V_(lambda e, o=yap: e.tensor_tensor(o, pool[:, LNTMP:LNTMP + 512], ps[:, bS2 * 512:bS2 * 512 + 512], op=OP.mult))
            pend[0] = mk_chain
        pend[0]()
        pend[0] = None
        ms["b67_ve"] = now("VE")
        ms["b45_ve"] = now("VE")
        ms["b45_se"] = now("SE")
        ms["pbank0_ve"] = now("VE")
        ms["pbank1_ve"] = now("VE")
        ms["b6_ve"] = now("VE")
        ms["b7_ve"] = now("VE")
        ms["b3_ve"] = now("VE")
        ms["b2_se"] = now("SE")

    RL2 = rows[0:2, 0:512]
    RL2B = hbf_t[0:2, 7176:7688]
    RREC = rows[0:1, 0:512]
    RRECB = hbf_t[0:1, 7176:7688]
    T2F = rows[:, 512:1024]
    wslotA = wst_t[:, 0:4096]
    wslotB = wst_t[:, 4096:8192]
    slot_guard = {"A": 0, "B": 0}
    ms["b2_se"] = 0; ms["b3_ve"] = 0; ms["b6_ve"] = 0; ms["b7_ve"] = 0; ms["z_ve"] = 0
    ms["w2p_ve"] = 0; ms["x_ve"] = 0

    def stream_phase(slot, src_ap):
        tgt = wslotA if slot == "A" else wslotB
        p.dma("gpsimd", tgt[:, 0:src_ap.shape[-1]], src_ap,
              waits=[("MM", slot_guard[slot])], inc="LW")
        return p.now("LW")

    def cast_zb(w):
        for k in range(KC):
            V_(lambda e, o=hbf_t[:, k * 513: k * 513 + 513],
               i=zfm[:, k * (T + 1) + w * 512: k * (T + 1) + w * 512 + 513]:
               e.tensor_copy(o, i),
               waits=[("MM", ms["zb_mm"])] if k == 0 else ())
        return p.now("VE")

    def zb_in(k, off, n):
        sh = 1 if k < 4 else 0
        return hbf_t[:, k * 513 + sh + off: k * 513 + sh + off + n]

    def l2norm_chain(l, bankP, out_ap, scol, cs_ap, ss_ap, mm0=None):
        if mm0 is None:
            mm0 = now("MM")
        sqb = poolb[:, SQB:SQB + 512]
        S_(lambda e: e.activation(sqb, bankP, AF.Square, bias=0.0, scale=1.0), waits=[("MM", mm0)])
        MMW(ps[0:2, 2 * 512:2 * 512 + 512], e2b, sqb, True, True,
            waits=[("SE", now("SE")), ("SE", ms["b2_se"]), ("VE", ms["b3_ve"])])
        S_(lambda e: e.activation(RL2, ps[0:2, 2 * 512:2 * 512 + 512], AF.Ln, bias=epsc[0:2, 1:2], scale=1.0),
           waits=[("MM", now("MM"))])
        S_(lambda e: e.activation(RL2B, RL2, AF.Exp, bias=0.0, scale=-0.5))
        ms["b2_se"] = now("SE")
        MMW(ps[:, 3 * 512:4 * 512], e2tb[:, :], RL2B, True, True,
            waits=[("SE", now("SE")), ("VE", ms["b3_ve"])])
        mmb = now("MM")
        V_(lambda e, s=scol: e.tensor_scalar(pool[:, T1F:T1F + 512], bankP, s, None, op0=OP.mult),
           waits=[("MM", mmb)])
        V_(lambda e, o=out_ap: e.tensor_tensor(o, pool[:, T1F:T1F + 512], ps[:, 3 * 512:4 * 512],
                                               op=OP.mult))
        ms["b3_ve"] = now("VE")
        MMW(ps[:, 7 * 512:8 * 512], rotb, out_ap, True, True,
            waits=[("VE", now("VE")), ("VE", ms["b7_ve"])])
        mmr = now("MM")
        V_(lambda e, i=out_ap, c_=cs_ap: e.tensor_tensor(pool[:, T1F:T1F + 512], i, c_, op=OP.mult))
        V_(lambda e, s=ss_ap: e.tensor_tensor(T2F, ps[:, 7 * 512:8 * 512], s, op=OP.mult),
           waits=[("MM", mmr)])
        ms["b7_ve"] = now("VE")
        V_(lambda e, o=out_ap: e.tensor_tensor(o, pool[:, T1F:T1F + 512], T2F, op=OP.add))

    def emit_mlp_ln(l, th, cast_base_bf, cast_guard_mm):
        t0h = th * 1024
        e0v, e0s = now("VE"), now("SE")
        cast_ve = [cast_guard_mm, cast_guard_mm]
        pend = [None]
        for tc in range(2):
            bS1, bS2 = (5, 4) if tc == 0 else (7, 6)
            for k in range(KC):
                sh = 1 if k < 4 else 0
                zin = zfm[:, k * (T + 1) + sh + t0h + tc * 512: k * (T + 1) + sh + t0h + tc * 512 + 512]
                cbA = cast_base_bf((k % 2) * 512)
                cbS = cast_base_bf(1024 + (k % 2) * 512)
                V_(lambda e, o=cbA, i=zin: e.tensor_copy(o, i), waits=[("MM", cast_ve[k % 2])])
                V_(lambda e, o=cbS, i=cbA: e.tensor_tensor(o, i, i, op=OP.mult))
                ve_n = now("VE")
                MMW(ps[0:1, bS1 * 512:bS1 * 512 + 512], onesb, cbA, k == 0, k == KC - 1,
                    waits=[("VE", ve_n)] + ([("VE", e0v), ("SE", e0s)] if k == 0 else []))
                MMW(ps[0:1, bS2 * 512:bS2 * 512 + 512], onesb, cbS, k == 0, k == KC - 1)
                cast_ve[k % 2] = now("MM")
            mm_stats = now("MM")
            if pend[0] is not None:
                pend[0]()

            def mk_chain(tc=tc, bS1=bS1, bS2=bS2, mm_stats=mm_stats):
                rb_ = 0
                RA = rows[0:1, 0:512]
                RB = rows[0:1, 512:1024]
                V_(lambda e, RA=RA: e.tensor_single_scalar(RA, ps[0:1, bS1 * 512:bS1 * 512 + 512], 1.0 / D, op=OP.mult),
                   waits=[("MM", now("MM"))])
                V_(lambda e, RA=RA, RB=RB: e.tensor_tensor(RB, RA, RA, op=OP.mult))
                V_(lambda e, RB=RB: e.scalar_tensor_tensor(RB, ps[0:1, bS2 * 512:bS2 * 512 + 512], 1.0 / D, RB,
                                                           op0=OP.mult, op1=OP.subtract))
                ve_rows = now("VE")
                S_(lambda e, RB=RB: e.activation(RB, RB, AF.Ln, bias=epsc[0:1, 0:1], scale=1.0),
                   waits=[("VE", ve_rows)])
                S_(lambda e, RB=RB: e.activation(RB, RB, AF.Exp, bias=0.0, scale=-0.5))
                se_rows = now("SE")
                MMW(ps[:, bS1 * 512:bS1 * 512 + 512], onesf[rb_:rb_ + 1, :], RA, True, True,
                    waits=[("SE", se_rows)])
                MMW(ps[:, bS2 * 512:bS2 * 512 + 512], onesf[rb_:rb_ + 1, :], RB, True, True)
                mm_bc = now("MM")
                for k in range(KC):
                    sh = 1 if k < 4 else 0
                    zin = zfm[:, k * (T + 1) + sh + t0h + tc * 512: k * (T + 1) + sh + t0h + tc * 512 + 512]
                    V_(lambda e, i=zin: e.tensor_tensor(pool[:, LNTMP:LNTMP + 512], i,
                                                        ps[:, bS1 * 512:bS1 * 512 + 512], op=OP.subtract),
                       waits=[("MM", mm_bc)] if k == 0 else ())
                    V_(lambda e: e.tensor_tensor(pool[:, LNTMP:LNTMP + 512], pool[:, LNTMP:LNTMP + 512],
                                                 ps[:, bS2 * 512:bS2 * 512 + 512], op=OP.mult))
                    V_(lambda e, o=hbf_t[:, k * 1024 + tc * 512: k * 1024 + tc * 512 + 512],
                       g_=Vs("f_lng", l * KC + k), b_=Vs("f_lnb", l * KC + k):
                       e.tensor_scalar(o, pool[:, LNTMP:LNTMP + 512], g_, b_, op0=OP.mult, op1=OP.add))
                ms["mlpln_tc%d_ve" % tc] = now("VE")
            pend[0] = mk_chain
        pend[0]()
        ms["mlpln_ve"] = now("VE")
        ms["mlpln_mm"] = now("MM")

    def emit_ffn_layer(l, parts=("la", "mlp")):
        for i in range(4):
            p.dma("sync", poolb[:, TAB[i]:TAB[i] + 2048], d_tabs[i],
                  waits=[("MM", ms["pool_mm"]), ("VE", ms["pool_ve"])], inc="LS")
        tab_ls = now("LS")
        zb_ve = [0]
        BQ = (0, 1, 4, 5)
        entry_ve, entry_se = now("VE"), now("SE")
        pbank_ve = {0: ms.get("pbank0_ve", 0), 1: ms.get("pbank1_ve", 0),
                    4: entry_ve, 5: entry_ve}
        ms["stb_se"] = entry_se

        def proj_q(w):
            ls_need = stream_phase("A", d_fqkv[l, 0])
            pend = []
            for c in range(4):
                bk = ps[:, BQ[c] * 512:BQ[c] * 512 + 512]
                for k in range(KC):
                    w_ = [("VE", pbank_ve[BQ[c]])] if k == 0 else []
                    if k == 0 and c >= 2:
                        w_ += [("SE", ms["stb_se"])]
                    if c == 0 and k == 0:
                        w_ += [("LW", ls_need), ("VE", zb_ve[0]), ("LS", tab_ls)]
                    MMW(bk, wslotA[:, k * 512 + c * 128: k * 512 + c * 128 + 128],
                        zb_in(k, 0, 512), k == 0, k == KC - 1, waits=w_)
                mm_c = now("MM")

                def mk(c=c, bk=bk, mm_c=mm_c):
                    cs = poolb[:, TAB[0] + w * 512: TAB[0] + w * 512 + 512]
                    ss = poolb[:, TAB[1] + w * 512: TAB[1] + w * 512 + 512]
                    l2norm_chain(l, bk, poolb[:, QWIN + c * 512: QWIN + c * 512 + 512],
                                 pcol[:, PC_QS80 + l:PC_QS80 + l + 1], cs, ss, mm0=mm_c)
                    pbank_ve[BQ[c]] = now("VE")
                if len(pend) >= 2:
                    pend.pop(0)()
                pend.append(mk)
            for f in pend:
                f()
            slot_guard["A"] = now("MM")

        def proj_k(wk):
            ls_need = stream_phase("B", d_fqkv[l, 1])
            kbase = KRING + (wk % 2) * 2048
            pend = []
            for c in range(4):
                bk = ps[:, BQ[c] * 512:BQ[c] * 512 + 512]
                for k in range(KC):
                    w_ = [("VE", pbank_ve[BQ[c]])] if k == 0 else []
                    if k == 0 and c >= 2:
                        w_ += [("SE", ms["stb_se"])]
                    if c == 0 and k == 0:
                        w_ += [("LW", ls_need), ("VE", zb_ve[0])]
                    MMW(bk, wslotB[:, k * 512 + c * 128: k * 512 + c * 128 + 128],
                        zb_in(k, 0, 512), k == 0, k == KC - 1, waits=w_)
                mm_c = now("MM")

                def mk(c=c, bk=bk, mm_c=mm_c):
                    cs = poolb[:, TAB[2] + wk * 512: TAB[2] + wk * 512 + 512]
                    ss = poolb[:, TAB[3] + wk * 512: TAB[3] + wk * 512 + 512]
                    l2norm_chain(l, bk, poolb[:, kbase + c * 512: kbase + c * 512 + 512],
                                 Vs("f_ks", l), cs, ss, mm0=mm_c)
                    pbank_ve[BQ[c]] = now("VE")
                if len(pend) >= 2:
                    pend.pop(0)()
                pend.append(mk)
            for f in pend:
                f()
            slot_guard["B"] = now("MM")

        def proj_v(wk):
            ls_need = stream_phase("A", d_fqkv[l, 2])
            vbase = VRING + (wk % 2) * 2080
            pend = []
            for tch in range(4):
                bk = ps[:, BQ[tch] * 512:BQ[tch] * 512 + 512]
                for k in range(KC):
                    w_ = [("VE", pbank_ve[BQ[tch]])] if k == 0 else []
                    if k == 0 and tch >= 2:
                        w_ += [("SE", ms["stb_se"])]
                    if tch == 0 and k == 0:
                        w_ += [("LW", ls_need), ("VE", zb_ve[0])]
                    MMW(bk, zb_in(k, tch * 128, 128), wslotA[:, k * 512: k * 512 + 512],
                        k == 0, k == KC - 1, waits=w_)
                mmv = now("MM")

                def mk(tch=tch, bk=bk, mmv=mmv):
                    dst = poolb[:, vbase + tch * 520: vbase + tch * 520 + 520]
                    V_(lambda e, o=dst.rearrange("p (h c) -> p h c", c=65),
                       i=bk.rearrange("p (h c) -> p h c", c=64):
                       e.tensor_copy(o[:, :, 0:64], i), waits=[("MM", mmv)])
                    V_(lambda e, o=dst.rearrange("p (h c) -> p h c", c=65): e.memset(o[:, :, 64:65], 1.0))
                    pbank_ve[BQ[tch]] = now("VE")
                if len(pend) >= 2:
                    pend.pop(0)()
                pend.append(mk)
            for f in pend:
                f()
            slot_guard["A"] = now("MM")

        def attention(w):
            ve_ready = now("VE")
            st_se = [ms["b2_se"], ms["b2_se"]]
            ptr_mm = [0, 0]
            st_i = [0]
            pend_evac = [None]

            def emit_evac(h, pvb, mm_pv):
                S_(lambda e, i=ps[64:65, pvb * 512: pvb * 512 + 512]:
                   e.activation(RREC, i, AF.Ln, bias=0.0, scale=1.0), waits=[("MM", mm_pv)])
                S_(lambda e: e.activation(RRECB, RREC, AF.Exp, bias=0.0, scale=-1.0))
                MMW(ps[0:64, 3 * 512:3 * 512 + 512], cb[0:1, 520:584], RRECB, True, True,
                    waits=[("SE", now("SE")), ("VE", ms["b3_ve"])])
                mmb = now("MM")
                r0 = 64 * (h % 2)
                oap = hbf_t[r0:r0 + 64, OW + (h // 2) * 512: OW + (h // 2) * 512 + 512]
                V_(lambda e, o=oap, i=ps[0:64, pvb * 512: pvb * 512 + 512]:
                   e.tensor_copy(o, i), waits=[("MM", mmb)])
                V_(lambda e, o=oap:
                   e.tensor_tensor(o, o, ps[0:64, 3 * 512:3 * 512 + 512], op=OP.mult))
                ms["b3_ve"] = now("VE")
                if pvb == 6:
                    ms["b6_ve"] = now("VE")
                else:
                    ms["b7_ve"] = now("VE")

            for h in range(HF):
                pvb = 6 + (h % 2)
                pv_guard = ms["b6_ve"] if pvb == 6 else ms["b7_ve"]
                pv_started = [False]
                cx_list = list(range(4, 8) if w == 0 else range(8))
                pend_pv = [None]
                for cx in cx_list:
                    par = st_i[0] % 2
                    st_i[0] += 1
                    lo = max(0, 128 * (cx - 4)); hi = min(512, 128 * (cx + 1)); width = hi - lo
                    win_of_cx = (w - 1) if cx < 4 else w
                    kbase = KRING + (win_of_cx % 2) * 2048
                    vbase = VRING + (win_of_cx % 2) * 2080
                    r0 = 64 * (h % 2)
                    stb = ps[:, (4 + par) * 512:(4 + par) * 512 + 512]
                    ktile = poolb[r0:r0 + 64, kbase + (h // 2) * 512 + (cx % 4) * 128:
                                  kbase + (h // 2) * 512 + (cx % 4) * 128 + 128]
                    qtile = poolb[r0:r0 + 64, QWIN + (h // 2) * 512 + lo: QWIN + (h // 2) * 512 + hi]
                    MMW(stb[:, 0:width], ktile, qtile, True, False,
                        waits=[("SE", st_se[par]), ("VE", ve_ready)])
                    if cx >= 4:
                        MMW(stb[:, 0:128], mgt, identb, False, True)
                    else:
                        MMW(stb[:, width - 128:width], mlt, identb, False, True)
                    mm_st = now("MM")
                    if pend_pv[0] is not None:
                        pend_pv[0]()
                        pend_pv[0] = None
                    ptr = hbf_t[:, PTR + par * 512: PTR + par * 512 + width]
                    S_(lambda e, o=ptr, i=stb[:, 0:width], b=pcol[:, PC_BF0 + l:PC_BF0 + l + 1]:
                       e.activation(o, i, AF.Exp, bias=b, scale=1.0),
                       waits=[("MM", mm_st), ("MM", ptr_mm[par])])
                    st_se[par] = now("SE")
                    se_exp = now("SE")

                    def mk_pv(cx=cx, lo=lo, se_exp=se_exp, vbase=vbase, ptr=ptr, par=par, pvb=pvb):
                        for d in range(max(0, cx - 4), min(3, cx) + 1):
                            first = not pv_started[0]; pv_started[0] = True
                            last = (cx == 7 and d == 3)
                            vtile = poolb[:, vbase + (cx % 4) * 520 + 65 * h:
                                          vbase + (cx % 4) * 520 + 65 * h + 65]
                            w_ = [("SE", se_exp)]
                            if first:
                                w_.append(("VE", pv_guard))
                            MMW(ps[0:65, pvb * 512 + 128 * d: pvb * 512 + 128 * d + 128],
                                vtile, ptr[:, 128 * d - lo:128 * d - lo + 128], first, last, waits=w_)
                        ptr_mm[par] = now("MM")
                    pend_pv[0] = mk_pv
                pend_pv[0]()
                pend_pv[0] = None
                mm_pv = now("MM")
                if pend_evac[0] is not None:
                    pend_evac[0]()
                pend_evac[0] = (lambda h=h, pvb=pvb, mm_pv=mm_pv: emit_evac(h, pvb, mm_pv))
            if pend_evac[0] is not None:
                pend_evac[0]()
                pend_evac[0] = None
            ms["stb_se"] = now("SE")

        def wo_stage(w):
            ls_need = stream_phase("B", d_fwo[l])
            ve_o = now("VE")
            for m in range(KC):
                bk = ps[:, (m % 2) * 512:(m % 2) * 512 + 512]
                for kc in range(4):
                    w_ = [("VE", pbank_ve[m % 2])] if kc == 0 else []
                    if m == 0 and kc == 0:
                        w_ += [("LW", ls_need), ("VE", ve_o)]
                    MMW(bk, wslotB[:, kc * 1024 + m * 128: kc * 1024 + m * 128 + 128],
                        hbf_t[:, OW + kc * 512: OW + kc * 512 + 512], kc == 0, kc == 3, waits=w_)
                mmw = now("MM")
                V_(lambda e, o=zc(m, w * 512, 512), b=bk: e.tensor_tensor(o, o, b, op=OP.add),
                   waits=[("MM", mmw)])
                pbank_ve[m % 2] = now("VE")
            slot_guard["B"] = now("MM")
            ms["z_ve"] = now("VE")

        for wi, w in enumerate([3, 2, 1, 0] if "la" in parts else []):
            if wi == 0:
                zb_ve[0] = cast_zb(w)
                proj_q(w)
                proj_k(w)
                proj_v(w)
                ms["zb_mm"] = now("MM")
                zb_ve[0] = cast_zb(w - 1)
                proj_k(w - 1)
                proj_v(w - 1)
                ms["zb_mm"] = now("MM")
            else:
                proj_q(w)
                ms["zb_mm"] = now("MM")
                if w > 0:
                    zb_ve[0] = cast_zb(w - 1)
                    proj_k(w - 1)
                    proj_v(w - 1)
                    ms["zb_mm"] = now("MM")
            attention(w)
            wo_stage(w)
        ms["pbank0_ve"] = pbank_ve[0]
        ms["pbank1_ve"] = pbank_ve[1]
        ms["pool_mm"] = now("MM")
        ms["pool_ve"] = now("VE")

        w1g = [slot_guard["A"], slot_guard["B"]]
        w2g = [0, 0]
        n_w1 = [0]
        n_w2 = [0]

        def load_w1(j):
            sl = n_w1[0] % 2
            p.dma("gpsimd", wst_t[:, sl * 2048:(sl + 1) * 2048], d_fw1[l, j],
                  waits=[("MM", w1g[sl])], inc="LW")
            n_w1[0] += 1
            return p.now("LW")

        def load_w2(mg, j):
            sl = n_w2[0] % 2
            p.dma("gpsimd", wst_t[:, 4096 + sl * 512:4096 + (sl + 1) * 512], d_fw2[l, mg, j],
                  waits=[("MM", w2g[sl])], inc="LW")
            n_w2[0] += 1
            return p.now("LW")

        ag_ve = [now("VE"), now("VE")]
        for th in ((1, 0) if "mlp" in parts else ()):
            t0h = th * 1024
            emit_mlp_ln(l, th, (lambda c0: poolb[:, c0:c0 + 512]), ms["pool_mm"])
            ln_ve = now("VE")
            ln_tc0_ve = ms["mlpln_tc0_ve"]
            load_w1(0)
            slot_of = {0: (n_w1[0] - 1) % 2}
            lw_of = {0: p.now("LW")}
            load_w1(1)
            slot_of[1] = (n_w1[0] - 1) % 2
            lw_of[1] = p.now("LW")
            for j in range(NM):
                sl = slot_of[j]
                w1v = wst_t[:, sl * 2048:(sl + 1) * 2048]
                for tc in range(2):
                    ssb = (j * 2 + tc) % 2
                    agA = ps[:, (ssb * 2) * 512:(ssb * 2) * 512 + 512]
                    agG = ps[:, (ssb * 2 + 1) * 512:(ssb * 2 + 1) * 512 + 512]
                    for k in range(KC):
                        rhs = hbf_t[:, k * 1024 + tc * 512: k * 1024 + tc * 512 + 512]
                        first = (k == 0); lastk = (k == KC - 1)
                        wts = []
                        if k == 0:
                            wts = [("LW", lw_of[j]), ("VE", max(ag_ve[ssb], ln_ve))]
                        MMW(agA, w1v[:, k * 256: k * 256 + 128], rhs, first, lastk, waits=wts)
                        MMW(agG, w1v[:, k * 256 + 128: k * 256 + 256], rhs, first, lastk)
                    w1g[sl] = now("MM")
                    mm_ag = now("MM")
                    glA = rows[:, 0:512] if ssb == 0 else rows[:, 512:1024]
                    S_(lambda e, o=glA, i=agG, b=Vs("f_b1g", l * NM + j):
                       e.activation(o, i, AF.Gelu, bias=b, scale=1.0), waits=[("MM", mm_ag)])
                    V_(lambda e, o=h2ap(j, tc * 512, 512), a=agA, s=Vs("f_b1a", l * NM + j), g_=glA:
                       e.scalar_tensor_tensor(o, a, s, g_, op0=OP.add, op1=OP.mult),
                       waits=[("SE", now("SE"))])
                    ag_ve[ssb] = now("VE")
                if j + 2 < NM:
                    slot_of[j + 2] = n_w1[0] % 2
                    lw_of[j + 2] = load_w1(j + 2)
                elif th == 1 and j + 2 >= NM and (j + 2 - NM) < 2:
                    pass
            h2_ve = now("VE")
            w2g[0] = now("MM"); w2g[1] = now("MM")
            load_w2(0, 0)
            w2slot_of = {(0, 0): (n_w2[0] - 1) % 2}
            w2lw_of = {(0, 0): p.now("LW")}
            load_w2(0, 1)
            w2slot_of[(0, 1)] = (n_w2[0] - 1) % 2
            w2lw_of[(0, 1)] = p.now("LW")
            for mg in range(2):
                for j in range(NM):
                    sl = w2slot_of[(mg, j)]
                    w2v = wst_t[:, 4096 + sl * 512:4096 + (sl + 1) * 512]
                    first = (j == 0); lastj = (j == NM - 1)
                    for m in range(4):
                        for q in range(2):
                            wts = []
                            if m == 0 and q == 0:
                                wts = [("LW", w2lw_of[(mg, j)])]
                                if j == 0:
                                    wts += [("VE", h2_ve), ("VE", ms["w2p_ve"])]
                            MMW(ps[:, m * 1024 + q * 512: m * 1024 + q * 512 + 512],
                                w2v[:, m * 128: m * 128 + 128],
                                h2ap(j, q * 512, 512), first, lastj, waits=wts)
                    w2g[sl] = now("MM")
                    nxt = j + 2
                    if nxt < NM:
                        w2slot_of[(mg, nxt)] = n_w2[0] % 2
                        w2lw_of[(mg, nxt)] = load_w2(mg, nxt)
                    elif mg == 0 and (nxt - NM) < 2:
                        w2slot_of[(1, nxt - NM)] = n_w2[0] % 2
                        w2lw_of[(1, nxt - NM)] = load_w2(1, nxt - NM)
                mm_w2 = now("MM")
                for m in range(4):
                    mo = mg * 4 + m
                    for q in range(2):
                        V_(lambda e, o=zc(mo, t0h + q * 512, 512),
                           i=ps[:, m * 1024 + q * 512: m * 1024 + q * 512 + 512],
                           s=Vs("f_b2", l * KC + mo):
                           e.scalar_tensor_tensor(o, i, s, o, op0=OP.add, op1=OP.add),
                           waits=[("MM", mm_w2)] if (m == 0 and q == 0) else ())
                ms["w2p_ve"] = now("VE")
            ms["z_ve"] = now("VE")
            ms["pool_mm"] = now("MM")
            ag_ve = [now("VE"), now("VE")]
        ms["pbank0_ve"] = now("VE")
        ms["pbank1_ve"] = now("VE")
        slot_guard["A"] = now("MM")
        slot_guard["B"] = now("MM")
        ms["pool_ve"] = now("VE")

    def emit_ffn_module():
        mmg0 = now("MM")
        for c in range(KC):
            V_(lambda e, c=c: e.memset(zfm[:, c * (T + 1):c * (T + 1) + 1], 0.0),
               waits=[("MM", mmg0)] if c == 0 else ())
        emit_outer_ln("z", (3, 2, 1, 0))
        for l in range(DEPTH):
            emit_ffn_layer(l)
        mm_g = now("MM")
        for k in range(KC):
            V_(lambda e, o=xc(k, 0, T), i=zc(k, 0, T):
               e.scalar_tensor_tensor(o, i, 0.5, o, op0=OP.mult, op1=OP.add))
        ms["x_ve"] = now("VE")

    def emit_attn_module():
        emit_outer_ln("ybf")
        p.dma("sync", pool[:, 0:1024], d_bvb, waits=[("MM", ms["pool_mm"]), ("VE", ms["pool_ve"])], inc="LS")
        bvb_ls = now("LS")
        sig = poolb[0:16, 2048:4096]
        QRB = [6144, 8192]
        PTB = [10240, 11264]

        def ybf_t(k, t0, n):
            return zfmb[:, k * T + t0: k * T + t0 + n]

        def vg_ap(tch, lg):
            if tch < 15:
                return zfmb[:, 24576 + tch * 520 + 65 * lg: 24576 + tch * 520 + 65 * lg + 65]
            return poolb[:, 14336 + 65 * lg: 14336 + 65 * lg + 65]

        def vg_full(tch):
            if tch < 15:
                return zfmb[:, 24576 + tch * 520: 24576 + tch * 520 + 520]
            return poolb[:, 14336: 14336 + 520]

        def o_ap(c, r0, t0, n):
            if c < 3:
                return hbf_t[r0:r0 + 64, c * 2048 + t0: c * 2048 + t0 + n]
            return poolb[r0:r0 + 64, 12288 + t0: 12288 + t0 + n]

        R0s = [pool[0:1, 7428:8452], pool[32:33, 7428:8452]]
        R2s = [pool[0:1, 8452:9476], pool[32:33, 8452:9476]]
        upar = [0]
        def o_full(c, qc):
            if c < 3:
                return hbf_t[:, c * 2048 + qc * 512: c * 2048 + qc * 512 + 512]
            return poolb[:, 12288 + qc * 512: 12288 + qc * 512 + 512]

        qr_mm = [0, 0]
        pt_mm = [0, 0]
        st_se_a = [0, 0]
        pend_gchain = [None]
        BQ2 = (0, 1, 4, 5)
        abank = {0: ms.get("pbank0_ve", 0), 1: ms.get("pbank1_ve", 0),
                 4: now("VE"), 5: now("VE")}
        ms["b45_se"] = now("SE")
        ms["b67_att_ve"] = now("VE")

        for hg in attn_groups:
            for k in range(KC):
                p.dma("gpsimd", wslotA[:, k * 512: k * 512 + 512],
                      d_wv[k * 128:(k + 1) * 128, hg * 512:(hg + 1) * 512],
                      waits=[("MM", slot_guard["A"])] if k == 0 else (), inc="LW")
            lw_v = now("LW")
            pendv = []
            for tch in range(16):
                bq = BQ2[tch % 4]
                bk = ps[:, bq * 512:bq * 512 + 512]
                for k in range(KC):
                    w_ = [("VE", abank[bq])] if k == 0 else []
                    if k == 0 and bq >= 4:
                        w_ += [("SE", ms["b45_se"])]
                    if tch == 0 and k == 0:
                        w_ += [("LW", lw_v), ("VE", now("VE")), ("LS", bvb_ls)]
                    MMW(bk, ybf_t(k, tch * 128, 128), wslotA[:, k * 512: k * 512 + 512],
                        k == 0, k == KC - 1, waits=w_)
                mmv = now("MM")

                def mkv(tch=tch, bk=bk, mmv=mmv, bq=bq):
                    dst = vg_full(tch)
                    V_(lambda e, o=dst.rearrange("p (h c) -> p h c", c=65),
                       i=bk.rearrange("p (h c) -> p h c", c=64),
                       bv=pool[:, hg * 512:hg * 512 + 512].rearrange("p (h c) -> p h c", c=64):
                       e.tensor_tensor(o[:, :, 0:64], i, bv, op=OP.add), waits=[("MM", mmv)])
                    V_(lambda e, o=dst.rearrange("p (h c) -> p h c", c=65): e.memset(o[:, :, 64:65], 1.0))
                    abank[bq] = now("VE")
                if len(pendv) >= 2:
                    pendv.pop(0)()
                pendv.append(mkv)
            for f in pendv:
                f()
            slot_guard["A"] = now("MM")
            for k in range(KC):
                p.dma("gpsimd", wslotB[:, k * 512: k * 512 + 512],
                      d_wk[k * 128:(k + 1) * 128, hg * 512:(hg + 1) * 512],
                      waits=[("MM", slot_guard["B"])] if k == 0 else (), inc="LW")
            lw_k = now("LW")

            def norm_chain(bk, bias_col, wcol, out_ap, mm0=None):
                ktmp = pool[:, 7428:7940]
                sqb = poolb[:, 15880:16392]
                if mm0 is None:
                    mm0 = now("MM")
                V_(lambda e, i=bk, b=bias_col: e.tensor_scalar(ktmp, i, b, None, op0=OP.add),
                   waits=[("MM", mm0)])
                V_(lambda e: e.tensor_tensor(sqb, ktmp, ktmp, op=OP.mult))
                MMW(ps[0:2, 2 * 512:2 * 512 + 512], e2b, sqb, True, True,
                    waits=[("VE", now("VE")), ("SE", now("SE"))])
                S_(lambda e: e.activation(RL2, ps[0:2, 2 * 512:2 * 512 + 512], AF.Ln,
                                          bias=epsc[0:2, 2:3], scale=1.0 / HD), waits=[("MM", now("MM"))])
                S_(lambda e: e.activation(RL2B, RL2, AF.Exp, bias=0.0, scale=-0.5))
                ms["b2_se"] = now("SE")
                MMW(ps[:, 3 * 512:4 * 512], e2tb[:, :], RL2B, True, True,
                    waits=[("SE", now("SE")), ("VE", ms["b3_ve"]), ("SE", now("SE"))])
                mmb = now("MM")
                V_(lambda e, o=out_ap, w=wcol: e.scalar_tensor_tensor(o, ktmp, w, ps[:, 3 * 512:4 * 512],
                                                                      op0=OP.mult, op1=OP.mult),
                   waits=[("MM", mmb)])
                ms["b3_ve"] = now("VE")

            pendn = []
            for c in range(4):
                for qc in range(4):
                    idx = c * 4 + qc
                    bq = BQ2[idx % 4]
                    bk = ps[:, bq * 512:bq * 512 + 512]
                    for k in range(KC):
                        w_ = [("VE", abank[bq])] if k == 0 else []
                        if k == 0 and bq >= 4:
                            w_ += [("SE", ms["b45_se"])]
                        if c == 0 and qc == 0 and k == 0:
                            w_ += [("LW", lw_k)]
                        MMW(bk, wslotB[:, k * 512 + c * 128: k * 512 + c * 128 + 128],
                            ybf_t(k, qc * 512, 512), k == 0, k == KC - 1, waits=w_)
                    mm_c = now("MM")

                    def mkn(c=c, qc=qc, bk=bk, bq=bq, mm_c=mm_c):
                        norm_chain(bk, Vs("a_bk", hg * 4 + c), V("a_kn"),
                                   zfmb[:, 16384 + c * 2048 + qc * 512: 16384 + c * 2048 + qc * 512 + 512],
                                   mm0=mm_c)
                        abank[bq] = now("VE")
                    if len(pendn) >= 2:
                        pendn.pop(0)()
                    pendn.append(mkn)
            for f in pendn:
                f()
            slot_guard["B"] = now("MM")
            for k in range(KC):
                p.dma("gpsimd", wslotA[:, k * 512: k * 512 + 512],
                      d_wq[k * 128:(k + 1) * 128, hg * 512:(hg + 1) * 512],
                      waits=[("MM", slot_guard["A"])] if k == 0 else (), inc="LW")
            lw_q = now("LW")
            if hg == 0:
                for k in range(KC):
                    p.dma("gpsimd", wslotB[:, k * 128: k * 128 + 128],
                          d_wq[k * 128:(k + 1) * 128, 1024:1152],
                          waits=[("MM", slot_guard["B"])] if k == 0 else (), inc="LW")
                lw_g = now("LW")
                for qc in range(4):
                    bk = ps[:, (qc % 2) * 512:(qc % 2) * 512 + 512]
                    for k in range(KC):
                        w_ = [("VE", abank[qc % 2]), ("SE", now("SE"))] if k == 0 else []
                        if qc == 0 and k == 0:
                            w_ += [("LW", lw_g)]
                        MMW(bk, wslotB[:, k * 128: k * 128 + 128], ybf_t(k, qc * 512, 512),
                            k == 0, k == KC - 1, waits=w_)
                    mmg = now("MM")
                    S_(lambda e, o=sig[:, qc * 512: qc * 512 + 512], i=bk[0:16, :]:
                       e.activation(o, i, AF.Sigmoid, bias=V("a_bgate")[0:16, :], scale=1.0),
                       waits=[("MM", mmg)])
                    abank[qc % 2] = now("MM") * 0 + abank[qc % 2]
                gate_se = now("SE")
                slot_guard["B"] = now("MM")
            sig_se = now("SE")
            for c in range(4):
                qsl = c % 2
                qbase = QRB[qsl]
                pendq = [None]
                for qc in range(4):
                    bq = BQ2[qc]
                    bk = ps[:, bq * 512:bq * 512 + 512]
                    for k in range(KC):
                        w_ = [("VE", abank[bq]), ("SE", now("SE"))] if k == 0 else []
                        if qc == 0 and k == 0:
                            w_ += [("LW", lw_q), ("MM", qr_mm[qsl])]
                        MMW(bk, wslotA[:, k * 512 + c * 128: k * 512 + c * 128 + 128],
                            ybf_t(k, qc * 512, 512), k == 0, k == KC - 1, waits=w_)
                    mm_c = now("MM")
                    if pendq[0] is not None:
                        pendq[0]()

                    def mkq(qc=qc, bk=bk, c=c, bq=bq, mm_c=mm_c):
                        norm_chain(bk, Vs("a_bq", hg * 4 + c), pcol[:, PC_QN8:PC_QN8 + 1],
                                   poolb[:, qbase + qc * 512: qbase + qc * 512 + 512],
                                   mm0=mm_c)
                        abank[bq] = now("VE")
                    pendq[0] = mkq
                pendq[0]()
                ve_pair = now("VE")
                for lh in range(2):
                    h = 2 * c + lh
                    hh = hg * 8 + h
                    r0 = 64 * lh
                    for qh in range(2):
                        pend_pv = [None]
                        pv_started = [False]
                        for kc in range(16):
                            stpar = kc % 2
                            stbase = (2 + 2 * stpar) * 512
                            for qq in range(2):
                                MMW(ps[:, stbase + qq * 512: stbase + qq * 512 + 512],
                                    zfmb[r0:r0 + 64, 16384 + c * 2048 + kc * 128: 16384 + c * 2048 + kc * 128 + 128],
                                    poolb[r0:r0 + 64, qbase + qh * 1024 + qq * 512: qbase + qh * 1024 + qq * 512 + 512],
                                    True, True,
                                    waits=([("VE", ve_pair), ("SE", st_se_a[stpar]), ("SE", ms["b2_se"])]
                                           if qq == 0 else ()))
                            mm_st = now("MM")
                            if pend_pv[0] is not None:
                                pend_pv[0]()
                                pend_pv[0] = None
                            ptr = poolb[:, PTB[stpar]:PTB[stpar] + 1024]
                            se_exps = []
                            for qq in range(2):
                                S_(lambda e, o=ptr[:, qq * 512: qq * 512 + 512],
                                   i=ps[:, stbase + qq * 512: stbase + qq * 512 + 512]:
                                   e.activation(o, i, AF.Exp, bias=pcol[:, PC_BA:PC_BA + 1], scale=1.0),
                                   waits=[("MM", mm_st), ("MM", pt_mm[stpar])] if qq == 0 else ())
                                se_exps.append(now("SE"))
                            st_se_a[stpar] = now("SE")

                            def mk_pv(kc=kc, h=h, ptr=ptr, stpar=stpar, se_exps=tuple(se_exps)):
                                for qq in range(2):
                                    first = (kc == 0)
                                    w_ = [("SE", se_exps[qq])]
                                    if first and qq == 0:
                                        w_.append(("VE", ms["b67_att_ve"]))
                                    MMW(ps[0:65, 6 * 512 + qq * 512: 6 * 512 + qq * 512 + 512],
                                        vg_ap(kc, h), ptr[:, qq * 512: qq * 512 + 512],
                                        first, kc == 15, waits=w_)
                                pt_mm[stpar] = now("MM")
                            pend_pv[0] = mk_pv
                        pend_pv[0]()
                        pend_pv[0] = None
                        mm_pv = now("MM")
                        up = upar[0]; upar[0] ^= 1
                        R0 = R0s[up]; R2 = R2s[up]
                        V_(lambda e, o=o_ap(c, r0, qh * 1024, 1024), i=ps[0:64, 6 * 512: 6 * 512 + 1024]:
                           e.tensor_copy(o, i), waits=[("MM", mm_pv)])
                        V_(lambda e, R2=R2: e.tensor_copy(R2, ps[64:65, 6 * 512: 6 * 512 + 1024]))
                        ms["b67_att_ve"] = now("VE")
                        ve_sum = now("VE")
                        if pend_gchain[0] is not None:
                            pend_gchain[0]()

                        def mk_gchain(c=c, r0=r0, qh=qh, hh=hh, ve_sum=ve_sum, R0=R0, R2=R2, up=up):
                            rbase = 32 * up
                            for qq in range(2):
                                MMW(ps[0:1, qq * 512: qq * 512 + 512],
                                    identg[:, hh:hh + 1], sig[:, qh * 1024 + qq * 512: qh * 1024 + qq * 512 + 512],
                                    True, True,
                                    waits=[("VE", ve_sum), ("SE", sig_se), ("VE", abank[qq])] if qq == 0
                                          else [("VE", abank[qq])])
                            mm_gx = now("MM")
                            V_(lambda e: e.tensor_copy(R0, ps[0:1, 0:1024]),
                               waits=[("MM", mm_gx)])
                            S_(lambda e: e.activation(R2, R2, AF.Ln, bias=0.0, scale=1.0),
                               waits=[("VE", now("VE"))])
                            S_(lambda e: e.activation(R2, R2, AF.Exp, bias=0.0, scale=-1.0))
                            R0B = poolb[rbase:rbase + 1, 2 * 9504: 2 * 9504 + 1024]
                            V_(lambda e, R0B=R0B: e.tensor_tensor(R0B, R0, R2, op=OP.mult),
                               waits=[("SE", now("SE"))])
                            ve_row = now("VE")
                            for qq in range(2):
                                MMW(ps[0:64, qq * 512: qq * 512 + 512],
                                    cb[rbase:rbase + 1, 520:584],
                                    R0B[:, qq * 512: qq * 512 + 512],
                                    True, True, waits=[("VE", ve_row)] if qq == 0 else ())
                            mm_g2 = now("MM")
                            V_(lambda e, o=o_ap(c, r0, qh * 1024, 1024):
                               e.tensor_tensor(o, o, ps[0:64, 0:1024], op=OP.mult),
                               waits=[("MM", mm_g2)])
                            abank[0] = now("VE")
                            abank[1] = now("VE")
                        pend_gchain[0] = mk_gchain
                if pend_gchain[0] is not None:
                    pend_gchain[0]()
                    pend_gchain[0] = None
                qr_mm[qsl] = now("MM")
                ms["b45_se"] = now("SE")
            slot_guard["A"] = now("MM")
            for kc in range(4):
                p.dma("gpsimd", wslotB[:, kc * 1024: kc * 1024 + 1024],
                      d_wo[hg * 512 + kc * 128: hg * 512 + kc * 128 + 128, :],
                      waits=[("MM", slot_guard["B"])] if kc == 0 else (), inc="LW")
            lw_o = now("LW")
            ve_o = now("VE")
            for m in range(KC):
                for qc in range(4):
                    bk = ps[:, (qc % 2) * 512:(qc % 2) * 512 + 512]
                    for kc in range(4):
                        w_ = [("VE", abank[qc % 2])] if kc == 0 else []
                        if m == 0 and qc == 0 and kc == 0:
                            w_ += [("LW", lw_o), ("VE", ve_o)]
                        MMW(bk, wslotB[:, kc * 1024 + m * 128: kc * 1024 + m * 128 + 128],
                            o_full(kc, qc),
                            kc == 0, kc == 3, waits=w_)
                    mmw = now("MM")
                    if hg == 0:
                        V_(lambda e, o=xc(m, qc * 512, 512), b=bk, s=Vs("a_bo", m):
                           e.scalar_tensor_tensor(o, b, s, o, op0=OP.add, op1=OP.add),
                           waits=[("MM", mmw)])
                    else:
                        V_(lambda e, o=xc(m, qc * 512, 512), b=bk: e.tensor_tensor(o, o, b, op=OP.add),
                           waits=[("MM", mmw)])
                    abank[qc % 2] = now("VE")
            slot_guard["B"] = now("MM")
        ms["pbank0_ve"] = now("VE")
        ms["pbank1_ve"] = now("VE")
        ms["pool_mm"] = now("MM")
        ms["pool_ve"] = now("VE")
        ms["x_ve"] = now("VE")

    dbg = [s for s in stages if s.startswith("dbg_")]
    if dbg and dbg[0] == "dbg_attn":
        emit_attn_module()
        d_dbg = nc.dram_tensor("dbg", [128, 32768], BF16, kind="ExternalOutput").ap()
        d_dbg2 = nc.dram_tensor("dbg2", [128, 3072], F32, kind="ExternalOutput").ap()
        wts = [("VE", now("VE")), ("MM", now("MM")), ("SE", now("SE"))]
        p.dma("sync", d_dbg[:, 0:16384], zfmb[:, 0:16384], waits=wts, inc="LS")
        p.dma("sync", d_dbg[:, 16384:24576], zfmb[:, 16384:24576], inc="LS")
        p.dma("sync", d_dbg[:, 24576:26624], poolb[:, 6144:8192], inc="LS")
        p.dma("sync", d_dbg[:, 26624:28672], poolb[:, 8192:10240], inc="LS")
        p.dma("sync", d_dbg[:, 28672:30720], hbf_t[:, 0:2048], inc="LS")
        p.dma("sync", d_dbg2[:, 0:2048], pool[:, 1024:3072], inc="LS")
        p.dma("sync", d_dbg2[:, 2048:3072], pool[:, 0:1024], inc="LS")
        dbg = []
        dbg_tap = None
    elif dbg:
        sub = dbg[0][4:]
        emit_outer_ln("z")
        if sub in ("l0a", "l0"):
            emit_ffn_layer(0, parts=("la",) if sub == "l0a" else ("la", "mlp"))
        dbg_tap = "z"
        d_dbg = nc.dram_tensor("dbg", [128, 32768], BF16, kind="ExternalOutput").ap()
        d_dbg2 = nc.dram_tensor("dbg2", [128, 64], F32, kind="ExternalOutput").ap()
        p.dma("sync", d_dbg[:, 0:8192], poolb[:, 0:8192], waits=[("VE", now("VE")), ("MM", now("MM")), ("SE", now("SE"))], inc="LS")
        p.dma("sync", d_dbg[:, 8192:10240], poolb[:, QWIN:QWIN + 2048], inc="LS")
        p.dma("sync", d_dbg[:, 10240:14336], poolb[:, KRING:KRING + 4096], inc="LS")
        p.dma("sync", d_dbg[:, 14336:18496], poolb[:, VRING:VRING + 4160], inc="LS")
        p.dma("sync", d_dbg[:, 18496:22600], hbf_t[:, 0:4104], inc="LS")
        p.dma("sync", d_dbg[:, 22600:24648], hbf_t[:, OW:OW + 2048], inc="LS")
        p.dma("sync", d_dbg[:, 24648:25672], hbf_t[:, PTR:PTR + 1024], inc="LS")
        p.dma("sync", d_dbg2[:, 0:8], pcol[:], inc="LS")
    else:
        if "ffn0" in stages:
            emit_ffn_module()
        if "attn" in stages:
            emit_attn_module()
        if "ffn1" in stages:
            emit_ffn_module()
    io_e = pool[:, 2048:2176]
    ident_e = pool[:, 2176:2304]
    p.emit("gpsimd", lambda e: e.iota(io_e, [[1, 128]], channel_multiplier=-1,
                                      allow_small_or_imprecise_dtypes=True),
           waits=[("MM", ms["pool_mm"]), ("VE", ms["pool_ve"]), ("LS", now("LS"))], inc="GP")
    V_(lambda e: e.tensor_scalar(ident_e, io_e, 0.0, 1.0, op0=OP.is_equal, op1=OP.mult),
       waits=[("GP", p.now("GP"))])
    ve_id = now("VE")
    tout = [pool[:, 0:1024], pool[:, 1024:2048]]
    st_ls = [0, 0]

    def src_ap(k, tc):
        if dbg_tap == "z":
            return zc(k, tc * 128, 128)
        return xfm[:, k * T + tc * 128: k * T + tc * 128 + 128]

    for tc in range(16):
        to = tout[tc % 2]
        for k in range(KC):
            bkp = ps[:, (k % 2) * 512:(k % 2) * 512 + 512]
            T_(lambda e, o=bkp[:, 0:128], i=src_ap(k, tc), idn=ident_e: e.transpose(o, i, idn),
               waits=([("VE", ve_id), ("LS", st_ls[tc % 2])] if k == 0 else [("VE", now("VE"))]))
            V_(lambda e, o=to[:, k * 128: k * 128 + 128], i=bkp[:, 0:128]: e.tensor_copy(o, i),
               waits=[("MM", now("MM"))])
        p.dma("sync", d_out[tc * 128:(tc + 1) * 128, :], to, waits=[("VE", now("VE"))], inc="LS")
        st_ls[tc % 2] = now("LS")
    p.emit("sync", lambda e: e.nop(), waits=[("LS", now("LS"))])
    p.finish()
    for t in reversed(ctx):
        t.__exit__(None, None, None)
    return nc



def kernel(**inputs):
    staged, vix, nv = _stage_weights(inputs)
    nc = build_program(nv, vix)
    x = np.asarray(inputs["L0_attn_x"], np.float32)
    in_maps = []
    for i in range(B):
        m = dict(staged)
        m["xin"] = np.ascontiguousarray(x[i])
        in_maps.append(m)
    res = run_bass_kernel_spmd(nc, in_maps, core_ids=list(range(B)))
    return np.stack([np.asarray(r["out"], np.float32) for r in res.results], 0)


# version 14
# speedup vs baseline: 1.1473x; 1.0280x over previous
import numpy as np
import concourse.bass as bass
import concourse.mybir as mybir
import concourse.bass_utils as _bass_utils
from concourse.bass_utils import run_bass_kernel_spmd
from concourse.alu_op_type import AluOpType as OP
import ml_dtypes

_orig_run_command = _bass_utils.run_command


def _run_command_ldwopt(cmd, **kw):
    if isinstance(cmd, list):
        cmd = ["--enable-ldw-opt=true" if c == "--enable-ldw-opt=false" else c
               for c in cmd]
    return _orig_run_command(cmd, **kw)


_bass_utils.run_command = _run_command_ldwopt

F32 = mybir.dt.float32
BF16 = mybir.dt.bfloat16
AF = mybir.ActivationFunctionType
AX = mybir.AxisListType.X

B = 8
D = 1024
T = 2048
KC = 8
HA = 16
HD = 64
DEPTH = 2
HF = 8
WIN = 512
NW = T // WIN
HID = 2730
HIDP = 2816
NM = HIDP // 128
NEGBIG = -3.0e38
LN2_EPS = 1e-5
RMS_EPS = 1e-6

ENGINES = ("sync", "gpsimd", "tensor", "vector", "scalar")


class Prog:
    def __init__(self, nc):
        self.nc = nc
        self.items = {e: [] for e in ENGINES}
        self.sems = {}
        self.counts = {}

    def sem(self, name):
        h = self.nc.semaphore(name)
        self.sems[name] = h.__enter__()
        self.counts[name] = 0

    def now(self, s):
        return self.counts[s]

    def emit(self, eng, fn, waits=(), inc=None, inc_amt=1):
        for s, v in waits:
            assert v <= self.counts[s], f"wait on future: {s} {v} > {self.counts[s]}"
        if inc is not None:
            self.counts[inc] += inc_amt
        waits = tuple((s, v) for s, v in waits if v > 0)
        sems = self.sems

        def item(engobj, fn=fn, waits=waits, inc=inc, inc_amt=inc_amt):
            for s, v in waits:
                engobj.wait_ge(sems[s], v)
            inst = fn(engobj)
            if inc is not None:
                inst.then_inc(sems[inc], inc_amt)
        self.items[eng].append(item)

    def dma(self, eng, out, in_, waits=(), inc=None):
        self.emit(eng, lambda e: e.dma_start(out=out, in_=in_), waits, inc, 16)

    def finish(self):
        nc = self.nc
        with nc.Block() as block:
            for e in ENGINES:
                lst = self.items[e]
                if not lst:
                    continue

                def body(engobj, lst=lst):
                    for item in lst:
                        item(engobj)
                getattr(block, e)(body)



def _stage_weights(iw):
    f32 = np.float32
    out = {}
    wq = np.asarray(iw["L0_attn_Wq"], f32)
    wqp = np.zeros((D, 1152), f32)
    wqp[:, :1040] = wq
    out["a_wq"] = np.ascontiguousarray(wqp)
    out["a_wk"] = np.ascontiguousarray(np.asarray(iw["L0_attn_Wk"], f32))
    out["a_wv"] = np.ascontiguousarray(np.asarray(iw["L0_attn_Wv"], f32))
    out["a_wo"] = np.ascontiguousarray(np.asarray(iw["L0_attn_Wo"], f32))
    wqkv = np.asarray(iw["L0_ffn_wqkv"], f32)
    s = wqkv.reshape(DEPTH, KC, 128, 3, 512).transpose(0, 3, 2, 1, 4)
    out["f_wqkv"] = np.ascontiguousarray(s.reshape(DEPTH, 3, 128, KC * 512))
    wo = np.asarray(iw["L0_ffn_wo"], f32)
    s = wo.reshape(DEPTH, 4, 128, 1024).transpose(0, 2, 1, 3)
    out["f_wo"] = np.ascontiguousarray(s.reshape(DEPTH, 128, 4 * 1024))
    w1 = np.asarray(iw["L0_ffn_w1"], f32)
    a = np.zeros((DEPTH, D, HIDP), f32); a[:, :, :HID] = w1[:, :, :HID]
    g = np.zeros((DEPTH, D, HIDP), f32); g[:, :, :HID] = w1[:, :, HID:]
    inter = np.empty((DEPTH, D, NM, 256), f32)
    inter[:, :, :, :128] = a.reshape(DEPTH, D, NM, 128)
    inter[:, :, :, 128:] = g.reshape(DEPTH, D, NM, 128)
    s = inter.transpose(0, 2, 1, 3).reshape(DEPTH, NM, KC, 128, 256).transpose(0, 1, 3, 2, 4)
    out["f_w1"] = np.ascontiguousarray(s.reshape(DEPTH, NM, 128, KC * 256))
    w2 = np.asarray(iw["L0_ffn_w2"], f32)
    w2p = np.zeros((DEPTH, HIDP, D), f32); w2p[:, :HID, :] = w2
    s = w2p.reshape(DEPTH, NM, 128, 2, 512).transpose(0, 3, 1, 2, 4)
    out["f_w2"] = np.ascontiguousarray(s)
    cols = []

    def addcol(v):
        v = np.asarray(v, f32)
        if v.ndim == 1:
            v = v[:, None]
        cols.append(v)
        return sum(c.shape[1] for c in cols) - v.shape[1]

    ix = {}
    bq = np.asarray(iw["L0_attn_bq"], f32)
    ix["a_bq"] = addcol(bq[:D].reshape(KC, 128).T)
    ix["a_bgate"] = addcol(np.pad(bq[D:], (0, 112)))
    ix["a_bk"] = addcol(np.asarray(iw["L0_attn_bk"], f32).reshape(KC, 128).T)
    ix["a_bv"] = addcol(np.asarray(iw["L0_attn_bv"], f32).reshape(KC, 128).T)
    ix["a_bo"] = addcol(np.asarray(iw["L0_attn_bo"], f32).reshape(KC, 128).T)
    ix["a_qn"] = addcol(np.tile(np.asarray(iw["L0_attn_qn_w"], f32), 2))
    ix["a_kn"] = addcol(np.tile(np.asarray(iw["L0_attn_kn_w"], f32), 2))
    qs = np.asarray(iw["L0_ffn_q_scale"], f32)
    ks = np.asarray(iw["L0_ffn_k_scale"], f32)
    ix["f_qs"] = addcol(np.tile(qs, (1, 2)).T.reshape(128, DEPTH, order="F") if False else np.stack([np.tile(qs[l], 2) for l in range(DEPTH)], 1))
    ix["f_ks"] = addcol(np.stack([np.tile(ks[l], 2) for l in range(DEPTH)], 1))
    lng = np.asarray(iw["L0_ffn_ln_g"], f32)
    lnb = np.asarray(iw["L0_ffn_ln_b"], f32)
    ix["f_lng"] = addcol(np.concatenate([lng[l].reshape(KC, 128).T for l in range(DEPTH)], 1))
    ix["f_lnb"] = addcol(np.concatenate([lnb[l].reshape(KC, 128).T for l in range(DEPTH)], 1))
    b1 = np.asarray(iw["L0_ffn_b1"], f32)
    b1a = np.zeros((DEPTH, HIDP), f32); b1a[:, :HID] = b1[:, :HID]
    b1g = np.zeros((DEPTH, HIDP), f32); b1g[:, :HID] = b1[:, HID:]
    ix["f_b1a"] = addcol(np.concatenate([b1a[l].reshape(NM, 128).T for l in range(DEPTH)], 1))
    ix["f_b1g"] = addcol(np.concatenate([b1g[l].reshape(NM, 128).T for l in range(DEPTH)], 1))
    b2 = np.asarray(iw["L0_ffn_b2"], f32)
    ix["f_b2"] = addcol(np.concatenate([b2[l].reshape(KC, 128).T for l in range(DEPTH)], 1))
    ix["a_qn_bc"] = addcol(np.broadcast_to(np.asarray(iw["L0_attn_qn_w"], f32), (128, HD)))
    ix["a_kn_bc"] = addcol(np.broadcast_to(np.asarray(iw["L0_attn_kn_w"], f32), (128, HD)))
    ix["f_qs_bc"] = addcol(np.concatenate([np.broadcast_to(qs[l], (128, HD)) for l in range(DEPTH)], 1))
    ix["f_ks_bc"] = addcol(np.concatenate([np.broadcast_to(ks[l], (128, HD)) for l in range(DEPTH)], 1))
    vecs = np.concatenate(cols, 1)
    out["vecs"] = np.ascontiguousarray(vecs)
    out["bvb"] = np.ascontiguousarray(np.broadcast_to(np.asarray(iw["L0_attn_bv"], f32), (128, D)))
    return out, ix, vecs.shape[1]



def build_program(nv_cols, vix, stages=("ffn0", "attn", "ffn1"), dbg_tap=None, attn_groups=(0, 1)):
    dbg_tap = dbg_tap
    nc = bass.Bass("TRN2", target_bir_lowering=False, detect_race_conditions=False)
    PI = 3.141592653589793
    xin = nc.dram_tensor("xin", [T, D], F32, kind="ExternalInput").ap()
    d_wq = nc.dram_tensor("a_wq", [D, 1152], F32, kind="ExternalInput").ap()
    d_wk = nc.dram_tensor("a_wk", [D, D], F32, kind="ExternalInput").ap()
    d_wv = nc.dram_tensor("a_wv", [D, D], F32, kind="ExternalInput").ap()
    d_wo = nc.dram_tensor("a_wo", [D, D], F32, kind="ExternalInput").ap()
    d_fqkv = nc.dram_tensor("f_wqkv", [DEPTH, 3, 128, KC * 512], F32, kind="ExternalInput").ap()
    d_fwo = nc.dram_tensor("f_wo", [DEPTH, 128, 4 * 1024], F32, kind="ExternalInput").ap()
    d_fw1 = nc.dram_tensor("f_w1", [DEPTH, NM, 128, KC * 256], F32, kind="ExternalInput").ap()
    d_fw2 = nc.dram_tensor("f_w2", [DEPTH, 2, NM, 128, 512], F32, kind="ExternalInput").ap()
    d_vecs = nc.dram_tensor("vecs", [128, nv_cols], F32, kind="ExternalInput").ap()
    d_bvb = nc.dram_tensor("bvb", [128, D], F32, kind="ExternalInput").ap()
    d_out = nc.dram_tensor("out", [T, D], F32, kind="ExternalOutput").ap()
    d_tabs = nc.dram_tensor("tabs", [4, 128, T], BF16, kind="Internal").ap()

    ctx = []

    def sb(name, shape, dt):
        t = nc.sbuf_tensor(name, shape, dt); ctx.append(t); return t.__enter__()

    xfm = sb("xfm", [128, KC * T], F32)
    zfm = sb("zfm", [128, KC * (T + 1)], F32)
    hbf_t = sb("hbf", [128, 8192], BF16)
    wst_t = sb("wst", [128, 8192], BF16)
    pool = sb("pool", [128, 10016], F32)
    rows = sb("rows", [128, 1024], F32)
    vecs = sb("vecs_sb", [128, nv_cols], F32)
    pcol = sb("pcol", [128, 8], F32)
    cb = sb("cb", [128, 584], BF16)
    e2tf = sb("e2tf", [2, 128], F32)
    e2tb = sb("e2tb", [2, 128], BF16)
    onesf = sb("onesf", [128, 128], F32)
    identg = sb("identg", [16, 16], BF16)
    epsc = sb("epsc", [128, 4], F32)
    pst = nc.psum_tensor("pst", [128, 4096], F32); ctx.append(pst); ps = pst.__enter__()

    poolb = pool.bitcast(BF16)
    hbff = hbf_t.bitcast(F32)
    zfmb = zfm.bitcast(BF16)

    def bank(i, n=1):
        return ps[:, i * 512:(i + n) * 512]

    mgt = cb[:, 0:128]; mlt = cb[:, 128:256]; identb = cb[:, 256:384]; e2b = cb[:, 384:386]

    PC_BA, PC_BF0, PC_BF1, PC_QS80, PC_QS81, PC_QN8 = 0, 1, 2, 3, 4, 5

    def zc(c, t0, n):
        return zfm[:, c * (T + 1) + 1 + t0: c * (T + 1) + 1 + t0 + n]

    def xc(c, t0, n):
        return xfm[:, c * T + t0: c * T + t0 + n]

    V = lambda name: vecs[:, vix[name]:vix[name] + 1]

    def Vs(name, off, n=1):
        return vecs[:, vix[name] + off: vix[name] + off + n]

    p0 = Prog(nc)
    for s in ("LS", "LW", "MM", "VE", "SE", "GP"):
        p0.sem(s)
    P = p0

    def t_(fn, waits=(), inc=None):
        P.emit("tensor", fn, waits, inc)

    def v_(fn, waits=(), inc=None):
        P.emit("vector", fn, waits, inc)

    def s_(fn, waits=(), inc=None):
        P.emit("scalar", fn, waits, inc)

    def g_(fn, waits=(), inc=None):
        P.emit("gpsimd", fn, waits, inc)

    p0.dma("sync", vecs[:], d_vecs, inc="LS")
    io128 = pool[:, 2048:2176]
    ident = pool[:, 2176:2304]
    iop = pool[:, 2304:2305]
    g_(lambda e: e.iota(io128, [[1, 128]], channel_multiplier=-1, allow_small_or_imprecise_dtypes=True), inc="GP")
    g_(lambda e: e.iota(iop, [[1, 1]], channel_multiplier=1, allow_small_or_imprecise_dtypes=True), inc="GP")
    gp0 = P.now("GP")
    v_(lambda e: e.tensor_scalar(ident, io128, 0.0, 1.0, op0=OP.is_equal, op1=OP.mult), waits=[("GP", gp0)])
    g_(lambda e: e.tensor_scalar(identg[:, :], io128[0:16, 0:16], 0.0, 1.0, op0=OP.is_equal, op1=OP.mult),
       waits=[("GP", gp0)], inc="GP")
    v_(lambda e: e.tensor_scalar(mgt, io128, 0.0, NEGBIG, op0=OP.is_gt, op1=OP.mult), waits=[("GP", gp0)])
    v_(lambda e: e.tensor_scalar(mlt, io128, 0.0, NEGBIG, op0=OP.is_lt, op1=OP.mult), waits=[("GP", gp0)])
    v_(lambda e: e.tensor_copy(identb, ident), waits=[("GP", gp0)])
    g_(lambda e: e.tensor_scalar(e2b[:, 0:1], iop, 64.0, 1.0, op0=OP.is_lt, op1=OP.mult), waits=[("GP", gp0)])
    g_(lambda e: e.tensor_scalar(e2b[:, 1:2], iop, 64.0, 1.0, op0=OP.is_ge, op1=OP.mult), waits=[("GP", gp0)])
    v_(lambda e: e.memset(onesf[:], 1.0))
    v_(lambda e: e.memset(epsc[:, 0:1], 1e-5))
    v_(lambda e: e.memset(epsc[:, 1:2], 1e-30))
    v_(lambda e: e.memset(epsc[:, 2:3], 1e-6))
    v_(lambda e: e.memset(epsc[:, 3:4], -80.0))
    io2r = pool[:, 2305:2433]
    g_(lambda e: e.iota(pool[0:2, 2305:2433], [[1, 128]], channel_multiplier=64, allow_small_or_imprecise_dtypes=True), inc="GP")
    gp1 = P.now("GP")
    v_(lambda e: e.tensor_scalar(e2tf[:, :], pool[0:2, 2305:2433], 64.0, 1.0, op0=OP.is_lt, op1=OP.mult), waits=[("GP", gp1)])
    v_(lambda e: e.tensor_scalar(pool[0:2, 2433:2561], pool[0:2, 2305:2433], 128.0, 1.0, op0=OP.is_ge, op1=OP.mult), waits=[("GP", gp1)])
    v_(lambda e: e.tensor_tensor(e2tf[:, :], e2tf[:, :], pool[0:2, 2433:2561], op=OP.add), inc="VE")
    v_(lambda e: e.tensor_copy(e2tb[:, :], e2tf[:, :]))
    v_(lambda e: e.memset(cb[:, 520:584], 1.0))
    rotb = cb[:, 387:515]
    iof = pool[:, 2561:2689]
    fm64 = pool[:, 2689:2817]
    sA = pool[:, 2817:2945]
    sB = pool[:, 2945:3073]
    g_(lambda e: e.iota(iof, [[1, 128]], channel_multiplier=0, allow_small_or_imprecise_dtypes=True), inc="GP")
    gp2 = P.now("GP")
    v_(lambda e: e.tensor_scalar(fm64, iof, 64.0, 64.0, op0=OP.is_ge, op1=OP.mult), waits=[("GP", gp2)])
    v_(lambda e: e.tensor_tensor(fm64, iof, fm64, op=OP.subtract))
    v_(lambda e: e.tensor_single_scalar(sA, io128, -32.0, op=OP.is_equal), waits=[("GP", gp2)])
    v_(lambda e: e.tensor_scalar(sB, fm64, 32.0, -1.0, op0=OP.is_lt, op1=OP.mult))
    v_(lambda e: e.tensor_tensor(sA, sA, sB, op=OP.mult))
    v_(lambda e: e.tensor_single_scalar(sB, io128, 32.0, op=OP.is_equal))
    v_(lambda e: e.tensor_copy(rotb, sA))
    v_(lambda e: e.tensor_scalar(sA, fm64, 32.0, 1.0, op0=OP.is_ge, op1=OP.mult))
    v_(lambda e: e.tensor_tensor(sA, sA, sB, op=OP.mult))
    v_(lambda e: e.tensor_tensor(pool[:, 2817:2945], sA, sA, op=OP.bypass) if False else
       e.tensor_tensor(sB, sA, sA, op=OP.mult))
    v_(lambda e: e.scalar_tensor_tensor(rotb, sA, 1.0, rotb, op0=OP.mult, op1=OP.add), inc="VE")
    iop_m = pool[:, 3076:3077]
    invf = pool[:, 3073:3074]
    lnb_ = pool[:, 3074:3075]
    trow = pool[:, 3584:5632]
    fr = pool[:, 5632:7680]
    tA = pool[:, 7680:9728]
    sin_t = rows[:, 0:1024]
    LN1E4_32 = 9.210340371976184 / 32.0
    _t32 = pool[:, 3075:3076]
    g_(lambda e: e.tensor_scalar(iop_m, iop, 32.0, 32.0, op0=OP.is_ge, op1=OP.mult),
       waits=[("GP", gp0), ("VE", P.now("VE"))], inc="GP")
    g_(lambda e: e.tensor_scalar(_t32, iop, 64.0, 32.0, op0=OP.is_ge, op1=OP.mult), waits=[("GP", P.now("GP"))], inc="GP")
    g_(lambda e: e.tensor_tensor(iop_m, iop_m, _t32, op=OP.add), waits=[("GP", P.now("GP"))], inc="GP")
    g_(lambda e: e.tensor_scalar(_t32, iop, 96.0, 32.0, op0=OP.is_ge, op1=OP.mult), waits=[("GP", P.now("GP"))], inc="GP")
    g_(lambda e: e.tensor_tensor(iop_m, iop_m, _t32, op=OP.add), waits=[("GP", P.now("GP"))], inc="GP")
    g_(lambda e: e.tensor_tensor(iop_m, iop, iop_m, op=OP.subtract), waits=[("GP", P.now("GP"))], inc="GP")
    g_(lambda e: e.tensor_scalar(lnb_, iop_m, 2.0 / 89.6, 25.6 / 89.6, op0=OP.mult, op1=OP.add), waits=[("GP", P.now("GP"))], inc="GP")
    s_(lambda e: e.activation(invf, iop_m, AF.Exp, bias=0.0, scale=-LN1E4_32), waits=[("GP", P.now("GP"))])
    s_(lambda e: e.activation(lnb_, lnb_, AF.Ln, bias=0.0, scale=1.0), inc="SE")
    g_(lambda e: e.iota(trow, [[1, 2048]], channel_multiplier=0, allow_small_or_imprecise_dtypes=True), inc="GP")
    gp3 = P.now("GP")
    PIh = PI / 2.0
    TWO_PI = 2.0 * PI
    pooli = pool.bitcast(mybir.dt.int32)
    R_K = pool[:, 0:2048]
    R_Ki = pooli[:, 0:2048]
    tAb = poolb[:, 0:4096]
    se_tab = [P.now("SE")]
    C1 = 6.28125
    C2 = 6.283185307179586 - 6.28125
    INV2PI = 1.0 / 6.283185307179586

    def build_tab(idx, is_sin, inv_scale):
        g_(lambda e: e.iota(trow, [[1, 2048]], channel_multiplier=0,
                            allow_small_or_imprecise_dtypes=True),
           waits=[("VE", P.now("VE"))], inc="GP")
        gp_t = P.now("GP")
        v_(lambda e: e.tensor_scalar_mul(fr, trow, invf),
           waits=[("GP", gp_t), ("SE", se_tab[0])])
        if not is_sin:
            v_(lambda e: e.tensor_single_scalar(fr, fr, PIh, op=OP.add))
        v_(lambda e: e.tensor_scalar(trow, trow, -1024.0, 1.0 / 256.0, op0=OP.add, op1=OP.mult))
        v_(lambda e: e.tensor_scalar_mul(tA, fr, INV2PI))
        v_(lambda e: e.tensor_copy(R_Ki, tA), waits=[("LS", P.now("LS"))])
        v_(lambda e: e.tensor_copy(R_K, R_Ki))
        v_(lambda e: e.scalar_tensor_tensor(fr, R_K, -C1, fr, op0=OP.mult, op1=OP.add))
        v_(lambda e: e.scalar_tensor_tensor(fr, R_K, -C2, fr, op0=OP.mult, op1=OP.add))
        v_(lambda e: e.tensor_scalar(tA, fr, PI, TWO_PI, op0=OP.is_gt, op1=OP.mult))
        v_(lambda e: e.tensor_tensor(fr, fr, tA, op=OP.subtract))
        v_(lambda e: e.tensor_scalar(tA, fr, -PI, TWO_PI, op0=OP.is_lt, op1=OP.mult))
        v_(lambda e: e.tensor_tensor(fr, fr, tA, op=OP.add), inc="VE")
        s_(lambda e: e.activation(fr, fr, AF.Sin, bias=0.0, scale=1.0), waits=[("VE", P.now("VE"))], inc="SE")
        v_(lambda e: e.tensor_scalar_mul(tA, trow, lnb_), waits=[("SE", P.now("SE"))], inc="VE")
        s_(lambda e: e.activation(tA, tA, AF.Exp, bias=0.0, scale=(-1.0 if inv_scale else 1.0)),
           waits=[("VE", P.now("VE"))], inc="SE")
        v_(lambda e: e.tensor_tensor(tA, fr, tA, op=OP.mult), waits=[("SE", P.now("SE"))])
        v_(lambda e: e.tensor_copy(tAb[:, 0:2048], tA), inc="VE")
        P.dma("sync", d_tabs[idx], tAb[:, 0:2048], waits=[("VE", P.now("VE"))], inc="LS")
        se_tab[0] = P.now("SE")
    build_tab(0, False, False)
    build_tab(1, True, False)
    build_tab(2, False, True)
    build_tab(3, True, True)
    ls_vec = 16
    v_(lambda e: e.reduce_max(pcol[:, 6:7], Vs("a_qn_bc", 0, HD), axis=AX, apply_absolute_value=True),
       waits=[("LS", ls_vec)], inc="VE")
    v_(lambda e: e.reduce_max(pcol[:, 7:8], Vs("a_kn_bc", 0, HD), axis=AX, apply_absolute_value=True), inc="VE")
    g_(lambda e: e.tensor_tensor(pcol[:, PC_BA:PC_BA + 1], pcol[:, 6:7], pcol[:, 7:8], op=OP.mult),
       waits=[("VE", P.now("VE"))], inc="GP")
    s_(lambda e: e.activation(pcol[:, PC_BA:PC_BA + 1], pcol[:, PC_BA:PC_BA + 1], AF.Relu,
                              bias=epsc[:, 3:4], scale=8.0), waits=[("GP", P.now("GP"))])
    s_(lambda e: e.mul(pcol[:, PC_BA:PC_BA + 1], pcol[:, PC_BA:PC_BA + 1], -1.0), inc="SE")
    for l in range(DEPTH):
        v_(lambda e, l=l: e.reduce_max(pcol[:, 6:7], Vs("f_qs_bc", l * HD, HD), axis=AX, apply_absolute_value=True), inc="VE")
        v_(lambda e, l=l: e.reduce_max(pcol[:, 7:8], Vs("f_ks_bc", l * HD, HD), axis=AX, apply_absolute_value=True), inc="VE")
        g_(lambda e, l=l: e.tensor_tensor(pcol[:, PC_BF0 + l:PC_BF0 + l + 1], pcol[:, 6:7], pcol[:, 7:8], op=OP.mult),
           waits=[("VE", P.now("VE"))], inc="GP")
        s_(lambda e, l=l: e.activation(pcol[:, PC_BF0 + l:PC_BF0 + l + 1], pcol[:, PC_BF0 + l:PC_BF0 + l + 1],
                                       AF.Relu, bias=epsc[:, 3:4], scale=8.0), waits=[("GP", P.now("GP"))])
        s_(lambda e, l=l: e.mul(pcol[:, PC_BF0 + l:PC_BF0 + l + 1], pcol[:, PC_BF0 + l:PC_BF0 + l + 1], -1.0), inc="SE")
        g_(lambda e, l=l: e.tensor_single_scalar(pcol[:, PC_QS80 + l:PC_QS80 + l + 1],
                                                 Vs("f_qs", l), 8.0, op=OP.mult))
    g_(lambda e: e.tensor_single_scalar(pcol[:, PC_QN8:PC_QN8 + 1], V("a_qn"), 0.125, op=OP.mult), inc="GP")
    tin0 = pool[:, 0:1024]
    tin1 = pool[:, 1024:2048]
    v_(lambda e: e.engine_nop() if False else e.memset(rows[:, 0:1], 0.0), inc="VE")
    ve_ident = P.now("VE")
    mm_done_tin = [0, 0]
    for tc in range(16):
        tin = (tin0, tin1)[tc % 2]
        P.dma("sync", tin, xin[tc * 128:(tc + 1) * 128, :],
              waits=[("MM", mm_done_tin[tc % 2])], inc="LS")
        ls_need = P.now("LS")
        for k in range(KC):
            bk_ = bank(k % 2)
            t_(lambda e, o=bk_[:, 0:128], i=tin[:, k * 128:(k + 1) * 128]:
               e.transpose(o, i, ident),
               waits=([("LS", ls_need), ("VE", max(ve_ident, P.now("VE")))] if k == 0 else
                      [("VE", P.now("VE"))]),
               inc="MM")
            v_(lambda e, o=xfm[:, k * T + tc * 128: k * T + (tc) * 128 + 128], i=bk_[:, 0:128]:
               e.tensor_copy(o, i),
               waits=[("MM", P.now("MM"))], inc="VE")
        mm_done_tin[tc % 2] = P.now("MM")
    onesb = cb[:, 386:387]
    v_(lambda e: e.memset(onesb, 1.0))
    for c in range(KC):
        v_(lambda e, c=c: e.memset(zfm[:, c * (T + 1):c * (T + 1) + 1], 0.0))
    p0.finish()

    p = Prog(nc)
    for s in ("LS2", "LW2", "MM2", "VE2", "SE2", "GP2"):
        p.sem(s)
    for a, b in (("LS", "LS2"), ("LW", "LW2"), ("MM", "MM2"), ("VE", "VE2"), ("SE", "SE2"), ("GP", "GP2")):
        p.sems[a] = p.sems[b]
        p.counts[a] = 0

    def T_(fn, waits=(), inc="MM"):
        p.emit("tensor", fn, waits, inc)

    def V_(fn, waits=(), inc="VE"):
        p.emit("vector", fn, waits, inc)

    def S_(fn, waits=(), inc="SE"):
        p.emit("scalar", fn, waits, inc)

    def MMW(out, lhsT, rhs, first, last, waits=(), inc="MM"):
        T_(lambda e, o=out, l=lhsT, r=rhs, f=first, s=last: e.matmul(o, l, r, start=f, stop=s),
           waits=waits, inc=inc)

    now = p.now

    TAB = [0, 2048, 4096, 6144]
    KRING = 8192
    VRING = 12288
    QWIN = 16448
    T1F = 9248
    SQB = 19520
    LNC0, LNC1 = 16448, 17472
    LNTMP = 9504
    ZB = 0
    OW = 4104
    PTR = 6152

    def h2ap(j, c0, n):
        if j < 19:
            return poolb[:, j * 1024 + c0: j * 1024 + c0 + n]
        return wst_t[:, 5120 + (j - 19) * 1024 + c0: 5120 + (j - 19) * 1024 + c0 + n]

    ms = {"pool_mm": 0, "pool_ve": 0, "pool_se": 0,
          "wstA_mm": 0, "wstB_mm": 0,
          "b45_ve": 0, "b45_se": 0, "b67_ve": 0,
          "zb_mm": 0}

    RW0 = rows[0:1, 0:512]
    RW1 = rows[0:1, 512:1024]

    def emit_outer_ln(dst, tc_order=(0, 1, 2, 3)):
        cast_ve = [ms["pool_mm"], ms["pool_mm"]]
        e0v, e0s = now("VE"), now("SE")
        pairs = {tc_order[i]: bp for i, bp in enumerate([(5, 4), (3, 2), (7, 6), (1, 0)])}
        rowsets = [(rows[0:1, 0:512], rows[0:1, 512:1024]),
                   (hbff[0:1, 3072:3584], hbff[0:1, 3584:4096])]
        pend = [None]
        for i, tc in enumerate(tc_order):
            t0 = tc * 512
            bS1, bS2 = pairs[tc]
            for k in range(KC):
                cbA = poolb[:, LNC0 + (k % 2) * 512: LNC0 + (k % 2) * 512 + 512]
                cbS = poolb[:, LNC1 + (k % 2) * 512: LNC1 + (k % 2) * 512 + 512]
                V_(lambda e, o=cbA, i=xc(k, t0, 512): e.tensor_copy(o, i),
                   waits=[("MM", cast_ve[k % 2])])
                V_(lambda e, o=cbS, i=cbA: e.tensor_tensor(o, i, i, op=OP.mult))
                ve_n = now("VE")
                MMW(ps[0:1, bS1 * 512:bS1 * 512 + 512], onesb, cbA, k == 0, k == KC - 1,
                    waits=[("VE", ve_n)] + ([("VE", e0v), ("SE", e0s)] if k == 0 else []))
                MMW(ps[0:1, bS2 * 512:bS2 * 512 + 512], onesb, cbS, k == 0, k == KC - 1)
                cast_ve[k % 2] = now("MM")
            mm_stats = now("MM")
            if pend[0] is not None:
                pend[0]()

            def mk_chain(tc=tc, t0=t0, bS1=bS1, bS2=bS2, mm_stats=mm_stats, i=i):
                RA, RB = rowsets[i % 2]
                V_(lambda e, RA=RA: e.tensor_single_scalar(RA, ps[0:1, bS1 * 512:bS1 * 512 + 512], 1.0 / D, op=OP.mult),
                   waits=[("MM", mm_stats)])
                V_(lambda e, RA=RA, RB=RB: e.tensor_tensor(RB, RA, RA, op=OP.mult))
                V_(lambda e, RB=RB: e.scalar_tensor_tensor(RB, ps[0:1, bS2 * 512:bS2 * 512 + 512], 1.0 / D, RB,
                                                           op0=OP.mult, op1=OP.subtract))
                ve_rows = now("VE")
                S_(lambda e, RB=RB: e.activation(RB, RB, AF.Ln, bias=epsc[0:1, 0:1], scale=1.0),
                   waits=[("VE", ve_rows)])
                S_(lambda e, RB=RB: e.activation(RB, RB, AF.Exp, bias=0.0, scale=-0.5))
                se_rows = now("SE")
                MMW(ps[:, bS1 * 512:bS1 * 512 + 512], onesf[0:1, :], RA, True, True,
                    waits=[("SE", se_rows)])
                MMW(ps[:, bS2 * 512:bS2 * 512 + 512], onesf[0:1, :], RB, True, True)
                mm_bc = now("MM")
                for k in range(KC):
                    if dst == "z":
                        V_(lambda e, o=zc(k, t0, 512), i=xc(k, t0, 512):
                           e.tensor_tensor(o, i, ps[:, bS1 * 512:bS1 * 512 + 512], op=OP.subtract),
                           waits=[("MM", mm_bc)] if k == 0 else ())
                        V_(lambda e, o=zc(k, t0, 512): e.tensor_tensor(o, o, ps[:, bS2 * 512:bS2 * 512 + 512], op=OP.mult))
                    else:
                        yap = zfmb[:, k * T + t0: k * T + t0 + 512]
                        V_(lambda e, i=xc(k, t0, 512):
                           e.tensor_tensor(pool[:, LNTMP:LNTMP + 512], i, ps[:, bS1 * 512:bS1 * 512 + 512], op=OP.subtract),
                           waits=[("MM", mm_bc)] if k == 0 else ())
                        V_(lambda e, o=yap: e.tensor_tensor(o, pool[:, LNTMP:LNTMP + 512], ps[:, bS2 * 512:bS2 * 512 + 512], op=OP.mult))
            pend[0] = mk_chain
        pend[0]()
        pend[0] = None
        ms["b67_ve"] = now("VE")
        ms["b45_ve"] = now("VE")
        ms["b45_se"] = now("SE")
        ms["pbank0_ve"] = now("VE")
        ms["pbank1_ve"] = now("VE")
        ms["b6_ve"] = now("VE")
        ms["b7_ve"] = now("VE")
        ms["b3_ve"] = now("VE")
        ms["b2_se"] = now("SE")

    RL2 = rows[0:2, 0:512]
    RL2B = hbf_t[0:2, 7176:7688]
    RREC = rows[0:1, 0:512]
    RRECB = hbf_t[0:1, 7176:7688]
    T2F = rows[:, 512:1024]
    wslotA = wst_t[:, 0:4096]
    wslotB = wst_t[:, 4096:8192]
    slot_guard = {"A": 0, "B": 0}
    ms["b2_se"] = 0; ms["b3_ve"] = 0; ms["b6_ve"] = 0; ms["b7_ve"] = 0; ms["z_ve"] = 0
    ms["w2p_ve"] = 0; ms["x_ve"] = 0

    def stream_phase(slot, src_ap):
        tgt = wslotA if slot == "A" else wslotB
        p.dma("gpsimd", tgt[:, 0:src_ap.shape[-1]], src_ap,
              waits=[("MM", slot_guard[slot])], inc="LW")
        return p.now("LW")

    def cast_zb(w):
        for k in range(KC):
            V_(lambda e, o=hbf_t[:, k * 513: k * 513 + 513],
               i=zfm[:, k * (T + 1) + w * 512: k * (T + 1) + w * 512 + 513]:
               e.tensor_copy(o, i),
               waits=[("MM", ms["zb_mm"])] if k == 0 else ())
        return p.now("VE")

    def zb_in(k, off, n):
        sh = 1 if k < 4 else 0
        return hbf_t[:, k * 513 + sh + off: k * 513 + sh + off + n]

    def l2norm_chain(l, bankP, out_ap, scol, cs_ap, ss_ap, mm0=None, sqb=None, se_sq=None):
        if sqb is None:
            if mm0 is None:
                mm0 = now("MM")
            sqb = poolb[:, SQB:SQB + 512]
            S_(lambda e, sqb=sqb: e.activation(sqb, bankP, AF.Square, bias=0.0, scale=1.0),
               waits=[("MM", mm0)])
            se_sq = now("SE")
        MMW(ps[0:2, 2 * 512:2 * 512 + 512], e2b, sqb, True, True,
            waits=[("SE", se_sq), ("SE", ms["b2_se"]), ("VE", ms["b3_ve"])])
        S_(lambda e: e.activation(RL2, ps[0:2, 2 * 512:2 * 512 + 512], AF.Ln, bias=epsc[0:2, 1:2], scale=1.0),
           waits=[("MM", now("MM"))])
        S_(lambda e: e.activation(RL2B, RL2, AF.Exp, bias=0.0, scale=-0.5))
        ms["b2_se"] = now("SE")
        MMW(ps[:, 3 * 512:4 * 512], e2tb[:, :], RL2B, True, True,
            waits=[("SE", now("SE")), ("VE", ms["b3_ve"])])
        mmb = now("MM")
        V_(lambda e, s=scol: e.tensor_scalar(pool[:, T1F:T1F + 512], bankP, s, None, op0=OP.mult),
           waits=[("MM", mmb)])
        V_(lambda e, o=out_ap: e.tensor_tensor(o, pool[:, T1F:T1F + 512], ps[:, 3 * 512:4 * 512],
                                               op=OP.mult))
        ms["b3_ve"] = now("VE")
        MMW(ps[:, 7 * 512:8 * 512], rotb, out_ap, True, True,
            waits=[("VE", now("VE")), ("VE", ms["b7_ve"])])
        mmr = now("MM")
        V_(lambda e, i=out_ap, c_=cs_ap: e.tensor_tensor(pool[:, T1F:T1F + 512], i, c_, op=OP.mult))
        V_(lambda e, s=ss_ap: e.tensor_tensor(T2F, ps[:, 7 * 512:8 * 512], s, op=OP.mult),
           waits=[("MM", mmr)])
        ms["b7_ve"] = now("VE")
        V_(lambda e, o=out_ap: e.tensor_tensor(o, pool[:, T1F:T1F + 512], T2F, op=OP.add))

    def emit_mlp_ln(l, th, cast_base_bf, cast_guard_mm):
        t0h = th * 1024
        e0v, e0s = now("VE"), now("SE")
        cast_ve = [cast_guard_mm, cast_guard_mm]
        pend = [None]
        for tc in range(2):
            bS1, bS2 = (5, 4) if tc == 0 else (7, 6)
            for k in range(KC):
                sh = 1 if k < 4 else 0
                zin = zfm[:, k * (T + 1) + sh + t0h + tc * 512: k * (T + 1) + sh + t0h + tc * 512 + 512]
                cbA = cast_base_bf((k % 2) * 512)
                cbS = cast_base_bf(1024 + (k % 2) * 512)
                V_(lambda e, o=cbA, i=zin: e.tensor_copy(o, i), waits=[("MM", cast_ve[k % 2])])
                V_(lambda e, o=cbS, i=cbA: e.tensor_tensor(o, i, i, op=OP.mult))
                ve_n = now("VE")
                MMW(ps[0:1, bS1 * 512:bS1 * 512 + 512], onesb, cbA, k == 0, k == KC - 1,
                    waits=[("VE", ve_n)] + ([("VE", e0v), ("SE", e0s)] if k == 0 else []))
                MMW(ps[0:1, bS2 * 512:bS2 * 512 + 512], onesb, cbS, k == 0, k == KC - 1)
                cast_ve[k % 2] = now("MM")
            mm_stats = now("MM")
            if pend[0] is not None:
                pend[0]()

            def mk_chain(tc=tc, bS1=bS1, bS2=bS2, mm_stats=mm_stats):
                rb_ = 0
                RA = rows[0:1, 0:512]
                RB = rows[0:1, 512:1024]
                V_(lambda e, RA=RA: e.tensor_single_scalar(RA, ps[0:1, bS1 * 512:bS1 * 512 + 512], 1.0 / D, op=OP.mult),
                   waits=[("MM", now("MM"))])
                V_(lambda e, RA=RA, RB=RB: e.tensor_tensor(RB, RA, RA, op=OP.mult))
                V_(lambda e, RB=RB: e.scalar_tensor_tensor(RB, ps[0:1, bS2 * 512:bS2 * 512 + 512], 1.0 / D, RB,
                                                           op0=OP.mult, op1=OP.subtract))
                ve_rows = now("VE")
                S_(lambda e, RB=RB: e.activation(RB, RB, AF.Ln, bias=epsc[0:1, 0:1], scale=1.0),
                   waits=[("VE", ve_rows)])
                S_(lambda e, RB=RB: e.activation(RB, RB, AF.Exp, bias=0.0, scale=-0.5))
                se_rows = now("SE")
                MMW(ps[:, bS1 * 512:bS1 * 512 + 512], onesf[rb_:rb_ + 1, :], RA, True, True,
                    waits=[("SE", se_rows)])
                MMW(ps[:, bS2 * 512:bS2 * 512 + 512], onesf[rb_:rb_ + 1, :], RB, True, True)
                mm_bc = now("MM")
                for k in range(KC):
                    sh = 1 if k < 4 else 0
                    zin = zfm[:, k * (T + 1) + sh + t0h + tc * 512: k * (T + 1) + sh + t0h + tc * 512 + 512]
                    V_(lambda e, i=zin: e.tensor_tensor(pool[:, LNTMP:LNTMP + 512], i,
                                                        ps[:, bS1 * 512:bS1 * 512 + 512], op=OP.subtract),
                       waits=[("MM", mm_bc)] if k == 0 else ())
                    V_(lambda e: e.tensor_tensor(pool[:, LNTMP:LNTMP + 512], pool[:, LNTMP:LNTMP + 512],
                                                 ps[:, bS2 * 512:bS2 * 512 + 512], op=OP.mult))
                    V_(lambda e, o=hbf_t[:, k * 1024 + tc * 512: k * 1024 + tc * 512 + 512],
                       g_=Vs("f_lng", l * KC + k), b_=Vs("f_lnb", l * KC + k):
                       e.tensor_scalar(o, pool[:, LNTMP:LNTMP + 512], g_, b_, op0=OP.mult, op1=OP.add))
                ms["mlpln_tc%d_ve" % tc] = now("VE")
            pend[0] = mk_chain
        pend[0]()
        ms["mlpln_ve"] = now("VE")
        ms["mlpln_mm"] = now("MM")

    def emit_ffn_layer(l, parts=("la", "mlp")):
        for i in range(4):
            p.dma("sync", poolb[:, TAB[i]:TAB[i] + 2048], d_tabs[i],
                  waits=[("MM", ms["pool_mm"]), ("VE", ms["pool_ve"])], inc="LS")
        tab_ls = now("LS")
        zb_ve = [0]
        BQ = (0, 1, 4, 5)
        SQ2 = (poolb[:, SQB:SQB + 512], hbf_t[:, PTR:PTR + 512])
        entry_ve, entry_se = now("VE"), now("SE")
        pbank_ve = {0: ms.get("pbank0_ve", 0), 1: ms.get("pbank1_ve", 0),
                    4: entry_ve, 5: entry_ve}
        ms["stb_se"] = entry_se

        def proj_q(w):
            ls_need = stream_phase("A", d_fqkv[l, 0])
            pend = []
            for c in range(4):
                bk = ps[:, BQ[c] * 512:BQ[c] * 512 + 512]
                for k in range(KC):
                    w_ = [("VE", pbank_ve[BQ[c]])] if k == 0 else []
                    if k == 0 and c >= 2:
                        w_ += [("SE", ms["stb_se"])]
                    if c == 0 and k == 0:
                        w_ += [("LW", ls_need), ("VE", zb_ve[0]), ("LS", tab_ls)]
                    MMW(bk, wslotA[:, k * 512 + c * 128: k * 512 + c * 128 + 128],
                        zb_in(k, 0, 512), k == 0, k == KC - 1, waits=w_)
                if len(pend) >= 2:
                    pend.pop(0)()
                sq = SQ2[c % 2]
                S_(lambda e, sq=sq, bk=bk: e.activation(sq, bk, AF.Square, bias=0.0, scale=1.0),
                   waits=[("MM", now("MM"))])
                se_sq = now("SE")

                def mk(c=c, bk=bk, sq=sq, se_sq=se_sq):
                    cs = poolb[:, TAB[0] + w * 512: TAB[0] + w * 512 + 512]
                    ss = poolb[:, TAB[1] + w * 512: TAB[1] + w * 512 + 512]
                    l2norm_chain(l, bk, poolb[:, QWIN + c * 512: QWIN + c * 512 + 512],
                                 pcol[:, PC_QS80 + l:PC_QS80 + l + 1], cs, ss,
                                 sqb=sq, se_sq=se_sq)
                    pbank_ve[BQ[c]] = now("VE")
                pend.append(mk)
            for f in pend:
                f()
            slot_guard["A"] = now("MM")

        def proj_k(wk):
            ls_need = stream_phase("B", d_fqkv[l, 1])
            kbase = KRING + (wk % 2) * 2048
            pend = []
            for c in range(4):
                bk = ps[:, BQ[c] * 512:BQ[c] * 512 + 512]
                for k in range(KC):
                    w_ = [("VE", pbank_ve[BQ[c]])] if k == 0 else []
                    if k == 0 and c >= 2:
                        w_ += [("SE", ms["stb_se"])]
                    if c == 0 and k == 0:
                        w_ += [("LW", ls_need), ("VE", zb_ve[0])]
                    MMW(bk, wslotB[:, k * 512 + c * 128: k * 512 + c * 128 + 128],
                        zb_in(k, 0, 512), k == 0, k == KC - 1, waits=w_)
                if len(pend) >= 2:
                    pend.pop(0)()
                sq = SQ2[c % 2]
                S_(lambda e, sq=sq, bk=bk: e.activation(sq, bk, AF.Square, bias=0.0, scale=1.0),
                   waits=[("MM", now("MM"))])
                se_sq = now("SE")

                def mk(c=c, bk=bk, sq=sq, se_sq=se_sq):
                    cs = poolb[:, TAB[2] + wk * 512: TAB[2] + wk * 512 + 512]
                    ss = poolb[:, TAB[3] + wk * 512: TAB[3] + wk * 512 + 512]
                    l2norm_chain(l, bk, poolb[:, kbase + c * 512: kbase + c * 512 + 512],
                                 Vs("f_ks", l), cs, ss, sqb=sq, se_sq=se_sq)
                    pbank_ve[BQ[c]] = now("VE")
                pend.append(mk)
            for f in pend:
                f()
            slot_guard["B"] = now("MM")

        def proj_v(wk):
            ls_need = stream_phase("A", d_fqkv[l, 2])
            vbase = VRING + (wk % 2) * 2080
            pend = []
            for tch in range(4):
                bk = ps[:, BQ[tch] * 512:BQ[tch] * 512 + 512]
                for k in range(KC):
                    w_ = [("VE", pbank_ve[BQ[tch]])] if k == 0 else []
                    if k == 0 and tch >= 2:
                        w_ += [("SE", ms["stb_se"])]
                    if tch == 0 and k == 0:
                        w_ += [("LW", ls_need), ("VE", zb_ve[0])]
                    MMW(bk, zb_in(k, tch * 128, 128), wslotA[:, k * 512: k * 512 + 512],
                        k == 0, k == KC - 1, waits=w_)
                mmv = now("MM")

                def mk(tch=tch, bk=bk, mmv=mmv):
                    dst = poolb[:, vbase + tch * 520: vbase + tch * 520 + 520]
                    V_(lambda e, o=dst.rearrange("p (h c) -> p h c", c=65),
                       i=bk.rearrange("p (h c) -> p h c", c=64):
                       e.tensor_copy(o[:, :, 0:64], i), waits=[("MM", mmv)])
                    V_(lambda e, o=dst.rearrange("p (h c) -> p h c", c=65): e.memset(o[:, :, 64:65], 1.0))
                    pbank_ve[BQ[tch]] = now("VE")
                if len(pend) >= 2:
                    pend.pop(0)()
                pend.append(mk)
            for f in pend:
                f()
            slot_guard["A"] = now("MM")

        def attention(w):
            ve_ready = now("VE")
            st_se = [ms["b2_se"], ms["b2_se"]]
            ptr_mm = [0, 0]
            st_i = [0]
            pend_evac = [None]

            def emit_evac(h, pvb, mm_pv):
                S_(lambda e, i=ps[64:65, pvb * 512: pvb * 512 + 512]:
                   e.activation(RREC, i, AF.Ln, bias=0.0, scale=1.0), waits=[("MM", mm_pv)])
                S_(lambda e: e.activation(RRECB, RREC, AF.Exp, bias=0.0, scale=-1.0))
                MMW(ps[0:64, 3 * 512:3 * 512 + 512], cb[0:1, 520:584], RRECB, True, True,
                    waits=[("SE", now("SE")), ("VE", ms["b3_ve"])])
                mmb = now("MM")
                r0 = 64 * (h % 2)
                oap = hbf_t[r0:r0 + 64, OW + (h // 2) * 512: OW + (h // 2) * 512 + 512]
                V_(lambda e, o=oap, i=ps[0:64, pvb * 512: pvb * 512 + 512]:
                   e.tensor_copy(o, i), waits=[("MM", mmb)])
                V_(lambda e, o=oap:
                   e.tensor_tensor(o, o, ps[0:64, 3 * 512:3 * 512 + 512], op=OP.mult))
                ms["b3_ve"] = now("VE")
                if pvb == 6:
                    ms["b6_ve"] = now("VE")
                else:
                    ms["b7_ve"] = now("VE")

            for h in range(HF):
                pvb = 6 + (h % 2)
                pv_guard = ms["b6_ve"] if pvb == 6 else ms["b7_ve"]
                pv_started = [False]
                cx_list = list(range(4, 8) if w == 0 else range(8))
                pend_pv = [None]
                for cx in cx_list:
                    par = st_i[0] % 2
                    st_i[0] += 1
                    lo = max(0, 128 * (cx - 4)); hi = min(512, 128 * (cx + 1)); width = hi - lo
                    win_of_cx = (w - 1) if cx < 4 else w
                    kbase = KRING + (win_of_cx % 2) * 2048
                    vbase = VRING + (win_of_cx % 2) * 2080
                    r0 = 64 * (h % 2)
                    stb = ps[:, (4 + par) * 512:(4 + par) * 512 + 512]
                    ktile = poolb[r0:r0 + 64, kbase + (h // 2) * 512 + (cx % 4) * 128:
                                  kbase + (h // 2) * 512 + (cx % 4) * 128 + 128]
                    qtile = poolb[r0:r0 + 64, QWIN + (h // 2) * 512 + lo: QWIN + (h // 2) * 512 + hi]
                    MMW(stb[:, 0:width], ktile, qtile, True, False,
                        waits=[("SE", st_se[par]), ("VE", ve_ready)])
                    if cx >= 4:
                        MMW(stb[:, 0:128], mgt, identb, False, True)
                    else:
                        MMW(stb[:, width - 128:width], mlt, identb, False, True)
                    mm_st = now("MM")
                    if pend_pv[0] is not None:
                        pend_pv[0]()
                        pend_pv[0] = None
                    if cx == cx_list[0] and pend_evac[0] is not None:
                        pend_evac[0]()
                        pend_evac[0] = None
                    ptr = hbf_t[:, PTR + par * 512: PTR + par * 512 + width]
                    S_(lambda e, o=ptr, i=stb[:, 0:width], b=pcol[:, PC_BF0 + l:PC_BF0 + l + 1]:
                       e.activation(o, i, AF.Exp, bias=b, scale=1.0),
                       waits=[("MM", mm_st), ("MM", ptr_mm[par])])
                    st_se[par] = now("SE")
                    se_exp = now("SE")

                    def mk_pv(cx=cx, lo=lo, se_exp=se_exp, vbase=vbase, ptr=ptr, par=par, pvb=pvb):
                        for d in range(max(0, cx - 4), min(3, cx) + 1):
                            first = not pv_started[0]; pv_started[0] = True
                            last = (cx == 7 and d == 3)
                            vtile = poolb[:, vbase + (cx % 4) * 520 + 65 * h:
                                          vbase + (cx % 4) * 520 + 65 * h + 65]
                            w_ = [("SE", se_exp)]
                            if first:
                                w_.append(("VE", pv_guard))
                            MMW(ps[0:65, pvb * 512 + 128 * d: pvb * 512 + 128 * d + 128],
                                vtile, ptr[:, 128 * d - lo:128 * d - lo + 128], first, last, waits=w_)
                        ptr_mm[par] = now("MM")
                    pend_pv[0] = mk_pv
                pend_pv[0]()
                pend_pv[0] = None
                mm_pv = now("MM")
                if pend_evac[0] is not None:
                    pend_evac[0]()
                pend_evac[0] = (lambda h=h, pvb=pvb, mm_pv=mm_pv: emit_evac(h, pvb, mm_pv))
            if pend_evac[0] is not None:
                pend_evac[0]()
                pend_evac[0] = None
            ms["stb_se"] = now("SE")

        def wo_stage(w):
            ls_need = stream_phase("B", d_fwo[l])
            ve_o = now("VE")
            for m in range(KC):
                bk = ps[:, (m % 2) * 512:(m % 2) * 512 + 512]
                for kc in range(4):
                    w_ = [("VE", pbank_ve[m % 2])] if kc == 0 else []
                    if m == 0 and kc == 0:
                        w_ += [("LW", ls_need), ("VE", ve_o)]
                    MMW(bk, wslotB[:, kc * 1024 + m * 128: kc * 1024 + m * 128 + 128],
                        hbf_t[:, OW + kc * 512: OW + kc * 512 + 512], kc == 0, kc == 3, waits=w_)
                mmw = now("MM")
                V_(lambda e, o=zc(m, w * 512, 512), b=bk: e.tensor_tensor(o, o, b, op=OP.add),
                   waits=[("MM", mmw)])
                pbank_ve[m % 2] = now("VE")
            slot_guard["B"] = now("MM")
            ms["z_ve"] = now("VE")

        for wi, w in enumerate([3, 2, 1, 0] if "la" in parts else []):
            if wi == 0:
                zb_ve[0] = cast_zb(w)
                proj_q(w)
                proj_k(w)
                proj_v(w)
                ms["zb_mm"] = now("MM")
                zb_ve[0] = cast_zb(w - 1)
                proj_k(w - 1)
                proj_v(w - 1)
                ms["zb_mm"] = now("MM")
            else:
                proj_q(w)
                ms["zb_mm"] = now("MM")
                if w > 0:
                    zb_ve[0] = cast_zb(w - 1)
                    proj_k(w - 1)
                    proj_v(w - 1)
                    ms["zb_mm"] = now("MM")
            attention(w)
            wo_stage(w)
        ms["pbank0_ve"] = pbank_ve[0]
        ms["pbank1_ve"] = pbank_ve[1]
        ms["pool_mm"] = now("MM")
        ms["pool_ve"] = now("VE")

        w1g = [slot_guard["A"], slot_guard["B"]]
        w2g = [0, 0]
        n_w1 = [0]
        n_w2 = [0]

        def load_w1(j):
            sl = n_w1[0] % 2
            p.dma("gpsimd", wst_t[:, sl * 2048:(sl + 1) * 2048], d_fw1[l, j],
                  waits=[("MM", w1g[sl])], inc="LW")
            n_w1[0] += 1
            return p.now("LW")

        def load_w2(mg, j):
            sl = n_w2[0] % 2
            p.dma("gpsimd", wst_t[:, 4096 + sl * 512:4096 + (sl + 1) * 512], d_fw2[l, mg, j],
                  waits=[("MM", w2g[sl])], inc="LW")
            n_w2[0] += 1
            return p.now("LW")

        ag_ve = [now("VE"), now("VE")]
        for th in ((1, 0) if "mlp" in parts else ()):
            t0h = th * 1024
            emit_mlp_ln(l, th, (lambda c0: poolb[:, c0:c0 + 512]), ms["pool_mm"])
            ln_ve = now("VE")
            ln_tc0_ve = ms["mlpln_tc0_ve"]
            load_w1(0)
            slot_of = {0: (n_w1[0] - 1) % 2}
            lw_of = {0: p.now("LW")}
            load_w1(1)
            slot_of[1] = (n_w1[0] - 1) % 2
            lw_of[1] = p.now("LW")
            for j in range(NM):
                sl = slot_of[j]
                w1v = wst_t[:, sl * 2048:(sl + 1) * 2048]
                for tc in range(2):
                    ssb = (j * 2 + tc) % 2
                    agA = ps[:, (ssb * 2) * 512:(ssb * 2) * 512 + 512]
                    agG = ps[:, (ssb * 2 + 1) * 512:(ssb * 2 + 1) * 512 + 512]
                    for k in range(KC):
                        rhs = hbf_t[:, k * 1024 + tc * 512: k * 1024 + tc * 512 + 512]
                        first = (k == 0); lastk = (k == KC - 1)
                        wts = []
                        if k == 0:
                            ln_need = ln_tc0_ve if (j == 0 and tc == 0) else ln_ve
                            wts = [("LW", lw_of[j]), ("VE", max(ag_ve[ssb], ln_need))]
                        MMW(agA, w1v[:, k * 256: k * 256 + 128], rhs, first, lastk, waits=wts)
                        MMW(agG, w1v[:, k * 256 + 128: k * 256 + 256], rhs, first, lastk)
                    w1g[sl] = now("MM")
                    mm_ag = now("MM")
                    glA = rows[:, 0:512] if ssb == 0 else rows[:, 512:1024]
                    S_(lambda e, o=glA, i=agG, b=Vs("f_b1g", l * NM + j):
                       e.activation(o, i, AF.Gelu, bias=b, scale=1.0), waits=[("MM", mm_ag)])
                    V_(lambda e, o=h2ap(j, tc * 512, 512), a=agA, s=Vs("f_b1a", l * NM + j), g_=glA:
                       e.scalar_tensor_tensor(o, a, s, g_, op0=OP.add, op1=OP.mult),
                       waits=[("SE", now("SE"))])
                    ag_ve[ssb] = now("VE")
                if j + 2 < NM:
                    slot_of[j + 2] = n_w1[0] % 2
                    lw_of[j + 2] = load_w1(j + 2)
                elif th == 1 and j + 2 >= NM and (j + 2 - NM) < 2:
                    pass
            h2_ve = now("VE")
            w2g[0] = now("MM"); w2g[1] = now("MM")
            load_w2(0, 0)
            w2slot_of = {(0, 0): (n_w2[0] - 1) % 2}
            w2lw_of = {(0, 0): p.now("LW")}
            load_w2(0, 1)
            w2slot_of[(0, 1)] = (n_w2[0] - 1) % 2
            w2lw_of[(0, 1)] = p.now("LW")
            for mg in range(2):
                for j in range(NM):
                    sl = w2slot_of[(mg, j)]
                    w2v = wst_t[:, 4096 + sl * 512:4096 + (sl + 1) * 512]
                    first = (j == 0); lastj = (j == NM - 1)
                    for m in range(4):
                        for q in range(2):
                            wts = []
                            if m == 0 and q == 0:
                                wts = [("LW", w2lw_of[(mg, j)])]
                                if j == 0:
                                    wts += [("VE", h2_ve), ("VE", ms["w2p_ve"])]
                            MMW(ps[:, m * 1024 + q * 512: m * 1024 + q * 512 + 512],
                                w2v[:, m * 128: m * 128 + 128],
                                h2ap(j, q * 512, 512), first, lastj, waits=wts)
                    w2g[sl] = now("MM")
                    nxt = j + 2
                    if nxt < NM:
                        w2slot_of[(mg, nxt)] = n_w2[0] % 2
                        w2lw_of[(mg, nxt)] = load_w2(mg, nxt)
                    elif mg == 0 and (nxt - NM) < 2:
                        w2slot_of[(1, nxt - NM)] = n_w2[0] % 2
                        w2lw_of[(1, nxt - NM)] = load_w2(1, nxt - NM)
                mm_w2 = now("MM")
                for m in range(4):
                    mo = mg * 4 + m
                    for q in range(2):
                        V_(lambda e, o=zc(mo, t0h + q * 512, 512),
                           i=ps[:, m * 1024 + q * 512: m * 1024 + q * 512 + 512],
                           s=Vs("f_b2", l * KC + mo):
                           e.scalar_tensor_tensor(o, i, s, o, op0=OP.add, op1=OP.add),
                           waits=[("MM", mm_w2)] if (m == 0 and q == 0) else ())
                ms["w2p_ve"] = now("VE")
            ms["z_ve"] = now("VE")
            ms["pool_mm"] = now("MM")
            ag_ve = [now("VE"), now("VE")]
        ms["pbank0_ve"] = now("VE")
        ms["pbank1_ve"] = now("VE")
        slot_guard["A"] = now("MM")
        slot_guard["B"] = now("MM")
        ms["pool_ve"] = now("VE")

    def emit_ffn_module():
        mmg0 = now("MM")
        for c in range(KC):
            V_(lambda e, c=c: e.memset(zfm[:, c * (T + 1):c * (T + 1) + 1], 0.0),
               waits=[("MM", mmg0)] if c == 0 else ())
        emit_outer_ln("z", (3, 2, 1, 0))
        for l in range(DEPTH):
            emit_ffn_layer(l)
        mm_g = now("MM")
        for k in range(KC):
            V_(lambda e, o=xc(k, 0, T), i=zc(k, 0, T):
               e.scalar_tensor_tensor(o, i, 0.5, o, op0=OP.mult, op1=OP.add))
        ms["x_ve"] = now("VE")

    def emit_attn_module():
        emit_outer_ln("ybf")
        p.dma("sync", pool[:, 0:1024], d_bvb, waits=[("MM", ms["pool_mm"]), ("VE", ms["pool_ve"])], inc="LS")
        bvb_ls = now("LS")
        sig = poolb[0:16, 2048:4096]
        QRB = [6144, 8192]
        PTB = [10240, 11264]

        def ybf_t(k, t0, n):
            return zfmb[:, k * T + t0: k * T + t0 + n]

        def vg_ap(tch, lg):
            if tch < 15:
                return zfmb[:, 24576 + tch * 520 + 65 * lg: 24576 + tch * 520 + 65 * lg + 65]
            return poolb[:, 14336 + 65 * lg: 14336 + 65 * lg + 65]

        def vg_full(tch):
            if tch < 15:
                return zfmb[:, 24576 + tch * 520: 24576 + tch * 520 + 520]
            return poolb[:, 14336: 14336 + 520]

        def o_ap(c, r0, t0, n):
            if c < 3:
                return hbf_t[r0:r0 + 64, c * 2048 + t0: c * 2048 + t0 + n]
            return poolb[r0:r0 + 64, 12288 + t0: 12288 + t0 + n]

        R0s = [pool[0:1, 7428:8452], pool[32:33, 7428:8452]]
        R2s = [pool[0:1, 8452:9476], pool[32:33, 8452:9476]]
        upar = [0]
        def o_full(c, qc):
            if c < 3:
                return hbf_t[:, c * 2048 + qc * 512: c * 2048 + qc * 512 + 512]
            return poolb[:, 12288 + qc * 512: 12288 + qc * 512 + 512]

        qr_mm = [0, 0]
        pt_mm = [0, 0]
        st_se_a = [0, 0]
        pend_gchain = [None]
        BQ2 = (0, 1, 4, 5)
        abank = {0: ms.get("pbank0_ve", 0), 1: ms.get("pbank1_ve", 0),
                 4: now("VE"), 5: now("VE")}
        ms["b45_se"] = now("SE")
        ms["b67_att_ve"] = now("VE")

        for hg in attn_groups:
            for k in range(KC):
                p.dma("gpsimd", wslotA[:, k * 512: k * 512 + 512],
                      d_wv[k * 128:(k + 1) * 128, hg * 512:(hg + 1) * 512],
                      waits=[("MM", slot_guard["A"])] if k == 0 else (), inc="LW")
            lw_v = now("LW")
            pendv = []
            for tch in range(16):
                bq = BQ2[tch % 4]
                bk = ps[:, bq * 512:bq * 512 + 512]
                for k in range(KC):
                    w_ = [("VE", abank[bq])] if k == 0 else []
                    if k == 0 and bq >= 4:
                        w_ += [("SE", ms["b45_se"])]
                    if tch == 0 and k == 0:
                        w_ += [("LW", lw_v), ("VE", now("VE")), ("LS", bvb_ls)]
                    MMW(bk, ybf_t(k, tch * 128, 128), wslotA[:, k * 512: k * 512 + 512],
                        k == 0, k == KC - 1, waits=w_)
                mmv = now("MM")

                def mkv(tch=tch, bk=bk, mmv=mmv, bq=bq):
                    dst = vg_full(tch)
                    V_(lambda e, o=dst.rearrange("p (h c) -> p h c", c=65),
                       i=bk.rearrange("p (h c) -> p h c", c=64),
                       bv=pool[:, hg * 512:hg * 512 + 512].rearrange("p (h c) -> p h c", c=64):
                       e.tensor_tensor(o[:, :, 0:64], i, bv, op=OP.add), waits=[("MM", mmv)])
                    V_(lambda e, o=dst.rearrange("p (h c) -> p h c", c=65): e.memset(o[:, :, 64:65], 1.0))
                    abank[bq] = now("VE")
                if len(pendv) >= 2:
                    pendv.pop(0)()
                pendv.append(mkv)
            for f in pendv:
                f()
            slot_guard["A"] = now("MM")
            for k in range(KC):
                p.dma("gpsimd", wslotB[:, k * 512: k * 512 + 512],
                      d_wk[k * 128:(k + 1) * 128, hg * 512:(hg + 1) * 512],
                      waits=[("MM", slot_guard["B"])] if k == 0 else (), inc="LW")
            lw_k = now("LW")

            def norm_chain(bk, bias_col, wcol, out_ap, mm0=None):
                ktmp = pool[:, 7428:7940]
                sqb = poolb[:, 15880:16392]
                if mm0 is None:
                    mm0 = now("MM")
                V_(lambda e, i=bk, b=bias_col: e.tensor_scalar(ktmp, i, b, None, op0=OP.add),
                   waits=[("MM", mm0)])
                V_(lambda e: e.tensor_tensor(sqb, ktmp, ktmp, op=OP.mult))
                MMW(ps[0:2, 2 * 512:2 * 512 + 512], e2b, sqb, True, True,
                    waits=[("VE", now("VE")), ("SE", now("SE"))])
                S_(lambda e: e.activation(RL2, ps[0:2, 2 * 512:2 * 512 + 512], AF.Ln,
                                          bias=epsc[0:2, 2:3], scale=1.0 / HD), waits=[("MM", now("MM"))])
                S_(lambda e: e.activation(RL2B, RL2, AF.Exp, bias=0.0, scale=-0.5))
                ms["b2_se"] = now("SE")
                MMW(ps[:, 3 * 512:4 * 512], e2tb[:, :], RL2B, True, True,
                    waits=[("SE", now("SE")), ("VE", ms["b3_ve"]), ("SE", now("SE"))])
                mmb = now("MM")
                V_(lambda e, o=out_ap, w=wcol: e.scalar_tensor_tensor(o, ktmp, w, ps[:, 3 * 512:4 * 512],
                                                                      op0=OP.mult, op1=OP.mult),
                   waits=[("MM", mmb)])
                ms["b3_ve"] = now("VE")

            pendn = []
            for c in range(4):
                for qc in range(4):
                    idx = c * 4 + qc
                    bq = BQ2[idx % 4]
                    bk = ps[:, bq * 512:bq * 512 + 512]
                    for k in range(KC):
                        w_ = [("VE", abank[bq])] if k == 0 else []
                        if k == 0 and bq >= 4:
                            w_ += [("SE", ms["b45_se"])]
                        if c == 0 and qc == 0 and k == 0:
                            w_ += [("LW", lw_k)]
                        MMW(bk, wslotB[:, k * 512 + c * 128: k * 512 + c * 128 + 128],
                            ybf_t(k, qc * 512, 512), k == 0, k == KC - 1, waits=w_)
                    mm_c = now("MM")

                    def mkn(c=c, qc=qc, bk=bk, bq=bq, mm_c=mm_c):
                        norm_chain(bk, Vs("a_bk", hg * 4 + c), V("a_kn"),
                                   zfmb[:, 16384 + c * 2048 + qc * 512: 16384 + c * 2048 + qc * 512 + 512],
                                   mm0=mm_c)
                        abank[bq] = now("VE")
                    if len(pendn) >= 2:
                        pendn.pop(0)()
                    pendn.append(mkn)
            for f in pendn:
                f()
            slot_guard["B"] = now("MM")
            for k in range(KC):
                p.dma("gpsimd", wslotA[:, k * 512: k * 512 + 512],
                      d_wq[k * 128:(k + 1) * 128, hg * 512:(hg + 1) * 512],
                      waits=[("MM", slot_guard["A"])] if k == 0 else (), inc="LW")
            lw_q = now("LW")
            if hg == 0:
                for k in range(KC):
                    p.dma("gpsimd", wslotB[:, k * 128: k * 128 + 128],
                          d_wq[k * 128:(k + 1) * 128, 1024:1152],
                          waits=[("MM", slot_guard["B"])] if k == 0 else (), inc="LW")
                lw_g = now("LW")
                for qc in range(4):
                    bk = ps[:, (qc % 2) * 512:(qc % 2) * 512 + 512]
                    for k in range(KC):
                        w_ = [("VE", abank[qc % 2]), ("SE", now("SE"))] if k == 0 else []
                        if qc == 0 and k == 0:
                            w_ += [("LW", lw_g)]
                        MMW(bk, wslotB[:, k * 128: k * 128 + 128], ybf_t(k, qc * 512, 512),
                            k == 0, k == KC - 1, waits=w_)
                    mmg = now("MM")
                    S_(lambda e, o=sig[:, qc * 512: qc * 512 + 512], i=bk[0:16, :]:
                       e.activation(o, i, AF.Sigmoid, bias=V("a_bgate")[0:16, :], scale=1.0),
                       waits=[("MM", mmg)])
                    abank[qc % 2] = now("MM") * 0 + abank[qc % 2]
                gate_se = now("SE")
                slot_guard["B"] = now("MM")
            sig_se = now("SE")
            for c in range(4):
                qsl = c % 2
                qbase = QRB[qsl]
                pendq = [None]
                for qc in range(4):
                    bq = BQ2[qc]
                    bk = ps[:, bq * 512:bq * 512 + 512]
                    for k in range(KC):
                        w_ = [("VE", abank[bq]), ("SE", now("SE"))] if k == 0 else []
                        if qc == 0 and k == 0:
                            w_ += [("LW", lw_q), ("MM", qr_mm[qsl])]
                        MMW(bk, wslotA[:, k * 512 + c * 128: k * 512 + c * 128 + 128],
                            ybf_t(k, qc * 512, 512), k == 0, k == KC - 1, waits=w_)
                    mm_c = now("MM")
                    if pendq[0] is not None:
                        pendq[0]()

                    def mkq(qc=qc, bk=bk, c=c, bq=bq, mm_c=mm_c):
                        norm_chain(bk, Vs("a_bq", hg * 4 + c), pcol[:, PC_QN8:PC_QN8 + 1],
                                   poolb[:, qbase + qc * 512: qbase + qc * 512 + 512],
                                   mm0=mm_c)
                        abank[bq] = now("VE")
                    pendq[0] = mkq
                pendq[0]()
                ve_pair = now("VE")
                for lh in range(2):
                    h = 2 * c + lh
                    hh = hg * 8 + h
                    r0 = 64 * lh
                    for qh in range(2):
                        pend_pv = [None]
                        pv_started = [False]
                        for kc in range(16):
                            stpar = kc % 2
                            stbase = (2 + 2 * stpar) * 512
                            for qq in range(2):
                                MMW(ps[:, stbase + qq * 512: stbase + qq * 512 + 512],
                                    zfmb[r0:r0 + 64, 16384 + c * 2048 + kc * 128: 16384 + c * 2048 + kc * 128 + 128],
                                    poolb[r0:r0 + 64, qbase + qh * 1024 + qq * 512: qbase + qh * 1024 + qq * 512 + 512],
                                    True, True,
                                    waits=([("VE", ve_pair), ("SE", st_se_a[stpar]), ("SE", ms["b2_se"])]
                                           if qq == 0 else ()))
                            mm_st = now("MM")
                            if pend_pv[0] is not None:
                                pend_pv[0]()
                                pend_pv[0] = None
                            ptr = poolb[:, PTB[stpar]:PTB[stpar] + 1024]
                            se_exps = []
                            for qq in range(2):
                                S_(lambda e, o=ptr[:, qq * 512: qq * 512 + 512],
                                   i=ps[:, stbase + qq * 512: stbase + qq * 512 + 512]:
                                   e.activation(o, i, AF.Exp, bias=pcol[:, PC_BA:PC_BA + 1], scale=1.0),
                                   waits=[("MM", mm_st), ("MM", pt_mm[stpar])] if qq == 0 else ())
                                se_exps.append(now("SE"))
                            st_se_a[stpar] = now("SE")

                            def mk_pv(kc=kc, h=h, ptr=ptr, stpar=stpar, se_exps=tuple(se_exps)):
                                for qq in range(2):
                                    first = (kc == 0)
                                    w_ = [("SE", se_exps[qq])]
                                    if first and qq == 0:
                                        w_.append(("VE", ms["b67_att_ve"]))
                                    MMW(ps[0:65, 6 * 512 + qq * 512: 6 * 512 + qq * 512 + 512],
                                        vg_ap(kc, h), ptr[:, qq * 512: qq * 512 + 512],
                                        first, kc == 15, waits=w_)
                                pt_mm[stpar] = now("MM")
                            pend_pv[0] = mk_pv
                        pend_pv[0]()
                        pend_pv[0] = None
                        mm_pv = now("MM")
                        up = upar[0]; upar[0] ^= 1
                        R0 = R0s[up]; R2 = R2s[up]
                        V_(lambda e, o=o_ap(c, r0, qh * 1024, 1024), i=ps[0:64, 6 * 512: 6 * 512 + 1024]:
                           e.tensor_copy(o, i), waits=[("MM", mm_pv)])
                        V_(lambda e, R2=R2: e.tensor_copy(R2, ps[64:65, 6 * 512: 6 * 512 + 1024]))
                        ms["b67_att_ve"] = now("VE")
                        ve_sum = now("VE")
                        if pend_gchain[0] is not None:
                            pend_gchain[0]()

                        def mk_gchain(c=c, r0=r0, qh=qh, hh=hh, ve_sum=ve_sum, R0=R0, R2=R2, up=up):
                            rbase = 32 * up
                            for qq in range(2):
                                MMW(ps[0:1, qq * 512: qq * 512 + 512],
                                    identg[:, hh:hh + 1], sig[:, qh * 1024 + qq * 512: qh * 1024 + qq * 512 + 512],
                                    True, True,
                                    waits=[("VE", ve_sum), ("SE", sig_se), ("VE", abank[qq])] if qq == 0
                                          else [("VE", abank[qq])])
                            mm_gx = now("MM")
                            V_(lambda e: e.tensor_copy(R0, ps[0:1, 0:1024]),
                               waits=[("MM", mm_gx)])
                            S_(lambda e: e.activation(R2, R2, AF.Ln, bias=0.0, scale=1.0),
                               waits=[("VE", now("VE"))])
                            S_(lambda e: e.activation(R2, R2, AF.Exp, bias=0.0, scale=-1.0))
                            R0B = poolb[rbase:rbase + 1, 2 * 9504: 2 * 9504 + 1024]
                            V_(lambda e, R0B=R0B: e.tensor_tensor(R0B, R0, R2, op=OP.mult),
                               waits=[("SE", now("SE"))])
                            ve_row = now("VE")
                            for qq in range(2):
                                MMW(ps[0:64, qq * 512: qq * 512 + 512],
                                    cb[rbase:rbase + 1, 520:584],
                                    R0B[:, qq * 512: qq * 512 + 512],
                                    True, True, waits=[("VE", ve_row)] if qq == 0 else ())
                            mm_g2 = now("MM")
                            V_(lambda e, o=o_ap(c, r0, qh * 1024, 1024):
                               e.tensor_tensor(o, o, ps[0:64, 0:1024], op=OP.mult),
                               waits=[("MM", mm_g2)])
                            abank[0] = now("VE")
                            abank[1] = now("VE")
                        pend_gchain[0] = mk_gchain
                if pend_gchain[0] is not None:
                    pend_gchain[0]()
                    pend_gchain[0] = None
                qr_mm[qsl] = now("MM")
                ms["b45_se"] = now("SE")
            slot_guard["A"] = now("MM")
            for kc in range(4):
                p.dma("gpsimd", wslotB[:, kc * 1024: kc * 1024 + 1024],
                      d_wo[hg * 512 + kc * 128: hg * 512 + kc * 128 + 128, :],
                      waits=[("MM", slot_guard["B"])] if kc == 0 else (), inc="LW")
            lw_o = now("LW")
            ve_o = now("VE")
            for m in range(KC):
                for qc in range(4):
                    bk = ps[:, (qc % 2) * 512:(qc % 2) * 512 + 512]
                    for kc in range(4):
                        w_ = [("VE", abank[qc % 2])] if kc == 0 else []
                        if m == 0 and qc == 0 and kc == 0:
                            w_ += [("LW", lw_o), ("VE", ve_o)]
                        MMW(bk, wslotB[:, kc * 1024 + m * 128: kc * 1024 + m * 128 + 128],
                            o_full(kc, qc),
                            kc == 0, kc == 3, waits=w_)
                    mmw = now("MM")
                    if hg == 0:
                        V_(lambda e, o=xc(m, qc * 512, 512), b=bk, s=Vs("a_bo", m):
                           e.scalar_tensor_tensor(o, b, s, o, op0=OP.add, op1=OP.add),
                           waits=[("MM", mmw)])
                    else:
                        V_(lambda e, o=xc(m, qc * 512, 512), b=bk: e.tensor_tensor(o, o, b, op=OP.add),
                           waits=[("MM", mmw)])
                    abank[qc % 2] = now("VE")
            slot_guard["B"] = now("MM")
        ms["pbank0_ve"] = now("VE")
        ms["pbank1_ve"] = now("VE")
        ms["pool_mm"] = now("MM")
        ms["pool_ve"] = now("VE")
        ms["x_ve"] = now("VE")

    dbg = [s for s in stages if s.startswith("dbg_")]
    if dbg and dbg[0] == "dbg_attn":
        emit_attn_module()
        d_dbg = nc.dram_tensor("dbg", [128, 32768], BF16, kind="ExternalOutput").ap()
        d_dbg2 = nc.dram_tensor("dbg2", [128, 3072], F32, kind="ExternalOutput").ap()
        wts = [("VE", now("VE")), ("MM", now("MM")), ("SE", now("SE"))]
        p.dma("sync", d_dbg[:, 0:16384], zfmb[:, 0:16384], waits=wts, inc="LS")
        p.dma("sync", d_dbg[:, 16384:24576], zfmb[:, 16384:24576], inc="LS")
        p.dma("sync", d_dbg[:, 24576:26624], poolb[:, 6144:8192], inc="LS")
        p.dma("sync", d_dbg[:, 26624:28672], poolb[:, 8192:10240], inc="LS")
        p.dma("sync", d_dbg[:, 28672:30720], hbf_t[:, 0:2048], inc="LS")
        p.dma("sync", d_dbg2[:, 0:2048], pool[:, 1024:3072], inc="LS")
        p.dma("sync", d_dbg2[:, 2048:3072], pool[:, 0:1024], inc="LS")
        dbg = []
        dbg_tap = None
    elif dbg:
        sub = dbg[0][4:]
        emit_outer_ln("z")
        if sub in ("l0a", "l0"):
            emit_ffn_layer(0, parts=("la",) if sub == "l0a" else ("la", "mlp"))
        dbg_tap = "z"
        d_dbg = nc.dram_tensor("dbg", [128, 32768], BF16, kind="ExternalOutput").ap()
        d_dbg2 = nc.dram_tensor("dbg2", [128, 64], F32, kind="ExternalOutput").ap()
        p.dma("sync", d_dbg[:, 0:8192], poolb[:, 0:8192], waits=[("VE", now("VE")), ("MM", now("MM")), ("SE", now("SE"))], inc="LS")
        p.dma("sync", d_dbg[:, 8192:10240], poolb[:, QWIN:QWIN + 2048], inc="LS")
        p.dma("sync", d_dbg[:, 10240:14336], poolb[:, KRING:KRING + 4096], inc="LS")
        p.dma("sync", d_dbg[:, 14336:18496], poolb[:, VRING:VRING + 4160], inc="LS")
        p.dma("sync", d_dbg[:, 18496:22600], hbf_t[:, 0:4104], inc="LS")
        p.dma("sync", d_dbg[:, 22600:24648], hbf_t[:, OW:OW + 2048], inc="LS")
        p.dma("sync", d_dbg[:, 24648:25672], hbf_t[:, PTR:PTR + 1024], inc="LS")
        p.dma("sync", d_dbg2[:, 0:8], pcol[:], inc="LS")
    else:
        if "ffn0" in stages:
            emit_ffn_module()
        if "attn" in stages:
            emit_attn_module()
        if "ffn1" in stages:
            emit_ffn_module()
    io_e = pool[:, 2048:2176]
    ident_e = pool[:, 2176:2304]
    p.emit("gpsimd", lambda e: e.iota(io_e, [[1, 128]], channel_multiplier=-1,
                                      allow_small_or_imprecise_dtypes=True),
           waits=[("MM", ms["pool_mm"]), ("VE", ms["pool_ve"]), ("LS", now("LS"))], inc="GP")
    V_(lambda e: e.tensor_scalar(ident_e, io_e, 0.0, 1.0, op0=OP.is_equal, op1=OP.mult),
       waits=[("GP", p.now("GP"))])
    ve_id = now("VE")
    tout = [pool[:, 0:1024], pool[:, 1024:2048]]
    st_ls = [0, 0]

    def src_ap(k, tc):
        if dbg_tap == "z":
            return zc(k, tc * 128, 128)
        return xfm[:, k * T + tc * 128: k * T + tc * 128 + 128]

    for tc in range(16):
        to = tout[tc % 2]
        for k in range(KC):
            bkp = ps[:, (k % 2) * 512:(k % 2) * 512 + 512]
            T_(lambda e, o=bkp[:, 0:128], i=src_ap(k, tc), idn=ident_e: e.transpose(o, i, idn),
               waits=([("VE", ve_id), ("LS", st_ls[tc % 2])] if k == 0 else [("VE", now("VE"))]))
            V_(lambda e, o=to[:, k * 128: k * 128 + 128], i=bkp[:, 0:128]: e.tensor_copy(o, i),
               waits=[("MM", now("MM"))])
        p.dma("sync", d_out[tc * 128:(tc + 1) * 128, :], to, waits=[("VE", now("VE"))], inc="LS")
        st_ls[tc % 2] = now("LS")
    p.emit("sync", lambda e: e.nop(), waits=[("LS", now("LS"))])
    p.finish()
    for t in reversed(ctx):
        t.__exit__(None, None, None)
    return nc



def kernel(**inputs):
    staged, vix, nv = _stage_weights(inputs)
    nc = build_program(nv, vix)
    x = np.asarray(inputs["L0_attn_x"], np.float32)
    in_maps = []
    for i in range(B):
        m = dict(staged)
        m["xin"] = np.ascontiguousarray(x[i])
        in_maps.append(m)
    res = run_bass_kernel_spmd(nc, in_maps, core_ids=list(range(B)))
    return np.stack([np.asarray(r["out"], np.float32) for r in res.results], 0)


# version 15
# speedup vs baseline: 1.1591x; 1.0102x over previous
import numpy as np
import concourse.bass as bass
import concourse.mybir as mybir
import concourse.bass_utils as _bass_utils
from concourse.bass_utils import run_bass_kernel_spmd
from concourse.alu_op_type import AluOpType as OP
import ml_dtypes

_orig_run_command = _bass_utils.run_command


def _run_command_ldwopt(cmd, **kw):
    if isinstance(cmd, list):
        cmd = ["--enable-ldw-opt=true" if c == "--enable-ldw-opt=false" else c
               for c in cmd]
    return _orig_run_command(cmd, **kw)


_bass_utils.run_command = _run_command_ldwopt

F32 = mybir.dt.float32
BF16 = mybir.dt.bfloat16
AF = mybir.ActivationFunctionType
AX = mybir.AxisListType.X

B = 8
D = 1024
T = 2048
KC = 8
HA = 16
HD = 64
DEPTH = 2
HF = 8
WIN = 512
NW = T // WIN
HID = 2730
HIDP = 2816
NM = HIDP // 128
NEGBIG = -3.0e38
LN2_EPS = 1e-5
RMS_EPS = 1e-6

ENGINES = ("sync", "gpsimd", "tensor", "vector", "scalar")


class Prog:
    def __init__(self, nc):
        self.nc = nc
        self.items = {e: [] for e in ENGINES}
        self.sems = {}
        self.counts = {}

    def sem(self, name):
        h = self.nc.semaphore(name)
        self.sems[name] = h.__enter__()
        self.counts[name] = 0

    def now(self, s):
        return self.counts[s]

    def emit(self, eng, fn, waits=(), inc=None, inc_amt=1):
        for s, v in waits:
            assert v <= self.counts[s], f"wait on future: {s} {v} > {self.counts[s]}"
        if inc is not None:
            self.counts[inc] += inc_amt
        waits = tuple((s, v) for s, v in waits if v > 0)
        sems = self.sems

        def item(engobj, fn=fn, waits=waits, inc=inc, inc_amt=inc_amt):
            for s, v in waits:
                engobj.wait_ge(sems[s], v)
            inst = fn(engobj)
            if inc is not None:
                inst.then_inc(sems[inc], inc_amt)
        self.items[eng].append(item)

    def dma(self, eng, out, in_, waits=(), inc=None):
        self.emit(eng, lambda e: e.dma_start(out=out, in_=in_), waits, inc, 16)

    def finish(self):
        nc = self.nc
        with nc.Block() as block:
            for e in ENGINES:
                lst = self.items[e]
                if not lst:
                    continue

                def body(engobj, lst=lst):
                    for item in lst:
                        item(engobj)
                getattr(block, e)(body)



def _stage_weights(iw):
    f32 = np.float32
    out = {}
    wq = np.asarray(iw["L0_attn_Wq"], f32)
    wqp = np.zeros((D, 1152), f32)
    wqp[:, :1040] = wq
    out["a_wq"] = np.ascontiguousarray(wqp)
    out["a_wk"] = np.ascontiguousarray(np.asarray(iw["L0_attn_Wk"], f32))
    out["a_wv"] = np.ascontiguousarray(np.asarray(iw["L0_attn_Wv"], f32))
    out["a_wo"] = np.ascontiguousarray(np.asarray(iw["L0_attn_Wo"], f32))
    wqkv = np.asarray(iw["L0_ffn_wqkv"], f32)
    s = wqkv.reshape(DEPTH, KC, 128, 3, 512).transpose(0, 3, 2, 1, 4)
    out["f_wqkv"] = np.ascontiguousarray(s.reshape(DEPTH, 3, 128, KC * 512))
    wo = np.asarray(iw["L0_ffn_wo"], f32)
    s = wo.reshape(DEPTH, 4, 128, 1024).transpose(0, 2, 1, 3)
    out["f_wo"] = np.ascontiguousarray(s.reshape(DEPTH, 128, 4 * 1024))
    w1 = np.asarray(iw["L0_ffn_w1"], f32)
    a = np.zeros((DEPTH, D, HIDP), f32); a[:, :, :HID] = w1[:, :, :HID]
    g = np.zeros((DEPTH, D, HIDP), f32); g[:, :, :HID] = w1[:, :, HID:]
    inter = np.empty((DEPTH, D, NM, 256), f32)
    inter[:, :, :, :128] = a.reshape(DEPTH, D, NM, 128)
    inter[:, :, :, 128:] = g.reshape(DEPTH, D, NM, 128)
    s = inter.transpose(0, 2, 1, 3).reshape(DEPTH, NM, KC, 128, 256).transpose(0, 1, 3, 2, 4)
    out["f_w1"] = np.ascontiguousarray(s.reshape(DEPTH, NM, 128, KC * 256))
    w2 = np.asarray(iw["L0_ffn_w2"], f32)
    w2p = np.zeros((DEPTH, HIDP, D), f32); w2p[:, :HID, :] = w2
    s = w2p.reshape(DEPTH, NM, 128, 2, 512).transpose(0, 3, 1, 2, 4)
    out["f_w2"] = np.ascontiguousarray(s)
    cols = []

    def addcol(v):
        v = np.asarray(v, f32)
        if v.ndim == 1:
            v = v[:, None]
        cols.append(v)
        return sum(c.shape[1] for c in cols) - v.shape[1]

    ix = {}
    bq = np.asarray(iw["L0_attn_bq"], f32)
    ix["a_bq"] = addcol(bq[:D].reshape(KC, 128).T)
    ix["a_bgate"] = addcol(np.pad(bq[D:], (0, 112)))
    ix["a_bk"] = addcol(np.asarray(iw["L0_attn_bk"], f32).reshape(KC, 128).T)
    ix["a_bv"] = addcol(np.asarray(iw["L0_attn_bv"], f32).reshape(KC, 128).T)
    ix["a_bo"] = addcol(np.asarray(iw["L0_attn_bo"], f32).reshape(KC, 128).T)
    ix["a_qn"] = addcol(np.tile(np.asarray(iw["L0_attn_qn_w"], f32), 2))
    ix["a_kn"] = addcol(np.tile(np.asarray(iw["L0_attn_kn_w"], f32), 2))
    qs = np.asarray(iw["L0_ffn_q_scale"], f32)
    ks = np.asarray(iw["L0_ffn_k_scale"], f32)
    ix["f_qs"] = addcol(np.tile(qs, (1, 2)).T.reshape(128, DEPTH, order="F") if False else np.stack([np.tile(qs[l], 2) for l in range(DEPTH)], 1))
    ix["f_ks"] = addcol(np.stack([np.tile(ks[l], 2) for l in range(DEPTH)], 1))
    lng = np.asarray(iw["L0_ffn_ln_g"], f32)
    lnb = np.asarray(iw["L0_ffn_ln_b"], f32)
    ix["f_lng"] = addcol(np.concatenate([lng[l].reshape(KC, 128).T for l in range(DEPTH)], 1))
    ix["f_lnb"] = addcol(np.concatenate([lnb[l].reshape(KC, 128).T for l in range(DEPTH)], 1))
    b1 = np.asarray(iw["L0_ffn_b1"], f32)
    b1a = np.zeros((DEPTH, HIDP), f32); b1a[:, :HID] = b1[:, :HID]
    b1g = np.zeros((DEPTH, HIDP), f32); b1g[:, :HID] = b1[:, HID:]
    ix["f_b1a"] = addcol(np.concatenate([b1a[l].reshape(NM, 128).T for l in range(DEPTH)], 1))
    ix["f_b1g"] = addcol(np.concatenate([b1g[l].reshape(NM, 128).T for l in range(DEPTH)], 1))
    b2 = np.asarray(iw["L0_ffn_b2"], f32)
    ix["f_b2"] = addcol(np.concatenate([b2[l].reshape(KC, 128).T for l in range(DEPTH)], 1))
    ix["a_qn_bc"] = addcol(np.broadcast_to(np.asarray(iw["L0_attn_qn_w"], f32), (128, HD)))
    ix["a_kn_bc"] = addcol(np.broadcast_to(np.asarray(iw["L0_attn_kn_w"], f32), (128, HD)))
    ix["f_qs_bc"] = addcol(np.concatenate([np.broadcast_to(qs[l], (128, HD)) for l in range(DEPTH)], 1))
    ix["f_ks_bc"] = addcol(np.concatenate([np.broadcast_to(ks[l], (128, HD)) for l in range(DEPTH)], 1))
    vecs = np.concatenate(cols, 1)
    out["vecs"] = np.ascontiguousarray(vecs)
    out["bvb"] = np.ascontiguousarray(np.broadcast_to(np.asarray(iw["L0_attn_bv"], f32), (128, D)))
    return out, ix, vecs.shape[1]



def build_program(nv_cols, vix, stages=("ffn0", "attn", "ffn1"), dbg_tap=None, attn_groups=(0, 1)):
    dbg_tap = dbg_tap
    nc = bass.Bass("TRN2", target_bir_lowering=False, detect_race_conditions=False)
    PI = 3.141592653589793
    xin = nc.dram_tensor("xin", [T, D], F32, kind="ExternalInput").ap()
    d_wq = nc.dram_tensor("a_wq", [D, 1152], F32, kind="ExternalInput").ap()
    d_wk = nc.dram_tensor("a_wk", [D, D], F32, kind="ExternalInput").ap()
    d_wv = nc.dram_tensor("a_wv", [D, D], F32, kind="ExternalInput").ap()
    d_wo = nc.dram_tensor("a_wo", [D, D], F32, kind="ExternalInput").ap()
    d_fqkv = nc.dram_tensor("f_wqkv", [DEPTH, 3, 128, KC * 512], F32, kind="ExternalInput").ap()
    d_fwo = nc.dram_tensor("f_wo", [DEPTH, 128, 4 * 1024], F32, kind="ExternalInput").ap()
    d_fw1 = nc.dram_tensor("f_w1", [DEPTH, NM, 128, KC * 256], F32, kind="ExternalInput").ap()
    d_fw2 = nc.dram_tensor("f_w2", [DEPTH, 2, NM, 128, 512], F32, kind="ExternalInput").ap()
    d_vecs = nc.dram_tensor("vecs", [128, nv_cols], F32, kind="ExternalInput").ap()
    d_bvb = nc.dram_tensor("bvb", [128, D], F32, kind="ExternalInput").ap()
    d_out = nc.dram_tensor("out", [T, D], F32, kind="ExternalOutput").ap()
    d_tabs = nc.dram_tensor("tabs", [4, 128, T], BF16, kind="Internal").ap()

    ctx = []

    def sb(name, shape, dt):
        t = nc.sbuf_tensor(name, shape, dt); ctx.append(t); return t.__enter__()

    xfm = sb("xfm", [128, KC * T], F32)
    zfm = sb("zfm", [128, KC * (T + 1)], F32)
    hbf_t = sb("hbf", [128, 8192], BF16)
    wst_t = sb("wst", [128, 8192], BF16)
    pool = sb("pool", [128, 10016], F32)
    rows = sb("rows", [128, 1024], F32)
    vecs = sb("vecs_sb", [128, nv_cols], F32)
    pcol = sb("pcol", [128, 8], F32)
    cb = sb("cb", [128, 584], BF16)
    e2tf = sb("e2tf", [2, 128], F32)
    e2tb = sb("e2tb", [2, 128], BF16)
    onesf = sb("onesf", [128, 128], F32)
    identg = sb("identg", [16, 16], BF16)
    epsc = sb("epsc", [128, 4], F32)
    pst = nc.psum_tensor("pst", [128, 4096], F32); ctx.append(pst); ps = pst.__enter__()

    poolb = pool.bitcast(BF16)
    hbff = hbf_t.bitcast(F32)
    zfmb = zfm.bitcast(BF16)

    def bank(i, n=1):
        return ps[:, i * 512:(i + n) * 512]

    mgt = cb[:, 0:128]; mlt = cb[:, 128:256]; identb = cb[:, 256:384]; e2b = cb[:, 384:386]

    PC_BA, PC_BF0, PC_BF1, PC_QS80, PC_QS81, PC_QN8 = 0, 1, 2, 3, 4, 5

    def zc(c, t0, n):
        return zfm[:, c * (T + 1) + 1 + t0: c * (T + 1) + 1 + t0 + n]

    def xc(c, t0, n):
        return xfm[:, c * T + t0: c * T + t0 + n]

    V = lambda name: vecs[:, vix[name]:vix[name] + 1]

    def Vs(name, off, n=1):
        return vecs[:, vix[name] + off: vix[name] + off + n]

    p0 = Prog(nc)
    for s in ("LS", "LW", "MM", "VE", "SE", "GP"):
        p0.sem(s)
    P = p0

    def t_(fn, waits=(), inc=None):
        P.emit("tensor", fn, waits, inc)

    def v_(fn, waits=(), inc=None):
        P.emit("vector", fn, waits, inc)

    def s_(fn, waits=(), inc=None):
        P.emit("scalar", fn, waits, inc)

    def g_(fn, waits=(), inc=None):
        P.emit("gpsimd", fn, waits, inc)

    p0.dma("sync", vecs[:], d_vecs, inc="LS")
    io128 = pool[:, 2048:2176]
    ident = pool[:, 2176:2304]
    iop = pool[:, 2304:2305]
    g_(lambda e: e.iota(io128, [[1, 128]], channel_multiplier=-1, allow_small_or_imprecise_dtypes=True), inc="GP")
    g_(lambda e: e.iota(iop, [[1, 1]], channel_multiplier=1, allow_small_or_imprecise_dtypes=True), inc="GP")
    gp0 = P.now("GP")
    v_(lambda e: e.tensor_scalar(ident, io128, 0.0, 1.0, op0=OP.is_equal, op1=OP.mult), waits=[("GP", gp0)])
    g_(lambda e: e.tensor_scalar(identg[:, :], io128[0:16, 0:16], 0.0, 1.0, op0=OP.is_equal, op1=OP.mult),
       waits=[("GP", gp0)], inc="GP")
    v_(lambda e: e.tensor_scalar(mgt, io128, 0.0, NEGBIG, op0=OP.is_gt, op1=OP.mult), waits=[("GP", gp0)])
    v_(lambda e: e.tensor_scalar(mlt, io128, 0.0, NEGBIG, op0=OP.is_lt, op1=OP.mult), waits=[("GP", gp0)])
    v_(lambda e: e.tensor_copy(identb, ident), waits=[("GP", gp0)])
    g_(lambda e: e.tensor_scalar(e2b[:, 0:1], iop, 64.0, 1.0, op0=OP.is_lt, op1=OP.mult), waits=[("GP", gp0)])
    g_(lambda e: e.tensor_scalar(e2b[:, 1:2], iop, 64.0, 1.0, op0=OP.is_ge, op1=OP.mult), waits=[("GP", gp0)])
    v_(lambda e: e.memset(onesf[:], 1.0))
    v_(lambda e: e.memset(epsc[:, 0:1], 1e-5))
    v_(lambda e: e.memset(epsc[:, 1:2], 1e-30))
    v_(lambda e: e.memset(epsc[:, 2:3], 1e-6))
    v_(lambda e: e.memset(epsc[:, 3:4], -80.0))
    io2r = pool[:, 2305:2433]
    g_(lambda e: e.iota(pool[0:2, 2305:2433], [[1, 128]], channel_multiplier=64, allow_small_or_imprecise_dtypes=True), inc="GP")
    gp1 = P.now("GP")
    v_(lambda e: e.tensor_scalar(e2tf[:, :], pool[0:2, 2305:2433], 64.0, 1.0, op0=OP.is_lt, op1=OP.mult), waits=[("GP", gp1)])
    v_(lambda e: e.tensor_scalar(pool[0:2, 2433:2561], pool[0:2, 2305:2433], 128.0, 1.0, op0=OP.is_ge, op1=OP.mult), waits=[("GP", gp1)])
    v_(lambda e: e.tensor_tensor(e2tf[:, :], e2tf[:, :], pool[0:2, 2433:2561], op=OP.add), inc="VE")
    v_(lambda e: e.tensor_copy(e2tb[:, :], e2tf[:, :]))
    v_(lambda e: e.memset(cb[:, 520:584], 1.0))
    rotb = cb[:, 387:515]
    iof = pool[:, 2561:2689]
    fm64 = pool[:, 2689:2817]
    sA = pool[:, 2817:2945]
    sB = pool[:, 2945:3073]
    g_(lambda e: e.iota(iof, [[1, 128]], channel_multiplier=0, allow_small_or_imprecise_dtypes=True), inc="GP")
    gp2 = P.now("GP")
    v_(lambda e: e.tensor_scalar(fm64, iof, 64.0, 64.0, op0=OP.is_ge, op1=OP.mult), waits=[("GP", gp2)])
    v_(lambda e: e.tensor_tensor(fm64, iof, fm64, op=OP.subtract))
    v_(lambda e: e.tensor_single_scalar(sA, io128, -32.0, op=OP.is_equal), waits=[("GP", gp2)])
    v_(lambda e: e.tensor_scalar(sB, fm64, 32.0, -1.0, op0=OP.is_lt, op1=OP.mult))
    v_(lambda e: e.tensor_tensor(sA, sA, sB, op=OP.mult))
    v_(lambda e: e.tensor_single_scalar(sB, io128, 32.0, op=OP.is_equal))
    v_(lambda e: e.tensor_copy(rotb, sA))
    v_(lambda e: e.tensor_scalar(sA, fm64, 32.0, 1.0, op0=OP.is_ge, op1=OP.mult))
    v_(lambda e: e.tensor_tensor(sA, sA, sB, op=OP.mult))
    v_(lambda e: e.tensor_tensor(pool[:, 2817:2945], sA, sA, op=OP.bypass) if False else
       e.tensor_tensor(sB, sA, sA, op=OP.mult))
    v_(lambda e: e.scalar_tensor_tensor(rotb, sA, 1.0, rotb, op0=OP.mult, op1=OP.add), inc="VE")
    iop_m = pool[:, 3076:3077]
    invf = pool[:, 3073:3074]
    lnb_ = pool[:, 3074:3075]
    trow = pool[:, 3584:5632]
    fr = pool[:, 5632:7680]
    tA = pool[:, 7680:9728]
    sin_t = rows[:, 0:1024]
    LN1E4_32 = 9.210340371976184 / 32.0
    _t32 = pool[:, 3075:3076]
    g_(lambda e: e.tensor_scalar(iop_m, iop, 32.0, 32.0, op0=OP.is_ge, op1=OP.mult),
       waits=[("GP", gp0), ("VE", P.now("VE"))], inc="GP")
    g_(lambda e: e.tensor_scalar(_t32, iop, 64.0, 32.0, op0=OP.is_ge, op1=OP.mult), waits=[("GP", P.now("GP"))], inc="GP")
    g_(lambda e: e.tensor_tensor(iop_m, iop_m, _t32, op=OP.add), waits=[("GP", P.now("GP"))], inc="GP")
    g_(lambda e: e.tensor_scalar(_t32, iop, 96.0, 32.0, op0=OP.is_ge, op1=OP.mult), waits=[("GP", P.now("GP"))], inc="GP")
    g_(lambda e: e.tensor_tensor(iop_m, iop_m, _t32, op=OP.add), waits=[("GP", P.now("GP"))], inc="GP")
    g_(lambda e: e.tensor_tensor(iop_m, iop, iop_m, op=OP.subtract), waits=[("GP", P.now("GP"))], inc="GP")
    g_(lambda e: e.tensor_scalar(lnb_, iop_m, 2.0 / 89.6, 25.6 / 89.6, op0=OP.mult, op1=OP.add), waits=[("GP", P.now("GP"))], inc="GP")
    s_(lambda e: e.activation(invf, iop_m, AF.Exp, bias=0.0, scale=-LN1E4_32), waits=[("GP", P.now("GP"))])
    s_(lambda e: e.activation(lnb_, lnb_, AF.Ln, bias=0.0, scale=1.0), inc="SE")
    g_(lambda e: e.iota(trow, [[1, 2048]], channel_multiplier=0, allow_small_or_imprecise_dtypes=True), inc="GP")
    gp3 = P.now("GP")
    PIh = PI / 2.0
    TWO_PI = 2.0 * PI
    pooli = pool.bitcast(mybir.dt.int32)
    R_K = pool[:, 0:2048]
    R_Ki = pooli[:, 0:2048]
    tAb = poolb[:, 0:4096]
    se_tab = [P.now("SE")]
    C1 = 6.28125
    C2 = 6.283185307179586 - 6.28125
    INV2PI = 1.0 / 6.283185307179586

    def build_tab(idx, is_sin, inv_scale):
        g_(lambda e: e.iota(trow, [[1, 2048]], channel_multiplier=0,
                            allow_small_or_imprecise_dtypes=True),
           waits=[("VE", P.now("VE"))], inc="GP")
        gp_t = P.now("GP")
        v_(lambda e: e.tensor_scalar_mul(fr, trow, invf),
           waits=[("GP", gp_t), ("SE", se_tab[0])])
        if not is_sin:
            v_(lambda e: e.tensor_single_scalar(fr, fr, PIh, op=OP.add))
        v_(lambda e: e.tensor_scalar(trow, trow, -1024.0, 1.0 / 256.0, op0=OP.add, op1=OP.mult))
        v_(lambda e: e.tensor_scalar_mul(tA, fr, INV2PI))
        v_(lambda e: e.tensor_copy(R_Ki, tA), waits=[("LS", P.now("LS"))])
        v_(lambda e: e.tensor_copy(R_K, R_Ki))
        v_(lambda e: e.scalar_tensor_tensor(fr, R_K, -C1, fr, op0=OP.mult, op1=OP.add))
        v_(lambda e: e.scalar_tensor_tensor(fr, R_K, -C2, fr, op0=OP.mult, op1=OP.add))
        v_(lambda e: e.tensor_scalar(tA, fr, PI, TWO_PI, op0=OP.is_gt, op1=OP.mult))
        v_(lambda e: e.tensor_tensor(fr, fr, tA, op=OP.subtract))
        v_(lambda e: e.tensor_scalar(tA, fr, -PI, TWO_PI, op0=OP.is_lt, op1=OP.mult))
        v_(lambda e: e.tensor_tensor(fr, fr, tA, op=OP.add), inc="VE")
        s_(lambda e: e.activation(fr, fr, AF.Sin, bias=0.0, scale=1.0), waits=[("VE", P.now("VE"))], inc="SE")
        v_(lambda e: e.tensor_scalar_mul(tA, trow, lnb_), waits=[("SE", P.now("SE"))], inc="VE")
        s_(lambda e: e.activation(tA, tA, AF.Exp, bias=0.0, scale=(-1.0 if inv_scale else 1.0)),
           waits=[("VE", P.now("VE"))], inc="SE")
        v_(lambda e: e.tensor_tensor(tA, fr, tA, op=OP.mult), waits=[("SE", P.now("SE"))])
        v_(lambda e: e.tensor_copy(tAb[:, 0:2048], tA), inc="VE")
        P.dma("sync", d_tabs[idx], tAb[:, 0:2048], waits=[("VE", P.now("VE"))], inc="LS")
        se_tab[0] = P.now("SE")
    build_tab(0, False, False)
    build_tab(1, True, False)
    build_tab(2, False, True)
    build_tab(3, True, True)
    ls_vec = 16
    v_(lambda e: e.reduce_max(pcol[:, 6:7], Vs("a_qn_bc", 0, HD), axis=AX, apply_absolute_value=True),
       waits=[("LS", ls_vec)], inc="VE")
    v_(lambda e: e.reduce_max(pcol[:, 7:8], Vs("a_kn_bc", 0, HD), axis=AX, apply_absolute_value=True), inc="VE")
    g_(lambda e: e.tensor_tensor(pcol[:, PC_BA:PC_BA + 1], pcol[:, 6:7], pcol[:, 7:8], op=OP.mult),
       waits=[("VE", P.now("VE"))], inc="GP")
    s_(lambda e: e.activation(pcol[:, PC_BA:PC_BA + 1], pcol[:, PC_BA:PC_BA + 1], AF.Relu,
                              bias=epsc[:, 3:4], scale=8.0), waits=[("GP", P.now("GP"))])
    s_(lambda e: e.mul(pcol[:, PC_BA:PC_BA + 1], pcol[:, PC_BA:PC_BA + 1], -1.0), inc="SE")
    for l in range(DEPTH):
        v_(lambda e, l=l: e.reduce_max(pcol[:, 6:7], Vs("f_qs_bc", l * HD, HD), axis=AX, apply_absolute_value=True), inc="VE")
        v_(lambda e, l=l: e.reduce_max(pcol[:, 7:8], Vs("f_ks_bc", l * HD, HD), axis=AX, apply_absolute_value=True), inc="VE")
        g_(lambda e, l=l: e.tensor_tensor(pcol[:, PC_BF0 + l:PC_BF0 + l + 1], pcol[:, 6:7], pcol[:, 7:8], op=OP.mult),
           waits=[("VE", P.now("VE"))], inc="GP")
        s_(lambda e, l=l: e.activation(pcol[:, PC_BF0 + l:PC_BF0 + l + 1], pcol[:, PC_BF0 + l:PC_BF0 + l + 1],
                                       AF.Relu, bias=epsc[:, 3:4], scale=8.0), waits=[("GP", P.now("GP"))])
        s_(lambda e, l=l: e.mul(pcol[:, PC_BF0 + l:PC_BF0 + l + 1], pcol[:, PC_BF0 + l:PC_BF0 + l + 1], -1.0), inc="SE")
        g_(lambda e, l=l: e.tensor_single_scalar(pcol[:, PC_QS80 + l:PC_QS80 + l + 1],
                                                 Vs("f_qs", l), 8.0, op=OP.mult))
    g_(lambda e: e.tensor_single_scalar(pcol[:, PC_QN8:PC_QN8 + 1], V("a_qn"), 0.125, op=OP.mult), inc="GP")
    tin0 = pool[:, 0:1024]
    tin1 = pool[:, 1024:2048]
    v_(lambda e: e.engine_nop() if False else e.memset(rows[:, 0:1], 0.0), inc="VE")
    ve_ident = P.now("VE")
    mm_done_tin = [0, 0]
    for tc in range(16):
        tin = (tin0, tin1)[tc % 2]
        P.dma("sync", tin, xin[tc * 128:(tc + 1) * 128, :],
              waits=[("MM", mm_done_tin[tc % 2])], inc="LS")
        ls_need = P.now("LS")
        for k in range(KC):
            bk_ = bank(k % 2)
            t_(lambda e, o=bk_[:, 0:128], i=tin[:, k * 128:(k + 1) * 128]:
               e.transpose(o, i, ident),
               waits=([("LS", ls_need), ("VE", max(ve_ident, P.now("VE")))] if k == 0 else
                      [("VE", P.now("VE"))]),
               inc="MM")
            v_(lambda e, o=xfm[:, k * T + tc * 128: k * T + (tc) * 128 + 128], i=bk_[:, 0:128]:
               e.tensor_copy(o, i),
               waits=[("MM", P.now("MM"))], inc="VE")
        mm_done_tin[tc % 2] = P.now("MM")
    onesb = cb[:, 386:387]
    v_(lambda e: e.memset(onesb, 1.0))
    for c in range(KC):
        v_(lambda e, c=c: e.memset(zfm[:, c * (T + 1):c * (T + 1) + 1], 0.0))
    p0.finish()

    p = Prog(nc)
    for s in ("LS2", "LW2", "MM2", "VE2", "SE2", "GP2"):
        p.sem(s)
    for a, b in (("LS", "LS2"), ("LW", "LW2"), ("MM", "MM2"), ("VE", "VE2"), ("SE", "SE2"), ("GP", "GP2")):
        p.sems[a] = p.sems[b]
        p.counts[a] = 0

    def T_(fn, waits=(), inc="MM"):
        p.emit("tensor", fn, waits, inc)

    def V_(fn, waits=(), inc="VE"):
        p.emit("vector", fn, waits, inc)

    def S_(fn, waits=(), inc="SE"):
        p.emit("scalar", fn, waits, inc)

    def MMW(out, lhsT, rhs, first, last, waits=(), inc="MM"):
        T_(lambda e, o=out, l=lhsT, r=rhs, f=first, s=last: e.matmul(o, l, r, start=f, stop=s),
           waits=waits, inc=inc)

    now = p.now

    TAB = [0, 2048, 4096, 6144]
    KRING = 8192
    VRING = 12288
    QWIN = 16448
    T1F = 9248
    SQB = 19520
    LNC0, LNC1 = 16448, 17472
    LNTMP = 9504
    ZB = 0
    OW = 4104
    PTR = 6152

    def h2ap(j, c0, n):
        if j < 19:
            return poolb[:, j * 1024 + c0: j * 1024 + c0 + n]
        return wst_t[:, 5120 + (j - 19) * 1024 + c0: 5120 + (j - 19) * 1024 + c0 + n]

    ms = {"pool_mm": 0, "pool_ve": 0, "pool_se": 0,
          "wstA_mm": 0, "wstB_mm": 0,
          "b45_ve": 0, "b45_se": 0, "b67_ve": 0,
          "zb_mm": 0}

    RW0 = rows[0:1, 0:512]
    RW1 = rows[0:1, 512:1024]

    def emit_outer_ln(dst, tc_order=(0, 1, 2, 3)):
        cast_ve = [ms["pool_mm"], ms["pool_mm"]]
        e0v, e0s = now("VE"), now("SE")
        pairs = {tc_order[i]: bp for i, bp in enumerate([(5, 4), (3, 2), (7, 6), (1, 0)])}
        rowsets = [(rows[0:1, 0:512], rows[0:1, 512:1024]),
                   (hbff[0:1, 3072:3584], hbff[0:1, 3584:4096])]
        pend = [None]
        for i, tc in enumerate(tc_order):
            t0 = tc * 512
            bS1, bS2 = pairs[tc]
            for k in range(KC):
                cbA = poolb[:, LNC0 + (k % 2) * 512: LNC0 + (k % 2) * 512 + 512]
                cbS = poolb[:, LNC1 + (k % 2) * 512: LNC1 + (k % 2) * 512 + 512]
                V_(lambda e, o=cbA, i=xc(k, t0, 512): e.tensor_copy(o, i),
                   waits=[("MM", cast_ve[k % 2])])
                V_(lambda e, o=cbS, i=cbA: e.tensor_tensor(o, i, i, op=OP.mult))
                ve_n = now("VE")
                MMW(ps[0:1, bS1 * 512:bS1 * 512 + 512], onesb, cbA, k == 0, k == KC - 1,
                    waits=[("VE", ve_n)] + ([("VE", e0v), ("SE", e0s)] if k == 0 else []))
                MMW(ps[0:1, bS2 * 512:bS2 * 512 + 512], onesb, cbS, k == 0, k == KC - 1)
                cast_ve[k % 2] = now("MM")
            mm_stats = now("MM")
            if pend[0] is not None:
                pend[0]()

            def mk_chain(tc=tc, t0=t0, bS1=bS1, bS2=bS2, mm_stats=mm_stats, i=i):
                RA, RB = rowsets[i % 2]
                V_(lambda e, RA=RA: e.tensor_single_scalar(RA, ps[0:1, bS1 * 512:bS1 * 512 + 512], 1.0 / D, op=OP.mult),
                   waits=[("MM", mm_stats)])
                V_(lambda e, RA=RA, RB=RB: e.tensor_tensor(RB, RA, RA, op=OP.mult))
                V_(lambda e, RB=RB: e.scalar_tensor_tensor(RB, ps[0:1, bS2 * 512:bS2 * 512 + 512], 1.0 / D, RB,
                                                           op0=OP.mult, op1=OP.subtract))
                ve_rows = now("VE")
                S_(lambda e, RB=RB: e.activation(RB, RB, AF.Ln, bias=epsc[0:1, 0:1], scale=1.0),
                   waits=[("VE", ve_rows)])
                S_(lambda e, RB=RB: e.activation(RB, RB, AF.Exp, bias=0.0, scale=-0.5))
                se_rows = now("SE")
                MMW(ps[:, bS1 * 512:bS1 * 512 + 512], onesf[0:1, :], RA, True, True,
                    waits=[("SE", se_rows)])
                MMW(ps[:, bS2 * 512:bS2 * 512 + 512], onesf[0:1, :], RB, True, True)
                mm_bc = now("MM")
                for k in range(KC):
                    if dst == "z":
                        V_(lambda e, o=zc(k, t0, 512), i=xc(k, t0, 512):
                           e.tensor_tensor(o, i, ps[:, bS1 * 512:bS1 * 512 + 512], op=OP.subtract),
                           waits=[("MM", mm_bc)] if k == 0 else ())
                        V_(lambda e, o=zc(k, t0, 512): e.tensor_tensor(o, o, ps[:, bS2 * 512:bS2 * 512 + 512], op=OP.mult))
                    else:
                        yap = zfmb[:, k * T + t0: k * T + t0 + 512]
                        V_(lambda e, i=xc(k, t0, 512):
                           e.tensor_tensor(pool[:, LNTMP:LNTMP + 512], i, ps[:, bS1 * 512:bS1 * 512 + 512], op=OP.subtract),
                           waits=[("MM", mm_bc)] if k == 0 else ())
                        V_(lambda e, o=yap: e.tensor_tensor(o, pool[:, LNTMP:LNTMP + 512], ps[:, bS2 * 512:bS2 * 512 + 512], op=OP.mult))
            pend[0] = mk_chain
        pend[0]()
        pend[0] = None
        ms["b67_ve"] = now("VE")
        ms["b45_ve"] = now("VE")
        ms["b45_se"] = now("SE")
        ms["pbank0_ve"] = now("VE")
        ms["pbank1_ve"] = now("VE")
        ms["b6_ve"] = now("VE")
        ms["b7_ve"] = now("VE")
        ms["b3_ve"] = now("VE")
        ms["b2_se"] = now("SE")

    RL2 = rows[0:2, 0:512]
    RL2B = hbf_t[0:2, 7176:7688]
    RREC = rows[0:1, 0:512]
    RRECB = hbf_t[0:1, 7176:7688]
    T2F = rows[:, 512:1024]
    wslotA = wst_t[:, 0:4096]
    wslotB = wst_t[:, 4096:8192]
    slot_guard = {"A": 0, "B": 0}
    ms["b2_se"] = 0; ms["b3_ve"] = 0; ms["b6_ve"] = 0; ms["b7_ve"] = 0; ms["z_ve"] = 0
    ms["w2p_ve"] = 0; ms["x_ve"] = 0

    def stream_phase(slot, src_ap):
        tgt = wslotA if slot == "A" else wslotB
        p.dma("gpsimd", tgt[:, 0:src_ap.shape[-1]], src_ap,
              waits=[("MM", slot_guard[slot])], inc="LW")
        return p.now("LW")

    def cast_zb(w):
        for k in range(KC):
            V_(lambda e, o=hbf_t[:, k * 513: k * 513 + 513],
               i=zfm[:, k * (T + 1) + w * 512: k * (T + 1) + w * 512 + 513]:
               e.tensor_copy(o, i),
               waits=[("MM", ms["zb_mm"])] if k == 0 else ())
        return p.now("VE")

    def zb_in(k, off, n):
        sh = 1 if k < 4 else 0
        return hbf_t[:, k * 513 + sh + off: k * 513 + sh + off + n]

    def l2norm_chain(l, bankP, out_ap, scol, cs_ap, ss_ap, mm0=None, sqb=None, se_sq=None):
        if sqb is None:
            if mm0 is None:
                mm0 = now("MM")
            sqb = poolb[:, SQB:SQB + 512]
            S_(lambda e, sqb=sqb: e.activation(sqb, bankP, AF.Square, bias=0.0, scale=1.0),
               waits=[("MM", mm0)])
            se_sq = now("SE")
        MMW(ps[0:2, 2 * 512:2 * 512 + 512], e2b, sqb, True, True,
            waits=[("SE", se_sq), ("SE", ms["b2_se"]), ("VE", ms["b3_ve"])])
        S_(lambda e: e.activation(RL2, ps[0:2, 2 * 512:2 * 512 + 512], AF.Ln, bias=epsc[0:2, 1:2], scale=1.0),
           waits=[("MM", now("MM"))])
        S_(lambda e: e.activation(RL2B, RL2, AF.Exp, bias=0.0, scale=-0.5))
        ms["b2_se"] = now("SE")
        MMW(ps[:, 3 * 512:4 * 512], e2tb[:, :], RL2B, True, True,
            waits=[("SE", now("SE")), ("VE", ms["b3_ve"])])
        mmb = now("MM")
        V_(lambda e, s=scol: e.tensor_scalar(pool[:, T1F:T1F + 512], bankP, s, None, op0=OP.mult),
           waits=[("MM", mmb)])
        V_(lambda e, o=out_ap: e.tensor_tensor(o, pool[:, T1F:T1F + 512], ps[:, 3 * 512:4 * 512],
                                               op=OP.mult))
        ms["b3_ve"] = now("VE")
        MMW(ps[:, 7 * 512:8 * 512], rotb, out_ap, True, True,
            waits=[("VE", now("VE")), ("VE", ms["b7_ve"])])
        mmr = now("MM")
        V_(lambda e, i=out_ap, c_=cs_ap: e.tensor_tensor(pool[:, T1F:T1F + 512], i, c_, op=OP.mult))
        V_(lambda e, s=ss_ap: e.tensor_tensor(T2F, ps[:, 7 * 512:8 * 512], s, op=OP.mult),
           waits=[("MM", mmr)])
        ms["b7_ve"] = now("VE")
        V_(lambda e, o=out_ap: e.tensor_tensor(o, pool[:, T1F:T1F + 512], T2F, op=OP.add))

    def emit_mlp_ln(l, th, cast_base_bf, cast_guard_mm):
        t0h = th * 1024
        e0v, e0s = now("VE"), now("SE")
        cast_ve = [cast_guard_mm, cast_guard_mm]
        pend = [None]
        for tc in range(2):
            bS1, bS2 = (5, 4) if tc == 0 else (7, 6)
            for k in range(KC):
                sh = 1 if k < 4 else 0
                zin = zfm[:, k * (T + 1) + sh + t0h + tc * 512: k * (T + 1) + sh + t0h + tc * 512 + 512]
                cbA = cast_base_bf((k % 2) * 512)
                cbS = cast_base_bf(1024 + (k % 2) * 512)
                V_(lambda e, o=cbA, i=zin: e.tensor_copy(o, i), waits=[("MM", cast_ve[k % 2])])
                V_(lambda e, o=cbS, i=cbA: e.tensor_tensor(o, i, i, op=OP.mult))
                ve_n = now("VE")
                MMW(ps[0:1, bS1 * 512:bS1 * 512 + 512], onesb, cbA, k == 0, k == KC - 1,
                    waits=[("VE", ve_n)] + ([("VE", e0v), ("SE", e0s)] if k == 0 else []))
                MMW(ps[0:1, bS2 * 512:bS2 * 512 + 512], onesb, cbS, k == 0, k == KC - 1)
                cast_ve[k % 2] = now("MM")
            mm_stats = now("MM")
            if pend[0] is not None:
                pend[0]()

            def mk_chain(tc=tc, bS1=bS1, bS2=bS2, mm_stats=mm_stats):
                rb_ = 0
                RA = rows[0:1, 0:512]
                RB = rows[0:1, 512:1024]
                V_(lambda e, RA=RA: e.tensor_single_scalar(RA, ps[0:1, bS1 * 512:bS1 * 512 + 512], 1.0 / D, op=OP.mult),
                   waits=[("MM", now("MM"))])
                V_(lambda e, RA=RA, RB=RB: e.tensor_tensor(RB, RA, RA, op=OP.mult))
                V_(lambda e, RB=RB: e.scalar_tensor_tensor(RB, ps[0:1, bS2 * 512:bS2 * 512 + 512], 1.0 / D, RB,
                                                           op0=OP.mult, op1=OP.subtract))
                ve_rows = now("VE")
                S_(lambda e, RB=RB: e.activation(RB, RB, AF.Ln, bias=epsc[0:1, 0:1], scale=1.0),
                   waits=[("VE", ve_rows)])
                S_(lambda e, RB=RB: e.activation(RB, RB, AF.Exp, bias=0.0, scale=-0.5))
                se_rows = now("SE")
                MMW(ps[:, bS1 * 512:bS1 * 512 + 512], onesf[rb_:rb_ + 1, :], RA, True, True,
                    waits=[("SE", se_rows)])
                MMW(ps[:, bS2 * 512:bS2 * 512 + 512], onesf[rb_:rb_ + 1, :], RB, True, True)
                mm_bc = now("MM")
                for k in range(KC):
                    sh = 1 if k < 4 else 0
                    zin = zfm[:, k * (T + 1) + sh + t0h + tc * 512: k * (T + 1) + sh + t0h + tc * 512 + 512]
                    V_(lambda e, i=zin: e.tensor_tensor(pool[:, LNTMP:LNTMP + 512], i,
                                                        ps[:, bS1 * 512:bS1 * 512 + 512], op=OP.subtract),
                       waits=[("MM", mm_bc)] if k == 0 else ())
                    V_(lambda e: e.tensor_tensor(pool[:, LNTMP:LNTMP + 512], pool[:, LNTMP:LNTMP + 512],
                                                 ps[:, bS2 * 512:bS2 * 512 + 512], op=OP.mult))
                    V_(lambda e, o=hbf_t[:, k * 1024 + tc * 512: k * 1024 + tc * 512 + 512],
                       g_=Vs("f_lng", l * KC + k), b_=Vs("f_lnb", l * KC + k):
                       e.tensor_scalar(o, pool[:, LNTMP:LNTMP + 512], g_, b_, op0=OP.mult, op1=OP.add))
                ms["mlpln_tc%d_ve" % tc] = now("VE")
            pend[0] = mk_chain
        pend[0]()
        ms["mlpln_ve"] = now("VE")
        ms["mlpln_mm"] = now("MM")

    def emit_ffn_layer(l, parts=("la", "mlp")):
        for i in range(4):
            p.dma("sync", poolb[:, TAB[i]:TAB[i] + 2048], d_tabs[i],
                  waits=[("MM", ms["pool_mm"]), ("VE", ms["pool_ve"])], inc="LS")
        tab_ls = now("LS")
        zb_ve = [0]
        BQ = (0, 1, 4, 5)
        SQ2 = (poolb[:, SQB:SQB + 512], hbf_t[:, PTR:PTR + 512])
        entry_ve, entry_se = now("VE"), now("SE")
        pbank_ve = {0: ms.get("pbank0_ve", 0), 1: ms.get("pbank1_ve", 0),
                    4: entry_ve, 5: entry_ve}
        ms["stb_se"] = entry_se

        def proj_q(w):
            ls_need = stream_phase("A", d_fqkv[l, 0])
            pend = []
            for c in range(4):
                bk = ps[:, BQ[c] * 512:BQ[c] * 512 + 512]
                for k in range(KC):
                    w_ = [("VE", pbank_ve[BQ[c]])] if k == 0 else []
                    if k == 0 and c >= 2:
                        w_ += [("SE", ms["stb_se"])]
                    if c == 0 and k == 0:
                        w_ += [("LW", ls_need), ("VE", zb_ve[0]), ("LS", tab_ls)]
                    MMW(bk, wslotA[:, k * 512 + c * 128: k * 512 + c * 128 + 128],
                        zb_in(k, 0, 512), k == 0, k == KC - 1, waits=w_)
                if len(pend) >= 2:
                    pend.pop(0)()
                sq = SQ2[c % 2]
                S_(lambda e, sq=sq, bk=bk: e.activation(sq, bk, AF.Square, bias=0.0, scale=1.0),
                   waits=[("MM", now("MM"))])
                se_sq = now("SE")

                def mk(c=c, bk=bk, sq=sq, se_sq=se_sq):
                    cs = poolb[:, TAB[0] + w * 512: TAB[0] + w * 512 + 512]
                    ss = poolb[:, TAB[1] + w * 512: TAB[1] + w * 512 + 512]
                    l2norm_chain(l, bk, poolb[:, QWIN + c * 512: QWIN + c * 512 + 512],
                                 pcol[:, PC_QS80 + l:PC_QS80 + l + 1], cs, ss,
                                 sqb=sq, se_sq=se_sq)
                    pbank_ve[BQ[c]] = now("VE")
                pend.append(mk)
            for f in pend:
                f()
            slot_guard["A"] = now("MM")

        def proj_k(wk):
            ls_need = stream_phase("B", d_fqkv[l, 1])
            kbase = KRING + (wk % 2) * 2048
            pend = []
            for c in range(4):
                bk = ps[:, BQ[c] * 512:BQ[c] * 512 + 512]
                for k in range(KC):
                    w_ = [("VE", pbank_ve[BQ[c]])] if k == 0 else []
                    if k == 0 and c >= 2:
                        w_ += [("SE", ms["stb_se"])]
                    if c == 0 and k == 0:
                        w_ += [("LW", ls_need), ("VE", zb_ve[0])]
                    MMW(bk, wslotB[:, k * 512 + c * 128: k * 512 + c * 128 + 128],
                        zb_in(k, 0, 512), k == 0, k == KC - 1, waits=w_)
                if len(pend) >= 2:
                    pend.pop(0)()
                sq = SQ2[c % 2]
                S_(lambda e, sq=sq, bk=bk: e.activation(sq, bk, AF.Square, bias=0.0, scale=1.0),
                   waits=[("MM", now("MM"))])
                se_sq = now("SE")

                def mk(c=c, bk=bk, sq=sq, se_sq=se_sq):
                    cs = poolb[:, TAB[2] + wk * 512: TAB[2] + wk * 512 + 512]
                    ss = poolb[:, TAB[3] + wk * 512: TAB[3] + wk * 512 + 512]
                    l2norm_chain(l, bk, poolb[:, kbase + c * 512: kbase + c * 512 + 512],
                                 Vs("f_ks", l), cs, ss, sqb=sq, se_sq=se_sq)
                    pbank_ve[BQ[c]] = now("VE")
                pend.append(mk)
            for f in pend:
                f()
            slot_guard["B"] = now("MM")

        def proj_v(wk):
            ls_need = stream_phase("A", d_fqkv[l, 2])
            vbase = VRING + (wk % 2) * 2080
            pend = []
            for tch in range(4):
                bk = ps[:, BQ[tch] * 512:BQ[tch] * 512 + 512]
                for k in range(KC):
                    w_ = [("VE", pbank_ve[BQ[tch]])] if k == 0 else []
                    if k == 0 and tch >= 2:
                        w_ += [("SE", ms["stb_se"])]
                    if tch == 0 and k == 0:
                        w_ += [("LW", ls_need), ("VE", zb_ve[0])]
                    MMW(bk, zb_in(k, tch * 128, 128), wslotA[:, k * 512: k * 512 + 512],
                        k == 0, k == KC - 1, waits=w_)
                mmv = now("MM")

                def mk(tch=tch, bk=bk, mmv=mmv):
                    dst = poolb[:, vbase + tch * 520: vbase + tch * 520 + 520]
                    V_(lambda e, o=dst.rearrange("p (h c) -> p h c", c=65),
                       i=bk.rearrange("p (h c) -> p h c", c=64):
                       e.tensor_copy(o[:, :, 0:64], i), waits=[("MM", mmv)])
                    V_(lambda e, o=dst.rearrange("p (h c) -> p h c", c=65): e.memset(o[:, :, 64:65], 1.0))
                    pbank_ve[BQ[tch]] = now("VE")
                if len(pend) >= 2:
                    pend.pop(0)()
                pend.append(mk)
            for f in pend:
                f()
            slot_guard["A"] = now("MM")

        def attention(w):
            ve_ready = now("VE")
            st_se = [ms["b2_se"], ms["b2_se"]]
            ptr_mm = [0, 0]
            st_i = [0]
            pend_evac = [None]

            def emit_evac(h, pvb, mm_pv):
                S_(lambda e, i=ps[64:65, pvb * 512: pvb * 512 + 512]:
                   e.activation(RREC, i, AF.Ln, bias=0.0, scale=1.0), waits=[("MM", mm_pv)])
                S_(lambda e: e.activation(RRECB, RREC, AF.Exp, bias=0.0, scale=-1.0))
                MMW(ps[0:64, 3 * 512:3 * 512 + 512], cb[0:1, 520:584], RRECB, True, True,
                    waits=[("SE", now("SE")), ("VE", ms["b3_ve"])])
                mmb = now("MM")
                r0 = 64 * (h % 2)
                oap = hbf_t[r0:r0 + 64, OW + (h // 2) * 512: OW + (h // 2) * 512 + 512]
                V_(lambda e, o=oap, i=ps[0:64, pvb * 512: pvb * 512 + 512]:
                   e.tensor_copy(o, i), waits=[("MM", mmb)])
                V_(lambda e, o=oap:
                   e.tensor_tensor(o, o, ps[0:64, 3 * 512:3 * 512 + 512], op=OP.mult))
                ms["b3_ve"] = now("VE")
                if pvb == 6:
                    ms["b6_ve"] = now("VE")
                else:
                    ms["b7_ve"] = now("VE")

            for h in range(HF):
                pvb = 6 + (h % 2)
                pv_guard = ms["b6_ve"] if pvb == 6 else ms["b7_ve"]
                pv_started = [False]
                cx_list = list(range(4, 8) if w == 0 else range(8))
                pend_pv = [None]
                for cx in cx_list:
                    par = st_i[0] % 2
                    st_i[0] += 1
                    lo = max(0, 128 * (cx - 4)); hi = min(512, 128 * (cx + 1)); width = hi - lo
                    win_of_cx = (w - 1) if cx < 4 else w
                    kbase = KRING + (win_of_cx % 2) * 2048
                    vbase = VRING + (win_of_cx % 2) * 2080
                    r0 = 64 * (h % 2)
                    stb = ps[:, (4 + par) * 512:(4 + par) * 512 + 512]
                    ktile = poolb[r0:r0 + 64, kbase + (h // 2) * 512 + (cx % 4) * 128:
                                  kbase + (h // 2) * 512 + (cx % 4) * 128 + 128]
                    qtile = poolb[r0:r0 + 64, QWIN + (h // 2) * 512 + lo: QWIN + (h // 2) * 512 + hi]
                    MMW(stb[:, 0:width], ktile, qtile, True, False,
                        waits=[("SE", st_se[par]), ("VE", ve_ready)])
                    if cx >= 4:
                        MMW(stb[:, 0:128], mgt, identb, False, True)
                    else:
                        MMW(stb[:, width - 128:width], mlt, identb, False, True)
                    mm_st = now("MM")
                    if pend_pv[0] is not None:
                        pend_pv[0]()
                        pend_pv[0] = None
                    if cx == cx_list[0] and pend_evac[0] is not None:
                        pend_evac[0]()
                        pend_evac[0] = None
                    ptr = hbf_t[:, PTR + par * 512: PTR + par * 512 + width]
                    S_(lambda e, o=ptr, i=stb[:, 0:width], b=pcol[:, PC_BF0 + l:PC_BF0 + l + 1]:
                       e.activation(o, i, AF.Exp, bias=b, scale=1.0),
                       waits=[("MM", mm_st), ("MM", ptr_mm[par])])
                    st_se[par] = now("SE")
                    se_exp = now("SE")

                    def mk_pv(cx=cx, lo=lo, se_exp=se_exp, vbase=vbase, ptr=ptr, par=par, pvb=pvb):
                        for d in range(max(0, cx - 4), min(3, cx) + 1):
                            first = not pv_started[0]; pv_started[0] = True
                            last = (cx == 7 and d == 3)
                            vtile = poolb[:, vbase + (cx % 4) * 520 + 65 * h:
                                          vbase + (cx % 4) * 520 + 65 * h + 65]
                            w_ = [("SE", se_exp)]
                            if first:
                                w_.append(("VE", pv_guard))
                            MMW(ps[0:65, pvb * 512 + 128 * d: pvb * 512 + 128 * d + 128],
                                vtile, ptr[:, 128 * d - lo:128 * d - lo + 128], first, last, waits=w_)
                        ptr_mm[par] = now("MM")
                    pend_pv[0] = mk_pv
                pend_pv[0]()
                pend_pv[0] = None
                mm_pv = now("MM")
                if pend_evac[0] is not None:
                    pend_evac[0]()
                pend_evac[0] = (lambda h=h, pvb=pvb, mm_pv=mm_pv: emit_evac(h, pvb, mm_pv))
            if pend_evac[0] is not None:
                pend_evac[0]()
                pend_evac[0] = None
            ms["stb_se"] = now("SE")

        def wo_stage(w):
            ls_need = stream_phase("B", d_fwo[l])
            ve_o = now("VE")
            for m in range(KC):
                bk = ps[:, (m % 2) * 512:(m % 2) * 512 + 512]
                for kc in range(4):
                    w_ = [("VE", pbank_ve[m % 2])] if kc == 0 else []
                    if m == 0 and kc == 0:
                        w_ += [("LW", ls_need), ("VE", ve_o)]
                    MMW(bk, wslotB[:, kc * 1024 + m * 128: kc * 1024 + m * 128 + 128],
                        hbf_t[:, OW + kc * 512: OW + kc * 512 + 512], kc == 0, kc == 3, waits=w_)
                mmw = now("MM")
                V_(lambda e, o=zc(m, w * 512, 512), b=bk: e.tensor_tensor(o, o, b, op=OP.add),
                   waits=[("MM", mmw)])
                pbank_ve[m % 2] = now("VE")
            slot_guard["B"] = now("MM")
            ms["z_ve"] = now("VE")

        for wi, w in enumerate([3, 2, 1, 0] if "la" in parts else []):
            if wi == 0:
                zb_ve[0] = cast_zb(w)
                proj_q(w)
                proj_k(w)
                proj_v(w)
                ms["zb_mm"] = now("MM")
                zb_ve[0] = cast_zb(w - 1)
                proj_k(w - 1)
                proj_v(w - 1)
                ms["zb_mm"] = now("MM")
            else:
                proj_q(w)
                ms["zb_mm"] = now("MM")
                if w > 0:
                    zb_ve[0] = cast_zb(w - 1)
                    proj_k(w - 1)
                    proj_v(w - 1)
                    ms["zb_mm"] = now("MM")
            attention(w)
            wo_stage(w)
        ms["pbank0_ve"] = pbank_ve[0]
        ms["pbank1_ve"] = pbank_ve[1]
        ms["pool_mm"] = now("MM")
        ms["pool_ve"] = now("VE")

        w1g = [slot_guard["A"], slot_guard["B"]]
        w2g = [0, 0]
        n_w1 = [0]
        n_w2 = [0]

        def load_w1(j):
            sl = n_w1[0] % 2
            p.dma("gpsimd", wst_t[:, sl * 2048:(sl + 1) * 2048], d_fw1[l, j],
                  waits=[("MM", w1g[sl])], inc="LW")
            n_w1[0] += 1
            return p.now("LW")

        def load_w2(mg, j):
            sl = n_w2[0] % 2
            p.dma("gpsimd", wst_t[:, 4096 + sl * 512:4096 + (sl + 1) * 512], d_fw2[l, mg, j],
                  waits=[("MM", w2g[sl])], inc="LW")
            n_w2[0] += 1
            return p.now("LW")

        ag_ve = [now("VE"), now("VE")]
        for th in ((1, 0) if "mlp" in parts else ()):
            t0h = th * 1024
            emit_mlp_ln(l, th, (lambda c0: poolb[:, c0:c0 + 512]), ms["pool_mm"])
            ln_ve = now("VE")
            ln_tc0_ve = ms["mlpln_tc0_ve"]
            load_w1(0)
            slot_of = {0: (n_w1[0] - 1) % 2}
            lw_of = {0: p.now("LW")}
            load_w1(1)
            slot_of[1] = (n_w1[0] - 1) % 2
            lw_of[1] = p.now("LW")
            for j in range(NM):
                sl = slot_of[j]
                w1v = wst_t[:, sl * 2048:(sl + 1) * 2048]
                for tc in range(2):
                    ssb = (j * 2 + tc) % 2
                    agA = ps[:, (ssb * 2) * 512:(ssb * 2) * 512 + 512]
                    agG = ps[:, (ssb * 2 + 1) * 512:(ssb * 2 + 1) * 512 + 512]
                    for k in range(KC):
                        rhs = hbf_t[:, k * 1024 + tc * 512: k * 1024 + tc * 512 + 512]
                        first = (k == 0); lastk = (k == KC - 1)
                        wts = []
                        if k == 0:
                            ln_need = ln_tc0_ve if (j == 0 and tc == 0) else ln_ve
                            wts = [("LW", lw_of[j]), ("VE", max(ag_ve[ssb], ln_need))]
                        MMW(agA, w1v[:, k * 256: k * 256 + 128], rhs, first, lastk, waits=wts)
                        MMW(agG, w1v[:, k * 256 + 128: k * 256 + 256], rhs, first, lastk)
                    w1g[sl] = now("MM")
                    mm_ag = now("MM")
                    glA = rows[:, 0:512] if ssb == 0 else rows[:, 512:1024]
                    S_(lambda e, o=glA, i=agG, b=Vs("f_b1g", l * NM + j):
                       e.activation(o, i, AF.Gelu, bias=b, scale=1.0), waits=[("MM", mm_ag)])
                    V_(lambda e, o=h2ap(j, tc * 512, 512), a=agA, s=Vs("f_b1a", l * NM + j), g_=glA:
                       e.scalar_tensor_tensor(o, a, s, g_, op0=OP.add, op1=OP.mult),
                       waits=[("SE", now("SE"))])
                    ag_ve[ssb] = now("VE")
                if j + 2 < NM:
                    slot_of[j + 2] = n_w1[0] % 2
                    lw_of[j + 2] = load_w1(j + 2)
                elif th == 1 and j + 2 >= NM and (j + 2 - NM) < 2:
                    pass
            h2_ve = now("VE")
            w2g[0] = now("MM"); w2g[1] = now("MM")
            load_w2(0, 0)
            w2slot_of = {(0, 0): (n_w2[0] - 1) % 2}
            w2lw_of = {(0, 0): p.now("LW")}
            load_w2(0, 1)
            w2slot_of[(0, 1)] = (n_w2[0] - 1) % 2
            w2lw_of[(0, 1)] = p.now("LW")
            for mg in range(2):
                for j in range(NM):
                    sl = w2slot_of[(mg, j)]
                    w2v = wst_t[:, 4096 + sl * 512:4096 + (sl + 1) * 512]
                    first = (j == 0); lastj = (j == NM - 1)
                    for m in range(4):
                        for q in range(2):
                            wts = []
                            if m == 0 and q == 0:
                                wts = [("LW", w2lw_of[(mg, j)])]
                                if j == 0:
                                    wts += [("VE", h2_ve), ("VE", ms["w2p_ve"])]
                            MMW(ps[:, m * 1024 + q * 512: m * 1024 + q * 512 + 512],
                                w2v[:, m * 128: m * 128 + 128],
                                h2ap(j, q * 512, 512), first, lastj, waits=wts)
                    w2g[sl] = now("MM")
                    nxt = j + 2
                    if nxt < NM:
                        w2slot_of[(mg, nxt)] = n_w2[0] % 2
                        w2lw_of[(mg, nxt)] = load_w2(mg, nxt)
                    elif mg == 0 and (nxt - NM) < 2:
                        w2slot_of[(1, nxt - NM)] = n_w2[0] % 2
                        w2lw_of[(1, nxt - NM)] = load_w2(1, nxt - NM)
                mm_w2 = now("MM")
                for m in range(4):
                    mo = mg * 4 + m
                    for q in range(2):
                        V_(lambda e, o=zc(mo, t0h + q * 512, 512),
                           i=ps[:, m * 1024 + q * 512: m * 1024 + q * 512 + 512],
                           s=Vs("f_b2", l * KC + mo):
                           e.scalar_tensor_tensor(o, i, s, o, op0=OP.add, op1=OP.add),
                           waits=[("MM", mm_w2)] if (m == 0 and q == 0) else ())
                ms["w2p_ve"] = now("VE")
            ms["z_ve"] = now("VE")
            ms["pool_mm"] = now("MM")
            ag_ve = [now("VE"), now("VE")]
        ms["pbank0_ve"] = now("VE")
        ms["pbank1_ve"] = now("VE")
        slot_guard["A"] = now("MM")
        slot_guard["B"] = now("MM")
        ms["pool_ve"] = now("VE")

    def emit_ffn_module():
        mmg0 = now("MM")
        for c in range(KC):
            V_(lambda e, c=c: e.memset(zfm[:, c * (T + 1):c * (T + 1) + 1], 0.0),
               waits=[("MM", mmg0)] if c == 0 else ())
        emit_outer_ln("z", (3, 2, 1, 0))
        for l in range(DEPTH):
            emit_ffn_layer(l)
        mm_g = now("MM")
        for k in range(KC):
            V_(lambda e, o=xc(k, 0, T), i=zc(k, 0, T):
               e.scalar_tensor_tensor(o, i, 0.5, o, op0=OP.mult, op1=OP.add))
        ms["x_ve"] = now("VE")

    def emit_attn_module():
        emit_outer_ln("ybf")
        p.dma("sync", pool[:, 0:1024], d_bvb, waits=[("MM", ms["pool_mm"]), ("VE", ms["pool_ve"])], inc="LS")
        bvb_ls = now("LS")
        sig = poolb[0:16, 2048:4096]
        QRB = [6144, 8192]
        PTB = [10240, 11264]

        def ybf_t(k, t0, n):
            return zfmb[:, k * T + t0: k * T + t0 + n]

        def vg_ap(tch, lg):
            if tch < 15:
                return zfmb[:, 24576 + tch * 520 + 65 * lg: 24576 + tch * 520 + 65 * lg + 65]
            return poolb[:, 14336 + 65 * lg: 14336 + 65 * lg + 65]

        def vg_full(tch):
            if tch < 15:
                return zfmb[:, 24576 + tch * 520: 24576 + tch * 520 + 520]
            return poolb[:, 14336: 14336 + 520]

        def o_ap(c, r0, t0, n):
            if c < 3:
                return hbf_t[r0:r0 + 64, c * 2048 + t0: c * 2048 + t0 + n]
            return poolb[r0:r0 + 64, 12288 + t0: 12288 + t0 + n]

        R0s = [pool[0:1, 7428:8452], pool[32:33, 7428:8452]]
        R2s = [pool[0:1, 8452:9476], pool[32:33, 8452:9476]]
        upar = [0]
        def o_full(c, qc):
            if c < 3:
                return hbf_t[:, c * 2048 + qc * 512: c * 2048 + qc * 512 + 512]
            return poolb[:, 12288 + qc * 512: 12288 + qc * 512 + 512]

        qr_mm = [0, 0]
        pt_mm = [0, 0]
        st_se_a = [0, 0]
        pend_gchain = [None]
        BQ2 = (0, 1, 4, 5)
        abank = {0: ms.get("pbank0_ve", 0), 1: ms.get("pbank1_ve", 0),
                 4: now("VE"), 5: now("VE")}
        ms["b45_se"] = now("SE")
        ms["b67_att_ve"] = now("VE")

        for hg in attn_groups:
            for k in range(KC):
                p.dma("gpsimd", wslotA[:, k * 512: k * 512 + 512],
                      d_wv[k * 128:(k + 1) * 128, hg * 512:(hg + 1) * 512],
                      waits=[("MM", slot_guard["A"])] if k == 0 else (), inc="LW")
            lw_v = now("LW")
            pendv = []
            for tch in range(16):
                bq = BQ2[tch % 4]
                bk = ps[:, bq * 512:bq * 512 + 512]
                for k in range(KC):
                    w_ = [("VE", abank[bq])] if k == 0 else []
                    if k == 0 and bq >= 4:
                        w_ += [("SE", ms["b45_se"])]
                    if tch == 0 and k == 0:
                        w_ += [("LW", lw_v), ("VE", now("VE")), ("LS", bvb_ls)]
                    MMW(bk, ybf_t(k, tch * 128, 128), wslotA[:, k * 512: k * 512 + 512],
                        k == 0, k == KC - 1, waits=w_)
                mmv = now("MM")

                def mkv(tch=tch, bk=bk, mmv=mmv, bq=bq):
                    dst = vg_full(tch)
                    V_(lambda e, o=dst.rearrange("p (h c) -> p h c", c=65),
                       i=bk.rearrange("p (h c) -> p h c", c=64),
                       bv=pool[:, hg * 512:hg * 512 + 512].rearrange("p (h c) -> p h c", c=64):
                       e.tensor_tensor(o[:, :, 0:64], i, bv, op=OP.add), waits=[("MM", mmv)])
                    V_(lambda e, o=dst.rearrange("p (h c) -> p h c", c=65): e.memset(o[:, :, 64:65], 1.0))
                    abank[bq] = now("VE")
                if len(pendv) >= 2:
                    pendv.pop(0)()
                pendv.append(mkv)
            for f in pendv:
                f()
            slot_guard["A"] = now("MM")
            for k in range(KC):
                p.dma("gpsimd", wslotB[:, k * 512: k * 512 + 512],
                      d_wk[k * 128:(k + 1) * 128, hg * 512:(hg + 1) * 512],
                      waits=[("MM", slot_guard["B"])] if k == 0 else (), inc="LW")
            lw_k = now("LW")

            def norm_chain(bk, bias_col, wcol, out_ap, mm0=None):
                ktmp = pool[:, 7428:7940]
                sqb = poolb[:, 15880:16392]
                if mm0 is None:
                    mm0 = now("MM")
                V_(lambda e, i=bk, b=bias_col: e.tensor_scalar(ktmp, i, b, None, op0=OP.add),
                   waits=[("MM", mm0)])
                V_(lambda e: e.tensor_tensor(sqb, ktmp, ktmp, op=OP.mult))
                MMW(ps[0:2, 2 * 512:2 * 512 + 512], e2b, sqb, True, True,
                    waits=[("VE", now("VE")), ("SE", now("SE"))])
                S_(lambda e: e.activation(RL2, ps[0:2, 2 * 512:2 * 512 + 512], AF.Ln,
                                          bias=epsc[0:2, 2:3], scale=1.0 / HD), waits=[("MM", now("MM"))])
                S_(lambda e: e.activation(RL2B, RL2, AF.Exp, bias=0.0, scale=-0.5))
                ms["b2_se"] = now("SE")
                MMW(ps[:, 3 * 512:4 * 512], e2tb[:, :], RL2B, True, True,
                    waits=[("SE", now("SE")), ("VE", ms["b3_ve"]), ("SE", now("SE"))])
                mmb = now("MM")
                V_(lambda e, o=out_ap, w=wcol: e.scalar_tensor_tensor(o, ktmp, w, ps[:, 3 * 512:4 * 512],
                                                                      op0=OP.mult, op1=OP.mult),
                   waits=[("MM", mmb)])
                ms["b3_ve"] = now("VE")

            pendn = []
            for c in range(4):
                for qc in range(4):
                    idx = c * 4 + qc
                    bq = BQ2[idx % 4]
                    bk = ps[:, bq * 512:bq * 512 + 512]
                    for k in range(KC):
                        w_ = [("VE", abank[bq])] if k == 0 else []
                        if k == 0 and bq >= 4:
                            w_ += [("SE", ms["b45_se"])]
                        if c == 0 and qc == 0 and k == 0:
                            w_ += [("LW", lw_k)]
                        MMW(bk, wslotB[:, k * 512 + c * 128: k * 512 + c * 128 + 128],
                            ybf_t(k, qc * 512, 512), k == 0, k == KC - 1, waits=w_)
                    mm_c = now("MM")

                    def mkn(c=c, qc=qc, bk=bk, bq=bq, mm_c=mm_c):
                        norm_chain(bk, Vs("a_bk", hg * 4 + c), V("a_kn"),
                                   zfmb[:, 16384 + c * 2048 + qc * 512: 16384 + c * 2048 + qc * 512 + 512],
                                   mm0=mm_c)
                        abank[bq] = now("VE")
                    if len(pendn) >= 2:
                        pendn.pop(0)()
                    pendn.append(mkn)
            for f in pendn:
                f()
            slot_guard["B"] = now("MM")
            for k in range(KC):
                p.dma("gpsimd", wslotA[:, k * 512: k * 512 + 512],
                      d_wq[k * 128:(k + 1) * 128, hg * 512:(hg + 1) * 512],
                      waits=[("MM", slot_guard["A"])] if k == 0 else (), inc="LW")
            lw_q = now("LW")
            if hg == 0:
                for k in range(KC):
                    p.dma("gpsimd", wslotB[:, k * 128: k * 128 + 128],
                          d_wq[k * 128:(k + 1) * 128, 1024:1152],
                          waits=[("MM", slot_guard["B"])] if k == 0 else (), inc="LW")
                lw_g = now("LW")
                for qc in range(4):
                    bk = ps[:, (qc % 2) * 512:(qc % 2) * 512 + 512]
                    for k in range(KC):
                        w_ = [("VE", abank[qc % 2]), ("SE", now("SE"))] if k == 0 else []
                        if qc == 0 and k == 0:
                            w_ += [("LW", lw_g)]
                        MMW(bk, wslotB[:, k * 128: k * 128 + 128], ybf_t(k, qc * 512, 512),
                            k == 0, k == KC - 1, waits=w_)
                    mmg = now("MM")
                    S_(lambda e, o=sig[:, qc * 512: qc * 512 + 512], i=bk[0:16, :]:
                       e.activation(o, i, AF.Sigmoid, bias=V("a_bgate")[0:16, :], scale=1.0),
                       waits=[("MM", mmg)])
                    abank[qc % 2] = now("MM") * 0 + abank[qc % 2]
                gate_se = now("SE")
                slot_guard["B"] = now("MM")
            sig_se = now("SE")
            for c in range(4):
                qsl = c % 2
                qbase = QRB[qsl]
                pendq = [None]
                for qc in range(4):
                    bq = BQ2[qc]
                    bk = ps[:, bq * 512:bq * 512 + 512]
                    for k in range(KC):
                        w_ = [("VE", abank[bq]), ("SE", now("SE"))] if k == 0 else []
                        if qc == 0 and k == 0:
                            w_ += [("LW", lw_q), ("MM", qr_mm[qsl])]
                        MMW(bk, wslotA[:, k * 512 + c * 128: k * 512 + c * 128 + 128],
                            ybf_t(k, qc * 512, 512), k == 0, k == KC - 1, waits=w_)
                    mm_c = now("MM")
                    if pendq[0] is not None:
                        pendq[0]()

                    def mkq(qc=qc, bk=bk, c=c, bq=bq, mm_c=mm_c):
                        norm_chain(bk, Vs("a_bq", hg * 4 + c), pcol[:, PC_QN8:PC_QN8 + 1],
                                   poolb[:, qbase + qc * 512: qbase + qc * 512 + 512],
                                   mm0=mm_c)
                        abank[bq] = now("VE")
                    pendq[0] = mkq
                pendq[0]()
                ve_pair = now("VE")
                for lh in range(2):
                    h = 2 * c + lh
                    hh = hg * 8 + h
                    r0 = 64 * lh
                    for qh in range(2):
                        pend_pv = [None]
                        pv_started = [False]
                        for kc in range(16):
                            stpar = kc % 2
                            stbase = (2 + 2 * stpar) * 512
                            for qq in range(2):
                                MMW(ps[:, stbase + qq * 512: stbase + qq * 512 + 512],
                                    zfmb[r0:r0 + 64, 16384 + c * 2048 + kc * 128: 16384 + c * 2048 + kc * 128 + 128],
                                    poolb[r0:r0 + 64, qbase + qh * 1024 + qq * 512: qbase + qh * 1024 + qq * 512 + 512],
                                    True, True,
                                    waits=([("VE", ve_pair), ("SE", st_se_a[stpar]), ("SE", ms["b2_se"])]
                                           if qq == 0 else ()))
                            mm_st = now("MM")
                            if pend_pv[0] is not None:
                                pend_pv[0]()
                                pend_pv[0] = None
                            if kc == 0 and pend_gchain[0] is not None:
                                pend_gchain[0]()
                                pend_gchain[0] = None
                            ptr = poolb[:, PTB[stpar]:PTB[stpar] + 1024]
                            se_exps = []
                            for qq in range(2):
                                S_(lambda e, o=ptr[:, qq * 512: qq * 512 + 512],
                                   i=ps[:, stbase + qq * 512: stbase + qq * 512 + 512]:
                                   e.activation(o, i, AF.Exp, bias=pcol[:, PC_BA:PC_BA + 1], scale=1.0),
                                   waits=[("MM", mm_st), ("MM", pt_mm[stpar])] if qq == 0 else ())
                                se_exps.append(now("SE"))
                            st_se_a[stpar] = now("SE")

                            def mk_pv(kc=kc, h=h, ptr=ptr, stpar=stpar, se_exps=tuple(se_exps)):
                                for qq in range(2):
                                    first = (kc == 0)
                                    w_ = [("SE", se_exps[qq])]
                                    if first and qq == 0:
                                        w_.append(("VE", ms["b67_att_ve"]))
                                    MMW(ps[0:65, 6 * 512 + qq * 512: 6 * 512 + qq * 512 + 512],
                                        vg_ap(kc, h), ptr[:, qq * 512: qq * 512 + 512],
                                        first, kc == 15, waits=w_)
                                pt_mm[stpar] = now("MM")
                            pend_pv[0] = mk_pv
                        pend_pv[0]()
                        pend_pv[0] = None
                        mm_pv = now("MM")
                        up = upar[0]; upar[0] ^= 1
                        R0 = R0s[up]; R2 = R2s[up]
                        V_(lambda e, o=o_ap(c, r0, qh * 1024, 1024), i=ps[0:64, 6 * 512: 6 * 512 + 1024]:
                           e.tensor_copy(o, i), waits=[("MM", mm_pv)])
                        V_(lambda e, R2=R2: e.tensor_copy(R2, ps[64:65, 6 * 512: 6 * 512 + 1024]))
                        ms["b67_att_ve"] = now("VE")
                        ve_sum = now("VE")
                        if pend_gchain[0] is not None:
                            pend_gchain[0]()

                        def mk_gchain(c=c, r0=r0, qh=qh, hh=hh, ve_sum=ve_sum, R0=R0, R2=R2, up=up):
                            rbase = 32 * up
                            for qq in range(2):
                                MMW(ps[0:1, qq * 512: qq * 512 + 512],
                                    identg[:, hh:hh + 1], sig[:, qh * 1024 + qq * 512: qh * 1024 + qq * 512 + 512],
                                    True, True,
                                    waits=[("VE", ve_sum), ("SE", sig_se), ("VE", abank[qq])] if qq == 0
                                          else [("VE", abank[qq])])
                            mm_gx = now("MM")
                            V_(lambda e: e.tensor_copy(R0, ps[0:1, 0:1024]),
                               waits=[("MM", mm_gx)])
                            S_(lambda e: e.activation(R2, R2, AF.Ln, bias=0.0, scale=1.0),
                               waits=[("VE", now("VE"))])
                            S_(lambda e: e.activation(R2, R2, AF.Exp, bias=0.0, scale=-1.0))
                            R0B = poolb[rbase:rbase + 1, 2 * 9504: 2 * 9504 + 1024]
                            V_(lambda e, R0B=R0B: e.tensor_tensor(R0B, R0, R2, op=OP.mult),
                               waits=[("SE", now("SE"))])
                            ve_row = now("VE")
                            for qq in range(2):
                                MMW(ps[0:64, qq * 512: qq * 512 + 512],
                                    cb[rbase:rbase + 1, 520:584],
                                    R0B[:, qq * 512: qq * 512 + 512],
                                    True, True, waits=[("VE", ve_row)] if qq == 0 else ())
                            mm_g2 = now("MM")
                            V_(lambda e, o=o_ap(c, r0, qh * 1024, 1024):
                               e.tensor_tensor(o, o, ps[0:64, 0:1024], op=OP.mult),
                               waits=[("MM", mm_g2)])
                            abank[0] = now("VE")
                            abank[1] = now("VE")
                        pend_gchain[0] = mk_gchain
                if pend_gchain[0] is not None:
                    pend_gchain[0]()
                    pend_gchain[0] = None
                qr_mm[qsl] = now("MM")
                ms["b45_se"] = now("SE")
            slot_guard["A"] = now("MM")
            for kc in range(4):
                p.dma("gpsimd", wslotB[:, kc * 1024: kc * 1024 + 1024],
                      d_wo[hg * 512 + kc * 128: hg * 512 + kc * 128 + 128, :],
                      waits=[("MM", slot_guard["B"])] if kc == 0 else (), inc="LW")
            lw_o = now("LW")
            ve_o = now("VE")
            for m in range(KC):
                for qc in range(4):
                    bk = ps[:, (qc % 2) * 512:(qc % 2) * 512 + 512]
                    for kc in range(4):
                        w_ = [("VE", abank[qc % 2])] if kc == 0 else []
                        if m == 0 and qc == 0 and kc == 0:
                            w_ += [("LW", lw_o), ("VE", ve_o)]
                        MMW(bk, wslotB[:, kc * 1024 + m * 128: kc * 1024 + m * 128 + 128],
                            o_full(kc, qc),
                            kc == 0, kc == 3, waits=w_)
                    mmw = now("MM")
                    if hg == 0:
                        V_(lambda e, o=xc(m, qc * 512, 512), b=bk, s=Vs("a_bo", m):
                           e.scalar_tensor_tensor(o, b, s, o, op0=OP.add, op1=OP.add),
                           waits=[("MM", mmw)])
                    else:
                        V_(lambda e, o=xc(m, qc * 512, 512), b=bk: e.tensor_tensor(o, o, b, op=OP.add),
                           waits=[("MM", mmw)])
                    abank[qc % 2] = now("VE")
            slot_guard["B"] = now("MM")
        ms["pbank0_ve"] = now("VE")
        ms["pbank1_ve"] = now("VE")
        ms["pool_mm"] = now("MM")
        ms["pool_ve"] = now("VE")
        ms["x_ve"] = now("VE")

    dbg = [s for s in stages if s.startswith("dbg_")]
    if dbg and dbg[0] == "dbg_attn":
        emit_attn_module()
        d_dbg = nc.dram_tensor("dbg", [128, 32768], BF16, kind="ExternalOutput").ap()
        d_dbg2 = nc.dram_tensor("dbg2", [128, 3072], F32, kind="ExternalOutput").ap()
        wts = [("VE", now("VE")), ("MM", now("MM")), ("SE", now("SE"))]
        p.dma("sync", d_dbg[:, 0:16384], zfmb[:, 0:16384], waits=wts, inc="LS")
        p.dma("sync", d_dbg[:, 16384:24576], zfmb[:, 16384:24576], inc="LS")
        p.dma("sync", d_dbg[:, 24576:26624], poolb[:, 6144:8192], inc="LS")
        p.dma("sync", d_dbg[:, 26624:28672], poolb[:, 8192:10240], inc="LS")
        p.dma("sync", d_dbg[:, 28672:30720], hbf_t[:, 0:2048], inc="LS")
        p.dma("sync", d_dbg2[:, 0:2048], pool[:, 1024:3072], inc="LS")
        p.dma("sync", d_dbg2[:, 2048:3072], pool[:, 0:1024], inc="LS")
        dbg = []
        dbg_tap = None
    elif dbg:
        sub = dbg[0][4:]
        emit_outer_ln("z")
        if sub in ("l0a", "l0"):
            emit_ffn_layer(0, parts=("la",) if sub == "l0a" else ("la", "mlp"))
        dbg_tap = "z"
        d_dbg = nc.dram_tensor("dbg", [128, 32768], BF16, kind="ExternalOutput").ap()
        d_dbg2 = nc.dram_tensor("dbg2", [128, 64], F32, kind="ExternalOutput").ap()
        p.dma("sync", d_dbg[:, 0:8192], poolb[:, 0:8192], waits=[("VE", now("VE")), ("MM", now("MM")), ("SE", now("SE"))], inc="LS")
        p.dma("sync", d_dbg[:, 8192:10240], poolb[:, QWIN:QWIN + 2048], inc="LS")
        p.dma("sync", d_dbg[:, 10240:14336], poolb[:, KRING:KRING + 4096], inc="LS")
        p.dma("sync", d_dbg[:, 14336:18496], poolb[:, VRING:VRING + 4160], inc="LS")
        p.dma("sync", d_dbg[:, 18496:22600], hbf_t[:, 0:4104], inc="LS")
        p.dma("sync", d_dbg[:, 22600:24648], hbf_t[:, OW:OW + 2048], inc="LS")
        p.dma("sync", d_dbg[:, 24648:25672], hbf_t[:, PTR:PTR + 1024], inc="LS")
        p.dma("sync", d_dbg2[:, 0:8], pcol[:], inc="LS")
    else:
        if "ffn0" in stages:
            emit_ffn_module()
        if "attn" in stages:
            emit_attn_module()
        if "ffn1" in stages:
            emit_ffn_module()
    io_e = pool[:, 2048:2176]
    ident_e = pool[:, 2176:2304]
    p.emit("gpsimd", lambda e: e.iota(io_e, [[1, 128]], channel_multiplier=-1,
                                      allow_small_or_imprecise_dtypes=True),
           waits=[("MM", ms["pool_mm"]), ("VE", ms["pool_ve"]), ("LS", now("LS"))], inc="GP")
    V_(lambda e: e.tensor_scalar(ident_e, io_e, 0.0, 1.0, op0=OP.is_equal, op1=OP.mult),
       waits=[("GP", p.now("GP"))])
    ve_id = now("VE")
    tout = [pool[:, 0:1024], pool[:, 1024:2048]]
    st_ls = [0, 0]

    def src_ap(k, tc):
        if dbg_tap == "z":
            return zc(k, tc * 128, 128)
        return xfm[:, k * T + tc * 128: k * T + tc * 128 + 128]

    for tc in range(16):
        to = tout[tc % 2]
        for k in range(KC):
            bkp = ps[:, (k % 2) * 512:(k % 2) * 512 + 512]
            T_(lambda e, o=bkp[:, 0:128], i=src_ap(k, tc), idn=ident_e: e.transpose(o, i, idn),
               waits=([("VE", ve_id), ("LS", st_ls[tc % 2])] if k == 0 else [("VE", now("VE"))]))
            V_(lambda e, o=to[:, k * 128: k * 128 + 128], i=bkp[:, 0:128]: e.tensor_copy(o, i),
               waits=[("MM", now("MM"))])
        p.dma("sync", d_out[tc * 128:(tc + 1) * 128, :], to, waits=[("VE", now("VE"))], inc="LS")
        st_ls[tc % 2] = now("LS")
    p.emit("sync", lambda e: e.nop(), waits=[("LS", now("LS"))])
    p.finish()
    for t in reversed(ctx):
        t.__exit__(None, None, None)
    return nc



def kernel(**inputs):
    staged, vix, nv = _stage_weights(inputs)
    nc = build_program(nv, vix)
    x = np.asarray(inputs["L0_attn_x"], np.float32)
    in_maps = []
    for i in range(B):
        m = dict(staged)
        m["xin"] = np.ascontiguousarray(x[i])
        in_maps.append(m)
    res = run_bass_kernel_spmd(nc, in_maps, core_ids=list(range(B)))
    return np.stack([np.asarray(r["out"], np.float32) for r in res.results], 0)
